# version 03
# baseline (speedup 1.0000x reference)
import numpy as np
import ml_dtypes
import contextlib
import concourse.bass as bass
import concourse.mybir as mybir
from concourse.bass_utils import run_bass_kernel_spmd

F32 = mybir.dt.float32
BF16 = mybir.dt.bfloat16
AF = mybir.ActivationFunctionType
ALU = mybir.AluOpType
BF = ml_dtypes.bfloat16

NC = 8
B = 2
S = 8192
D = 1024
DI = 2048
DIC = DI // NC
NST = 16
HMOE = 4096
E = 8
TOK = (B * S) // NC
NH, HS = 16, 64
EPS = 1e-5
EM10 = float(np.exp(np.float64(-10.0)))

ENGS = ("sync", "tensor", "vector", "scalar", "gpsimd")


class Prog:

    def __init__(self, nc, block, tick, dsems):
        self.nc = nc
        self.block = block
        self.tick = tick
        self.k = 0
        self.dsems = dsems
        self.dctr = {e: 0 for e in dsems}
        self.ccn = 0

    def step(self, **work):
        self.k += 1
        prev = 5 * (self.k - 1)
        for name in ENGS:
            fn = work.get(name)

            def emit(eng, fn=fn):
                eng.wait_ge(self.tick, prev)
                last = fn(eng) if fn is not None else None
                if last is None:
                    last = eng.nop()
                last.then_inc(self.tick, 1)
            getattr(self.block, name)(emit)

    def dmas(self, eng_name):
        sem = self.dsems[eng_name]

        def go(eng, transfers, **kw):
            for out, in_ in transfers:
                eng.dma_start(out=out, in_=in_, **kw).then_inc(sem, 16)
                self.dctr[eng_name] += 16
            eng.wait_ge(sem, self.dctr[eng_name])
            return eng.nop()
        return go


def alloc(nc, st, sbufs=(), psums=(), sems=()):
    d = {}
    for name, shape, dt in sbufs:
        d[name] = st.enter_context(nc.sbuf_tensor(name, list(shape), dt))
    for name, shape, dt in psums:
        d[name] = st.enter_context(nc.psum_tensor(name, list(shape), dt))
    for name in sems:
        d[name] = st.enter_context(nc.semaphore(name))
    return d



def _col8(v):
    return np.ascontiguousarray(v.reshape(8, 128).T)


def host_prep(I):
    P = {}
    x = I["L0_attn_x"]
    xf = np.asarray(x).reshape(B * S, D)
    P["x_fm"] = [np.ascontiguousarray(xf[c * TOK:(c + 1) * TOK].T) for c in range(NC)]
    P["lng"] = _col8(I["L0_attn_ln_g"])
    P["lnb"] = _col8(I["L0_attn_ln_b"])
    in_w = I["L0_attn_in_w"]
    P["inwT"] = []
    for c in range(NC):
        rows = np.concatenate([in_w[DIC * c:DIC * (c + 1)],
                               in_w[DI + DIC * c:DI + DIC * (c + 1)]], axis=0)
        P["inwT"].append(np.ascontiguousarray(rows.T))
    cw_ = I["L0_attn_conv_w"][:, 0, :]
    P["convw"] = [np.ascontiguousarray(
        cw_[DIC * c:DIC * (c + 1)].reshape(2, 128, 4).transpose(1, 0, 2).reshape(128, 8))
        for c in range(NC)]
    P["convb"] = [np.ascontiguousarray(
        I["L0_attn_conv_b"][DIC * c:DIC * (c + 1)].reshape(2, 128).T) for c in range(NC)]
    xp_w = I["L0_attn_xproj_w"]
    P["xprojT"] = [np.ascontiguousarray(
        xp_w[:, DIC * c:DIC * (c + 1)].T.reshape(2, 128, 33).transpose(1, 0, 2).reshape(128, 66))
        for c in range(NC)]
    P["dtw"] = [np.ascontiguousarray(I["L0_attn_dt_w"][DIC * c:DIC * (c + 1), 0]
                                     .reshape(2, 128).T) for c in range(NC)]
    P["dtb"] = [np.ascontiguousarray(I["L0_attn_dt_b"][DIC * c:DIC * (c + 1)]
                                     .reshape(2, 128).T) for c in range(NC)]
    P["Dc"] = [np.ascontiguousarray(I["L0_attn_D"][DIC * c:DIC * (c + 1)]
                                    .reshape(2, 128).T) for c in range(NC)]
    P["A_log"] = np.ascontiguousarray(I["L0_attn_A_log"].reshape(1, 16))
    P["outwT"] = np.ascontiguousarray(I["L0_attn_out_w"].T)
    P["gate_w"] = np.ascontiguousarray(I["L0_ffn_gate_w"])
    P["w1"] = I["L0_ffn_w1"]
    P["b1c"] = np.ascontiguousarray(
        I["L0_ffn_b1"].reshape(8, 32, 128).transpose(2, 0, 1).reshape(128, 8 * 32))
    P["w2"] = I["L0_ffn_w2"]
    P["b2row"] = np.ascontiguousarray(I["L0_ffn_b2"].reshape(1, 8 * 1024))
    P["gamma"] = _col8(I["L0_ffn_gamma"])
    P["beta"] = _col8(I["L0_ffn_beta"])
    P["wattnT"] = np.ascontiguousarray(I["L1_attn_w_attn"].T)
    P["wprojT"] = np.ascontiguousarray(I["L1_attn_w_proj"].T)
    P["W_base"] = np.ascontiguousarray(I["L1_ffn_W_base"])
    P["bbase"] = _col8(I["L1_ffn_b_base"])
    P["Wd1"] = I["L1_ffn_Wd1"]
    P["bd1c"] = np.ascontiguousarray(
        I["L1_ffn_bd1"].reshape(8, 2, 128).transpose(2, 0, 1).reshape(128, 16))
    P["Wd2"] = I["L1_ffn_Wd2"]
    P["bd2c"] = np.ascontiguousarray(
        I["L1_ffn_bd2"].reshape(8, 8, 128).transpose(2, 0, 1).reshape(128, 64))
    P["Wad"] = I["L1_ffn_Wa_down"]
    P["badc"] = np.ascontiguousarray(I["L1_ffn_ba_down"].T)
    P["Wau"] = I["L1_ffn_Wa_up"]
    P["baup8"] = np.ascontiguousarray(I["L1_ffn_ba_up"])
    selm2 = np.zeros((8, 2), np.float32)
    selm2[0:4, 0] = 1.0
    selm2[4:8, 1] = 1.0
    P["selb"] = selm2
    selm = np.zeros((16, 16 * 128), np.float32)
    for n in range(16):
        selm[n, n * 128:(n + 1) * 128] = 1.0
    P["sel"] = selm
    m = np.zeros((4, 128, 512), np.float32)
    for r in range(4):
        for p in range(128):
            m[r, p, :] = (np.arange(512) >= p + 128 * r)
    P["dmask"] = np.ascontiguousarray(m.reshape(4 * 128, 512))
    return P



def emit_ln_stats(prog, x_tile, sq_sb, ps_mu, ps_sq, ones_col,
                  mvrow, arow, brow, eps_col, ncols=512, n_chunks=4, nd=8,
                  nfeat=1024):
    inv = 1.0 / nfeat
    G = sq_sb.shape[1]
    for n in range(n_chunks):
        n_sl = slice(n * ncols, (n + 1) * ncols)
        for g0 in range(0, nd, G):
            def sqs(s, g0=g0, n_sl=n_sl):
                last = None
                for i in range(G):
                    last = s.activation(sq_sb[:, i, :], x_tile(g0 + i, n_sl), AF.Square)
                return last
            prog.step(scalar=sqs)

            def mms(t, g0=g0, n_sl=n_sl):
                last = None
                for i in range(G):
                    j = g0 + i
                    t.matmul(ps_mu[0:1, :], ones_col[:, 0:1], x_tile(j, n_sl),
                             start=(j == 0), stop=(j == nd - 1))
                    last = t.matmul(ps_sq[0:1, :], ones_col[:, 0:1], sq_sb[:, i, :],
                                    start=(j == 0), stop=(j == nd - 1))
                return last
            prog.step(tensor=mms)

        mu = mvrow[0:1, 0:ncols]
        va = mvrow[0:1, ncols:2 * ncols]

        def stats(v, n_sl=n_sl, mu=mu, va=va):
            v.tensor_scalar(mu, ps_mu[0:1, :], inv, None, ALU.mult)
            v.tensor_scalar(va, ps_sq[0:1, :], inv, None, ALU.mult)
            v.tensor_tensor(arow[0:1, n_sl], mu, mu, ALU.mult)
            return v.tensor_tensor(va, va, arow[0:1, n_sl], ALU.subtract)
        prog.step(vector=stats)

        def rstd(s, n_sl=n_sl, va=va):
            s.activation(arow[0:1, n_sl], va, AF.Ln, bias=eps_col[0:1, 0:1])
            return s.activation(arow[0:1, n_sl], arow[0:1, n_sl], AF.Exp, scale=-0.5)
        prog.step(scalar=rstd)

        def bro(v, n_sl=n_sl, mu=mu):
            return v.scalar_tensor_tensor(brow[0:1, n_sl], mu, -1.0,
                                          arow[0:1, n_sl], ALU.mult, ALU.mult)
        prog.step(vector=bro)


def emit_ln_apply(prog, x_tile, out_fns, ps_a, ps_b, ones_row, arow, brow, tmp_sb,
                  ncols=512, n_chunks=4, nd=8):
    for n in range(n_chunks):
        n_sl = slice(n * ncols, (n + 1) * ncols)

        def bcast(t, n_sl=n_sl):
            t.matmul(ps_a[:, :], ones_row[0:1, :], arow[0:1, n_sl], start=True, stop=True)
            return t.matmul(ps_b[:, :], ones_row[0:1, :], brow[0:1, n_sl],
                            start=True, stop=True)
        prog.step(tensor=bcast)

        def apply(v, n=n, n_sl=n_sl):
            last = None
            for j in range(nd):
                v.tensor_tensor(tmp_sb[:, 0:ncols], x_tile(j, n_sl), ps_a[:, :], ALU.mult)
                v.tensor_tensor(tmp_sb[:, 0:ncols], tmp_sb[:, 0:ncols], ps_b[:, :], ALU.add)
                last = out_fns(j, n, n_sl, v, tmp_sb[:, 0:ncols])
            return last
        prog.step(vector=apply)


def emit_ln_fused(prog, x_tile, out_fn, sq_sb, ps_mu, ps_sq, ps_a, ps_b,
                  ones_col, mv4, eps_col, ncols=512, n_chunks=4, nd=8, nfeat=1024,
                  post_chunk=None):
    inv = 1.0 / nfeat
    G = sq_sb.shape[1]
    for n in range(n_chunks):
        n_sl = slice(n * ncols, (n + 1) * ncols)
        for g0 in range(0, nd, G):
            def sqs(s, g0=g0, n_sl=n_sl):
                last = None
                for i in range(G):
                    last = s.activation(sq_sb[:, i, :], x_tile(g0 + i, n_sl), AF.Square)
                return last
            prog.step(scalar=sqs)

            def mms(t, g0=g0, n_sl=n_sl):
                last = None
                for i in range(G):
                    j = g0 + i
                    t.matmul(ps_mu[0:1, :], ones_col[:, 0:1], x_tile(j, n_sl),
                             start=(j == 0), stop=(j == nd - 1))
                    last = t.matmul(ps_sq[0:1, :], ones_col[:, 0:1], sq_sb[:, i, :],
                                    start=(j == 0), stop=(j == nd - 1))
                return last
            prog.step(tensor=mms)

        mu = mv4[0:1, 0:ncols]
        va = mv4[0:1, ncols:2 * ncols]
        ar = mv4[0:1, 2 * ncols:3 * ncols]
        br = mv4[0:1, 3 * ncols:4 * ncols]

        def stats(v, mu=mu, va=va, ar=ar):
            v.tensor_scalar(mu, ps_mu[0:1, :], inv, None, ALU.mult)
            v.tensor_scalar(va, ps_sq[0:1, :], inv, None, ALU.mult)
            v.tensor_tensor(ar, mu, mu, ALU.mult)
            return v.tensor_tensor(va, va, ar, ALU.subtract)
        prog.step(vector=stats)

        def rstd(s, va=va, ar=ar):
            s.activation(ar, va, AF.Ln, bias=eps_col[0:1, 0:1])
            return s.activation(ar, ar, AF.Exp, scale=-0.5)
        prog.step(scalar=rstd)

        def bro(v, mu=mu, ar=ar, br=br):
            return v.scalar_tensor_tensor(br, mu, -1.0, ar, ALU.mult, ALU.mult)
        prog.step(vector=bro)

        def bcast(t, ar=ar, br=br):
            t.matmul(ps_a[:], ones_col[0:1, 0:128], ar, start=True, stop=True)
            return t.matmul(ps_b[:], ones_col[0:1, 0:128], br, start=True, stop=True)
        prog.step(tensor=bcast)

        def apply(v, n=n, n_sl=n_sl):
            last = None
            for j in range(nd):
                last = out_fn(j, n, n_sl, v, x_tile(j, n_sl), ps_a, ps_b)
            return last
        prog.step(vector=apply)
        if post_chunk is not None:
            post_chunk(prog, n, n_sl)



def build_full(nc):
    x_in = nc.dram_tensor("x_fm", [D, TOK], F32, kind="ExternalInput").ap()
    lng = nc.dram_tensor("lng", [128, 8], F32, kind="ExternalInput").ap()
    lnb = nc.dram_tensor("lnb", [128, 8], F32, kind="ExternalInput").ap()
    inwT = nc.dram_tensor("inwT", [D, 4 * 128], F32, kind="ExternalInput").ap()
    convw = nc.dram_tensor("convw", [128, 8], F32, kind="ExternalInput").ap()
    convb = nc.dram_tensor("convb", [128, 2], F32, kind="ExternalInput").ap()
    xprojT = nc.dram_tensor("xprojT", [128, 66], F32, kind="ExternalInput").ap()
    alog = nc.dram_tensor("A_log", [1, 16], F32, kind="ExternalInput").ap()
    dtw = nc.dram_tensor("dtw", [128, 2], F32, kind="ExternalInput").ap()
    dtb = nc.dram_tensor("dtb", [128, 2], F32, kind="ExternalInput").ap()
    Dc = nc.dram_tensor("Dc", [128, 2], F32, kind="ExternalInput").ap()
    sel = nc.dram_tensor("sel", [16, 16 * 128], F32, kind="ExternalInput").ap()
    outwT = nc.dram_tensor("outwT", [DI, D], F32, kind="ExternalInput").ap()
    gamma = nc.dram_tensor("gamma", [128, 8], F32, kind="ExternalInput").ap()
    beta = nc.dram_tensor("beta", [128, 8], F32, kind="ExternalInput").ap()
    gate_w = nc.dram_tensor("gate_w", [D, 8], F32, kind="ExternalInput").ap()
    w1 = nc.dram_tensor("w1", [E, D, HMOE], F32, kind="ExternalInput").ap()
    b1c = nc.dram_tensor("b1c", [128, E * 32], F32, kind="ExternalInput").ap()
    w2 = nc.dram_tensor("w2", [E, HMOE, D], F32, kind="ExternalInput").ap()
    b2row = nc.dram_tensor("b2row", [1, E * D], F32, kind="ExternalInput").ap()
    wattnT = nc.dram_tensor("wattnT", [D, 3 * D], F32, kind="ExternalInput").ap()
    ident = nc.dram_tensor("ident", [128, 128], F32, kind="ExternalInput").ap()
    dmask = nc.dram_tensor("dmask", [4 * 128, 512], F32, kind="ExternalInput").ap()
    wprojT = nc.dram_tensor("wprojT", [D, D], F32, kind="ExternalInput").ap()
    Wb = nc.dram_tensor("W_base", [D, D], F32, kind="ExternalInput").ap()
    bbase = nc.dram_tensor("bbase", [128, 8], F32, kind="ExternalInput").ap()
    Wd1 = nc.dram_tensor("Wd1", [E, D, 256], F32, kind="ExternalInput").ap()
    bd1c = nc.dram_tensor("bd1c", [128, 16], F32, kind="ExternalInput").ap()
    Wd2 = nc.dram_tensor("Wd2", [E, 256, D], F32, kind="ExternalInput").ap()
    bd2c = nc.dram_tensor("bd2c", [128, 64], F32, kind="ExternalInput").ap()
    selb = nc.dram_tensor("selb", [E, 2], F32, kind="ExternalInput").ap()
    badc = nc.dram_tensor("badc", [64, E], F32, kind="ExternalInput").ap()
    Wad = nc.dram_tensor("Wad", [E, D, 64], F32, kind="ExternalInput").ap()
    Wau = nc.dram_tensor("Wau", [E, 64, D], F32, kind="ExternalInput").ap()
    baup8 = nc.dram_tensor("baup8", [E, D], F32, kind="ExternalInput").ap()
    out_d = nc.dram_tensor("out_slab", [D, TOK], F32, kind="ExternalOutput").ap()

    xn0_loc_h = nc.dram_tensor("xn0_loc", [D, TOK], BF16, kind="Internal")
    xn0_all_h = nc.dram_tensor("xn0_all", [NC, D, TOK], BF16, kind="Internal")
    xs_scr = nc.dram_tensor("xs_scr", [2 * DIC, S], F32, kind="Internal").ap()
    z_scr = nc.dram_tensor("z_scr", [2 * DIC, S], BF16, kind="Internal").ap()
    xp_loc_h = nc.dram_tensor("xp_loc", [66, S], F32, kind="Internal")
    xp_sum_h = nc.dram_tensor("xp_sum", [66, S], F32, kind="Internal")
    y_loc_h = nc.dram_tensor("y_loc", [NC, 2 * 128, TOK], BF16, kind="Internal")
    y_all_h = nc.dram_tensor("y_all", [NC * NC, 2 * 128, TOK], BF16, kind="Internal")
    x2_scr = nc.dram_tensor("x2_scr", [D, TOK], F32, kind="Internal").ap()
    y_cols = nc.dram_tensor("y_cols", [DI, TOK], BF16, kind="Internal").ap()
    qk_loc_h = nc.dram_tensor("qk_loc", [4, 8, 64, TOK], BF16, kind="Internal")
    qk_all_h = nc.dram_tensor("qk_all", [32, 8, 64, TOK], BF16, kind="Internal")
    qk_p = nc.dram_tensor("qk_p", [4, 8, 64, TOK], BF16, kind="Internal").ap()
    v_loc_h = nc.dram_tensor("v_loc", [16, 128, 16, 64], BF16, kind="Internal")
    v_all_h = nc.dram_tensor("v_all", [NC * 16, 128, 16, 64], BF16, kind="Internal")
    v_p = nc.dram_tensor("v_p", [16, 128, 16, 64], BF16, kind="Internal").ap()
    yat_loc_h = nc.dram_tensor("yat_loc", [4, 256, TOK], BF16, kind="Internal")
    yat_all_h = nc.dram_tensor("yat_all", [NC * 4, 256, TOK], BF16, kind="Internal")
    yat_cols = nc.dram_tensor("yat_cols", [D, TOK], BF16, kind="Internal").ap()
    x3b_scr = nc.dram_tensor("x3b_scr", [D, TOK], BF16, kind="Internal").ap()
    bx3_scr = nc.dram_tensor("bx3_scr", [D, TOK], F32, kind="Internal").ap()
    lrow_loc_h = nc.dram_tensor("lrow_loc", [1, E], F32, kind="Internal")
    lparts_all_h = nc.dram_tensor("lparts_all", [NC, E], F32, kind="Internal")
    l2scr = nc.dram_tensor("l2scr", [2, E], F32, kind="Internal").ap()

    xn0_loc = xn0_loc_h.ap()
    xn0_all = xn0_all_h.ap()
    xp_loc = xp_loc_h.ap()
    xp_sum = xp_sum_h.ap()
    y_loc = y_loc_h.ap()
    y_all = y_all_h.ap()
    qk_loc = qk_loc_h.ap()
    qk_all = qk_all_h.ap()
    v_loc = v_loc_h.ap()
    v_all = v_all_h.ap()
    yat_loc = yat_loc_h.ap()
    yat_all = yat_all_h.ap()
    lrow_loc = lrow_loc_h.ap()
    lparts_all = lparts_all_h.ap()

    st0 = contextlib.ExitStack()
    sh = alloc(nc, st0, sbufs=[
        ("ones", (128, 128), F32),
        ("idn", (128, 128), F32),
        ("consts", (128, 4), F32),
    ], psums=[(f"ps{i}", (128, 512), F32) for i in range(8)],
       sems=["tick", "dm_sync", "dm_gp", "cc_sem"])
    ones, idn, consts = sh["ones"], sh["idn"], sh["consts"]
    ps = [sh[f"ps{i}"] for i in range(8)]
    tick, dm_sync, dm_gp, cc_sem = sh["tick"], sh["dm_sync"], sh["dm_gp"], sh["cc_sem"]

    with nc.Block() as block:
        prog = Prog(nc, block, tick, {"sync": dm_sync, "gpsimd": dm_gp})
        dma_s = prog.dmas("sync")
        dma_g = prog.dmas("gpsimd")

        def cc(kind, op, in_h, out_h):
            def go(g):
                prog.ccn += 1
                g.collective_compute(
                    kind, op, replica_groups=[list(range(NC))],
                    ins=[in_h.ap().opt()], outs=[out_h.ap().opt()],
                ).then_inc(cc_sem)
                g.wait_ge(cc_sem, prog.ccn)
                return g.nop()
            prog.step(gpsimd=go)

        dyn = {}

        def mkidx_s(e):
            rank = e.partition_id()
            r4 = rank // 4
            dyn["r4"] = r4
            base = rank + 12 * r4
            dyn["qv"] = [base + 4 * ji for ji in range(4)]
            return None

        def mkidx_g(g):
            rank_g = g.partition_id()
            dyn["ym"] = [rank_g + 8 * j for j in range(8)]
            return None

        prog.step(sync=lambda e: (mkidx_s(e), dma_s(e, [(idn[:], ident)]))[-1],
                  gpsimd=mkidx_g,
                  vector=lambda v: (v.memset(ones[:], 1.0),
                                    v.memset(consts[:, 0:1], EPS),
                                    v.memset(consts[:, 1:2], -10.0))[-1])

        st1 = contextlib.ExitStack()
        t1 = alloc(nc, st1, sbufs=[
            ("p1_xT", (128, 8, TOK), F32), ("p1_xn", (128, 8, TOK), BF16),
            ("p1_sq", (128, 8, 512), F32), ("p1_rows", (1, 4 * TOK), F32),
            ("p1_gb", (128, 16), F32), ("p1_tmp", (128, 512), F32),
        ])
        xT, xn, sq1, rows1, gb1, tmp1 = (t1["p1_xT"], t1["p1_xn"], t1["p1_sq"],
                                         t1["p1_rows"], t1["p1_gb"], t1["p1_tmp"])
        arow = rows1[:, 2 * TOK:3 * TOK]
        brow = rows1[:, 3 * TOK:4 * TOK]

        prog.step(sync=lambda e: dma_s(e, [
            (xT[:], x_in.rearrange("(j p) t -> p j t", p=128)),
            (gb1[:, 0:8], lng), (gb1[:, 8:16], lnb),
        ]))

        emit_ln_stats(prog, lambda j, n_sl: xT[:, j, n_sl], sq1, ps[0], ps[1],
                      ones, rows1[:, 0:1024], arow, brow, consts)

        def out_fn1(j, n, n_sl, v, normed):
            return v.tensor_scalar(xn[:, j, n_sl], normed, gb1[:, j:j + 1],
                                   gb1[:, 8 + j:9 + j], ALU.mult, ALU.add)
        emit_ln_apply(prog, lambda j, n_sl: xT[:, j, n_sl], out_fn1,
                      ps[2], ps[3], ones, arow, brow, tmp1)

        prog.step(sync=lambda e: dma_s(e, [
            (xn0_loc.rearrange("(j p) t -> p j t", p=128), xn[:]),
        ]))
        st1.close()

        cc("AllGather", ALU.bypass, xn0_loc_h, xn0_all_h)

        TCH = 2048
        st2 = contextlib.ExitStack()
        t2 = alloc(nc, st2, sbufs=[
            ("p2_inw", (128, 8, 4 * 128), BF16), ("p2_xnc", (128, 8, TCH), BF16),
            ("p2_xi", (128, 2, TCH + 3), F32), ("p2_xc", (128, 2, TCH), F32),
            ("p2_xsc", (128, 2, TCH), F32), ("p2_zst", (128, 2, TCH), BF16),
            ("p2_cw", (128, 8), F32), ("p2_cb", (128, 2), F32),
            ("p2_xpj", (128, 2, 33), F32), ("p2_xpst", (33, 2, TCH), F32),
            ("p2_carry", (128, 2, 3), F32),
        ])
        inw, xnc, xi, xc, xsc, zst = (t2["p2_inw"], t2["p2_xnc"], t2["p2_xi"],
                                      t2["p2_xc"], t2["p2_xsc"], t2["p2_zst"])
        cw_sb, cb_sb, xpj, xpst, carry = (t2["p2_cw"], t2["p2_cb"], t2["p2_xpj"],
                                          t2["p2_xpst"], t2["p2_carry"])

        prog.step(gpsimd=lambda g: dma_g(g, [
            (inw[:], inwT.rearrange("(k p) m -> p k m", p=128)),
        ]), sync=lambda e: dma_s(e, [
            (cw_sb[:], convw), (cb_sb[:], convb),
            (xpj[:], xprojT.rearrange("p (d m) -> p d m", d=2)),
        ]))

        for b in range(B):
            for ch in range(S // TCH):
                rk = 4 * b + ch

                def ldxn(e, rk=rk):
                    return dma_s(e, [(xnc[:], xn0_all[rk]
                                      .rearrange("(j p) t -> p j t", p=128))])
                prog.step(sync=ldxn)

                for n in range(TCH // 512):
                    n_sl = slice(n * 512, (n + 1) * 512)

                    def mms(t, n_sl=n_sl):
                        last = None
                        for m in range(4):
                            for k in range(8):
                                last = t.matmul(ps[m][:], inw[:, k, m * 128:(m + 1) * 128],
                                                xnc[:, k, n_sl],
                                                start=(k == 0), stop=(k == 7))
                        return last

                    def evac(v, n=n, n_sl=n_sl, ch=ch):
                        x_sl = slice(3 + n * 512, 3 + (n + 1) * 512)
                        if n == 0:
                            if ch == 0:
                                v.memset(carry[:], 0.0)
                            v.tensor_copy(xi[:, :, 0:3], carry[:])
                        v.tensor_copy(xi[:, 0, x_sl], ps[0][:])
                        v.tensor_copy(xi[:, 1, x_sl], ps[1][:])
                        v.tensor_copy(zst[:, 0, n_sl], ps[2][:])
                        return v.tensor_copy(zst[:, 1, n_sl], ps[3][:])
                    prog.step(tensor=mms)
                    prog.step(vector=evac)

                def conv(v):
                    last = None
                    for dt in range(2):
                        v.tensor_scalar(xc[:, dt, :], xi[:, dt, 0:TCH],
                                        cw_sb[:, dt * 4:dt * 4 + 1], None, ALU.mult)
                        for k in range(1, 4):
                            last = v.scalar_tensor_tensor(
                                xc[:, dt, :], xi[:, dt, k:k + TCH],
                                cw_sb[:, dt * 4 + k:dt * 4 + k + 1],
                                xc[:, dt, :], ALU.mult, ALU.add)
                    return last
                prog.step(vector=conv)

                def silu_save(s):
                    s.activation(xsc[:, 0, :], xc[:, 0, :], AF.Silu, bias=cb_sb[:, 0:1])
                    return s.activation(xsc[:, 1, :], xc[:, 1, :], AF.Silu, bias=cb_sb[:, 1:2])

                def carry_save(v):
                    return v.tensor_copy(carry[:], xi[:, :, TCH:TCH + 3])
                prog.step(scalar=silu_save, vector=carry_save)

                def xpmm(t):
                    last = None
                    for n in range(4):
                        for dt in range(2):
                            last = t.matmul(ps[n][0:33, :], xpj[:, dt, :],
                                            xsc[:, dt, n * 512:(n + 1) * 512],
                                            start=(dt == 0), stop=(dt == 1))
                    return last
                prog.step(tensor=xpmm)

                def xpev(v):
                    last = None
                    for n in range(4):
                        last = v.tensor_copy(xpst[:, 0, n * 512:(n + 1) * 512], ps[n][0:33, :])
                    return last
                prog.step(vector=xpev)

                def store(e, b=b, ch=ch):
                    c_sl = slice(ch * TCH, (ch + 1) * TCH)
                    return dma_s(e, [
                        (xs_scr[b * 256:b * 256 + 128, c_sl], xsc[:, 0, :]),
                        (xs_scr[b * 256 + 128:b * 256 + 256, c_sl], xsc[:, 1, :]),
                        (z_scr[b * 256:b * 256 + 128, c_sl], zst[:, 0, :]),
                        (z_scr[b * 256 + 128:b * 256 + 256, c_sl], zst[:, 1, :]),
                        (xp_loc[b * 33:b * 33 + 33, c_sl], xpst[:, 0, :]),
                    ])
                prog.step(sync=store)
        st2.close()

        cc("AllReduce", ALU.add, xp_loc_h, xp_sum_h)

        st3 = contextlib.ExitStack()
        t3 = alloc(nc, st3, sbufs=[
            ("p3_xp0", (1, TCH), F32),
            ("p3_B_t", (16, TCH), F32), ("p3_C_t", (16, TCH), F32),
            ("p3_sel", (16, 16 * 128), F32),
            ("p3_Arep", (128, 16), F32), ("p3_Arow", (1, 16), F32),
            ("p3_cols", (128, 8), F32),
            ("p3_hstate", (128, 64), F32),
            ("p3_xs", (128, TCH), F32), ("p3_z", (128, TCH), BF16),
            ("p3_delta", (128, TCH), F32), ("p3_u", (128, TCH), F32),
            ("p3_dA", (128, TCH), F32), ("p3_dBx", (128, TCH), F32),
            ("p3_h", (128, TCH), F32), ("p3_yacc", (128, TCH), F32),
            ("p3_sz", (128, TCH), F32), ("p3_ybf", (128, TCH), BF16),
        ])
        (xp0, B_t, C_t, sel_sb, Arep, Arow, cols, hstate, xs_t, z_t,
         delta, u_t, dA, dBx, h_t, yacc, sz, ybf) = (
            t3["p3_xp0"], t3["p3_B_t"], t3["p3_C_t"], t3["p3_sel"], t3["p3_Arep"],
            t3["p3_Arow"], t3["p3_cols"], t3["p3_hstate"], t3["p3_xs"], t3["p3_z"],
            t3["p3_delta"], t3["p3_u"], t3["p3_dA"], t3["p3_dBx"], t3["p3_h"],
            t3["p3_yacc"], t3["p3_sz"], t3["p3_ybf"])
        psB = [ps[0], ps[1], ps[2], ps[3]]
        psC = [ps[4], ps[5], ps[6], ps[7]]

        prog.step(sync=lambda e: dma_s(e, [
            (Arow[:], alog), (cols[:, 0:2], dtw), (cols[:, 2:4], dtb),
            (cols[:, 4:6], Dc), (sel_sb[:], sel),
        ]), vector=lambda v: v.memset(hstate[:], 0.0))

        prog.step(scalar=lambda s: s.activation(Arow[:], Arow[:], AF.Exp))
        prog.step(vector=lambda v: v.tensor_scalar(Arow[:], Arow[:], -1.0, None, ALU.mult))
        prog.step(tensor=lambda t: t.matmul(ps[0][:, 0:16], ones[0:1, 0:128], Arow[:],
                                            start=True, stop=True))
        prog.step(vector=lambda v: v.tensor_copy(Arep[:], ps[0][:, 0:16]))

        for b in range(B):
            for g in range(2):
                for tsc in range(S // TCH):
                    t_sl = slice(tsc * TCH, (tsc + 1) * TCH)
                    row0 = b * 256 + g * 128

                    def lds(e, b=b, row0=row0, t_sl=t_sl):
                        return dma_s(e, [
                            (xs_t[:], xs_scr[row0:row0 + 128, t_sl]),
                            (z_t[:], z_scr[row0:row0 + 128, t_sl]),
                            (xp0[:], xp_sum[33 * b:33 * b + 1, t_sl]),
                            (B_t[:], xp_sum[1 + 33 * b:17 + 33 * b, t_sl]),
                            (C_t[:], xp_sum[17 + 33 * b:33 + 33 * b, t_sl]),
                        ])
                    prog.step(sync=lds)

                    def x0b(t, t_sl=t_sl):
                        last = None
                        for n in range(4):
                            last = t.matmul(psB[n][:], ones[0:1, 0:128],
                                            xp0[0:1, n * 512:(n + 1) * 512],
                                            start=True, stop=True)
                        return last
                    prog.step(tensor=x0b)

                    def sdelta(s, g=g):
                        last = None
                        for n in range(4):
                            last = s.activation(delta[:, n * 512:(n + 1) * 512], psB[n][:],
                                                AF.Exp, scale=cols[:, g:g + 1],
                                                bias=cols[:, 2 + g:3 + g])
                        return last
                    prog.step(scalar=sdelta)
                    prog.step(scalar=lambda s: s.activation(delta[:], delta[:], AF.Ln, bias=1.0))
                    prog.step(vector=lambda v: v.tensor_tensor(u_t[:], delta[:], xs_t[:], ALU.mult))

                    for n in range(NST):
                        hcol = (b * 2 + g) * 16 + n

                        def brep(t, n=n):
                            last = None
                            for q in range(4):
                                q_sl = slice(q * 512, (q + 1) * 512)
                                t.matmul(psB[q][:], sel_sb[:, n * 128:(n + 1) * 128],
                                         B_t[:, q_sl], start=True, stop=True)
                                last = t.matmul(psC[q][:], sel_sb[:, n * 128:(n + 1) * 128],
                                                C_t[:, q_sl], start=True, stop=True)
                            return last

                        def dAexp(s, n=n):
                            return s.activation(dA[:], delta[:], AF.Exp,
                                                scale=Arep[:, n:n + 1])
                        prog.step(tensor=brep, scalar=dAexp)

                        def scan_grp(v, n=n, hcol=hcol):
                            for q in range(4):
                                v.tensor_tensor(dBx[:, q * 512:(q + 1) * 512],
                                                u_t[:, q * 512:(q + 1) * 512],
                                                psB[q][:], ALU.mult)
                            v.tensor_tensor_scan(h_t[:], dA[:], dBx[:],
                                                 hstate[:, hcol:hcol + 1], ALU.mult, ALU.add)
                            v.tensor_copy(hstate[:, hcol:hcol + 1], h_t[:, TCH - 1:TCH])
                            last = None
                            for q in range(4):
                                q_sl = slice(q * 512, (q + 1) * 512)
                                if n == 0:
                                    last = v.tensor_tensor(yacc[:, q_sl], h_t[:, q_sl],
                                                           psC[q][:], ALU.mult)
                                else:
                                    v.tensor_tensor(dBx[:, q_sl], h_t[:, q_sl],
                                                    psC[q][:], ALU.mult)
                                    last = v.tensor_tensor(yacc[:, q_sl], yacc[:, q_sl],
                                                           dBx[:, q_sl], ALU.add)
                            return last
                        prog.step(vector=scan_grp)

                    def ypre(v, g=g):
                        return v.scalar_tensor_tensor(yacc[:], xs_t[:], cols[:, 4 + g:5 + g],
                                                      yacc[:], ALU.mult, ALU.add)
                    prog.step(vector=ypre)
                    prog.step(scalar=lambda s: s.activation(sz[:], z_t[:], AF.Silu))
                    prog.step(vector=lambda v: v.tensor_tensor(ybf[:], yacc[:], sz[:], ALU.mult))
                    prog.step(sync=lambda e, b=b, tsc=tsc, g=g: dma_s(e, [
                        (y_loc[b * 4 + tsc, g * 128:(g + 1) * 128, :], ybf[:]),
                    ]))
        st3.close()

        cc("AllGather", ALU.bypass, y_loc_h, y_all_h)

        def ypick(g):
            return dma_g(g, [(y_cols[256 * j:256 * (j + 1), :],
                              y_all[bass.ts(dyn["ym"][j], 1), :, :])
                             for j in range(NC)])
        prog.step(gpsimd=ypick)

        st4 = contextlib.ExitStack()
        t4 = alloc(nc, st4, sbufs=[
            ("p4_BIGA", (128, 16, 1024), F32),
            ("p4_gb", (128, 16), F32),
            ("p4_tmp", (128, 512), F32),
            ("p4_cwsb", (128, 128), F32),
            ("p4_glog", (128, 128), F32),
            ("p4_mcol", (128, 8), F32),
            ("p4_mcol16", (128, 16), F32),
            ("p4_m2col", (128, 16), F32),
            ("p4_rcol16", (128, 16), F32),
            ("p4_m8big", (128, 16, 8), F32),
        ])
        BIGA, gb4, tmp4 = t4["p4_BIGA"], t4["p4_gb"], t4["p4_tmp"]
        cwsb, glog, mcol = t4["p4_cwsb"], t4["p4_glog"], t4["p4_mcol"]
        mcol16, rcol16, m8big = t4["p4_mcol16"], t4["p4_rcol16"], t4["p4_m8big"]
        m2col = t4["p4_m2col"]

        st_xn1 = contextlib.ExitStack()
        xn1 = alloc(nc, st_xn1, sbufs=[("p4_xn1", (128, 8, TOK), BF16)])["p4_xn1"]
        stA = contextlib.ExitStack()
        sB = alloc(nc, stA, sbufs=[("p4_s", (128, 8, TOK), F32)])["p4_s"]

        stA1 = contextlib.ExitStack()
        tA1 = alloc(nc, stA1, sbufs=[
            ("p4_outwc", (128, 16, 256), BF16),
            ("p4_yb", (128, 16, 512), BF16),
            ("p4_xst2", (128, 2, 512), F32),
        ])
        outwc, yb4, xst2 = tA1["p4_outwc"], tA1["p4_yb"], tA1["p4_xst2"]

        prog.step(sync=lambda e: dma_s(e, [
            (gb4[:, 0:8], gamma), (gb4[:, 8:16], beta),
        ]))

        for n in range(4):
            n_sl = slice(n * 512, (n + 1) * 512)

            prog.step(sync=lambda e, n_sl=n_sl: dma_s(e, [
                (yb4[:], y_cols[:, n_sl].rearrange("(k p) t -> p k t", p=128)),
            ]))
            for mw in range(4):
                def ldw(g, mw=mw, n_sl=n_sl):
                    return dma_g(g, [
                        (outwc[:], outwT[:, mw * 256:(mw + 1) * 256]
                         .rearrange("(k p) m -> p k m", p=128)),
                        (xst2[:], x_in[:, n_sl]
                         .rearrange("(j p) t -> p j t", p=128)[:, 2 * mw:2 * mw + 2, :]),
                    ])
                prog.step(gpsimd=ldw)

                def mms(t):
                    last = None
                    for mi in range(2):
                        for k in range(16):
                            last = t.matmul(ps[mi][:], outwc[:, k, mi * 128:(mi + 1) * 128],
                                            yb4[:, k, :], start=(k == 0), stop=(k == 15))
                    return last

                def evac(v, mw=mw, n_sl=n_sl):
                    last = None
                    for mi in range(2):
                        m = mw * 2 + mi
                        last = v.scalar_tensor_tensor(sB[:, m, n_sl], xst2[:, mi, :], 2.0,
                                                      ps[mi][:], ALU.mult, ALU.add)
                    return last
                prog.step(tensor=mms)
                prog.step(vector=evac)
        stA1.close()

        stS = contextlib.ExitStack()
        tS = alloc(nc, stS, sbufs=[
            ("p4_sq", (128, 2, 512), F32),
            ("p4_rows", (1, 4 * TOK), F32),
            ("p4_mv", (1, 2 * 512), F32),
        ])
        sq4, rows4, mv4 = tS["p4_sq"], tS["p4_rows"], tS["p4_mv"]
        a1row = rows4[:, 0 * TOK:1 * TOK]
        b1row = rows4[:, 1 * TOK:2 * TOK]
        a2row = rows4[:, 2 * TOK:3 * TOK]
        b2row_s = rows4[:, 3 * TOK:4 * TOK]

        emit_ln_stats(prog, lambda j, n_sl: sB[:, j, n_sl], sq4, ps[0], ps[1],
                      ones, mv4, a1row, b1row, consts)

        for n in range(4):
            n_sl = slice(n * 512, (n + 1) * 512)

            def bc1(t, n_sl=n_sl):
                t.matmul(ps[2][:], ones[0:1, 0:128], a1row[0:1, n_sl], start=True, stop=True)
                return t.matmul(ps[3][:], ones[0:1, 0:128], b1row[0:1, n_sl],
                                start=True, stop=True)
            prog.step(tensor=bc1)
            for g0 in range(0, 8, 2):
                def x1mk(v, g0=g0, n_sl=n_sl):
                    last = None
                    for i in range(2):
                        v.tensor_tensor(sq4[:, i, :], sB[:, g0 + i, n_sl], ps[2][:], ALU.mult)
                        last = v.tensor_tensor(sq4[:, i, :], sq4[:, i, :], ps[3][:], ALU.add)
                    return last
                prog.step(vector=x1mk)

                def mmx1(t, g0=g0):
                    last = None
                    for i in range(2):
                        j = g0 + i
                        last = t.matmul(ps[0][0:1, :], ones[:, 0:1], sq4[:, i, :],
                                        start=(j == 0), stop=(j == 7))
                    return last
                prog.step(tensor=mmx1)

                def sqx1(s):
                    last = None
                    for i in range(2):
                        last = s.activation(sq4[:, i, :], sq4[:, i, :], AF.Square)
                    return last
                prog.step(scalar=sqx1)

                def mmq1(t, g0=g0):
                    last = None
                    for i in range(2):
                        j = g0 + i
                        last = t.matmul(ps[1][0:1, :], ones[:, 0:1], sq4[:, i, :],
                                        start=(j == 0), stop=(j == 7))
                    return last
                prog.step(tensor=mmq1)

            def st2f(v):
                v.tensor_scalar(mv4[0:1, 0:512], ps[0][0:1, :], 1.0 / D, None, ALU.mult)
                v.tensor_scalar(mv4[0:1, 512:1024], ps[1][0:1, :], 1.0 / D, None, ALU.mult)
                v.tensor_tensor(tmp4[0:1, 0:512], mv4[0:1, 0:512], mv4[0:1, 0:512], ALU.mult)
                return v.tensor_tensor(mv4[0:1, 512:1024], mv4[0:1, 512:1024],
                                       tmp4[0:1, 0:512], ALU.subtract)
            prog.step(vector=st2f)

            def rstd2(s):
                s.activation(mv4[0:1, 512:1024], mv4[0:1, 512:1024], AF.Ln,
                             bias=consts[0:1, 0:1])
                return s.activation(mv4[0:1, 512:1024], mv4[0:1, 512:1024],
                                    AF.Exp, scale=-0.5)
            prog.step(scalar=rstd2)

            def rows2(v, n_sl=n_sl):
                v.tensor_copy(a2row[0:1, n_sl], mv4[0:1, 512:1024])
                return v.scalar_tensor_tensor(b2row_s[0:1, n_sl], mv4[0:1, 0:512], -1.0,
                                              mv4[0:1, 512:1024], ALU.mult, ALU.mult)
            prog.step(vector=rows2)

        for n in range(4):
            n_sl = slice(n * 512, (n + 1) * 512)

            def bc2(t, n_sl=n_sl):
                t.matmul(ps[2][:], ones[0:1, 0:128], a1row[0:1, n_sl], start=True, stop=True)
                t.matmul(ps[3][:], ones[0:1, 0:128], b1row[0:1, n_sl], start=True, stop=True)
                t.matmul(ps[4][:], ones[0:1, 0:128], a2row[0:1, n_sl], start=True, stop=True)
                return t.matmul(ps[5][:], ones[0:1, 0:128], b2row_s[0:1, n_sl],
                                start=True, stop=True)
            prog.step(tensor=bc2)

            for j in range(8):
                def vx(v, j=j, n_sl=n_sl):
                    v.tensor_tensor(tmp4[:], sB[:, j, n_sl], ps[2][:], ALU.mult)
                    v.tensor_tensor(tmp4[:], tmp4[:], ps[3][:], ALU.add)
                    v.tensor_tensor(sq4[:, 0, :], tmp4[:], ps[4][:], ALU.mult)
                    v.tensor_tensor(sq4[:, 0, :], sq4[:, 0, :], ps[5][:], ALU.add)
                    return v.tensor_scalar(xn1[:, j, n_sl], sq4[:, 0, :],
                                           gb4[:, j:j + 1], gb4[:, 8 + j:9 + j],
                                           ALU.mult, ALU.add)
                prog.step(vector=vx)

                for h in range(2):
                    def tr(t, h=h):
                        t.transpose(ps[6][:, 0:128], tmp4[:, 256 * h:256 * h + 128], idn[:])
                        return t.transpose(ps[7][:, 0:128],
                                           tmp4[:, 256 * h + 128:256 * h + 256], idn[:])
                    prog.step(tensor=tr)

                    def ev(v, j=j, n=n, h=h):
                        tau0 = n * 4 + 2 * h
                        v.tensor_scalar(BIGA[:, tau0, j * 128:(j + 1) * 128],
                                        ps[6][:, 0:128], 2.0, None, ALU.mult)
                        return v.tensor_scalar(BIGA[:, tau0 + 1, j * 128:(j + 1) * 128],
                                               ps[7][:, 0:128], 2.0, None, ALU.mult)
                    prog.step(vector=ev)
        stS.close()
        stA.close()

        stG = contextlib.ExitStack()
        gwb = alloc(nc, stG, sbufs=[("p4_gwb", (128, 8, 8), BF16)])["p4_gwb"]
        prog.step(gpsimd=lambda g: dma_g(g, [
            (gwb[:], gate_w.rearrange("(k p) e -> p k e", p=128)),
        ]))

        def gmm(t):
            last = None
            for tau in range(16):
                for k in range(8):
                    last = t.matmul(ps[0][:, tau * 8:(tau + 1) * 8],
                                    xn1[:, k, tau * 128:(tau + 1) * 128],
                                    gwb[:, k, :], start=(k == 0), stop=(k == 7))
            return last
        prog.step(tensor=gmm)

        prog.step(vector=lambda v: [v.tensor_reduce(mcol16[:, tau:tau + 1],
                                                    ps[0][:, tau * 8:(tau + 1) * 8],
                                                    mybir.AxisListType.X, ALU.max)
                                    for tau in range(16)][-1])

        prog.step(vector=lambda v: [v.tensor_scalar(glog[:, tau * 8:(tau + 1) * 8],
                                                    ps[0][:, tau * 8:(tau + 1) * 8],
                                                    mcol16[:, tau:tau + 1], None,
                                                    ALU.subtract)
                                    for tau in range(16)][-1])
        prog.step(scalar=lambda s: s.activation(glog[:], glog[:], AF.Exp))
        prog.step(vector=lambda v: [v.max(out=m8big[:, tau, :],
                                           in_=glog[:, tau * 8:(tau + 1) * 8])
                                    for tau in range(16)][-1])
        prog.step(vector=lambda v: [v.tensor_copy(m2col[:, tau:tau + 1],
                                                  m8big[:, tau, 1:2])
                                    for tau in range(16)][-1])
        prog.step(vector=lambda v: [v.tensor_tensor(mcol16[:, tau:tau + 1],
                                                    m8big[:, tau, 0:1],
                                                    m8big[:, tau, 1:2], ALU.add)
                                    for tau in range(16)][-1])
        prog.step(vector=lambda v: v.reciprocal(rcol16[:], mcol16[:]))
        prog.step(vector=lambda v: [v.tensor_scalar(cwsb[:, tau * 8:(tau + 1) * 8],
                                                    glog[:, tau * 8:(tau + 1) * 8],
                                                    m2col[:, tau:tau + 1], None, ALU.is_ge)
                                    for tau in range(16)][-1])
        prog.step(vector=lambda v: v.tensor_tensor(cwsb[:], cwsb[:], glog[:], ALU.mult))
        prog.step(vector=lambda v: [v.tensor_scalar(cwsb[:, tau * 8:(tau + 1) * 8],
                                                    cwsb[:, tau * 8:(tau + 1) * 8],
                                                    rcol16[:, tau:tau + 1], None, ALU.mult)
                                    for tau in range(16)][-1])
        stG.close()

        stM = contextlib.ExitStack()
        tM = alloc(nc, stM, sbufs=[
            ("p4_hbf", (128, 32, 1024), BF16),
            ("p4_w1c", (128, 2, 8, 128), BF16),
            ("p4_w2w", (128, 2, 8, 512), BF16),
            ("p4_b1s", (128, E * 32), F32),
            ("p4_b2t", (1, D), F32),
        ])
        hbf, w1c, w2w, b1s, b2t = (tM["p4_hbf"], tM["p4_w1c"], tM["p4_w2w"],
                                   tM["p4_b1s"], tM["p4_b2t"])
        prog.step(sync=lambda e: dma_s(e, [(b1s[:], b1c)]))

        for ex in range(E):
            prog.step(gpsimd=lambda g, e2=ex: dma_g(g, [
                (b2t[:], b2row[0:1, e2 * D:(e2 + 1) * D])]))
            for nn in range(2):
                tok0 = nn * 1024
                prog.step(gpsimd=lambda g, e2=ex: dma_g(g, [
                    (w1c[:, 0], w1[e2, :, 0:128].rearrange("(k p) m -> p k m", p=128)),
                ]))
                for hm in range(33):
                    def work(t, hm=hm, tok0=tok0):
                        if hm >= 32:
                            return None
                        last = None
                        for th in range(2):
                            for k in range(8):
                                last = t.matmul(ps[(hm % 2) * 2 + th][:],
                                                w1c[:, hm % 2, k, :],
                                                xn1[:, k, tok0 + th * 512:tok0 + (th + 1) * 512],
                                                start=(k == 0), stop=(k == 7))
                        return last

                    def pre(g, e2=ex, hm=hm):
                        if hm + 1 >= 32:
                            return None
                        return dma_g(g, [(w1c[:, (hm + 1) % 2],
                                          w1[e2, :, (hm + 1) * 128:(hm + 2) * 128]
                                          .rearrange("(k p) m -> p k m", p=128))])

                    def gel(s, e2=ex, hm=hm):
                        if hm == 0:
                            return None
                        hp = hm - 1
                        s.activation(hbf[:, hp, 0:512], ps[(hp % 2) * 2][:], AF.Gelu,
                                     bias=b1s[:, e2 * 32 + hp:e2 * 32 + hp + 1])
                        return s.activation(hbf[:, hp, 512:1024], ps[(hp % 2) * 2 + 1][:],
                                            AF.Gelu, bias=b1s[:, e2 * 32 + hp:e2 * 32 + hp + 1])
                    prog.step(tensor=work, gpsimd=pre, scalar=gel)

                for nd in range(2):
                    prog.step(gpsimd=lambda g, e2=ex, nd=nd: dma_g(g, [
                        (w2w[:, 0], w2[e2, 0:1024, nd * 512:(nd + 1) * 512]
                         .rearrange("(k p) m -> p k m", p=128)),
                    ]))
                    for kw in range(4):
                        def mm2(t, kw=kw):
                            last = None
                            for ki in range(8):
                                k = kw * 8 + ki
                                for tm in range(8):
                                    last = t.matmul(ps[tm][:],
                                                    hbf[:, k, tm * 128:(tm + 1) * 128],
                                                    w2w[:, kw % 2, ki, :],
                                                    start=(k == 0), stop=False)
                            return last

                        def pw2(g, e2=ex, kw=kw, nd=nd):
                            if kw + 1 >= 4:
                                return None
                            k0 = (kw + 1) * 1024
                            return dma_g(g, [(w2w[:, (kw + 1) % 2],
                                              w2[e2, k0:k0 + 1024, nd * 512:(nd + 1) * 512]
                                              .rearrange("(k p) m -> p k m", p=128))])
                        prog.step(tensor=mm2, gpsimd=pw2)

                    def b2mm(t, nd=nd):
                        last = None
                        for tm in range(8):
                            last = t.matmul(ps[tm][:], ones[0:1, 0:128],
                                            b2t[0:1, nd * 512:(nd + 1) * 512],
                                            start=False, stop=True)
                        return last
                    prog.step(tensor=b2mm)

                    def moevac(v, e2=ex, nn=nn, nd=nd):
                        last = None
                        for tm in range(8):
                            tau = nn * 8 + tm
                            last = v.scalar_tensor_tensor(
                                BIGA[:, tau, nd * 512:(nd + 1) * 512], ps[tm][:],
                                cwsb[:, tau * 8 + e2:tau * 8 + e2 + 1],
                                BIGA[:, tau, nd * 512:(nd + 1) * 512], ALU.mult, ALU.add)
                        return last
                    prog.step(vector=moevac)
        stM.close()
        st_xn1.close()

        stQ = contextlib.ExitStack()
        tQ = alloc(nc, stQ, sbufs=[
            ("p4_x2bf", (128, 8, TOK), BF16),
            ("p4_wat", (128, 8, 3 * D), BF16),
            ("p4_tmq", (128, 1024), F32),
            ("p4_stg", (128, 8, 128), F32),
            ("p4_stgb", (128, 4, 512), BF16),
        ])
        x2bf, wat, tmq, stg, stgb = (tQ["p4_x2bf"], tQ["p4_wat"], tQ["p4_tmq"],
                                     tQ["p4_stg"], tQ["p4_stgb"])
        prog.step(gpsimd=lambda g: dma_g(g, [
            (wat[:], wattnT.rearrange("(k p) m -> p k m", p=128)),
        ]))

        for tau in range(16):
            def lnst(s, tau=tau):
                return s.activation(tmq[:], BIGA[:, tau, :], AF.Square,
                                    accum_out=mcol[:, 1:2])
            prog.step(scalar=lnst)

            def lnred(v, tau=tau):
                v.tensor_reduce(mcol[:, 0:1], BIGA[:, tau, :], mybir.AxisListType.X, ALU.add)
                v.tensor_scalar(mcol[:, 0:1], mcol[:, 0:1], 1.0 / D, None, ALU.mult)
                v.tensor_scalar(mcol[:, 1:2], mcol[:, 1:2], 1.0 / D, None, ALU.mult)
                v.tensor_tensor(mcol[:, 2:3], mcol[:, 0:1], mcol[:, 0:1], ALU.mult)
                v.tensor_tensor(mcol[:, 1:2], mcol[:, 1:2], mcol[:, 2:3], ALU.subtract)
                return v.tensor_scalar(mcol[:, 3:4], mcol[:, 0:1], -1.0, None, ALU.mult)
            prog.step(vector=lnred)

            def lnrs(s):
                s.activation(mcol[:, 1:2], mcol[:, 1:2], AF.Ln, bias=consts[:, 0:1])
                return s.activation(mcol[:, 1:2], mcol[:, 1:2], AF.Exp, scale=-0.5)
            prog.step(scalar=lnrs)

            def lnap(v, tau=tau):
                return v.tensor_scalar(tmq[:], BIGA[:, tau, :], mcol[:, 3:4],
                                       mcol[:, 1:2], ALU.add, ALU.mult)
            prog.step(vector=lnap)

            for h in range(4):
                def trx(t, h=h):
                    t.transpose(ps[6][:, 0:128], tmq[:, 256 * h:256 * h + 128], idn[:])
                    return t.transpose(ps[7][:, 0:128],
                                       tmq[:, 256 * h + 128:256 * h + 256], idn[:])
                prog.step(tensor=trx)

                def evx(v, h=h, tau=tau):
                    v.tensor_copy(stg[:, 2 * h, :], ps[6][:, 0:128])
                    v.tensor_copy(stg[:, 2 * h + 1, :], ps[7][:, 0:128])
                    v.tensor_copy(x2bf[:, 2 * h, tau * 128:(tau + 1) * 128], ps[6][:, 0:128])
                    return v.tensor_copy(x2bf[:, 2 * h + 1, tau * 128:(tau + 1) * 128],
                                         ps[7][:, 0:128])
                prog.step(vector=evx)

            prog.step(sync=lambda e, tau=tau: dma_s(e, [
                (x2_scr[:, tau * 128:(tau + 1) * 128]
                 .rearrange("(j p) t -> p j t", p=128), stg[:])]))

        for n in range(4):
            n_sl = slice(n * 512, (n + 1) * 512)
            for mw in range(4):
                def qmm(t, mw=mw, n_sl=n_sl):
                    last = None
                    for mi in range(4):
                        m = mw * 4 + mi
                        for k in range(8):
                            last = t.matmul(ps[mi][:], wat[:, k, m * 128:(m + 1) * 128],
                                            x2bf[:, k, n_sl], start=(k == 0), stop=(k == 7))
                    return last
                prog.step(tensor=qmm)

                def qev(v):
                    last = None
                    for mi in range(4):
                        last = v.tensor_copy(stgb[:, mi, :], ps[mi][:])
                    return last
                prog.step(vector=qev)

                def qst(e, mw=mw, n_sl=n_sl):
                    trs = []
                    for mi in range(4):
                        for half in range(2):
                            hraw = 8 * mw + 2 * mi + half
                            if hraw < 16:
                                slot, sub = hraw // 4, hraw % 4
                            else:
                                hk = hraw - 16
                                slot, sub = hk // 4, 4 + hk % 4
                            trs.append((qk_loc[slot, sub, :, n_sl],
                                        stgb[64 * half:64 * half + 64, mi, :]))
                    return dma_s(e, trs)
                prog.step(sync=qst)

        for tw in range(4):
            for nv in range(2):
                def vmm(t, tw=tw, nv=nv):
                    last = None
                    for ti in range(4):
                        tau = tw * 4 + ti
                        for k in range(8):
                            last = t.matmul(ps[ti][:], x2bf[:, k, tau * 128:(tau + 1) * 128],
                                            wat[:, k, 2048 + nv * 512:2048 + (nv + 1) * 512],
                                            start=(k == 0), stop=(k == 7))
                    return last
                prog.step(tensor=vmm)

                def vev(v):
                    last = None
                    for ti in range(4):
                        last = v.tensor_copy(stgb[:, ti, :], ps[ti][:])
                    return last
                prog.step(vector=vev)

                def vst(e, tw=tw, nv=nv):
                    trs = []
                    for hh in range(8):
                        h = 8 * nv + hh
                        trs.append((v_loc[h, :, 4 * tw:4 * tw + 4, :],
                                    stgb[:, :, 64 * hh:64 * hh + 64]))
                    return dma_s(e, trs)
                prog.step(sync=vst)
        stQ.close()
        st4.close()

        cc("AllGather", ALU.bypass, qk_loc_h, qk_all_h)
        cc("AllGather", ALU.bypass, v_loc_h, v_all_h)

        def qkvpick(e):
            qk_flat = qk_all.rearrange("a b c d -> a (b c d)")
            qk_p_flat = qk_p.rearrange("a b c d -> a (b c d)")
            v_flat = v_all.rearrange("(a b) p k c -> a (b p k c)", b=4)
            v_p_flat = v_p.rearrange("(a b) p k c -> a (b p k c)", b=4)
            trs = []
            for ji in range(4):
                trs.append((qk_p_flat[ji:ji + 1, :],
                            qk_flat[bass.ts(dyn["qv"][ji], 1), :]))
                trs.append((v_p_flat[ji:ji + 1, :],
                            v_flat[bass.ts(dyn["qv"][ji], 1), :]))
            return dma_s(e, trs)
        prog.step(sync=qkvpick)

        st5 = contextlib.ExitStack()
        t5 = alloc(nc, st5, sbufs=[
            ("p5_q2", (128, S), BF16), ("p5_k2", (128, S), BF16),
            ("p5_vA", (128, 64, 65), BF16), ("p5_vB", (128, 64, 65), BF16),
            ("p5_msk", (128, 4, 512), F32),
            ("p5_onesb", (128, 2), BF16),
            ("p5_ebf", (128, 2, 2, 512), BF16),
            ("p5_etmp", (128, 2, 2, 512), F32),
            ("p5_onesT", (65, 64), F32),
            ("p5_sufA", (65, 16), F32), ("p5_sufB", (65, 16), F32),
            ("p5_pref", (65, 64), F32), ("p5_tcol", (65, 1), F32),
            ("p5_yv", (65, 2, 512), F32),
            ("p5_rrow", (1, 2, 512), F32),
            ("p5_ystg", (64, 2, S), BF16),
        ])
        q2, k2, vA, vB, msk = (t5["p5_q2"], t5["p5_k2"], t5["p5_vA"], t5["p5_vB"],
                               t5["p5_msk"])
        onesb, ebf, etmp, onesT = (t5["p5_onesb"], t5["p5_ebf"], t5["p5_etmp"],
                                   t5["p5_onesT"])
        sufA, sufB, pref, tcol = t5["p5_sufA"], t5["p5_sufB"], t5["p5_pref"], t5["p5_tcol"]
        yv, rrow, ystg = t5["p5_yv"], t5["p5_rrow"], t5["p5_ystg"]
        psS = [[ps[0], ps[1]], [ps[2], ps[3]]]
        pyP = [ps[4][0:65, :], ps[5][0:65, :]]
        pxt = ps[6]

        prog.step(sync=lambda e: dma_s(e, [
            (msk[:], dmask.rearrange("(r p) f -> p r f", p=128)),
        ]), vector=lambda v: (v.memset(onesT[:], 1.0), v.memset(vA[:], 1.0),
                              v.memset(vB[:], 1.0), v.memset(onesb[:], 1.0))[-1])

        for pg in range(2):
            pA, pB = 2 * pg, 2 * pg + 1

            def ldqkv(e, pA=pA, pB=pB):
                trs = []
                for ji in range(4):
                    c_sl = slice(2048 * ji, 2048 * (ji + 1))
                    trs.append((q2[0:64, c_sl], qk_p[ji, pA, :, :]))
                    trs.append((q2[64:128, c_sl], qk_p[ji, pB, :, :]))
                    trs.append((k2[0:64, c_sl], qk_p[ji, 4 + pA, :, :]))
                    trs.append((k2[64:128, c_sl], qk_p[ji, 4 + pB, :, :]))
                    trs.append((vA[:, 16 * ji:16 * ji + 16, 0:64],
                                v_p[4 * ji + pA, :, :, :]))
                    trs.append((vB[:, 16 * ji:16 * ji + 16, 0:64],
                                v_p[4 * ji + pB, :, :, :]))
                return dma_s(e, trs)
            prog.step(sync=ldqkv)

            for pi, vt, suf in ((0, vA, sufA), (1, vB, sufB)):
                def tsum(t, vt=vt):
                    last = None
                    for kt in range(64):
                        last = t.matmul(pxt[0:65, kt:kt + 1], vt[:, kt, :],
                                        onesb[:, 0:1], start=True, stop=True)
                    return last
                prog.step(tensor=tsum)
                prog.step(vector=lambda v: v.tensor_tensor_scan(
                    pref[:], onesT[:], pxt[0:65, 0:64], 0.0, ALU.mult, ALU.add))
                prog.step(vector=lambda v: v.tensor_scalar(
                    tcol[:], pref[:, 63:64], EM10, None, ALU.mult))
                prog.step(vector=lambda v, suf=suf: v.tensor_scalar(
                    suf[:], pref[:, 3:64:4], -EM10, tcol[:, 0:1], ALU.mult, ALU.add))

            for qt in range(16):
                q_sl = slice(qt * 512, (qt + 1) * 512)
                nkt = 4 * qt + 4
                kts = list(range(nkt))
                waves = [kts[i:i + 2] for i in range(0, nkt, 2)]
                for w, wk in enumerate(waves):
                    diag = (wk[0] >= 4 * qt)

                    def smm(t, wk=wk, q_sl=q_sl):
                        last = None
                        for si, kt in enumerate(wk):
                            k_sl = slice(kt * 128, (kt + 1) * 128)
                            t.matmul(psS[0][si][:], k2[0:64, k_sl], q2[0:64, q_sl],
                                     start=True, stop=True, tile_position=(0, 0))
                            last = t.matmul(psS[1][si][:], k2[64:128, k_sl],
                                            q2[64:128, q_sl], start=True, stop=True,
                                            tile_position=(64, 0))
                        return last
                    prog.step(tensor=smm)

                    if diag:
                        def dmsk(v, wk=wk, qt=qt):
                            last = None
                            for si, kt in enumerate(wk):
                                rel = kt - 4 * qt
                                for pi in range(2):
                                    v.tensor_scalar(etmp[:, pi, si, :], psS[pi][si][:],
                                                    0.125, 10.0, ALU.mult, ALU.add)
                                    last = v.tensor_tensor(etmp[:, pi, si, :],
                                                           etmp[:, pi, si, :],
                                                           msk[:, rel, :], ALU.mult)
                            return last
                        prog.step(vector=dmsk)

                        def dexp(s, wk=wk):
                            last = None
                            for si in range(len(wk)):
                                for pi in range(2):
                                    last = s.activation(ebf[:, pi, si, :],
                                                        etmp[:, pi, si, :], AF.Exp,
                                                        bias=consts[:, 1:2])
                            return last
                        prog.step(scalar=dexp)
                    else:
                        def nexp(s, wk=wk):
                            last = None
                            for si in range(len(wk)):
                                for pi in range(2):
                                    last = s.activation(ebf[:, pi, si, :],
                                                        psS[pi][si][:], AF.Exp,
                                                        scale=0.125)
                            return last
                        prog.step(scalar=nexp)

                    def avm(t, wk=wk):
                        last = None
                        for si, kt in enumerate(wk):
                            for pi, vt in ((0, vA), (1, vB)):
                                last = t.matmul(pyP[pi], vt[:, kt, :],
                                                ebf[:, pi, si, :],
                                                start=(kt == 0), stop=(kt == nkt - 1))
                        return last
                    prog.step(tensor=avm)

                def ev1(v, qt=qt):
                    v.tensor_scalar(yv[:, 0, :], pyP[0], sufA[:, qt:qt + 1],
                                    None, ALU.add)
                    return v.tensor_scalar(yv[:, 1, :], pyP[1], sufB[:, qt:qt + 1],
                                           None, ALU.add)
                prog.step(vector=ev1)

                def ev2(v):
                    v.reciprocal(rrow[0:1, 0, :], yv[64:65, 0, :])
                    return v.reciprocal(rrow[0:1, 1, :], yv[64:65, 1, :])
                prog.step(vector=ev2)

                for pi in range(2):
                    prog.step(tensor=lambda t, pi=pi: t.matmul(
                        pxt[0:64, :], onesT[0:1, 0:64], rrow[0:1, pi, :],
                        start=True, stop=True))
                    prog.step(vector=lambda v, pi=pi, q_sl=q_sl: v.tensor_tensor(
                        ystg[:, pi, q_sl], yv[0:64, pi, :], pxt[0:64, :], ALU.mult))

            prog.step(sync=lambda e, pA=pA, pB=pB: dma_s(e, [
                (yat_loc[:, 64 * pA:64 * pA + 64, :].rearrange("u p t -> p u t"),
                 ystg[:, 0, :].rearrange("p (u t) -> p u t", u=4)),
                (yat_loc[:, 64 * pB:64 * pB + 64, :].rearrange("u p t -> p u t"),
                 ystg[:, 1, :].rearrange("p (u t) -> p u t", u=4)),
            ]))
        st5.close()

        cc("AllGather", ALU.bypass, yat_loc_h, yat_all_h)

        def yatpick(e):
            return dma_s(e, [(yat_cols[256 * ji:256 * (ji + 1), :],
                              yat_all[bass.ts(dyn["qv"][ji], 1), :, :])
                             for ji in range(4)])
        prog.step(sync=yatpick)

        st6 = contextlib.ExitStack()
        t6 = alloc(nc, st6, sbufs=[
            ("p6_sx", (128, 8, TOK), F32),
            ("p6_x3b", (128, 8, TOK), BF16),
            ("p6_bx3", (128, 8, TOK), F32),
            ("p6_tmp", (128, 512), F32),
            ("p6_wcast", (128, 8, 512), BF16),
            ("p6_xs2", (128, 4, 512), F32),
            ("p6_sq", (128, 1, 512), F32),
            ("p6_mv", (1, 4 * 512), F32),
            ("p6_bcols", (128, 9), F32),
            ("p6_lcols", (128, 33), F32),
            ("p6_lrow", (1, 8), F32),
            ("p6_h1", (128, 2, 512), BF16),
            ("p6_wd1c", (128, 8, 256), BF16),
            ("p6_wd2c", (128, 2, 512), BF16),
            ("p6_bd1s", (128, 16), F32),
            ("p6_bd2s", (128, 64), F32),
            ("p6_yb", (128, 8, 512), BF16),
        ])
        sx, x3b, bx3 = t6["p6_sx"], t6["p6_x3b"], t6["p6_bx3"]
        tmp6 = t6["p6_tmp"]
        wcast, xs2, sq6, mv6 = t6["p6_wcast"], t6["p6_xs2"], t6["p6_sq"], t6["p6_mv"]
        bcols, lcols, lrow = t6["p6_bcols"], t6["p6_lcols"], t6["p6_lrow"]
        h1, wd1c, wd2c, bd1s, bd2s, yb6 = (t6["p6_h1"], t6["p6_wd1c"], t6["p6_wd2c"],
                                           t6["p6_bd1s"], t6["p6_bd2s"], t6["p6_yb"])

        prog.step(sync=lambda e: dma_s(e, [
            (bcols[:, 0:8], bbase), (bd1s[:], bd1c), (bd2s[:], bd2c),
        ]))

        for n in range(4):
            n_sl = slice(n * 512, (n + 1) * 512)

            prog.step(sync=lambda e, n_sl=n_sl: dma_s(e, [
                (yb6[:], yat_cols[:, n_sl].rearrange("(k p) t -> p k t", p=128)),
            ]))
            for mw in range(2):
                prog.step(sync=lambda e, mw=mw, n_sl=n_sl: dma_s(e, [
                    (xs2[:], x2_scr[:, n_sl].rearrange("(j p) t -> p j t", p=128)
                     [:, 4 * mw:4 * mw + 4, :]),
                ]), gpsimd=lambda g, mw=mw: dma_g(g, [
                    (wcast[:], wprojT[:, mw * 512:(mw + 1) * 512]
                     .rearrange("(k p) m -> p k m", p=128)),
                ]))

                def mms6(t):
                    last = None
                    for mi in range(4):
                        for k in range(8):
                            last = t.matmul(ps[mi][:], wcast[:, k, mi * 128:(mi + 1) * 128],
                                            yb6[:, k, :], start=(k == 0), stop=(k == 7))
                    return last

                def evac6(v, mw=mw, n_sl=n_sl):
                    last = None
                    for mi in range(4):
                        m = mw * 4 + mi
                        last = v.scalar_tensor_tensor(sx[:, m, n_sl], xs2[:, mi, :], 1.0,
                                                      ps[mi][:], ALU.mult, ALU.add)
                    return last
                prog.step(tensor=mms6)
                prog.step(vector=evac6)

        def ln3_out(j, n, n_sl, v, x_ap, ps_a, ps_b):
            v.tensor_tensor(tmp6[:], x_ap, ps_a[:], ALU.mult)
            v.tensor_tensor(tmp6[:], tmp6[:], ps_b[:], ALU.add)
            v.tensor_copy(x3b[:, j, n_sl], tmp6[:])
            return v.tensor_copy(sx[:, j, n_sl], tmp6[:])
        emit_ln_fused(prog, lambda j, n_sl: sx[:, j, n_sl], ln3_out, sq6,
                      ps[0], ps[1], ps[2], ps[3], ones, mv6, consts)
        prog.step(sync=lambda e: dma_s(e, [
            (x3b_scr.rearrange("(j p) t -> p j t", p=128), x3b[:]),
        ]))

        for n in range(4):
            n_sl = slice(n * 512, (n + 1) * 512)
            for mw in range(2):
                prog.step(gpsimd=lambda g, mw=mw: dma_g(g, [
                    (wcast[:], Wb[:, mw * 512:(mw + 1) * 512]
                     .rearrange("(k p) m -> p k m", p=128)),
                ]))

                def mmb(t, mw=mw, n_sl=n_sl):
                    last = None
                    for mi in range(4):
                        for k in range(8):
                            last = t.matmul(ps[mi][:], wcast[:, k, mi * 128:(mi + 1) * 128],
                                            x3b[:, k, n_sl], start=(k == 0), stop=(k == 7))
                    return last

                def evb(v, mw=mw, n_sl=n_sl):
                    last = None
                    for mi in range(4):
                        m = mw * 4 + mi
                        v.tensor_scalar(tmp6[:], ps[mi][:], bcols[:, m:m + 1], None, ALU.add)
                        last = v.tensor_tensor(bx3[:, m, n_sl], tmp6[:], sx[:, m, n_sl],
                                               ALU.add)
                    return last
                prog.step(tensor=mmb)
                prog.step(vector=evb)
        prog.step(sync=lambda e: dma_s(e, [
            (bx3_scr.rearrange("(j p) t -> p j t", p=128), bx3[:]),
        ]))

        for e_ in range(E):
            prog.step(gpsimd=lambda g, e_=e_: dma_g(g, [
                (wd1c[:], Wd1[e_].rearrange("(k p) m -> p k m", p=128)),
            ]))
            for n in range(4):
                n_sl = slice(n * 512, (n + 1) * 512)

                def mh(t, n_sl=n_sl):
                    last = None
                    for mi in range(2):
                        for k in range(8):
                            last = t.matmul(ps[6 - mi][:], wd1c[:, k, mi * 128:(mi + 1) * 128],
                                            x3b[:, k, n_sl], start=(k == 0), stop=(k == 7))
                    return last
                prog.step(tensor=mh)

                def eh(s_, e_=e_):
                    s_.activation(h1[:, 0, :], ps[6][:], AF.Relu,
                                  bias=bd1s[:, 2 * e_:2 * e_ + 1])
                    return s_.activation(h1[:, 1, :], ps[5][:], AF.Relu,
                                         bias=bd1s[:, 2 * e_ + 1:2 * e_ + 2])
                prog.step(scalar=eh)
                for mw in range(2):
                    prog.step(gpsimd=lambda g, e_=e_, mw=mw: dma_g(g, [
                        (wd2c[:], Wd2[e_][:, mw * 512:(mw + 1) * 512]
                         .rearrange("(k p) m -> p k m", p=128)),
                    ]))

                    def mr(t, mw=mw):
                        last = None
                        for mi in range(4):
                            for k in range(2):
                                last = t.matmul(ps[mi][:], wd2c[:, k, mi * 128:(mi + 1) * 128],
                                                h1[:, k, :], start=(k == 0), stop=(k == 1))
                        return last
                    prog.step(tensor=mr)

                    def dv2(v, mw=mw, n_sl=n_sl, e_=e_):
                        last = None
                        for mi in range(4):
                            m = mw * 4 + mi
                            last = v.scalar_tensor_tensor(
                                xs2[:, mi, :], ps[mi][:], bd2s[:, e_ * 8 + m:e_ * 8 + m + 1],
                                sx[:, m, n_sl], ALU.add, ALU.subtract)
                        return last
                    prog.step(vector=dv2)

                    def sqa(s_, n=n, mw=mw):
                        last = None
                        for mi in range(4):
                            ci = (n * 2 + mw) * 4 + mi
                            last = s_.activation(tmp6[:], xs2[:, mi, :], AF.Square,
                                                 accum_out=lcols[:, ci:ci + 1])
                        return last
                    prog.step(scalar=sqa)
            prog.step(vector=lambda v: v.tensor_reduce(
                lcols[:, 32:33], lcols[:, 0:32], mybir.AxisListType.X, ALU.add))
            prog.step(tensor=lambda t: t.matmul(ps[4][0:1, 0:1], ones[:, 0:1],
                                                lcols[:, 32:33], start=True, stop=True))
            prog.step(vector=lambda v, e_=e_: v.tensor_copy(lrow[0:1, e_:e_ + 1],
                                                            ps[4][0:1, 0:1]))
        prog.step(sync=lambda e: dma_s(e, [(lrow_loc, lrow[:])]))
        st6.close()

        cc("AllGather", ALU.bypass, lrow_loc_h, lparts_all_h)

        st7 = contextlib.ExitStack()
        t7 = alloc(nc, st7, sbufs=[
            ("p7_bx3s", (128, 8, TOK), F32),
            ("p7_x3bs", (128, 8, TOK), BF16),
            ("p7_a_all", (64, 8, TOK), BF16),
            ("p7_wauc", (64, 8, D), BF16),
            ("p7_wadc", (128, 8, 64), BF16),
            ("p7_lp", (8, 8), F32),
            ("p7_selb", (8, 2), F32),
            ("p7_baups", (8, D), F32),
            ("p7_badcs", (64, 8), F32),
            ("p7_lrow", (1, 8), F32),
            ("p7_mcol", (1, 2), F32),
            ("p7_ind", (1, 8), F32),
            ("p7_indc8", (8, 1), F32),
            ("p7_ind64", (64, 8), F32),
            ("p7_bcol", (128, 8), F32),
            ("p7_tmp", (128, 512), F32),
            ("p7_sq", (128, 2, 512), F32),
            ("p7_mv", (1, 4 * 512), F32),
            ("p7_ostg", (128, 8, 512), F32),
        ])
        bx3s, x3bs, a_all, wauc, wadc = (t7["p7_bx3s"], t7["p7_x3bs"], t7["p7_a_all"],
                                         t7["p7_wauc"], t7["p7_wadc"])
        lp, sel7, baups, badcs = t7["p7_lp"], t7["p7_selb"], t7["p7_baups"], t7["p7_badcs"]
        lrow7, mcol7, ind, indc8, ind64, bcol = (t7["p7_lrow"], t7["p7_mcol"], t7["p7_ind"],
                                                 t7["p7_indc8"], t7["p7_ind64"],
                                                 t7["p7_bcol"])
        tmp7, sq7, mv7, ostg = t7["p7_tmp"], t7["p7_sq"], t7["p7_mv"], t7["p7_ostg"]

        prog.step(sync=lambda e: dma_s(e, [
            (lp[:], lparts_all), (sel7[:], selb), (baups[:], baup8), (badcs[:], badc),
            (bx3s[:], bx3_scr.rearrange("(j p) t -> p j t", p=128)),
            (x3bs[:], x3b_scr.rearrange("(j p) t -> p j t", p=128)),
        ]), gpsimd=lambda g: dma_g(g, [
            (wauc[:], Wau.rearrange("e k m -> k e m")),
        ]))

        prog.step(tensor=lambda t: t.matmul(ps[0][0:2, 0:8], sel7[:], lp[:],
                                            start=True, stop=True))
        prog.step(vector=lambda v: v.tensor_copy(lp[0:2, :], ps[0][0:2, 0:8]))
        prog.step(sync=lambda e: dma_s(e, [(l2scr[:], lp[0:2, :])]))

        def ownrow(e):
            return dma_s(e, [(lrow7[:], l2scr[bass.ts(dyn["r4"], 1), :])])
        prog.step(sync=ownrow)

        prog.step(vector=lambda v: v.tensor_reduce(mcol7[0:1, 0:1], lrow7[:],
                                                   mybir.AxisListType.X, ALU.min))
        prog.step(vector=lambda v: v.tensor_scalar(ind[:], lrow7[:], mcol7[0:1, 0:1],
                                                   None, ALU.is_le))
        prog.step(tensor=lambda t: (
            t.matmul(ps[0][0:8, 0:1], ind[:], ones[0:1, 0:1], start=True, stop=True),
            t.matmul(ps[1][0:64, 0:8], ones[0:1, 0:64], ind[:], start=True, stop=True))[-1])
        prog.step(vector=lambda v: (v.tensor_copy(indc8[:], ps[0][0:8, 0:1]),
                                    v.tensor_copy(ind64[:], ps[1][0:64, 0:8]))[-1])
        prog.step(tensor=lambda t: [t.matmul(ps[2][0:128, m:m + 1],
                                             baups[:, m * 128:(m + 1) * 128],
                                             indc8[:], start=True, stop=True)
                                    for m in range(8)][-1])
        prog.step(vector=lambda v: v.tensor_copy(bcol[:], ps[2][:, 0:8]))

        for e_ in range(E):
            prog.step(gpsimd=lambda g, e_=e_: dma_g(g, [
                (wadc[:], Wad[e_].rearrange("(k p) m -> p k m", p=128)),
            ]))
            for n in range(4):
                def ma(t, n=n):
                    last = None
                    for k in range(8):
                        last = t.matmul(ps[0][0:64, :], wadc[:, k, :],
                                        x3bs[:, k, n * 512:(n + 1) * 512],
                                        start=(k == 0), stop=(k == 7))
                    return last
                prog.step(tensor=ma)
                prog.step(scalar=lambda s, e_=e_, n=n: s.activation(
                    a_all[:, e_, n * 512:(n + 1) * 512], ps[0][0:64, :], AF.Relu,
                    bias=badcs[:, e_:e_ + 1]))
            prog.step(vector=lambda v, e_=e_: v.tensor_scalar(
                a_all[:, e_, :], a_all[:, e_, :], ind64[:, e_:e_ + 1], None, ALU.mult))

        for n in range(4):
            n_sl = slice(n * 512, (n + 1) * 512)
            for mw in range(2):
                def mu_(t, mw=mw, n_sl=n_sl):
                    last = None
                    for mi in range(4):
                        m = mw * 4 + mi
                        for e_ in range(E):
                            last = t.matmul(ps[mi][:], wauc[:, e_, m * 128:(m + 1) * 128],
                                            a_all[:, e_, n_sl],
                                            start=(e_ == 0), stop=(e_ == 7))
                    return last
                prog.step(tensor=mu_)

                def evu(v, mw=mw, n_sl=n_sl):
                    last = None
                    for mi in range(4):
                        m = mw * 4 + mi
                        v.tensor_scalar(tmp7[:], ps[mi][:], bcol[:, m:m + 1], None, ALU.add)
                        last = v.tensor_tensor(bx3s[:, m, n_sl], tmp7[:], bx3s[:, m, n_sl],
                                               ALU.add)
                    return last
                prog.step(vector=evu)

        def ln4_out(j, n, n_sl, v, x_ap, ps_a, ps_b):
            v.tensor_tensor(ostg[:, j, :], x_ap, ps_a[:], ALU.mult)
            return v.tensor_tensor(ostg[:, j, :], ostg[:, j, :], ps_b[:], ALU.add)

        def ln4_store(prog_, n, n_sl):
            prog_.step(sync=lambda e, n_sl=n_sl: dma_s(e, [
                (out_d[:, n_sl].rearrange("(j p) t -> p j t", p=128), ostg[:])]))
        emit_ln_fused(prog, lambda j, n_sl: bx3s[:, j, n_sl], ln4_out, sq7,
                      ps[0], ps[1], ps[2], ps[3], ones, mv7, consts,
                      post_chunk=ln4_store)
        st7.close()
    st0.close()
    return nc



def kernel(**I):
    P = host_prep(I)
    ident = np.eye(128, dtype=np.float32)
    cores = list(range(NC))

    nc = bass.Bass("TRN2", target_bir_lowering=False, debug=False)
    build_full(nc)
    in_maps = []
    for c in cores:
        in_maps.append({
            "x_fm": P["x_fm"][c], "lng": P["lng"], "lnb": P["lnb"],
            "inwT": P["inwT"][c], "convw": P["convw"][c], "convb": P["convb"][c],
            "xprojT": P["xprojT"][c], "A_log": P["A_log"], "dtw": P["dtw"][c],
            "dtb": P["dtb"][c], "Dc": P["Dc"][c], "sel": P["sel"],
            "outwT": P["outwT"], "gamma": P["gamma"], "beta": P["beta"],
            "gate_w": P["gate_w"], "w1": P["w1"], "b1c": P["b1c"],
            "w2": P["w2"], "b2row": P["b2row"], "wattnT": P["wattnT"],
            "ident": ident, "dmask": P["dmask"], "wprojT": P["wprojT"],
            "W_base": P["W_base"], "bbase": P["bbase"], "Wd1": P["Wd1"],
            "bd1c": P["bd1c"], "Wd2": P["Wd2"], "bd2c": P["bd2c"],
            "selb": P["selb"], "badc": P["badc"], "Wad": P["Wad"],
            "Wau": P["Wau"], "baup8": P["baup8"]})
    r = run_bass_kernel_spmd(nc, in_maps, core_ids=cores).results

    out = np.empty((B * S, D), np.float32)
    for c in cores:
        out[c * TOK:(c + 1) * TOK] = np.asarray(r[c]["out_slab"], np.float32).T
    return out.reshape(B, S, D)


# version 04
# speedup vs baseline: 1.0810x; 1.0810x over previous
import numpy as np
import ml_dtypes
import contextlib
import concourse.bass as bass
import concourse.mybir as mybir
from concourse.bass_utils import run_bass_kernel_spmd

F32 = mybir.dt.float32
BF16 = mybir.dt.bfloat16
AF = mybir.ActivationFunctionType
ALU = mybir.AluOpType
BF = ml_dtypes.bfloat16

NC = 8
B = 2
S = 8192
D = 1024
DI = 2048
DIC = DI // NC
NST = 16
HMOE = 4096
E = 8
TOK = (B * S) // NC
NH, HS = 16, 64
EPS = 1e-5
EM10 = float(np.exp(np.float64(-10.0)))

ENGS = ("sync", "tensor", "vector", "scalar", "gpsimd")


class Prog:

    def __init__(self, nc, block, tick, dsems):
        self.nc = nc
        self.block = block
        self.tick = tick
        self.k = 0
        self.dsems = dsems
        self.dctr = {e: 0 for e in dsems}
        self.ccn = 0

    def step(self, **work):
        self.k += 1
        prev = 5 * (self.k - 1)
        for name in ENGS:
            fn = work.get(name)

            def emit(eng, fn=fn):
                eng.wait_ge(self.tick, prev)
                last = fn(eng) if fn is not None else None
                if last is None:
                    last = eng.nop()
                last.then_inc(self.tick, 1)
            getattr(self.block, name)(emit)

    def dmas(self, eng_name):
        sem = self.dsems[eng_name]

        def go(eng, transfers, **kw):
            for out, in_ in transfers:
                eng.dma_start(out=out, in_=in_, **kw).then_inc(sem, 16)
                self.dctr[eng_name] += 16
            eng.wait_ge(sem, self.dctr[eng_name])
            return eng.nop()
        return go


def alloc(nc, st, sbufs=(), psums=(), sems=()):
    d = {}
    for name, shape, dt in sbufs:
        d[name] = st.enter_context(nc.sbuf_tensor(name, list(shape), dt))
    for name, shape, dt in psums:
        d[name] = st.enter_context(nc.psum_tensor(name, list(shape), dt))
    for name in sems:
        d[name] = st.enter_context(nc.semaphore(name))
    return d



def _col8(v):
    return np.ascontiguousarray(v.reshape(8, 128).T)


def host_prep(I):
    P = {}
    x = I["L0_attn_x"]
    xf = np.asarray(x).reshape(B * S, D)
    P["x_fm"] = [np.ascontiguousarray(xf[c * TOK:(c + 1) * TOK].T) for c in range(NC)]
    P["lng"] = _col8(I["L0_attn_ln_g"])
    P["lnb"] = _col8(I["L0_attn_ln_b"])
    in_w = I["L0_attn_in_w"]
    P["inwT"] = []
    for c in range(NC):
        rows = np.concatenate([in_w[DIC * c:DIC * (c + 1)],
                               in_w[DI + DIC * c:DI + DIC * (c + 1)]], axis=0)
        P["inwT"].append(np.ascontiguousarray(rows.T))
    cw_ = I["L0_attn_conv_w"][:, 0, :]
    P["convw"] = [np.ascontiguousarray(
        cw_[DIC * c:DIC * (c + 1)].reshape(2, 128, 4).transpose(1, 0, 2).reshape(128, 8))
        for c in range(NC)]
    P["convb"] = [np.ascontiguousarray(
        I["L0_attn_conv_b"][DIC * c:DIC * (c + 1)].reshape(2, 128).T) for c in range(NC)]
    xp_w = I["L0_attn_xproj_w"]
    P["xprojT"] = [np.ascontiguousarray(
        xp_w[:, DIC * c:DIC * (c + 1)].T.reshape(2, 128, 33).transpose(1, 0, 2).reshape(128, 66))
        for c in range(NC)]
    P["dtw"] = [np.ascontiguousarray(I["L0_attn_dt_w"][DIC * c:DIC * (c + 1), 0]
                                     .reshape(2, 128).T) for c in range(NC)]
    P["dtb"] = [np.ascontiguousarray(I["L0_attn_dt_b"][DIC * c:DIC * (c + 1)]
                                     .reshape(2, 128).T) for c in range(NC)]
    P["Dc"] = [np.ascontiguousarray(I["L0_attn_D"][DIC * c:DIC * (c + 1)]
                                    .reshape(2, 128).T) for c in range(NC)]
    P["A_log"] = np.ascontiguousarray(I["L0_attn_A_log"].reshape(1, 16))
    P["outwT"] = np.ascontiguousarray(I["L0_attn_out_w"].T)
    P["gate_w"] = np.ascontiguousarray(I["L0_ffn_gate_w"])
    P["w1"] = I["L0_ffn_w1"]
    P["b1c"] = np.ascontiguousarray(
        I["L0_ffn_b1"].reshape(8, 32, 128).transpose(2, 0, 1).reshape(128, 8 * 32))
    P["w2"] = I["L0_ffn_w2"]
    P["b2row"] = np.ascontiguousarray(I["L0_ffn_b2"].reshape(1, 8 * 1024))
    P["gamma"] = _col8(I["L0_ffn_gamma"])
    P["beta"] = _col8(I["L0_ffn_beta"])
    P["wattnT"] = np.ascontiguousarray(I["L1_attn_w_attn"].T)
    P["wprojT"] = np.ascontiguousarray(I["L1_attn_w_proj"].T)
    P["W_base"] = np.ascontiguousarray(I["L1_ffn_W_base"])
    P["bbase"] = _col8(I["L1_ffn_b_base"])
    P["Wd1"] = I["L1_ffn_Wd1"]
    P["bd1c"] = np.ascontiguousarray(
        I["L1_ffn_bd1"].reshape(8, 2, 128).transpose(2, 0, 1).reshape(128, 16))
    P["Wd2"] = I["L1_ffn_Wd2"]
    P["bd2c"] = np.ascontiguousarray(
        I["L1_ffn_bd2"].reshape(8, 8, 128).transpose(2, 0, 1).reshape(128, 64))
    P["Wad"] = I["L1_ffn_Wa_down"]
    P["badc"] = np.ascontiguousarray(I["L1_ffn_ba_down"].T)
    P["Wau"] = I["L1_ffn_Wa_up"]
    P["baup8"] = np.ascontiguousarray(I["L1_ffn_ba_up"])
    selm2 = np.zeros((8, 2), np.float32)
    selm2[0:4, 0] = 1.0
    selm2[4:8, 1] = 1.0
    P["selb"] = selm2
    selm = np.zeros((16, 16 * 128), np.float32)
    for n in range(16):
        selm[n, n * 128:(n + 1) * 128] = 1.0
    P["sel"] = selm
    m = np.zeros((4, 128, 512), np.float32)
    for r in range(4):
        for p in range(128):
            m[r, p, :] = (np.arange(512) >= p + 128 * r)
    P["dmask"] = np.ascontiguousarray(m.reshape(4 * 128, 512))
    return P



def emit_ln_stats(prog, x_tile, sq_sb, ps_mu, ps_sq, ones_col,
                  mvrow, arow, brow, eps_col, ncols=512, n_chunks=4, nd=8,
                  nfeat=1024):
    inv = 1.0 / nfeat
    G = sq_sb.shape[1]
    for n in range(n_chunks):
        n_sl = slice(n * ncols, (n + 1) * ncols)
        for g0 in range(0, nd, G):
            def sqs(s, g0=g0, n_sl=n_sl):
                last = None
                for i in range(G):
                    last = s.activation(sq_sb[:, i, :], x_tile(g0 + i, n_sl), AF.Square)
                return last
            prog.step(scalar=sqs)

            def mms(t, g0=g0, n_sl=n_sl):
                last = None
                for i in range(G):
                    j = g0 + i
                    t.matmul(ps_mu[0:1, :], ones_col[:, 0:1], x_tile(j, n_sl),
                             start=(j == 0), stop=(j == nd - 1))
                    last = t.matmul(ps_sq[0:1, :], ones_col[:, 0:1], sq_sb[:, i, :],
                                    start=(j == 0), stop=(j == nd - 1))
                return last
            prog.step(tensor=mms)

        mu = mvrow[0:1, 0:ncols]
        va = mvrow[0:1, ncols:2 * ncols]

        def stats(v, n_sl=n_sl, mu=mu, va=va):
            v.tensor_scalar(mu, ps_mu[0:1, :], inv, None, ALU.mult)
            v.tensor_scalar(va, ps_sq[0:1, :], inv, None, ALU.mult)
            v.tensor_tensor(arow[0:1, n_sl], mu, mu, ALU.mult)
            return v.tensor_tensor(va, va, arow[0:1, n_sl], ALU.subtract)
        prog.step(vector=stats)

        def rstd(s, n_sl=n_sl, va=va):
            s.activation(arow[0:1, n_sl], va, AF.Ln, bias=eps_col[0:1, 0:1])
            return s.activation(arow[0:1, n_sl], arow[0:1, n_sl], AF.Exp, scale=-0.5)
        prog.step(scalar=rstd)

        def bro(v, n_sl=n_sl, mu=mu):
            return v.scalar_tensor_tensor(brow[0:1, n_sl], mu, -1.0,
                                          arow[0:1, n_sl], ALU.mult, ALU.mult)
        prog.step(vector=bro)


def emit_ln_apply(prog, x_tile, out_fns, ps_a, ps_b, ones_row, arow, brow, tmp_sb,
                  ncols=512, n_chunks=4, nd=8):
    for n in range(n_chunks):
        n_sl = slice(n * ncols, (n + 1) * ncols)

        def bcast(t, n_sl=n_sl):
            t.matmul(ps_a[:, :], ones_row[0:1, :], arow[0:1, n_sl], start=True, stop=True)
            return t.matmul(ps_b[:, :], ones_row[0:1, :], brow[0:1, n_sl],
                            start=True, stop=True)
        prog.step(tensor=bcast)

        def apply(v, n=n, n_sl=n_sl):
            last = None
            for j in range(nd):
                v.tensor_tensor(tmp_sb[:, 0:ncols], x_tile(j, n_sl), ps_a[:, :], ALU.mult)
                v.tensor_tensor(tmp_sb[:, 0:ncols], tmp_sb[:, 0:ncols], ps_b[:, :], ALU.add)
                last = out_fns(j, n, n_sl, v, tmp_sb[:, 0:ncols])
            return last
        prog.step(vector=apply)


def emit_ln_fused(prog, x_tile, out_fn, sq_sb, ps_mu, ps_sq, ps_a, ps_b,
                  ones_col, mv4, eps_col, ncols=512, n_chunks=4, nd=8, nfeat=1024,
                  post_chunk=None):
    inv = 1.0 / nfeat
    G = sq_sb.shape[1]
    for n in range(n_chunks):
        n_sl = slice(n * ncols, (n + 1) * ncols)
        for g0 in range(0, nd, G):
            def sqs(s, g0=g0, n_sl=n_sl):
                last = None
                for i in range(G):
                    last = s.activation(sq_sb[:, i, :], x_tile(g0 + i, n_sl), AF.Square)
                return last
            prog.step(scalar=sqs)

            def mms(t, g0=g0, n_sl=n_sl):
                last = None
                for i in range(G):
                    j = g0 + i
                    t.matmul(ps_mu[0:1, :], ones_col[:, 0:1], x_tile(j, n_sl),
                             start=(j == 0), stop=(j == nd - 1))
                    last = t.matmul(ps_sq[0:1, :], ones_col[:, 0:1], sq_sb[:, i, :],
                                    start=(j == 0), stop=(j == nd - 1))
                return last
            prog.step(tensor=mms)

        mu = mv4[0:1, 0:ncols]
        va = mv4[0:1, ncols:2 * ncols]
        ar = mv4[0:1, 2 * ncols:3 * ncols]
        br = mv4[0:1, 3 * ncols:4 * ncols]

        def stats(v, mu=mu, va=va, ar=ar):
            v.tensor_scalar(mu, ps_mu[0:1, :], inv, None, ALU.mult)
            v.tensor_scalar(va, ps_sq[0:1, :], inv, None, ALU.mult)
            v.tensor_tensor(ar, mu, mu, ALU.mult)
            return v.tensor_tensor(va, va, ar, ALU.subtract)
        prog.step(vector=stats)

        def rstd(s, va=va, ar=ar):
            s.activation(ar, va, AF.Ln, bias=eps_col[0:1, 0:1])
            return s.activation(ar, ar, AF.Exp, scale=-0.5)
        prog.step(scalar=rstd)

        def bro(v, mu=mu, ar=ar, br=br):
            return v.scalar_tensor_tensor(br, mu, -1.0, ar, ALU.mult, ALU.mult)
        prog.step(vector=bro)

        def bcast(t, ar=ar, br=br):
            t.matmul(ps_a[:], ones_col[0:1, 0:128], ar, start=True, stop=True)
            return t.matmul(ps_b[:], ones_col[0:1, 0:128], br, start=True, stop=True)
        prog.step(tensor=bcast)

        def apply(v, n=n, n_sl=n_sl):
            last = None
            for j in range(nd):
                last = out_fn(j, n, n_sl, v, x_tile(j, n_sl), ps_a, ps_b)
            return last
        prog.step(vector=apply)
        if post_chunk is not None:
            post_chunk(prog, n, n_sl)



def build_full(nc):
    x_in = nc.dram_tensor("x_fm", [D, TOK], F32, kind="ExternalInput").ap()
    lng = nc.dram_tensor("lng", [128, 8], F32, kind="ExternalInput").ap()
    lnb = nc.dram_tensor("lnb", [128, 8], F32, kind="ExternalInput").ap()
    inwT = nc.dram_tensor("inwT", [D, 4 * 128], F32, kind="ExternalInput").ap()
    convw = nc.dram_tensor("convw", [128, 8], F32, kind="ExternalInput").ap()
    convb = nc.dram_tensor("convb", [128, 2], F32, kind="ExternalInput").ap()
    xprojT = nc.dram_tensor("xprojT", [128, 66], F32, kind="ExternalInput").ap()
    alog = nc.dram_tensor("A_log", [1, 16], F32, kind="ExternalInput").ap()
    dtw = nc.dram_tensor("dtw", [128, 2], F32, kind="ExternalInput").ap()
    dtb = nc.dram_tensor("dtb", [128, 2], F32, kind="ExternalInput").ap()
    Dc = nc.dram_tensor("Dc", [128, 2], F32, kind="ExternalInput").ap()
    sel = nc.dram_tensor("sel", [16, 16 * 128], F32, kind="ExternalInput").ap()
    outwT = nc.dram_tensor("outwT", [DI, D], F32, kind="ExternalInput").ap()
    gamma = nc.dram_tensor("gamma", [128, 8], F32, kind="ExternalInput").ap()
    beta = nc.dram_tensor("beta", [128, 8], F32, kind="ExternalInput").ap()
    gate_w = nc.dram_tensor("gate_w", [D, 8], F32, kind="ExternalInput").ap()
    w1 = nc.dram_tensor("w1", [E, D, HMOE], F32, kind="ExternalInput").ap()
    b1c = nc.dram_tensor("b1c", [128, E * 32], F32, kind="ExternalInput").ap()
    w2 = nc.dram_tensor("w2", [E, HMOE, D], F32, kind="ExternalInput").ap()
    b2row = nc.dram_tensor("b2row", [1, E * D], F32, kind="ExternalInput").ap()
    wattnT = nc.dram_tensor("wattnT", [D, 3 * D], F32, kind="ExternalInput").ap()
    ident = nc.dram_tensor("ident", [128, 128], F32, kind="ExternalInput").ap()
    dmask = nc.dram_tensor("dmask", [4 * 128, 512], F32, kind="ExternalInput").ap()
    wprojT = nc.dram_tensor("wprojT", [D, D], F32, kind="ExternalInput").ap()
    Wb = nc.dram_tensor("W_base", [D, D], F32, kind="ExternalInput").ap()
    bbase = nc.dram_tensor("bbase", [128, 8], F32, kind="ExternalInput").ap()
    Wd1 = nc.dram_tensor("Wd1", [E, D, 256], F32, kind="ExternalInput").ap()
    bd1c = nc.dram_tensor("bd1c", [128, 16], F32, kind="ExternalInput").ap()
    Wd2 = nc.dram_tensor("Wd2", [E, 256, D], F32, kind="ExternalInput").ap()
    bd2c = nc.dram_tensor("bd2c", [128, 64], F32, kind="ExternalInput").ap()
    selb = nc.dram_tensor("selb", [E, 2], F32, kind="ExternalInput").ap()
    badc = nc.dram_tensor("badc", [64, E], F32, kind="ExternalInput").ap()
    Wad = nc.dram_tensor("Wad", [E, D, 64], F32, kind="ExternalInput").ap()
    Wau = nc.dram_tensor("Wau", [E, 64, D], F32, kind="ExternalInput").ap()
    baup8 = nc.dram_tensor("baup8", [E, D], F32, kind="ExternalInput").ap()
    out_d = nc.dram_tensor("out_slab", [D, TOK], F32, kind="ExternalOutput").ap()

    xn0_loc_h = nc.dram_tensor("xn0_loc", [D, TOK], BF16, kind="Internal")
    xn0_all_h = nc.dram_tensor("xn0_all", [NC, D, TOK], BF16, kind="Internal")
    xs_scr = nc.dram_tensor("xs_scr", [2 * DIC, S], F32, kind="Internal").ap()
    z_scr = nc.dram_tensor("z_scr", [2 * DIC, S], BF16, kind="Internal").ap()
    xp_loc_h = nc.dram_tensor("xp_loc", [66, S], F32, kind="Internal")
    xp_sum_h = nc.dram_tensor("xp_sum", [66, S], F32, kind="Internal")
    y_loc_h = nc.dram_tensor("y_loc", [NC, 2 * 128, TOK], BF16, kind="Internal")
    y_all_h = nc.dram_tensor("y_all", [NC * NC, 2 * 128, TOK], BF16, kind="Internal")
    x2_scr = nc.dram_tensor("x2_scr", [D, TOK], F32, kind="Internal").ap()
    y_cols = nc.dram_tensor("y_cols", [DI, TOK], BF16, kind="Internal").ap()
    qk_loc_h = nc.dram_tensor("qk_loc", [4, 8, 64, TOK], BF16, kind="Internal")
    qk_all_h = nc.dram_tensor("qk_all", [32, 8, 64, TOK], BF16, kind="Internal")
    qk_p = nc.dram_tensor("qk_p", [4, 8, 64, TOK], BF16, kind="Internal").ap()
    v_loc_h = nc.dram_tensor("v_loc", [16, 128, 16, 64], BF16, kind="Internal")
    v_all_h = nc.dram_tensor("v_all", [NC * 16, 128, 16, 64], BF16, kind="Internal")
    v_p = nc.dram_tensor("v_p", [16, 128, 16, 64], BF16, kind="Internal").ap()
    yat_loc_h = nc.dram_tensor("yat_loc", [4, 256, TOK], BF16, kind="Internal")
    yat_all_h = nc.dram_tensor("yat_all", [NC * 4, 256, TOK], BF16, kind="Internal")
    yat_cols = nc.dram_tensor("yat_cols", [D, TOK], BF16, kind="Internal").ap()
    x3b_scr = nc.dram_tensor("x3b_scr", [D, TOK], BF16, kind="Internal").ap()
    bx3_scr = nc.dram_tensor("bx3_scr", [D, TOK], F32, kind="Internal").ap()
    lrow_loc_h = nc.dram_tensor("lrow_loc", [1, E], F32, kind="Internal")
    lparts_all_h = nc.dram_tensor("lparts_all", [NC, E], F32, kind="Internal")
    l2scr = nc.dram_tensor("l2scr", [2, E], F32, kind="Internal").ap()

    xn0_loc = xn0_loc_h.ap()
    xn0_all = xn0_all_h.ap()
    xp_loc = xp_loc_h.ap()
    xp_sum = xp_sum_h.ap()
    y_loc = y_loc_h.ap()
    y_all = y_all_h.ap()
    qk_loc = qk_loc_h.ap()
    qk_all = qk_all_h.ap()
    v_loc = v_loc_h.ap()
    v_all = v_all_h.ap()
    yat_loc = yat_loc_h.ap()
    yat_all = yat_all_h.ap()
    lrow_loc = lrow_loc_h.ap()
    lparts_all = lparts_all_h.ap()

    st0 = contextlib.ExitStack()
    sh = alloc(nc, st0, sbufs=[
        ("ones", (128, 128), F32),
        ("idn", (128, 128), F32),
        ("consts", (128, 4), F32),
    ], psums=[(f"ps{i}", (128, 512), F32) for i in range(8)],
       sems=["tick", "dm_sync", "dm_gp", "dm_sc", "dm_ve", "cc_sem"])
    ones, idn, consts = sh["ones"], sh["idn"], sh["consts"]
    ps = [sh[f"ps{i}"] for i in range(8)]
    tick, dm_sync, dm_gp, cc_sem = sh["tick"], sh["dm_sync"], sh["dm_gp"], sh["cc_sem"]

    with nc.Block() as block:
        prog = Prog(nc, block, tick, {"sync": dm_sync, "gpsimd": dm_gp,
                                      "scalar": sh["dm_sc"], "vector": sh["dm_ve"]})
        dma_s = prog.dmas("sync")
        dma_g = prog.dmas("gpsimd")
        dma_c = prog.dmas("scalar")
        dma_v = prog.dmas("vector")

        def cc(kind, op, in_h, out_h):
            def go(g):
                prog.ccn += 1
                g.collective_compute(
                    kind, op, replica_groups=[list(range(NC))],
                    ins=[in_h.ap().opt()], outs=[out_h.ap().opt()],
                ).then_inc(cc_sem)
                g.wait_ge(cc_sem, prog.ccn)
                return g.nop()
            prog.step(gpsimd=go)

        dyn = {}

        def mkidx_s(e):
            rank = e.partition_id()
            r4 = rank // 4
            dyn["r4"] = r4
            base = rank + 12 * r4
            dyn["qv"] = [base + 4 * ji for ji in range(4)]
            return None

        def mkidx_g(g):
            rank_g = g.partition_id()
            dyn["ym"] = [rank_g + 8 * j for j in range(4)]
            return None

        def mkidx_c(s):
            rank_c = s.partition_id()
            base_c = rank_c + 12 * (rank_c // 4)
            dyn["qvc"] = [base_c + 4 * ji for ji in range(4)]
            dyn["ymc"] = [rank_c + 8 * j for j in range(4, 8)]
            return None

        prog.step(sync=lambda e: (mkidx_s(e), dma_s(e, [(idn[:], ident)]))[-1],
                  gpsimd=mkidx_g,
                  scalar=mkidx_c,
                  vector=lambda v: (v.memset(ones[:], 1.0),
                                    v.memset(consts[:, 0:1], EPS),
                                    v.memset(consts[:, 1:2], -10.0))[-1])

        st1 = contextlib.ExitStack()
        t1 = alloc(nc, st1, sbufs=[
            ("p1_xT", (128, 8, TOK), F32), ("p1_xn", (128, 8, TOK), BF16),
            ("p1_sq", (128, 8, 512), F32), ("p1_rows", (1, 4 * TOK), F32),
            ("p1_gb", (128, 16), F32), ("p1_tmp", (128, 512), F32),
        ])
        xT, xn, sq1, rows1, gb1, tmp1 = (t1["p1_xT"], t1["p1_xn"], t1["p1_sq"],
                                         t1["p1_rows"], t1["p1_gb"], t1["p1_tmp"])
        arow = rows1[:, 2 * TOK:3 * TOK]
        brow = rows1[:, 3 * TOK:4 * TOK]

        prog.step(sync=lambda e: dma_s(e, [
            (xT[:, 0:4, :], x_in[0:512, :].rearrange("(j p) t -> p j t", p=128)),
            (gb1[:, 0:8], lng), (gb1[:, 8:16], lnb),
        ]), gpsimd=lambda g: dma_g(g, [
            (xT[:, 4:8, :], x_in[512:1024, :].rearrange("(j p) t -> p j t", p=128)),
        ]))

        emit_ln_stats(prog, lambda j, n_sl: xT[:, j, n_sl], sq1, ps[0], ps[1],
                      ones, rows1[:, 0:1024], arow, brow, consts)

        def out_fn1(j, n, n_sl, v, normed):
            return v.tensor_scalar(xn[:, j, n_sl], normed, gb1[:, j:j + 1],
                                   gb1[:, 8 + j:9 + j], ALU.mult, ALU.add)
        emit_ln_apply(prog, lambda j, n_sl: xT[:, j, n_sl], out_fn1,
                      ps[2], ps[3], ones, arow, brow, tmp1)

        prog.step(sync=lambda e: dma_s(e, [
            (xn0_loc.rearrange("(j p) t -> p j t", p=128), xn[:]),
        ]))
        st1.close()

        cc("AllGather", ALU.bypass, xn0_loc_h, xn0_all_h)

        TCH = 2048
        st2 = contextlib.ExitStack()
        t2 = alloc(nc, st2, sbufs=[
            ("p2_inw", (128, 8, 4 * 128), BF16), ("p2_xnc", (128, 8, TCH), BF16),
            ("p2_xi", (128, 2, TCH + 3), F32), ("p2_xc", (128, 2, TCH), F32),
            ("p2_xsc", (128, 2, TCH), F32), ("p2_zst", (128, 2, TCH), BF16),
            ("p2_cw", (128, 8), F32), ("p2_cb", (128, 2), F32),
            ("p2_xpj", (128, 2, 33), F32), ("p2_xpst", (33, 2, TCH), F32),
            ("p2_carry", (128, 2, 3), F32),
        ])
        inw, xnc, xi, xc, xsc, zst = (t2["p2_inw"], t2["p2_xnc"], t2["p2_xi"],
                                      t2["p2_xc"], t2["p2_xsc"], t2["p2_zst"])
        cw_sb, cb_sb, xpj, xpst, carry = (t2["p2_cw"], t2["p2_cb"], t2["p2_xpj"],
                                          t2["p2_xpst"], t2["p2_carry"])

        prog.step(gpsimd=lambda g: dma_g(g, [
            (inw[:], inwT.rearrange("(k p) m -> p k m", p=128)),
        ]), sync=lambda e: dma_s(e, [
            (cw_sb[:], convw), (cb_sb[:], convb),
            (xpj[:], xprojT.rearrange("p (d m) -> p d m", d=2)),
        ]))

        for b in range(B):
            for ch in range(S // TCH):
                rk = 4 * b + ch

                def ldxn(e, rk=rk):
                    return dma_s(e, [(xnc[:, 0:4, :], xn0_all[rk, 0:512, :]
                                      .rearrange("(j p) t -> p j t", p=128))])

                def ldxn_g(g, rk=rk):
                    return dma_g(g, [(xnc[:, 4:8, :], xn0_all[rk, 512:1024, :]
                                      .rearrange("(j p) t -> p j t", p=128))])
                prog.step(sync=ldxn, gpsimd=ldxn_g)

                for n in range(TCH // 512):
                    n_sl = slice(n * 512, (n + 1) * 512)

                    def mms(t, n_sl=n_sl):
                        last = None
                        for m in range(4):
                            for k in range(8):
                                last = t.matmul(ps[m][:], inw[:, k, m * 128:(m + 1) * 128],
                                                xnc[:, k, n_sl],
                                                start=(k == 0), stop=(k == 7))
                        return last

                    def evac(v, n=n, n_sl=n_sl, ch=ch):
                        x_sl = slice(3 + n * 512, 3 + (n + 1) * 512)
                        if n == 0:
                            if ch == 0:
                                v.memset(carry[:], 0.0)
                            v.tensor_copy(xi[:, :, 0:3], carry[:])
                        v.tensor_copy(xi[:, 0, x_sl], ps[0][:])
                        v.tensor_copy(xi[:, 1, x_sl], ps[1][:])
                        v.tensor_copy(zst[:, 0, n_sl], ps[2][:])
                        return v.tensor_copy(zst[:, 1, n_sl], ps[3][:])
                    prog.step(tensor=mms)
                    prog.step(vector=evac)

                def conv(v):
                    last = None
                    for dt in range(2):
                        v.tensor_scalar(xc[:, dt, :], xi[:, dt, 0:TCH],
                                        cw_sb[:, dt * 4:dt * 4 + 1], None, ALU.mult)
                        for k in range(1, 4):
                            last = v.scalar_tensor_tensor(
                                xc[:, dt, :], xi[:, dt, k:k + TCH],
                                cw_sb[:, dt * 4 + k:dt * 4 + k + 1],
                                xc[:, dt, :], ALU.mult, ALU.add)
                    return last
                prog.step(vector=conv)

                def silu_save(s):
                    s.activation(xsc[:, 0, :], xc[:, 0, :], AF.Silu, bias=cb_sb[:, 0:1])
                    return s.activation(xsc[:, 1, :], xc[:, 1, :], AF.Silu, bias=cb_sb[:, 1:2])

                def carry_save(v):
                    return v.tensor_copy(carry[:], xi[:, :, TCH:TCH + 3])
                prog.step(scalar=silu_save, vector=carry_save)

                def xpmm(t):
                    last = None
                    for n in range(4):
                        for dt in range(2):
                            last = t.matmul(ps[n][0:33, :], xpj[:, dt, :],
                                            xsc[:, dt, n * 512:(n + 1) * 512],
                                            start=(dt == 0), stop=(dt == 1))
                    return last
                prog.step(tensor=xpmm)

                def xpev(v):
                    last = None
                    for n in range(4):
                        last = v.tensor_copy(xpst[:, 0, n * 512:(n + 1) * 512], ps[n][0:33, :])
                    return last
                prog.step(vector=xpev)

                def store(e, b=b, ch=ch):
                    c_sl = slice(ch * TCH, (ch + 1) * TCH)
                    return dma_s(e, [
                        (xs_scr[b * 256:b * 256 + 128, c_sl], xsc[:, 0, :]),
                        (xs_scr[b * 256 + 128:b * 256 + 256, c_sl], xsc[:, 1, :]),
                        (z_scr[b * 256:b * 256 + 128, c_sl], zst[:, 0, :]),
                        (z_scr[b * 256 + 128:b * 256 + 256, c_sl], zst[:, 1, :]),
                        (xp_loc[b * 33:b * 33 + 33, c_sl], xpst[:, 0, :]),
                    ])
                prog.step(sync=store)
        st2.close()

        cc("AllReduce", ALU.add, xp_loc_h, xp_sum_h)

        st3 = contextlib.ExitStack()
        t3 = alloc(nc, st3, sbufs=[
            ("p3_xp0", (1, TCH), F32),
            ("p3_B_t", (16, TCH), BF16), ("p3_C_t", (16, TCH), BF16),
            ("p3_sel", (16, 16 * 128), BF16),
            ("p3_Arep", (128, 16), F32), ("p3_Arow", (1, 16), F32),
            ("p3_cols", (128, 8), F32),
            ("p3_hstate", (128, 64), F32),
            ("p3_xs", (128, TCH), F32), ("p3_z", (128, TCH), BF16),
            ("p3_delta", (128, TCH), F32), ("p3_u", (128, TCH), F32),
            ("p3_dA", (128, TCH), F32), ("p3_dBx", (128, TCH), F32),
            ("p3_h", (128, TCH), F32), ("p3_yacc", (128, TCH), F32),
            ("p3_sz", (128, TCH), F32), ("p3_ybf", (128, TCH), BF16),
        ])
        (xp0, B_t, C_t, sel_sb, Arep, Arow, cols, hstate, xs_t, z_t,
         delta, u_t, dA, dBx, h_t, yacc, sz, ybf) = (
            t3["p3_xp0"], t3["p3_B_t"], t3["p3_C_t"], t3["p3_sel"], t3["p3_Arep"],
            t3["p3_Arow"], t3["p3_cols"], t3["p3_hstate"], t3["p3_xs"], t3["p3_z"],
            t3["p3_delta"], t3["p3_u"], t3["p3_dA"], t3["p3_dBx"], t3["p3_h"],
            t3["p3_yacc"], t3["p3_sz"], t3["p3_ybf"])
        psB = [ps[0], ps[1], ps[2], ps[3]]
        psC = [ps[4], ps[5], ps[6], ps[7]]

        prog.step(sync=lambda e: dma_s(e, [
            (Arow[:], alog), (cols[:, 0:2], dtw), (cols[:, 2:4], dtb),
            (cols[:, 4:6], Dc),
        ]), gpsimd=lambda g: dma_g(g, [(sel_sb[:], sel)]),
            vector=lambda v: v.memset(hstate[:], 0.0))

        prog.step(scalar=lambda s: s.activation(Arow[:], Arow[:], AF.Exp))
        prog.step(vector=lambda v: v.tensor_scalar(Arow[:], Arow[:], -1.0, None, ALU.mult))
        prog.step(tensor=lambda t: t.matmul(ps[0][:, 0:16], ones[0:1, 0:128], Arow[:],
                                            start=True, stop=True))
        prog.step(vector=lambda v: v.tensor_copy(Arep[:], ps[0][:, 0:16]))

        for b in range(B):
            for g in range(2):
                for tsc in range(S // TCH):
                    t_sl = slice(tsc * TCH, (tsc + 1) * TCH)
                    row0 = b * 256 + g * 128

                    def lds(e, b=b, row0=row0, t_sl=t_sl):
                        return dma_s(e, [
                            (xs_t[:], xs_scr[row0:row0 + 128, t_sl]),
                            (z_t[:], z_scr[row0:row0 + 128, t_sl]),
                            (xp0[:], xp_sum[33 * b:33 * b + 1, t_sl]),
                        ])

                    def ldbc(g, b=b, t_sl=t_sl):
                        return dma_g(g, [
                            (B_t[:], xp_sum[1 + 33 * b:17 + 33 * b, t_sl]),
                            (C_t[:], xp_sum[17 + 33 * b:33 + 33 * b, t_sl]),
                        ])
                    prog.step(sync=lds, gpsimd=ldbc)

                    def x0b(t, t_sl=t_sl):
                        last = None
                        for n in range(4):
                            last = t.matmul(psB[n][:], ones[0:1, 0:128],
                                            xp0[0:1, n * 512:(n + 1) * 512],
                                            start=True, stop=True)
                        return last
                    prog.step(tensor=x0b)

                    def sdelta(s, g=g):
                        last = None
                        for n in range(4):
                            last = s.activation(delta[:, n * 512:(n + 1) * 512], psB[n][:],
                                                AF.Exp, scale=cols[:, g:g + 1],
                                                bias=cols[:, 2 + g:3 + g])
                        return last
                    prog.step(scalar=sdelta)
                    prog.step(scalar=lambda s: s.activation(delta[:], delta[:], AF.Ln, bias=1.0))
                    prog.step(vector=lambda v: v.tensor_tensor(u_t[:], delta[:], xs_t[:], ALU.mult))

                    for n in range(NST):
                        hcol = (b * 2 + g) * 16 + n

                        def brep(t, n=n):
                            last = None
                            for q in range(4):
                                q_sl = slice(q * 512, (q + 1) * 512)
                                t.matmul(psB[q][:], sel_sb[:, n * 128:(n + 1) * 128],
                                         B_t[:, q_sl], start=True, stop=True)
                                last = t.matmul(psC[q][:], sel_sb[:, n * 128:(n + 1) * 128],
                                                C_t[:, q_sl], start=True, stop=True)
                            return last

                        def dAexp(s, n=n):
                            return s.activation(dA[:], delta[:], AF.Exp,
                                                scale=Arep[:, n:n + 1])
                        prog.step(tensor=brep, scalar=dAexp)

                        def scan_grp(v, n=n, hcol=hcol):
                            for q in range(4):
                                v.tensor_tensor(dBx[:, q * 512:(q + 1) * 512],
                                                u_t[:, q * 512:(q + 1) * 512],
                                                psB[q][:], ALU.mult)
                            v.tensor_tensor_scan(h_t[:], dA[:], dBx[:],
                                                 hstate[:, hcol:hcol + 1], ALU.mult, ALU.add)
                            v.tensor_copy(hstate[:, hcol:hcol + 1], h_t[:, TCH - 1:TCH])
                            last = None
                            for q in range(4):
                                q_sl = slice(q * 512, (q + 1) * 512)
                                if n == 0:
                                    last = v.tensor_tensor(yacc[:, q_sl], h_t[:, q_sl],
                                                           psC[q][:], ALU.mult)
                                else:
                                    v.tensor_tensor(dBx[:, q_sl], h_t[:, q_sl],
                                                    psC[q][:], ALU.mult)
                                    last = v.tensor_tensor(yacc[:, q_sl], yacc[:, q_sl],
                                                           dBx[:, q_sl], ALU.add)
                            return last
                        prog.step(vector=scan_grp)

                    def ypre(v, g=g):
                        return v.scalar_tensor_tensor(yacc[:], xs_t[:], cols[:, 4 + g:5 + g],
                                                      yacc[:], ALU.mult, ALU.add)
                    prog.step(vector=ypre)
                    prog.step(scalar=lambda s: s.activation(sz[:], z_t[:], AF.Silu))
                    prog.step(vector=lambda v: v.tensor_tensor(ybf[:], yacc[:], sz[:], ALU.mult))
                    prog.step(sync=lambda e, b=b, tsc=tsc, g=g: dma_s(e, [
                        (y_loc[b * 4 + tsc, g * 128:(g + 1) * 128, :], ybf[:]),
                    ]))
        st3.close()

        cc("AllGather", ALU.bypass, y_loc_h, y_all_h)

        def ypick_g(g):
            return dma_g(g, [(y_cols[256 * j:256 * (j + 1), :],
                              y_all[bass.ts(dyn["ym"][j], 1), :, :])
                             for j in range(4)])

        def ypick_c(s):
            return dma_c(s, [(y_cols[256 * j:256 * (j + 1), :],
                              y_all[bass.ts(dyn["ymc"][j - 4], 1), :, :])
                             for j in range(4, 8)])
        prog.step(gpsimd=ypick_g, scalar=ypick_c)

        st4 = contextlib.ExitStack()
        t4 = alloc(nc, st4, sbufs=[
            ("p4_BIGA", (128, 16, 1024), F32),
            ("p4_gb", (128, 16), F32),
            ("p4_tmp", (128, 512), F32),
            ("p4_cwsb", (128, 128), F32),
            ("p4_glog", (128, 128), F32),
            ("p4_mcol", (128, 8), F32),
            ("p4_mcol16", (128, 16), F32),
            ("p4_m2col", (128, 16), F32),
            ("p4_rcol16", (128, 16), F32),
            ("p4_m8big", (128, 16, 8), F32),
        ])
        BIGA, gb4, tmp4 = t4["p4_BIGA"], t4["p4_gb"], t4["p4_tmp"]
        cwsb, glog, mcol = t4["p4_cwsb"], t4["p4_glog"], t4["p4_mcol"]
        mcol16, rcol16, m8big = t4["p4_mcol16"], t4["p4_rcol16"], t4["p4_m8big"]
        m2col = t4["p4_m2col"]

        st_xn1 = contextlib.ExitStack()
        xn1 = alloc(nc, st_xn1, sbufs=[("p4_xn1", (128, 8, TOK), BF16)])["p4_xn1"]
        stA = contextlib.ExitStack()
        sB = alloc(nc, stA, sbufs=[("p4_s", (128, 8, TOK), F32)])["p4_s"]

        stA1 = contextlib.ExitStack()
        tA1 = alloc(nc, stA1, sbufs=[
            ("p4_outwc", (128, 16, 256), BF16),
            ("p4_yb", (128, 16, 512), BF16),
            ("p4_xst2", (128, 2, 512), F32),
        ])
        outwc, yb4, xst2 = tA1["p4_outwc"], tA1["p4_yb"], tA1["p4_xst2"]

        prog.step(sync=lambda e: dma_s(e, [
            (gb4[:, 0:8], gamma), (gb4[:, 8:16], beta),
        ]))

        for n in range(4):
            n_sl = slice(n * 512, (n + 1) * 512)

            prog.step(sync=lambda e, n_sl=n_sl: dma_s(e, [
                (yb4[:, 0:8, :], y_cols[0:1024, n_sl]
                 .rearrange("(k p) t -> p k t", p=128)),
            ]), scalar=lambda s, n_sl=n_sl: dma_c(s, [
                (yb4[:, 8:16, :], y_cols[1024:2048, n_sl]
                 .rearrange("(k p) t -> p k t", p=128)),
            ]))
            for mw in range(4):
                def ldw(g, mw=mw, n_sl=n_sl):
                    return dma_g(g, [
                        (outwc[:], outwT[:, mw * 256:(mw + 1) * 256]
                         .rearrange("(k p) m -> p k m", p=128)),
                        (xst2[:], x_in[:, n_sl]
                         .rearrange("(j p) t -> p j t", p=128)[:, 2 * mw:2 * mw + 2, :]),
                    ])
                prog.step(gpsimd=ldw)

                def mms(t):
                    last = None
                    for mi in range(2):
                        for k in range(16):
                            last = t.matmul(ps[mi][:], outwc[:, k, mi * 128:(mi + 1) * 128],
                                            yb4[:, k, :], start=(k == 0), stop=(k == 15))
                    return last

                def evac(v, mw=mw, n_sl=n_sl):
                    last = None
                    for mi in range(2):
                        m = mw * 2 + mi
                        last = v.scalar_tensor_tensor(sB[:, m, n_sl], xst2[:, mi, :], 2.0,
                                                      ps[mi][:], ALU.mult, ALU.add)
                    return last
                prog.step(tensor=mms)
                prog.step(vector=evac)
        stA1.close()

        stS = contextlib.ExitStack()
        tS = alloc(nc, stS, sbufs=[
            ("p4_sq", (128, 2, 512), F32),
            ("p4_rows", (1, 4 * TOK), F32),
            ("p4_mv", (1, 2 * 512), F32),
        ])
        sq4, rows4, mv4 = tS["p4_sq"], tS["p4_rows"], tS["p4_mv"]
        a1row = rows4[:, 0 * TOK:1 * TOK]
        b1row = rows4[:, 1 * TOK:2 * TOK]
        a2row = rows4[:, 2 * TOK:3 * TOK]
        b2row_s = rows4[:, 3 * TOK:4 * TOK]

        emit_ln_stats(prog, lambda j, n_sl: sB[:, j, n_sl], sq4, ps[0], ps[1],
                      ones, mv4, a1row, b1row, consts)

        for n in range(4):
            n_sl = slice(n * 512, (n + 1) * 512)

            def bc1(t, n_sl=n_sl):
                t.matmul(ps[2][:], ones[0:1, 0:128], a1row[0:1, n_sl], start=True, stop=True)
                return t.matmul(ps[3][:], ones[0:1, 0:128], b1row[0:1, n_sl],
                                start=True, stop=True)
            prog.step(tensor=bc1)
            for g0 in range(0, 8, 2):
                def x1mk(v, g0=g0, n_sl=n_sl):
                    last = None
                    for i in range(2):
                        v.tensor_tensor(sq4[:, i, :], sB[:, g0 + i, n_sl], ps[2][:], ALU.mult)
                        last = v.tensor_tensor(sq4[:, i, :], sq4[:, i, :], ps[3][:], ALU.add)
                    return last
                prog.step(vector=x1mk)

                def mmx1(t, g0=g0):
                    last = None
                    for i in range(2):
                        j = g0 + i
                        last = t.matmul(ps[0][0:1, :], ones[:, 0:1], sq4[:, i, :],
                                        start=(j == 0), stop=(j == 7))
                    return last
                prog.step(tensor=mmx1)

                def sqx1(s):
                    last = None
                    for i in range(2):
                        last = s.activation(sq4[:, i, :], sq4[:, i, :], AF.Square)
                    return last
                prog.step(scalar=sqx1)

                def mmq1(t, g0=g0):
                    last = None
                    for i in range(2):
                        j = g0 + i
                        last = t.matmul(ps[1][0:1, :], ones[:, 0:1], sq4[:, i, :],
                                        start=(j == 0), stop=(j == 7))
                    return last
                prog.step(tensor=mmq1)

            def st2f(v):
                v.tensor_scalar(mv4[0:1, 0:512], ps[0][0:1, :], 1.0 / D, None, ALU.mult)
                v.tensor_scalar(mv4[0:1, 512:1024], ps[1][0:1, :], 1.0 / D, None, ALU.mult)
                v.tensor_tensor(tmp4[0:1, 0:512], mv4[0:1, 0:512], mv4[0:1, 0:512], ALU.mult)
                return v.tensor_tensor(mv4[0:1, 512:1024], mv4[0:1, 512:1024],
                                       tmp4[0:1, 0:512], ALU.subtract)
            prog.step(vector=st2f)

            def rstd2(s):
                s.activation(mv4[0:1, 512:1024], mv4[0:1, 512:1024], AF.Ln,
                             bias=consts[0:1, 0:1])
                return s.activation(mv4[0:1, 512:1024], mv4[0:1, 512:1024],
                                    AF.Exp, scale=-0.5)
            prog.step(scalar=rstd2)

            def rows2(v, n_sl=n_sl):
                v.tensor_copy(a2row[0:1, n_sl], mv4[0:1, 512:1024])
                return v.scalar_tensor_tensor(b2row_s[0:1, n_sl], mv4[0:1, 0:512], -1.0,
                                              mv4[0:1, 512:1024], ALU.mult, ALU.mult)
            prog.step(vector=rows2)

        for n in range(4):
            n_sl = slice(n * 512, (n + 1) * 512)

            def bc2(t, n_sl=n_sl):
                t.matmul(ps[2][:], ones[0:1, 0:128], a1row[0:1, n_sl], start=True, stop=True)
                t.matmul(ps[3][:], ones[0:1, 0:128], b1row[0:1, n_sl], start=True, stop=True)
                t.matmul(ps[4][:], ones[0:1, 0:128], a2row[0:1, n_sl], start=True, stop=True)
                return t.matmul(ps[5][:], ones[0:1, 0:128], b2row_s[0:1, n_sl],
                                start=True, stop=True)
            prog.step(tensor=bc2)

            for j in range(8):
                def vx(v, j=j, n_sl=n_sl):
                    v.tensor_tensor(tmp4[:], sB[:, j, n_sl], ps[2][:], ALU.mult)
                    v.tensor_tensor(tmp4[:], tmp4[:], ps[3][:], ALU.add)
                    v.tensor_tensor(sq4[:, 0, :], tmp4[:], ps[4][:], ALU.mult)
                    v.tensor_tensor(sq4[:, 0, :], sq4[:, 0, :], ps[5][:], ALU.add)
                    return v.tensor_scalar(xn1[:, j, n_sl], sq4[:, 0, :],
                                           gb4[:, j:j + 1], gb4[:, 8 + j:9 + j],
                                           ALU.mult, ALU.add)
                prog.step(vector=vx)

                for h in range(2):
                    def tr(t, h=h):
                        t.transpose(ps[6][:, 0:128], tmp4[:, 256 * h:256 * h + 128], idn[:])
                        return t.transpose(ps[7][:, 0:128],
                                           tmp4[:, 256 * h + 128:256 * h + 256], idn[:])
                    prog.step(tensor=tr)

                    def ev(v, j=j, n=n, h=h):
                        tau0 = n * 4 + 2 * h
                        v.tensor_scalar(BIGA[:, tau0, j * 128:(j + 1) * 128],
                                        ps[6][:, 0:128], 2.0, None, ALU.mult)
                        return v.tensor_scalar(BIGA[:, tau0 + 1, j * 128:(j + 1) * 128],
                                               ps[7][:, 0:128], 2.0, None, ALU.mult)
                    prog.step(vector=ev)
        stS.close()
        stA.close()

        stG = contextlib.ExitStack()
        gwb = alloc(nc, stG, sbufs=[("p4_gwb", (128, 8, 8), BF16)])["p4_gwb"]
        prog.step(gpsimd=lambda g: dma_g(g, [
            (gwb[:], gate_w.rearrange("(k p) e -> p k e", p=128)),
        ]))

        def gmm(t):
            last = None
            for tau in range(16):
                for k in range(8):
                    last = t.matmul(ps[0][:, tau * 8:(tau + 1) * 8],
                                    xn1[:, k, tau * 128:(tau + 1) * 128],
                                    gwb[:, k, :], start=(k == 0), stop=(k == 7))
            return last
        prog.step(tensor=gmm)

        prog.step(vector=lambda v: [v.tensor_reduce(mcol16[:, tau:tau + 1],
                                                    ps[0][:, tau * 8:(tau + 1) * 8],
                                                    mybir.AxisListType.X, ALU.max)
                                    for tau in range(16)][-1])

        prog.step(vector=lambda v: [v.tensor_scalar(glog[:, tau * 8:(tau + 1) * 8],
                                                    ps[0][:, tau * 8:(tau + 1) * 8],
                                                    mcol16[:, tau:tau + 1], None,
                                                    ALU.subtract)
                                    for tau in range(16)][-1])
        prog.step(scalar=lambda s: s.activation(glog[:], glog[:], AF.Exp))
        prog.step(vector=lambda v: [v.max(out=m8big[:, tau, :],
                                           in_=glog[:, tau * 8:(tau + 1) * 8])
                                    for tau in range(16)][-1])
        prog.step(vector=lambda v: [v.tensor_copy(m2col[:, tau:tau + 1],
                                                  m8big[:, tau, 1:2])
                                    for tau in range(16)][-1])
        prog.step(vector=lambda v: [v.tensor_tensor(mcol16[:, tau:tau + 1],
                                                    m8big[:, tau, 0:1],
                                                    m8big[:, tau, 1:2], ALU.add)
                                    for tau in range(16)][-1])
        prog.step(vector=lambda v: v.reciprocal(rcol16[:], mcol16[:]))
        prog.step(vector=lambda v: [v.tensor_scalar(cwsb[:, tau * 8:(tau + 1) * 8],
                                                    glog[:, tau * 8:(tau + 1) * 8],
                                                    m2col[:, tau:tau + 1], None, ALU.is_ge)
                                    for tau in range(16)][-1])
        prog.step(vector=lambda v: v.tensor_tensor(cwsb[:], cwsb[:], glog[:], ALU.mult))
        prog.step(vector=lambda v: [v.tensor_scalar(cwsb[:, tau * 8:(tau + 1) * 8],
                                                    cwsb[:, tau * 8:(tau + 1) * 8],
                                                    rcol16[:, tau:tau + 1], None, ALU.mult)
                                    for tau in range(16)][-1])
        stG.close()

        stM = contextlib.ExitStack()
        tM = alloc(nc, stM, sbufs=[
            ("p4_hbf", (128, 32, 1024), BF16),
            ("p4_w1c", (128, 2, 8, 128), BF16),
            ("p4_w2w", (128, 2, 8, 512), BF16),
            ("p4_b1s", (128, E * 32), F32),
            ("p4_b2t", (1, D), BF16),
            ("p4_ob", (1, 128), BF16),
        ])
        hbf, w1c, w2w, b1s, b2t = (tM["p4_hbf"], tM["p4_w1c"], tM["p4_w2w"],
                                   tM["p4_b1s"], tM["p4_b2t"])
        ones_bf = tM["p4_ob"]
        prog.step(sync=lambda e: dma_s(e, [(b1s[:], b1c)]),
                  vector=lambda v: v.memset(ones_bf[:], 1.0))

        for ex in range(E):
            prog.step(gpsimd=lambda g, e2=ex: dma_g(g, [
                (b2t[:], b2row[0:1, e2 * D:(e2 + 1) * D])]))
            for nn in range(2):
                tok0 = nn * 1024

                def ldw1(g, e2, hm, buf):
                    return dma_g(g, [(w1c[:, buf],
                                      w1[e2, :, hm * 128:(hm + 1) * 128]
                                      .rearrange("(k p) m -> p k m", p=128))])
                prog.step(gpsimd=lambda g, e2=ex: ldw1(g, e2, 0, 0))
                for hm in range(33):
                    def work(t, hm=hm, tok0=tok0):
                        if hm >= 32:
                            return None
                        last = None
                        for th in range(2):
                            for k in range(8):
                                last = t.matmul(ps[(hm % 2) * 2 + th][:],
                                                w1c[:, hm % 2, k, :],
                                                xn1[:, k, tok0 + th * 512:tok0 + (th + 1) * 512],
                                                start=(k == 0), stop=(k == 7))
                        return last

                    def pre_g(g, e2=ex, hm=hm):
                        if hm + 1 >= 32:
                            return None
                        return ldw1(g, e2, hm + 1, (hm + 1) % 2)

                    def gel(s, e2=ex, hm=hm):
                        if hm == 0:
                            return None
                        hp = hm - 1
                        s.activation(hbf[:, hp, 0:512], ps[(hp % 2) * 2][:], AF.Gelu,
                                     bias=b1s[:, e2 * 32 + hp:e2 * 32 + hp + 1])
                        return s.activation(hbf[:, hp, 512:1024], ps[(hp % 2) * 2 + 1][:],
                                            AF.Gelu, bias=b1s[:, e2 * 32 + hp:e2 * 32 + hp + 1])
                    prog.step(tensor=work, gpsimd=pre_g, scalar=gel)

                for nd in range(2):
                    def ldw2(g, e2, nd, kw, buf):
                        r0 = kw * 1024
                        return dma_g(g, [(w2w[:, buf],
                                          w2[e2, r0:r0 + 1024, nd * 512:(nd + 1) * 512]
                                          .rearrange("(k p) m -> p k m", p=128))])
                    prog.step(gpsimd=lambda g, e2=ex, nd=nd: ldw2(g, e2, nd, 0, 0))
                    for kw in range(4):
                        def mm2(t, kw=kw):
                            last = None
                            for ki in range(8):
                                k = kw * 8 + ki
                                for tm in range(8):
                                    last = t.matmul(ps[tm][:],
                                                    hbf[:, k, tm * 128:(tm + 1) * 128],
                                                    w2w[:, kw % 2, ki, :],
                                                    start=(k == 0), stop=False)
                            return last

                        def pw2_g(g, e2=ex, kw=kw, nd=nd):
                            if kw + 1 >= 4:
                                return None
                            return ldw2(g, e2, nd, kw + 1, (kw + 1) % 2)
                        prog.step(tensor=mm2, gpsimd=pw2_g)

                    def b2mm(t, nd=nd):
                        last = None
                        for tm in range(8):
                            last = t.matmul(ps[tm][:], ones_bf[0:1, :],
                                            b2t[0:1, nd * 512:(nd + 1) * 512],
                                            start=False, stop=True)
                        return last
                    prog.step(tensor=b2mm)

                    def moevac(v, e2=ex, nn=nn, nd=nd):
                        last = None
                        for tm in range(8):
                            tau = nn * 8 + tm
                            last = v.scalar_tensor_tensor(
                                BIGA[:, tau, nd * 512:(nd + 1) * 512], ps[tm][:],
                                cwsb[:, tau * 8 + e2:tau * 8 + e2 + 1],
                                BIGA[:, tau, nd * 512:(nd + 1) * 512], ALU.mult, ALU.add)
                        return last
                    prog.step(vector=moevac)
        stM.close()
        st_xn1.close()

        stQ = contextlib.ExitStack()
        tQ = alloc(nc, stQ, sbufs=[
            ("p4_x2bf", (128, 8, TOK), BF16),
            ("p4_wat", (128, 8, 3 * D), BF16),
            ("p4_tmq", (128, 1024), F32),
            ("p4_stg", (128, 8, 128), F32),
            ("p4_stgb", (128, 4, 512), BF16),
        ])
        x2bf, wat, tmq, stg, stgb = (tQ["p4_x2bf"], tQ["p4_wat"], tQ["p4_tmq"],
                                     tQ["p4_stg"], tQ["p4_stgb"])
        prog.step(gpsimd=lambda g: dma_g(g, [
            (wat[:], wattnT.rearrange("(k p) m -> p k m", p=128)),
        ]))

        for tau in range(16):
            def lnst(s, tau=tau):
                return s.activation(tmq[:], BIGA[:, tau, :], AF.Square,
                                    accum_out=mcol[:, 1:2])
            prog.step(scalar=lnst)

            def lnred(v, tau=tau):
                v.tensor_reduce(mcol[:, 0:1], BIGA[:, tau, :], mybir.AxisListType.X, ALU.add)
                v.tensor_scalar(mcol[:, 0:1], mcol[:, 0:1], 1.0 / D, None, ALU.mult)
                v.tensor_scalar(mcol[:, 1:2], mcol[:, 1:2], 1.0 / D, None, ALU.mult)
                v.tensor_tensor(mcol[:, 2:3], mcol[:, 0:1], mcol[:, 0:1], ALU.mult)
                v.tensor_tensor(mcol[:, 1:2], mcol[:, 1:2], mcol[:, 2:3], ALU.subtract)
                return v.tensor_scalar(mcol[:, 3:4], mcol[:, 0:1], -1.0, None, ALU.mult)
            prog.step(vector=lnred)

            def lnrs(s):
                s.activation(mcol[:, 1:2], mcol[:, 1:2], AF.Ln, bias=consts[:, 0:1])
                return s.activation(mcol[:, 1:2], mcol[:, 1:2], AF.Exp, scale=-0.5)
            prog.step(scalar=lnrs)

            def lnap(v, tau=tau):
                return v.tensor_scalar(tmq[:], BIGA[:, tau, :], mcol[:, 3:4],
                                       mcol[:, 1:2], ALU.add, ALU.mult)
            prog.step(vector=lnap)

            for h in range(4):
                def trx(t, h=h):
                    t.transpose(ps[6][:, 0:128], tmq[:, 256 * h:256 * h + 128], idn[:])
                    return t.transpose(ps[7][:, 0:128],
                                       tmq[:, 256 * h + 128:256 * h + 256], idn[:])
                prog.step(tensor=trx)

                def evx(v, h=h, tau=tau):
                    v.tensor_copy(stg[:, 2 * h, :], ps[6][:, 0:128])
                    v.tensor_copy(stg[:, 2 * h + 1, :], ps[7][:, 0:128])
                    v.tensor_copy(x2bf[:, 2 * h, tau * 128:(tau + 1) * 128], ps[6][:, 0:128])
                    return v.tensor_copy(x2bf[:, 2 * h + 1, tau * 128:(tau + 1) * 128],
                                         ps[7][:, 0:128])
                prog.step(vector=evx)

            prog.step(sync=lambda e, tau=tau: dma_s(e, [
                (x2_scr[:, tau * 128:(tau + 1) * 128]
                 .rearrange("(j p) t -> p j t", p=128), stg[:])]))

        for n in range(4):
            n_sl = slice(n * 512, (n + 1) * 512)
            for mw in range(4):
                def qmm(t, mw=mw, n_sl=n_sl):
                    last = None
                    for mi in range(4):
                        m = mw * 4 + mi
                        for k in range(8):
                            last = t.matmul(ps[mi][:], wat[:, k, m * 128:(m + 1) * 128],
                                            x2bf[:, k, n_sl], start=(k == 0), stop=(k == 7))
                    return last
                prog.step(tensor=qmm)

                def qev(v):
                    last = None
                    for mi in range(4):
                        last = v.tensor_copy(stgb[:, mi, :], ps[mi][:])
                    return last
                prog.step(vector=qev)

                def qst(e, mw=mw, n_sl=n_sl):
                    trs = []
                    for mi in range(4):
                        for half in range(2):
                            hraw = 8 * mw + 2 * mi + half
                            if hraw < 16:
                                slot, sub = hraw // 4, hraw % 4
                            else:
                                hk = hraw - 16
                                slot, sub = hk // 4, 4 + hk % 4
                            trs.append((qk_loc[slot, sub, :, n_sl],
                                        stgb[64 * half:64 * half + 64, mi, :]))
                    return dma_s(e, trs)
                prog.step(sync=qst)

        for tw in range(4):
            for nv in range(2):
                def vmm(t, tw=tw, nv=nv):
                    last = None
                    for ti in range(4):
                        tau = tw * 4 + ti
                        for k in range(8):
                            last = t.matmul(ps[ti][:], x2bf[:, k, tau * 128:(tau + 1) * 128],
                                            wat[:, k, 2048 + nv * 512:2048 + (nv + 1) * 512],
                                            start=(k == 0), stop=(k == 7))
                    return last
                prog.step(tensor=vmm)

                def vev(v):
                    last = None
                    for ti in range(4):
                        last = v.tensor_copy(stgb[:, ti, :], ps[ti][:])
                    return last
                prog.step(vector=vev)

                def vst(e, tw=tw, nv=nv):
                    trs = []
                    for hh in range(8):
                        h = 8 * nv + hh
                        trs.append((v_loc[h, :, 4 * tw:4 * tw + 4, :],
                                    stgb[:, :, 64 * hh:64 * hh + 64]))
                    return dma_s(e, trs)
                prog.step(sync=vst)
        stQ.close()
        st4.close()

        cc("AllGather", ALU.bypass, qk_loc_h, qk_all_h)
        cc("AllGather", ALU.bypass, v_loc_h, v_all_h)

        def qkpick(e):
            qk_flat = qk_all.rearrange("a b c d -> a (b c d)")
            qk_p_flat = qk_p.rearrange("a b c d -> a (b c d)")
            return dma_s(e, [(qk_p_flat[ji:ji + 1, :],
                              qk_flat[bass.ts(dyn["qv"][ji], 1), :])
                             for ji in range(4)])

        def vpick(s):
            v_flat = v_all.rearrange("(a b) p k c -> a (b p k c)", b=4)
            v_p_flat = v_p.rearrange("(a b) p k c -> a (b p k c)", b=4)
            return dma_c(s, [(v_p_flat[ji:ji + 1, :],
                              v_flat[bass.ts(dyn["qvc"][ji], 1), :])
                             for ji in range(4)])
        prog.step(sync=qkpick, scalar=vpick)

        st5 = contextlib.ExitStack()
        t5 = alloc(nc, st5, sbufs=[
            ("p5_q2", (128, S), BF16), ("p5_k2", (128, S), BF16),
            ("p5_vA", (128, 64, 65), BF16), ("p5_vB", (128, 64, 65), BF16),
            ("p5_msk", (128, 4, 512), F32),
            ("p5_onesb", (128, 2), BF16),
            ("p5_ebf", (128, 2, 2, 512), BF16),
            ("p5_etmp", (128, 2, 2, 512), F32),
            ("p5_onesT", (65, 64), F32),
            ("p5_sufA", (65, 16), F32), ("p5_sufB", (65, 16), F32),
            ("p5_pref", (65, 64), F32), ("p5_tcol", (65, 1), F32),
            ("p5_yv", (65, 2, 512), F32),
            ("p5_rrow", (1, 2, 512), F32),
            ("p5_ystg", (64, 2, S), BF16),
        ])
        q2, k2, vA, vB, msk = (t5["p5_q2"], t5["p5_k2"], t5["p5_vA"], t5["p5_vB"],
                               t5["p5_msk"])
        onesb, ebf, etmp, onesT = (t5["p5_onesb"], t5["p5_ebf"], t5["p5_etmp"],
                                   t5["p5_onesT"])
        sufA, sufB, pref, tcol = t5["p5_sufA"], t5["p5_sufB"], t5["p5_pref"], t5["p5_tcol"]
        yv, rrow, ystg = t5["p5_yv"], t5["p5_rrow"], t5["p5_ystg"]
        psS = [[ps[0], ps[1]], [ps[2], ps[3]]]
        pyP = [ps[4][0:65, :], ps[5][0:65, :]]
        pxt = ps[6]

        prog.step(sync=lambda e: dma_s(e, [
            (msk[:], dmask.rearrange("(r p) f -> p r f", p=128)),
        ]), vector=lambda v: (v.memset(onesT[:], 1.0), v.memset(vA[:], 1.0),
                              v.memset(vB[:], 1.0), v.memset(onesb[:], 1.0))[-1])

        for pg in range(2):
            pA, pB = 2 * pg, 2 * pg + 1

            def ldq(e, pA=pA, pB=pB):
                trs = []
                for ji in range(4):
                    c_sl = slice(2048 * ji, 2048 * (ji + 1))
                    trs.append((q2[0:64, c_sl], qk_p[ji, pA, :, :]))
                    trs.append((q2[64:128, c_sl], qk_p[ji, pB, :, :]))
                return dma_s(e, trs)

            def ldk(g, pA=pA, pB=pB):
                trs = []
                for ji in range(4):
                    c_sl = slice(2048 * ji, 2048 * (ji + 1))
                    trs.append((k2[0:64, c_sl], qk_p[ji, 4 + pA, :, :]))
                    trs.append((k2[64:128, c_sl], qk_p[ji, 4 + pB, :, :]))
                return dma_g(g, trs)

            def ldv(s, pA=pA, pB=pB):
                trs = []
                for ji in range(4):
                    trs.append((vA[:, 16 * ji:16 * ji + 16, 0:64],
                                v_p[4 * ji + pA, :, :, :]))
                    trs.append((vB[:, 16 * ji:16 * ji + 16, 0:64],
                                v_p[4 * ji + pB, :, :, :]))
                return dma_c(s, trs)
            prog.step(sync=ldq, gpsimd=ldk, scalar=ldv)

            for pi, vt, suf in ((0, vA, sufA), (1, vB, sufB)):
                def tsum(t, vt=vt):
                    last = None
                    for kt in range(64):
                        last = t.matmul(pxt[0:65, kt:kt + 1], vt[:, kt, :],
                                        onesb[:, 0:1], start=True, stop=True)
                    return last
                prog.step(tensor=tsum)
                prog.step(vector=lambda v: v.tensor_tensor_scan(
                    pref[:], onesT[:], pxt[0:65, 0:64], 0.0, ALU.mult, ALU.add))
                prog.step(vector=lambda v: v.tensor_scalar(
                    tcol[:], pref[:, 63:64], EM10, None, ALU.mult))
                prog.step(vector=lambda v, suf=suf: v.tensor_scalar(
                    suf[:], pref[:, 3:64:4], -EM10, tcol[:, 0:1], ALU.mult, ALU.add))

            for qt in range(16):
                q_sl = slice(qt * 512, (qt + 1) * 512)
                nkt = 4 * qt + 4
                kts = list(range(nkt))
                waves = [kts[i:i + 2] for i in range(0, nkt, 2)]
                for w, wk in enumerate(waves):
                    diag = (wk[0] >= 4 * qt)

                    def smm(t, wk=wk, q_sl=q_sl):
                        last = None
                        for si, kt in enumerate(wk):
                            k_sl = slice(kt * 128, (kt + 1) * 128)
                            t.matmul(psS[0][si][:], k2[0:64, k_sl], q2[0:64, q_sl],
                                     start=True, stop=True, tile_position=(0, 0))
                            last = t.matmul(psS[1][si][:], k2[64:128, k_sl],
                                            q2[64:128, q_sl], start=True, stop=True,
                                            tile_position=(64, 0))
                        return last
                    prog.step(tensor=smm)

                    if diag:
                        def dmsk(v, wk=wk, qt=qt):
                            last = None
                            for si, kt in enumerate(wk):
                                rel = kt - 4 * qt
                                for pi in range(2):
                                    v.tensor_scalar(etmp[:, pi, si, :], psS[pi][si][:],
                                                    0.125, 10.0, ALU.mult, ALU.add)
                                    last = v.tensor_tensor(etmp[:, pi, si, :],
                                                           etmp[:, pi, si, :],
                                                           msk[:, rel, :], ALU.mult)
                            return last
                        prog.step(vector=dmsk)

                        def dexp(s, wk=wk):
                            last = None
                            for si in range(len(wk)):
                                for pi in range(2):
                                    last = s.activation(ebf[:, pi, si, :],
                                                        etmp[:, pi, si, :], AF.Exp,
                                                        bias=consts[:, 1:2])
                            return last
                        prog.step(scalar=dexp)
                    else:
                        def nexp(s, wk=wk):
                            last = None
                            for si in range(len(wk)):
                                for pi in range(2):
                                    last = s.activation(ebf[:, pi, si, :],
                                                        psS[pi][si][:], AF.Exp,
                                                        scale=0.125)
                            return last
                        prog.step(scalar=nexp)

                    def avm(t, wk=wk):
                        last = None
                        for si, kt in enumerate(wk):
                            for pi, vt in ((0, vA), (1, vB)):
                                last = t.matmul(pyP[pi], vt[:, kt, :],
                                                ebf[:, pi, si, :],
                                                start=(kt == 0), stop=(kt == nkt - 1))
                        return last
                    prog.step(tensor=avm)

                def ev1(v, qt=qt):
                    v.tensor_scalar(yv[:, 0, :], pyP[0], sufA[:, qt:qt + 1],
                                    None, ALU.add)
                    return v.tensor_scalar(yv[:, 1, :], pyP[1], sufB[:, qt:qt + 1],
                                           None, ALU.add)
                prog.step(vector=ev1)

                def ev2(v):
                    v.reciprocal(rrow[0:1, 0, :], yv[64:65, 0, :])
                    return v.reciprocal(rrow[0:1, 1, :], yv[64:65, 1, :])
                prog.step(vector=ev2)

                for pi in range(2):
                    prog.step(tensor=lambda t, pi=pi: t.matmul(
                        pxt[0:64, :], onesT[0:1, 0:64], rrow[0:1, pi, :],
                        start=True, stop=True))
                    prog.step(vector=lambda v, pi=pi, q_sl=q_sl: v.tensor_tensor(
                        ystg[:, pi, q_sl], yv[0:64, pi, :], pxt[0:64, :], ALU.mult))

            prog.step(sync=lambda e, pA=pA, pB=pB: dma_s(e, [
                (yat_loc[:, 64 * pA:64 * pA + 64, :].rearrange("u p t -> p u t"),
                 ystg[:, 0, :].rearrange("p (u t) -> p u t", u=4)),
                (yat_loc[:, 64 * pB:64 * pB + 64, :].rearrange("u p t -> p u t"),
                 ystg[:, 1, :].rearrange("p (u t) -> p u t", u=4)),
            ]))
        st5.close()

        cc("AllGather", ALU.bypass, yat_loc_h, yat_all_h)

        def yatpick_s(e):
            return dma_s(e, [(yat_cols[256 * ji:256 * (ji + 1), :],
                              yat_all[bass.ts(dyn["qv"][ji], 1), :, :])
                             for ji in range(2)])

        def yatpick_c(s):
            return dma_c(s, [(yat_cols[256 * ji:256 * (ji + 1), :],
                              yat_all[bass.ts(dyn["qvc"][ji], 1), :, :])
                             for ji in range(2, 4)])
        prog.step(sync=yatpick_s, scalar=yatpick_c)

        st6 = contextlib.ExitStack()
        t6 = alloc(nc, st6, sbufs=[
            ("p6_sx", (128, 8, TOK), F32),
            ("p6_x3b", (128, 8, TOK), BF16),
            ("p6_bx3", (128, 8, TOK), F32),
            ("p6_tmp", (128, 512), F32),
            ("p6_wcast", (128, 8, 512), BF16),
            ("p6_xs2", (128, 4, 512), F32),
            ("p6_sq", (128, 1, 512), F32),
            ("p6_mv", (1, 4 * 512), F32),
            ("p6_bcols", (128, 9), F32),
            ("p6_lcols", (128, 33), F32),
            ("p6_lrow", (1, 8), F32),
            ("p6_h1", (128, 2, 512), BF16),
            ("p6_wd1c", (128, 8, 256), BF16),
            ("p6_wd2c", (128, 2, 1024), BF16),
            ("p6_bd1s", (128, 16), F32),
            ("p6_bd2s", (128, 64), F32),
            ("p6_yb", (128, 8, 512), BF16),
        ])
        sx, x3b, bx3 = t6["p6_sx"], t6["p6_x3b"], t6["p6_bx3"]
        tmp6 = t6["p6_tmp"]
        wcast, xs2, sq6, mv6 = t6["p6_wcast"], t6["p6_xs2"], t6["p6_sq"], t6["p6_mv"]
        bcols, lcols, lrow = t6["p6_bcols"], t6["p6_lcols"], t6["p6_lrow"]
        h1, wd1c, wd2c, bd1s, bd2s, yb6 = (t6["p6_h1"], t6["p6_wd1c"], t6["p6_wd2c"],
                                           t6["p6_bd1s"], t6["p6_bd2s"], t6["p6_yb"])

        prog.step(sync=lambda e: dma_s(e, [
            (bcols[:, 0:8], bbase), (bd1s[:], bd1c), (bd2s[:], bd2c),
        ]))

        for n in range(4):
            n_sl = slice(n * 512, (n + 1) * 512)

            prog.step(sync=lambda e, n_sl=n_sl: dma_s(e, [
                (yb6[:], yat_cols[:, n_sl].rearrange("(k p) t -> p k t", p=128)),
            ]))
            for mw in range(2):
                prog.step(sync=lambda e, mw=mw, n_sl=n_sl: dma_s(e, [
                    (xs2[:], x2_scr[:, n_sl].rearrange("(j p) t -> p j t", p=128)
                     [:, 4 * mw:4 * mw + 4, :]),
                ]), gpsimd=lambda g, mw=mw: dma_g(g, [
                    (wcast[:], wprojT[:, mw * 512:(mw + 1) * 512]
                     .rearrange("(k p) m -> p k m", p=128)),
                ]))

                def mms6(t):
                    last = None
                    for mi in range(4):
                        for k in range(8):
                            last = t.matmul(ps[mi][:], wcast[:, k, mi * 128:(mi + 1) * 128],
                                            yb6[:, k, :], start=(k == 0), stop=(k == 7))
                    return last

                def evac6(v, mw=mw, n_sl=n_sl):
                    last = None
                    for mi in range(4):
                        m = mw * 4 + mi
                        last = v.scalar_tensor_tensor(sx[:, m, n_sl], xs2[:, mi, :], 1.0,
                                                      ps[mi][:], ALU.mult, ALU.add)
                    return last
                prog.step(tensor=mms6)
                prog.step(vector=evac6)

        def ln3_out(j, n, n_sl, v, x_ap, ps_a, ps_b):
            v.tensor_tensor(tmp6[:], x_ap, ps_a[:], ALU.mult)
            v.tensor_tensor(tmp6[:], tmp6[:], ps_b[:], ALU.add)
            v.tensor_copy(x3b[:, j, n_sl], tmp6[:])
            return v.tensor_copy(sx[:, j, n_sl], tmp6[:])
        emit_ln_fused(prog, lambda j, n_sl: sx[:, j, n_sl], ln3_out, sq6,
                      ps[0], ps[1], ps[2], ps[3], ones, mv6, consts)
        prog.step(sync=lambda e: dma_s(e, [
            (x3b_scr.rearrange("(j p) t -> p j t", p=128), x3b[:]),
        ]))

        for n in range(4):
            n_sl = slice(n * 512, (n + 1) * 512)
            for mw in range(2):
                prog.step(gpsimd=lambda g, mw=mw: dma_g(g, [
                    (wcast[:], Wb[:, mw * 512:(mw + 1) * 512]
                     .rearrange("(k p) m -> p k m", p=128)),
                ]))

                def mmb(t, mw=mw, n_sl=n_sl):
                    last = None
                    for mi in range(4):
                        for k in range(8):
                            last = t.matmul(ps[mi][:], wcast[:, k, mi * 128:(mi + 1) * 128],
                                            x3b[:, k, n_sl], start=(k == 0), stop=(k == 7))
                    return last

                def evb(v, mw=mw, n_sl=n_sl):
                    last = None
                    for mi in range(4):
                        m = mw * 4 + mi
                        v.tensor_scalar(tmp6[:], ps[mi][:], bcols[:, m:m + 1], None, ALU.add)
                        last = v.tensor_tensor(bx3[:, m, n_sl], tmp6[:], sx[:, m, n_sl],
                                               ALU.add)
                    return last
                prog.step(tensor=mmb)
                prog.step(vector=evb)
        prog.step(sync=lambda e: dma_s(e, [
            (bx3_scr.rearrange("(j p) t -> p j t", p=128), bx3[:]),
        ]))

        for e_ in range(E):
            prog.step(gpsimd=lambda g, e_=e_: dma_g(g, [
                (wd1c[:], Wd1[e_].rearrange("(k p) m -> p k m", p=128)),
                (wd2c[:], Wd2[e_].rearrange("(k p) m -> p k m", p=128)),
            ]))
            for n in range(4):
                n_sl = slice(n * 512, (n + 1) * 512)

                def mh(t, n_sl=n_sl):
                    last = None
                    for mi in range(2):
                        for k in range(8):
                            last = t.matmul(ps[6 - mi][:], wd1c[:, k, mi * 128:(mi + 1) * 128],
                                            x3b[:, k, n_sl], start=(k == 0), stop=(k == 7))
                    return last
                prog.step(tensor=mh)

                def eh(s_, e_=e_):
                    s_.activation(h1[:, 0, :], ps[6][:], AF.Relu,
                                  bias=bd1s[:, 2 * e_:2 * e_ + 1])
                    return s_.activation(h1[:, 1, :], ps[5][:], AF.Relu,
                                         bias=bd1s[:, 2 * e_ + 1:2 * e_ + 2])
                prog.step(scalar=eh)
                for mw in range(2):
                    def mr(t, mw=mw):
                        last = None
                        for mi in range(4):
                            for k in range(2):
                                last = t.matmul(
                                    ps[mi][:],
                                    wd2c[:, k, mw * 512 + mi * 128:mw * 512 + (mi + 1) * 128],
                                    h1[:, k, :], start=(k == 0), stop=(k == 1))
                        return last
                    prog.step(tensor=mr)

                    def dv2(v, mw=mw, n_sl=n_sl, e_=e_):
                        last = None
                        for mi in range(4):
                            m = mw * 4 + mi
                            last = v.scalar_tensor_tensor(
                                xs2[:, mi, :], ps[mi][:], bd2s[:, e_ * 8 + m:e_ * 8 + m + 1],
                                sx[:, m, n_sl], ALU.add, ALU.subtract)
                        return last
                    prog.step(vector=dv2)

                    def sqa(s_, n=n, mw=mw):
                        last = None
                        for mi in range(4):
                            ci = (n * 2 + mw) * 4 + mi
                            last = s_.activation(tmp6[:], xs2[:, mi, :], AF.Square,
                                                 accum_out=lcols[:, ci:ci + 1])
                        return last
                    prog.step(scalar=sqa)
            prog.step(vector=lambda v: v.tensor_reduce(
                lcols[:, 32:33], lcols[:, 0:32], mybir.AxisListType.X, ALU.add))
            prog.step(tensor=lambda t: t.matmul(ps[4][0:1, 0:1], ones[:, 0:1],
                                                lcols[:, 32:33], start=True, stop=True))
            prog.step(vector=lambda v, e_=e_: v.tensor_copy(lrow[0:1, e_:e_ + 1],
                                                            ps[4][0:1, 0:1]))
        prog.step(sync=lambda e: dma_s(e, [(lrow_loc, lrow[:])]))
        st6.close()

        cc("AllGather", ALU.bypass, lrow_loc_h, lparts_all_h)

        st7 = contextlib.ExitStack()
        t7 = alloc(nc, st7, sbufs=[
            ("p7_bx3s", (128, 8, TOK), F32),
            ("p7_x3bs", (128, 8, TOK), BF16),
            ("p7_a_all", (64, 8, TOK), BF16),
            ("p7_wauc", (64, 8, D), BF16),
            ("p7_wadc", (128, 8, 64), BF16),
            ("p7_lp", (8, 8), F32),
            ("p7_selb", (8, 2), F32),
            ("p7_baups", (8, D), F32),
            ("p7_badcs", (64, 8), F32),
            ("p7_lrow", (1, 8), F32),
            ("p7_mcol", (1, 2), F32),
            ("p7_ind", (1, 8), F32),
            ("p7_indc8", (8, 1), F32),
            ("p7_ind64", (64, 8), F32),
            ("p7_bcol", (128, 8), F32),
            ("p7_tmp", (128, 512), F32),
            ("p7_sq", (128, 2, 512), F32),
            ("p7_mv", (1, 4 * 512), F32),
            ("p7_ostg", (128, 8, 512), F32),
        ])
        bx3s, x3bs, a_all, wauc, wadc = (t7["p7_bx3s"], t7["p7_x3bs"], t7["p7_a_all"],
                                         t7["p7_wauc"], t7["p7_wadc"])
        lp, sel7, baups, badcs = t7["p7_lp"], t7["p7_selb"], t7["p7_baups"], t7["p7_badcs"]
        lrow7, mcol7, ind, indc8, ind64, bcol = (t7["p7_lrow"], t7["p7_mcol"], t7["p7_ind"],
                                                 t7["p7_indc8"], t7["p7_ind64"],
                                                 t7["p7_bcol"])
        tmp7, sq7, mv7, ostg = t7["p7_tmp"], t7["p7_sq"], t7["p7_mv"], t7["p7_ostg"]

        prog.step(sync=lambda e: dma_s(e, [
            (lp[:], lparts_all), (sel7[:], selb), (baups[:], baup8), (badcs[:], badc),
            (bx3s[:, 0:4, :], bx3_scr[0:512, :].rearrange("(j p) t -> p j t", p=128)),
        ]), scalar=lambda s: dma_c(s, [
            (bx3s[:, 4:8, :], bx3_scr[512:1024, :].rearrange("(j p) t -> p j t", p=128)),
            (x3bs[:], x3b_scr.rearrange("(j p) t -> p j t", p=128)),
        ]), gpsimd=lambda g: dma_g(g, [
            (wauc[:], Wau.rearrange("e k m -> k e m")),
        ]))

        prog.step(tensor=lambda t: t.matmul(ps[0][0:2, 0:8], sel7[:], lp[:],
                                            start=True, stop=True))
        prog.step(vector=lambda v: v.tensor_copy(lp[0:2, :], ps[0][0:2, 0:8]))
        prog.step(sync=lambda e: dma_s(e, [(l2scr[:], lp[0:2, :])]))

        def ownrow(e):
            return dma_s(e, [(lrow7[:], l2scr[bass.ts(dyn["r4"], 1), :])])
        prog.step(sync=ownrow)

        prog.step(vector=lambda v: v.tensor_reduce(mcol7[0:1, 0:1], lrow7[:],
                                                   mybir.AxisListType.X, ALU.min))
        prog.step(vector=lambda v: v.tensor_scalar(ind[:], lrow7[:], mcol7[0:1, 0:1],
                                                   None, ALU.is_le))
        prog.step(tensor=lambda t: (
            t.matmul(ps[0][0:8, 0:1], ind[:], ones[0:1, 0:1], start=True, stop=True),
            t.matmul(ps[1][0:64, 0:8], ones[0:1, 0:64], ind[:], start=True, stop=True))[-1])
        prog.step(vector=lambda v: (v.tensor_copy(indc8[:], ps[0][0:8, 0:1]),
                                    v.tensor_copy(ind64[:], ps[1][0:64, 0:8]))[-1])
        prog.step(tensor=lambda t: [t.matmul(ps[2][0:128, m:m + 1],
                                             baups[:, m * 128:(m + 1) * 128],
                                             indc8[:], start=True, stop=True)
                                    for m in range(8)][-1])
        prog.step(vector=lambda v: v.tensor_copy(bcol[:], ps[2][:, 0:8]))

        for e_ in range(E):
            prog.step(gpsimd=lambda g, e_=e_: dma_g(g, [
                (wadc[:], Wad[e_].rearrange("(k p) m -> p k m", p=128)),
            ]))
            for n in range(4):
                def ma(t, n=n):
                    last = None
                    for k in range(8):
                        last = t.matmul(ps[0][0:64, :], wadc[:, k, :],
                                        x3bs[:, k, n * 512:(n + 1) * 512],
                                        start=(k == 0), stop=(k == 7))
                    return last
                prog.step(tensor=ma)
                prog.step(scalar=lambda s, e_=e_, n=n: s.activation(
                    a_all[:, e_, n * 512:(n + 1) * 512], ps[0][0:64, :], AF.Relu,
                    bias=badcs[:, e_:e_ + 1]))
            prog.step(vector=lambda v, e_=e_: v.tensor_scalar(
                a_all[:, e_, :], a_all[:, e_, :], ind64[:, e_:e_ + 1], None, ALU.mult))

        for n in range(4):
            n_sl = slice(n * 512, (n + 1) * 512)
            for mw in range(2):
                def mu_(t, mw=mw, n_sl=n_sl):
                    last = None
                    for mi in range(4):
                        m = mw * 4 + mi
                        for e_ in range(E):
                            last = t.matmul(ps[mi][:], wauc[:, e_, m * 128:(m + 1) * 128],
                                            a_all[:, e_, n_sl],
                                            start=(e_ == 0), stop=(e_ == 7))
                    return last
                prog.step(tensor=mu_)

                def evu(v, mw=mw, n_sl=n_sl):
                    last = None
                    for mi in range(4):
                        m = mw * 4 + mi
                        v.tensor_scalar(tmp7[:], ps[mi][:], bcol[:, m:m + 1], None, ALU.add)
                        last = v.tensor_tensor(bx3s[:, m, n_sl], tmp7[:], bx3s[:, m, n_sl],
                                               ALU.add)
                    return last
                prog.step(vector=evu)

        def ln4_out(j, n, n_sl, v, x_ap, ps_a, ps_b):
            v.tensor_tensor(ostg[:, j, :], x_ap, ps_a[:], ALU.mult)
            return v.tensor_tensor(ostg[:, j, :], ostg[:, j, :], ps_b[:], ALU.add)

        def ln4_store(prog_, n, n_sl):
            prog_.step(sync=lambda e, n_sl=n_sl: dma_s(e, [
                (out_d[:, n_sl].rearrange("(j p) t -> p j t", p=128), ostg[:])]))
        emit_ln_fused(prog, lambda j, n_sl: bx3s[:, j, n_sl], ln4_out, sq7,
                      ps[0], ps[1], ps[2], ps[3], ones, mv7, consts,
                      post_chunk=ln4_store)
        st7.close()
    st0.close()
    return nc



def kernel(**I):
    P = host_prep(I)
    ident = np.eye(128, dtype=np.float32)
    cores = list(range(NC))

    nc = bass.Bass("TRN2", target_bir_lowering=False, debug=False)
    build_full(nc)
    in_maps = []
    for c in cores:
        in_maps.append({
            "x_fm": P["x_fm"][c], "lng": P["lng"], "lnb": P["lnb"],
            "inwT": P["inwT"][c], "convw": P["convw"][c], "convb": P["convb"][c],
            "xprojT": P["xprojT"][c], "A_log": P["A_log"], "dtw": P["dtw"][c],
            "dtb": P["dtb"][c], "Dc": P["Dc"][c], "sel": P["sel"],
            "outwT": P["outwT"], "gamma": P["gamma"], "beta": P["beta"],
            "gate_w": P["gate_w"], "w1": P["w1"], "b1c": P["b1c"],
            "w2": P["w2"], "b2row": P["b2row"], "wattnT": P["wattnT"],
            "ident": ident, "dmask": P["dmask"], "wprojT": P["wprojT"],
            "W_base": P["W_base"], "bbase": P["bbase"], "Wd1": P["Wd1"],
            "bd1c": P["bd1c"], "Wd2": P["Wd2"], "bd2c": P["bd2c"],
            "selb": P["selb"], "badc": P["badc"], "Wad": P["Wad"],
            "Wau": P["Wau"], "baup8": P["baup8"]})
    r = run_bass_kernel_spmd(nc, in_maps, core_ids=cores).results

    out = np.empty((B * S, D), np.float32)
    for c in cores:
        out[c * TOK:(c + 1) * TOK] = np.asarray(r[c]["out_slab"], np.float32).T
    return out.reshape(B, S, D)


# version 05
# speedup vs baseline: 1.2113x; 1.1206x over previous
import numpy as np
import ml_dtypes
import contextlib
import concourse.bass as bass
import concourse.mybir as mybir
from concourse.bass_utils import run_bass_kernel_spmd

F32 = mybir.dt.float32
BF16 = mybir.dt.bfloat16
AF = mybir.ActivationFunctionType
ALU = mybir.AluOpType
BF = ml_dtypes.bfloat16

NC = 8
B = 2
S = 8192
D = 1024
DI = 2048
DIC = DI // NC
NST = 16
HMOE = 4096
E = 8
TOK = (B * S) // NC
NH, HS = 16, 64
EPS = 1e-5
EM10 = float(np.exp(np.float64(-10.0)))

ENGS = ("sync", "tensor", "vector", "scalar", "gpsimd")


class Prog:

    def __init__(self, nc, block, tick, dsems):
        self.nc = nc
        self.block = block
        self.tick = tick
        self.k = 0
        self.dsems = dsems
        self.dctr = {e: 0 for e in dsems}
        self.ccn = 0

    def step(self, **work):
        self.k += 1
        prev = 5 * (self.k - 1)
        for name in ENGS:
            fn = work.get(name)

            def emit(eng, fn=fn):
                eng.wait_ge(self.tick, prev)
                last = fn(eng) if fn is not None else None
                if last is None:
                    last = eng.nop()
                last.then_inc(self.tick, 1)
            getattr(self.block, name)(emit)

    def dmas(self, eng_name):
        sem = self.dsems[eng_name]

        def go(eng, transfers, **kw):
            for out, in_ in transfers:
                eng.dma_start(out=out, in_=in_, **kw).then_inc(sem, 16)
                self.dctr[eng_name] += 16
            eng.wait_ge(sem, self.dctr[eng_name])
            return eng.nop()
        return go


def alloc(nc, st, sbufs=(), psums=(), sems=()):
    d = {}
    for name, shape, dt in sbufs:
        d[name] = st.enter_context(nc.sbuf_tensor(name, list(shape), dt))
    for name, shape, dt in psums:
        d[name] = st.enter_context(nc.psum_tensor(name, list(shape), dt))
    for name in sems:
        d[name] = st.enter_context(nc.semaphore(name))
    return d



def _col8(v):
    return np.ascontiguousarray(v.reshape(8, 128).T)


def host_prep(I):
    P = {}
    x = I["L0_attn_x"]
    xf = np.asarray(x).reshape(B * S, D)
    P["x_fm"] = [np.ascontiguousarray(xf[c * TOK:(c + 1) * TOK].T) for c in range(NC)]
    P["lng"] = _col8(I["L0_attn_ln_g"])
    P["lnb"] = _col8(I["L0_attn_ln_b"])
    in_w = I["L0_attn_in_w"]
    P["inwT"] = []
    for c in range(NC):
        rows = np.concatenate([in_w[DIC * c:DIC * (c + 1)],
                               in_w[DI + DIC * c:DI + DIC * (c + 1)]], axis=0)
        P["inwT"].append(np.ascontiguousarray(rows.T))
    cw_ = I["L0_attn_conv_w"][:, 0, :]
    P["convw"] = [np.ascontiguousarray(
        cw_[DIC * c:DIC * (c + 1)].reshape(2, 128, 4).transpose(1, 0, 2).reshape(128, 8))
        for c in range(NC)]
    P["convb"] = [np.ascontiguousarray(
        I["L0_attn_conv_b"][DIC * c:DIC * (c + 1)].reshape(2, 128).T) for c in range(NC)]
    xp_w = I["L0_attn_xproj_w"]
    P["xprojT"] = [np.ascontiguousarray(
        xp_w[:, DIC * c:DIC * (c + 1)].T.reshape(2, 128, 33).transpose(1, 0, 2).reshape(128, 66))
        for c in range(NC)]
    P["dtw"] = [np.ascontiguousarray(I["L0_attn_dt_w"][DIC * c:DIC * (c + 1), 0]
                                     .reshape(2, 128).T) for c in range(NC)]
    P["dtb"] = [np.ascontiguousarray(I["L0_attn_dt_b"][DIC * c:DIC * (c + 1)]
                                     .reshape(2, 128).T) for c in range(NC)]
    P["Dc"] = [np.ascontiguousarray(I["L0_attn_D"][DIC * c:DIC * (c + 1)]
                                    .reshape(2, 128).T) for c in range(NC)]
    P["A_log"] = np.ascontiguousarray(I["L0_attn_A_log"].reshape(1, 16))
    P["outwT"] = np.ascontiguousarray(I["L0_attn_out_w"].T)
    P["gate_w"] = np.ascontiguousarray(I["L0_ffn_gate_w"])
    P["w1"] = I["L0_ffn_w1"]
    P["b1c"] = np.ascontiguousarray(
        I["L0_ffn_b1"].reshape(8, 32, 128).transpose(2, 0, 1).reshape(128, 8 * 32))
    P["w2"] = I["L0_ffn_w2"]
    P["b2row"] = np.ascontiguousarray(I["L0_ffn_b2"].reshape(1, 8 * 1024))
    P["gamma"] = _col8(I["L0_ffn_gamma"])
    P["beta"] = _col8(I["L0_ffn_beta"])
    P["wattnT"] = np.ascontiguousarray(I["L1_attn_w_attn"].T)
    P["wprojT"] = np.ascontiguousarray(I["L1_attn_w_proj"].T)
    P["W_base"] = np.ascontiguousarray(I["L1_ffn_W_base"])
    P["bbase"] = _col8(I["L1_ffn_b_base"])
    P["Wd1"] = I["L1_ffn_Wd1"]
    P["bd1c"] = np.ascontiguousarray(
        I["L1_ffn_bd1"].reshape(8, 2, 128).transpose(2, 0, 1).reshape(128, 16))
    P["Wd2"] = I["L1_ffn_Wd2"]
    P["bd2c"] = np.ascontiguousarray(
        I["L1_ffn_bd2"].reshape(8, 8, 128).transpose(2, 0, 1).reshape(128, 64))
    P["Wad"] = I["L1_ffn_Wa_down"]
    P["badc"] = np.ascontiguousarray(I["L1_ffn_ba_down"].T)
    P["Wau"] = I["L1_ffn_Wa_up"]
    P["baup8"] = np.ascontiguousarray(I["L1_ffn_ba_up"])
    selm2 = np.zeros((8, 2), np.float32)
    selm2[0:4, 0] = 1.0
    selm2[4:8, 1] = 1.0
    P["selb"] = selm2
    selm = np.zeros((16, 16 * 128), np.float32)
    for n in range(16):
        selm[n, n * 128:(n + 1) * 128] = 1.0
    P["sel"] = selm
    m = np.zeros((4, 128, 512), np.float32)
    for r in range(4):
        for p in range(128):
            m[r, p, :] = (np.arange(512) >= p + 128 * r)
    P["dmask"] = np.ascontiguousarray(m.reshape(4 * 128, 512))
    return P



def emit_ln_stats(prog, x_tile, sq_sb, ps_mu, ps_sq, ones_col,
                  mvrow, arow, brow, eps_col, ncols=512, n_chunks=4, nd=8,
                  nfeat=1024):
    inv = 1.0 / nfeat
    G = sq_sb.shape[1]
    for n in range(n_chunks):
        n_sl = slice(n * ncols, (n + 1) * ncols)
        for g0 in range(0, nd, G):
            def sqs(s, g0=g0, n_sl=n_sl):
                last = None
                for i in range(G):
                    last = s.activation(sq_sb[:, i, :], x_tile(g0 + i, n_sl), AF.Square)
                return last
            prog.step(scalar=sqs)

            def mms(t, g0=g0, n_sl=n_sl):
                last = None
                for i in range(G):
                    j = g0 + i
                    t.matmul(ps_mu[0:1, :], ones_col[:, 0:1], x_tile(j, n_sl),
                             start=(j == 0), stop=(j == nd - 1))
                    last = t.matmul(ps_sq[0:1, :], ones_col[:, 0:1], sq_sb[:, i, :],
                                    start=(j == 0), stop=(j == nd - 1))
                return last
            prog.step(tensor=mms)

        mu = mvrow[0:1, 0:ncols]
        va = mvrow[0:1, ncols:2 * ncols]

        def stats(v, n_sl=n_sl, mu=mu, va=va):
            v.tensor_scalar(mu, ps_mu[0:1, :], inv, None, ALU.mult)
            v.tensor_scalar(va, ps_sq[0:1, :], inv, None, ALU.mult)
            v.tensor_tensor(arow[0:1, n_sl], mu, mu, ALU.mult)
            return v.tensor_tensor(va, va, arow[0:1, n_sl], ALU.subtract)
        prog.step(vector=stats)

        def rstd(s, n_sl=n_sl, va=va):
            s.activation(arow[0:1, n_sl], va, AF.Ln, bias=eps_col[0:1, 0:1])
            return s.activation(arow[0:1, n_sl], arow[0:1, n_sl], AF.Exp, scale=-0.5)
        prog.step(scalar=rstd)

        def bro(v, n_sl=n_sl, mu=mu):
            return v.scalar_tensor_tensor(brow[0:1, n_sl], mu, -1.0,
                                          arow[0:1, n_sl], ALU.mult, ALU.mult)
        prog.step(vector=bro)


def emit_ln_apply(prog, x_tile, out_fns, ps_a, ps_b, ones_row, arow, brow, tmp_sb,
                  ncols=512, n_chunks=4, nd=8):
    for n in range(n_chunks):
        n_sl = slice(n * ncols, (n + 1) * ncols)

        def bcast(t, n_sl=n_sl):
            t.matmul(ps_a[:, :], ones_row[0:1, :], arow[0:1, n_sl], start=True, stop=True)
            return t.matmul(ps_b[:, :], ones_row[0:1, :], brow[0:1, n_sl],
                            start=True, stop=True)
        prog.step(tensor=bcast)

        def apply(v, n=n, n_sl=n_sl):
            last = None
            for j in range(nd):
                v.tensor_tensor(tmp_sb[:, 0:ncols], x_tile(j, n_sl), ps_a[:, :], ALU.mult)
                v.tensor_tensor(tmp_sb[:, 0:ncols], tmp_sb[:, 0:ncols], ps_b[:, :], ALU.add)
                last = out_fns(j, n, n_sl, v, tmp_sb[:, 0:ncols])
            return last
        prog.step(vector=apply)


def emit_ln_fused(prog, x_tile, out_fn, sq_sb, ps_mu, ps_sq, ps_a, ps_b,
                  ones_col, mv4, eps_col, ncols=512, n_chunks=4, nd=8, nfeat=1024,
                  post_chunk=None):
    inv = 1.0 / nfeat
    G = sq_sb.shape[1]
    for n in range(n_chunks):
        n_sl = slice(n * ncols, (n + 1) * ncols)
        for g0 in range(0, nd, G):
            def sqs(s, g0=g0, n_sl=n_sl):
                last = None
                for i in range(G):
                    last = s.activation(sq_sb[:, i, :], x_tile(g0 + i, n_sl), AF.Square)
                return last
            prog.step(scalar=sqs)

            def mms(t, g0=g0, n_sl=n_sl):
                last = None
                for i in range(G):
                    j = g0 + i
                    t.matmul(ps_mu[0:1, :], ones_col[:, 0:1], x_tile(j, n_sl),
                             start=(j == 0), stop=(j == nd - 1))
                    last = t.matmul(ps_sq[0:1, :], ones_col[:, 0:1], sq_sb[:, i, :],
                                    start=(j == 0), stop=(j == nd - 1))
                return last
            prog.step(tensor=mms)

        mu = mv4[0:1, 0:ncols]
        va = mv4[0:1, ncols:2 * ncols]
        ar = mv4[0:1, 2 * ncols:3 * ncols]
        br = mv4[0:1, 3 * ncols:4 * ncols]

        def stats(v, mu=mu, va=va, ar=ar):
            v.tensor_scalar(mu, ps_mu[0:1, :], inv, None, ALU.mult)
            v.tensor_scalar(va, ps_sq[0:1, :], inv, None, ALU.mult)
            v.tensor_tensor(ar, mu, mu, ALU.mult)
            return v.tensor_tensor(va, va, ar, ALU.subtract)
        prog.step(vector=stats)

        def rstd(s, va=va, ar=ar):
            s.activation(ar, va, AF.Ln, bias=eps_col[0:1, 0:1])
            return s.activation(ar, ar, AF.Exp, scale=-0.5)
        prog.step(scalar=rstd)

        def bro(v, mu=mu, ar=ar, br=br):
            return v.scalar_tensor_tensor(br, mu, -1.0, ar, ALU.mult, ALU.mult)
        prog.step(vector=bro)

        def bcast(t, ar=ar, br=br):
            t.matmul(ps_a[:], ones_col[0:1, 0:128], ar, start=True, stop=True)
            return t.matmul(ps_b[:], ones_col[0:1, 0:128], br, start=True, stop=True)
        prog.step(tensor=bcast)

        def apply(v, n=n, n_sl=n_sl):
            last = None
            for j in range(nd):
                last = out_fn(j, n, n_sl, v, x_tile(j, n_sl), ps_a, ps_b)
            return last
        prog.step(vector=apply)
        if post_chunk is not None:
            post_chunk(prog, n, n_sl)



def build_full(nc):
    x_in = nc.dram_tensor("x_fm", [D, TOK], F32, kind="ExternalInput").ap()
    lng = nc.dram_tensor("lng", [128, 8], F32, kind="ExternalInput").ap()
    lnb = nc.dram_tensor("lnb", [128, 8], F32, kind="ExternalInput").ap()
    inwT = nc.dram_tensor("inwT", [D, 4 * 128], F32, kind="ExternalInput").ap()
    convw = nc.dram_tensor("convw", [128, 8], F32, kind="ExternalInput").ap()
    convb = nc.dram_tensor("convb", [128, 2], F32, kind="ExternalInput").ap()
    xprojT = nc.dram_tensor("xprojT", [128, 66], F32, kind="ExternalInput").ap()
    alog = nc.dram_tensor("A_log", [1, 16], F32, kind="ExternalInput").ap()
    dtw = nc.dram_tensor("dtw", [128, 2], F32, kind="ExternalInput").ap()
    dtb = nc.dram_tensor("dtb", [128, 2], F32, kind="ExternalInput").ap()
    Dc = nc.dram_tensor("Dc", [128, 2], F32, kind="ExternalInput").ap()
    sel = nc.dram_tensor("sel", [16, 16 * 128], F32, kind="ExternalInput").ap()
    outwT = nc.dram_tensor("outwT", [DI, D], F32, kind="ExternalInput").ap()
    gamma = nc.dram_tensor("gamma", [128, 8], F32, kind="ExternalInput").ap()
    beta = nc.dram_tensor("beta", [128, 8], F32, kind="ExternalInput").ap()
    gate_w = nc.dram_tensor("gate_w", [D, 8], F32, kind="ExternalInput").ap()
    w1 = nc.dram_tensor("w1", [E, D, HMOE], F32, kind="ExternalInput").ap()
    b1c = nc.dram_tensor("b1c", [128, E * 32], F32, kind="ExternalInput").ap()
    w2 = nc.dram_tensor("w2", [E, HMOE, D], F32, kind="ExternalInput").ap()
    b2row = nc.dram_tensor("b2row", [1, E * D], F32, kind="ExternalInput").ap()
    wattnT = nc.dram_tensor("wattnT", [D, 3 * D], F32, kind="ExternalInput").ap()
    ident = nc.dram_tensor("ident", [128, 128], F32, kind="ExternalInput").ap()
    dmask = nc.dram_tensor("dmask", [4 * 128, 512], F32, kind="ExternalInput").ap()
    wprojT = nc.dram_tensor("wprojT", [D, D], F32, kind="ExternalInput").ap()
    Wb = nc.dram_tensor("W_base", [D, D], F32, kind="ExternalInput").ap()
    bbase = nc.dram_tensor("bbase", [128, 8], F32, kind="ExternalInput").ap()
    Wd1 = nc.dram_tensor("Wd1", [E, D, 256], F32, kind="ExternalInput").ap()
    bd1c = nc.dram_tensor("bd1c", [128, 16], F32, kind="ExternalInput").ap()
    Wd2 = nc.dram_tensor("Wd2", [E, 256, D], F32, kind="ExternalInput").ap()
    bd2c = nc.dram_tensor("bd2c", [128, 64], F32, kind="ExternalInput").ap()
    selb = nc.dram_tensor("selb", [E, 2], F32, kind="ExternalInput").ap()
    badc = nc.dram_tensor("badc", [64, E], F32, kind="ExternalInput").ap()
    Wad = nc.dram_tensor("Wad", [E, D, 64], F32, kind="ExternalInput").ap()
    Wau = nc.dram_tensor("Wau", [E, 64, D], F32, kind="ExternalInput").ap()
    baup8 = nc.dram_tensor("baup8", [E, D], F32, kind="ExternalInput").ap()
    out_d = nc.dram_tensor("out_slab", [D, TOK], F32, kind="ExternalOutput").ap()

    xn0_loc_h = nc.dram_tensor("xn0_loc", [D, TOK], BF16, kind="Internal")
    xn0_all_h = nc.dram_tensor("xn0_all", [NC, D, TOK], BF16, kind="Internal")
    xs_scr = nc.dram_tensor("xs_scr", [2 * DIC, S], F32, kind="Internal").ap()
    z_scr = nc.dram_tensor("z_scr", [2 * DIC, S], BF16, kind="Internal").ap()
    xp_loc_h = nc.dram_tensor("xp_loc", [66, S], F32, kind="Internal")
    xp_sum_h = nc.dram_tensor("xp_sum", [66, S], F32, kind="Internal")
    y_loc_h = nc.dram_tensor("y_loc", [NC, 2 * 128, TOK], BF16, kind="Internal")
    y_all_h = nc.dram_tensor("y_all", [NC * NC, 2 * 128, TOK], BF16, kind="Internal")
    x2_scr = nc.dram_tensor("x2_scr", [D, TOK], F32, kind="Internal").ap()
    y_cols = nc.dram_tensor("y_cols", [DI, TOK], BF16, kind="Internal").ap()
    qk_loc_h = nc.dram_tensor("qk_loc", [4, 8, 64, TOK], BF16, kind="Internal")
    qk_all_h = nc.dram_tensor("qk_all", [32, 8, 64, TOK], BF16, kind="Internal")
    qk_p = nc.dram_tensor("qk_p", [4, 8, 64, TOK], BF16, kind="Internal").ap()
    v_loc_h = nc.dram_tensor("v_loc", [16, 128, 16, 64], BF16, kind="Internal")
    v_all_h = nc.dram_tensor("v_all", [NC * 16, 128, 16, 64], BF16, kind="Internal")
    v_p = nc.dram_tensor("v_p", [16, 128, 16, 64], BF16, kind="Internal").ap()
    yat_loc_h = nc.dram_tensor("yat_loc", [4, 256, TOK], BF16, kind="Internal")
    yat_all_h = nc.dram_tensor("yat_all", [NC * 4, 256, TOK], BF16, kind="Internal")
    yat_cols = nc.dram_tensor("yat_cols", [D, TOK], BF16, kind="Internal").ap()
    x3b_scr = nc.dram_tensor("x3b_scr", [D, TOK], BF16, kind="Internal").ap()
    bx3_scr = nc.dram_tensor("bx3_scr", [D, TOK], F32, kind="Internal").ap()
    lrow_loc_h = nc.dram_tensor("lrow_loc", [1, E], F32, kind="Internal")
    lparts_all_h = nc.dram_tensor("lparts_all", [NC, E], F32, kind="Internal")
    l2scr = nc.dram_tensor("l2scr", [2, E], F32, kind="Internal").ap()

    xn0_loc = xn0_loc_h.ap()
    xn0_all = xn0_all_h.ap()
    xp_loc = xp_loc_h.ap()
    xp_sum = xp_sum_h.ap()
    y_loc = y_loc_h.ap()
    y_all = y_all_h.ap()
    qk_loc = qk_loc_h.ap()
    qk_all = qk_all_h.ap()
    v_loc = v_loc_h.ap()
    v_all = v_all_h.ap()
    yat_loc = yat_loc_h.ap()
    yat_all = yat_all_h.ap()
    lrow_loc = lrow_loc_h.ap()
    lparts_all = lparts_all_h.ap()

    st0 = contextlib.ExitStack()
    sh = alloc(nc, st0, sbufs=[
        ("ones", (128, 128), F32),
        ("idn", (128, 128), F32),
        ("consts", (128, 4), F32),
    ], psums=[(f"ps{i}", (128, 512), F32) for i in range(8)],
       sems=["tick", "dm_sync", "dm_gp", "dm_sc", "dm_ve", "cc_sem"])
    ones, idn, consts = sh["ones"], sh["idn"], sh["consts"]
    ps = [sh[f"ps{i}"] for i in range(8)]
    tick, dm_sync, dm_gp, cc_sem = sh["tick"], sh["dm_sync"], sh["dm_gp"], sh["cc_sem"]

    with nc.Block() as block:
        prog = Prog(nc, block, tick, {"sync": dm_sync, "gpsimd": dm_gp,
                                      "scalar": sh["dm_sc"], "vector": sh["dm_ve"]})
        dma_s = prog.dmas("sync")
        dma_g = prog.dmas("gpsimd")
        dma_c = prog.dmas("scalar")
        dma_v = prog.dmas("vector")

        def cc(kind, op, in_h, out_h):
            def go(g):
                prog.ccn += 1
                g.collective_compute(
                    kind, op, replica_groups=[list(range(NC))],
                    ins=[in_h.ap().opt()], outs=[out_h.ap().opt()],
                ).then_inc(cc_sem)
                g.wait_ge(cc_sem, prog.ccn)
                return g.nop()
            prog.step(gpsimd=go)

        dyn = {}

        def mkidx_s(e):
            rank = e.partition_id()
            r4 = rank // 4
            dyn["r4"] = r4
            base = rank + 12 * r4
            dyn["qv"] = [base + 4 * ji for ji in range(4)]
            return None

        def mkidx_g(g):
            rank_g = g.partition_id()
            dyn["ym"] = [rank_g + 8 * j for j in range(4)]
            return None

        def mkidx_c(s):
            rank_c = s.partition_id()
            base_c = rank_c + 12 * (rank_c // 4)
            dyn["qvc"] = [base_c + 4 * ji for ji in range(4)]
            dyn["ymc"] = [rank_c + 8 * j for j in range(4, 8)]
            return None

        prog.step(sync=lambda e: (mkidx_s(e), dma_s(e, [(idn[:], ident)]))[-1],
                  gpsimd=mkidx_g,
                  scalar=mkidx_c,
                  vector=lambda v: (v.memset(ones[:], 1.0),
                                    v.memset(consts[:, 0:1], EPS),
                                    v.memset(consts[:, 1:2], -10.0))[-1])

        st1 = contextlib.ExitStack()
        t1 = alloc(nc, st1, sbufs=[
            ("p1_xT", (128, 8, TOK), F32), ("p1_xn", (128, 8, TOK), BF16),
            ("p1_sq", (128, 8, 512), F32), ("p1_rows", (1, 4 * TOK), F32),
            ("p1_gb", (128, 16), F32), ("p1_tmp", (128, 512), F32),
        ])
        xT, xn, sq1, rows1, gb1, tmp1 = (t1["p1_xT"], t1["p1_xn"], t1["p1_sq"],
                                         t1["p1_rows"], t1["p1_gb"], t1["p1_tmp"])
        arow = rows1[:, 2 * TOK:3 * TOK]
        brow = rows1[:, 3 * TOK:4 * TOK]

        prog.step(sync=lambda e: dma_s(e, [
            (xT[:, 0:4, :], x_in[0:512, :].rearrange("(j p) t -> p j t", p=128)),
            (gb1[:, 0:8], lng), (gb1[:, 8:16], lnb),
        ]), gpsimd=lambda g: dma_g(g, [
            (xT[:, 4:8, :], x_in[512:1024, :].rearrange("(j p) t -> p j t", p=128)),
        ]))

        emit_ln_stats(prog, lambda j, n_sl: xT[:, j, n_sl], sq1, ps[0], ps[1],
                      ones, rows1[:, 0:1024], arow, brow, consts)

        def out_fn1(j, n, n_sl, v, normed):
            return v.tensor_scalar(xn[:, j, n_sl], normed, gb1[:, j:j + 1],
                                   gb1[:, 8 + j:9 + j], ALU.mult, ALU.add)
        emit_ln_apply(prog, lambda j, n_sl: xT[:, j, n_sl], out_fn1,
                      ps[2], ps[3], ones, arow, brow, tmp1)

        prog.step(sync=lambda e: dma_s(e, [
            (xn0_loc.rearrange("(j p) t -> p j t", p=128), xn[:]),
        ]))
        st1.close()

        cc("AllGather", ALU.bypass, xn0_loc_h, xn0_all_h)

        TCH = 2048
        st2 = contextlib.ExitStack()
        t2 = alloc(nc, st2, sbufs=[
            ("p2_inw", (128, 8, 4 * 128), BF16), ("p2_xnc", (128, 8, TCH), BF16),
            ("p2_xi", (128, 2, TCH + 3), F32), ("p2_xc", (128, 2, TCH), F32),
            ("p2_xsc", (128, 2, TCH), F32), ("p2_zst", (128, 2, TCH), BF16),
            ("p2_cw", (128, 8), F32), ("p2_cb", (128, 2), F32),
            ("p2_xpj", (128, 2, 33), F32), ("p2_xpst", (33, 2, TCH), F32),
            ("p2_carry", (128, 2, 3), F32),
        ])
        inw, xnc, xi, xc, xsc, zst = (t2["p2_inw"], t2["p2_xnc"], t2["p2_xi"],
                                      t2["p2_xc"], t2["p2_xsc"], t2["p2_zst"])
        cw_sb, cb_sb, xpj, xpst, carry = (t2["p2_cw"], t2["p2_cb"], t2["p2_xpj"],
                                          t2["p2_xpst"], t2["p2_carry"])

        prog.step(gpsimd=lambda g: dma_g(g, [
            (inw[:], inwT.rearrange("(k p) m -> p k m", p=128)),
        ]), sync=lambda e: dma_s(e, [
            (cw_sb[:], convw), (cb_sb[:], convb),
            (xpj[:], xprojT.rearrange("p (d m) -> p d m", d=2)),
        ]))

        for b in range(B):
            for ch in range(S // TCH):
                rk = 4 * b + ch

                def ldxn(e, rk=rk):
                    return dma_s(e, [(xnc[:, 0:4, :], xn0_all[rk, 0:512, :]
                                      .rearrange("(j p) t -> p j t", p=128))])

                def ldxn_g(g, rk=rk):
                    return dma_g(g, [(xnc[:, 4:8, :], xn0_all[rk, 512:1024, :]
                                      .rearrange("(j p) t -> p j t", p=128))])
                prog.step(sync=ldxn, gpsimd=ldxn_g)

                for n in range(TCH // 512):
                    n_sl = slice(n * 512, (n + 1) * 512)

                    def mms(t, n_sl=n_sl):
                        last = None
                        for m in range(4):
                            for k in range(8):
                                last = t.matmul(ps[m][:], inw[:, k, m * 128:(m + 1) * 128],
                                                xnc[:, k, n_sl],
                                                start=(k == 0), stop=(k == 7))
                        return last

                    def evac(v, n=n, n_sl=n_sl, ch=ch):
                        x_sl = slice(3 + n * 512, 3 + (n + 1) * 512)
                        if n == 0:
                            if ch == 0:
                                v.memset(carry[:], 0.0)
                            v.tensor_copy(xi[:, :, 0:3], carry[:])
                        v.tensor_copy(xi[:, 0, x_sl], ps[0][:])
                        v.tensor_copy(xi[:, 1, x_sl], ps[1][:])
                        v.tensor_copy(zst[:, 0, n_sl], ps[2][:])
                        return v.tensor_copy(zst[:, 1, n_sl], ps[3][:])
                    prog.step(tensor=mms)
                    prog.step(vector=evac)

                def conv(v):
                    last = None
                    for dt in range(2):
                        v.tensor_scalar(xc[:, dt, :], xi[:, dt, 0:TCH],
                                        cw_sb[:, dt * 4:dt * 4 + 1], None, ALU.mult)
                        for k in range(1, 4):
                            last = v.scalar_tensor_tensor(
                                xc[:, dt, :], xi[:, dt, k:k + TCH],
                                cw_sb[:, dt * 4 + k:dt * 4 + k + 1],
                                xc[:, dt, :], ALU.mult, ALU.add)
                    return last
                prog.step(vector=conv)

                def silu_save(s):
                    s.activation(xsc[:, 0, :], xc[:, 0, :], AF.Silu, bias=cb_sb[:, 0:1])
                    return s.activation(xsc[:, 1, :], xc[:, 1, :], AF.Silu, bias=cb_sb[:, 1:2])

                def carry_save(v):
                    return v.tensor_copy(carry[:], xi[:, :, TCH:TCH + 3])
                prog.step(scalar=silu_save, vector=carry_save)

                def xpmm(t):
                    last = None
                    for n in range(4):
                        for dt in range(2):
                            last = t.matmul(ps[n][0:33, :], xpj[:, dt, :],
                                            xsc[:, dt, n * 512:(n + 1) * 512],
                                            start=(dt == 0), stop=(dt == 1))
                    return last
                prog.step(tensor=xpmm)

                def xpev(v):
                    last = None
                    for n in range(4):
                        last = v.tensor_copy(xpst[:, 0, n * 512:(n + 1) * 512], ps[n][0:33, :])
                    return last
                prog.step(vector=xpev)

                def store(e, b=b, ch=ch):
                    c_sl = slice(ch * TCH, (ch + 1) * TCH)
                    return dma_s(e, [
                        (xs_scr[b * 256:b * 256 + 128, c_sl], xsc[:, 0, :]),
                        (xs_scr[b * 256 + 128:b * 256 + 256, c_sl], xsc[:, 1, :]),
                        (z_scr[b * 256:b * 256 + 128, c_sl], zst[:, 0, :]),
                        (z_scr[b * 256 + 128:b * 256 + 256, c_sl], zst[:, 1, :]),
                        (xp_loc[b * 33:b * 33 + 33, c_sl], xpst[:, 0, :]),
                    ])
                prog.step(sync=store)
        st2.close()

        cc("AllReduce", ALU.add, xp_loc_h, xp_sum_h)

        st3 = contextlib.ExitStack()
        t3 = alloc(nc, st3, sbufs=[
            ("p3_xp0", (1, TCH), F32),
            ("p3_B_t", (16, TCH), BF16), ("p3_C_t", (16, TCH), BF16),
            ("p3_sel", (16, 16 * 128), BF16),
            ("p3_Arep", (128, 16), F32), ("p3_Arow", (1, 16), F32),
            ("p3_cols", (128, 8), F32),
            ("p3_hstate", (128, 64), F32),
            ("p3_xs", (128, TCH), F32), ("p3_z", (128, TCH), BF16),
            ("p3_delta", (128, TCH), F32), ("p3_u", (128, TCH), F32),
            ("p3_dA", (128, TCH), F32), ("p3_dBx", (128, TCH), F32),
            ("p3_h", (128, TCH), F32), ("p3_yacc", (128, TCH), F32),
            ("p3_sz", (128, TCH), F32), ("p3_ybf", (128, TCH), BF16),
        ])
        (xp0, B_t, C_t, sel_sb, Arep, Arow, cols, hstate, xs_t, z_t,
         delta, u_t, dA, dBx, h_t, yacc, sz, ybf) = (
            t3["p3_xp0"], t3["p3_B_t"], t3["p3_C_t"], t3["p3_sel"], t3["p3_Arep"],
            t3["p3_Arow"], t3["p3_cols"], t3["p3_hstate"], t3["p3_xs"], t3["p3_z"],
            t3["p3_delta"], t3["p3_u"], t3["p3_dA"], t3["p3_dBx"], t3["p3_h"],
            t3["p3_yacc"], t3["p3_sz"], t3["p3_ybf"])
        psB = [ps[0], ps[1], ps[2], ps[3]]
        psC = [ps[4], ps[5], ps[6], ps[7]]

        prog.step(sync=lambda e: dma_s(e, [
            (Arow[:], alog), (cols[:, 0:2], dtw), (cols[:, 2:4], dtb),
            (cols[:, 4:6], Dc),
        ]), gpsimd=lambda g: dma_g(g, [(sel_sb[:], sel)]),
            vector=lambda v: v.memset(hstate[:], 0.0))

        prog.step(scalar=lambda s: s.activation(Arow[:], Arow[:], AF.Exp))
        prog.step(vector=lambda v: v.tensor_scalar(Arow[:], Arow[:], -1.0, None, ALU.mult))
        prog.step(tensor=lambda t: t.matmul(ps[0][:, 0:16], ones[0:1, 0:128], Arow[:],
                                            start=True, stop=True))
        prog.step(vector=lambda v: v.tensor_copy(Arep[:], ps[0][:, 0:16]))

        for b in range(B):
            for g in range(2):
                for tsc in range(S // TCH):
                    t_sl = slice(tsc * TCH, (tsc + 1) * TCH)
                    row0 = b * 256 + g * 128

                    def lds(e, b=b, row0=row0, t_sl=t_sl):
                        return dma_s(e, [
                            (xs_t[:], xs_scr[row0:row0 + 128, t_sl]),
                            (z_t[:], z_scr[row0:row0 + 128, t_sl]),
                            (xp0[:], xp_sum[33 * b:33 * b + 1, t_sl]),
                        ])

                    def ldbc(g, b=b, t_sl=t_sl):
                        return dma_g(g, [
                            (B_t[:], xp_sum[1 + 33 * b:17 + 33 * b, t_sl]),
                            (C_t[:], xp_sum[17 + 33 * b:33 + 33 * b, t_sl]),
                        ])
                    prog.step(sync=lds, gpsimd=ldbc)

                    def x0b(t, t_sl=t_sl):
                        last = None
                        for n in range(4):
                            last = t.matmul(psB[n][:], ones[0:1, 0:128],
                                            xp0[0:1, n * 512:(n + 1) * 512],
                                            start=True, stop=True)
                        return last
                    prog.step(tensor=x0b)

                    def sdelta(s, g=g):
                        last = None
                        for n in range(4):
                            last = s.activation(delta[:, n * 512:(n + 1) * 512], psB[n][:],
                                                AF.Exp, scale=cols[:, g:g + 1],
                                                bias=cols[:, 2 + g:3 + g])
                        return last
                    prog.step(scalar=sdelta)
                    prog.step(scalar=lambda s: s.activation(delta[:], delta[:], AF.Ln, bias=1.0))
                    prog.step(vector=lambda v: v.tensor_tensor(u_t[:], delta[:], xs_t[:], ALU.mult))

                    for n in range(NST):
                        hcol = (b * 2 + g) * 16 + n

                        def brep(t, n=n):
                            last = None
                            for q in range(4):
                                q_sl = slice(q * 512, (q + 1) * 512)
                                t.matmul(psB[q][:], sel_sb[:, n * 128:(n + 1) * 128],
                                         B_t[:, q_sl], start=True, stop=True)
                                last = t.matmul(psC[q][:], sel_sb[:, n * 128:(n + 1) * 128],
                                                C_t[:, q_sl], start=True, stop=True)
                            return last

                        def dAexp(s, n=n):
                            return s.activation(dA[:], delta[:], AF.Exp,
                                                scale=Arep[:, n:n + 1])
                        prog.step(tensor=brep, scalar=dAexp)

                        def scan_grp(v, n=n, hcol=hcol):
                            for q in range(4):
                                v.tensor_tensor(dBx[:, q * 512:(q + 1) * 512],
                                                u_t[:, q * 512:(q + 1) * 512],
                                                psB[q][:], ALU.mult)
                            v.tensor_tensor_scan(h_t[:], dA[:], dBx[:],
                                                 hstate[:, hcol:hcol + 1], ALU.mult, ALU.add)
                            v.tensor_copy(hstate[:, hcol:hcol + 1], h_t[:, TCH - 1:TCH])
                            last = None
                            for q in range(4):
                                q_sl = slice(q * 512, (q + 1) * 512)
                                if n == 0:
                                    last = v.tensor_tensor(yacc[:, q_sl], h_t[:, q_sl],
                                                           psC[q][:], ALU.mult)
                                else:
                                    v.tensor_tensor(dBx[:, q_sl], h_t[:, q_sl],
                                                    psC[q][:], ALU.mult)
                                    last = v.tensor_tensor(yacc[:, q_sl], yacc[:, q_sl],
                                                           dBx[:, q_sl], ALU.add)
                            return last
                        prog.step(vector=scan_grp)

                    def ypre(v, g=g):
                        return v.scalar_tensor_tensor(yacc[:], xs_t[:], cols[:, 4 + g:5 + g],
                                                      yacc[:], ALU.mult, ALU.add)
                    prog.step(vector=ypre)
                    prog.step(scalar=lambda s: s.activation(sz[:], z_t[:], AF.Silu))
                    prog.step(vector=lambda v: v.tensor_tensor(ybf[:], yacc[:], sz[:], ALU.mult))
                    prog.step(sync=lambda e, b=b, tsc=tsc, g=g: dma_s(e, [
                        (y_loc[b * 4 + tsc, g * 128:(g + 1) * 128, :], ybf[:]),
                    ]))
        st3.close()

        cc("AllGather", ALU.bypass, y_loc_h, y_all_h)

        def ypick_g(g):
            return dma_g(g, [(y_cols[256 * j:256 * (j + 1), :],
                              y_all[bass.ts(dyn["ym"][j], 1), :, :])
                             for j in range(4)])

        def ypick_c(s):
            return dma_c(s, [(y_cols[256 * j:256 * (j + 1), :],
                              y_all[bass.ts(dyn["ymc"][j - 4], 1), :, :])
                             for j in range(4, 8)])
        prog.step(gpsimd=ypick_g, scalar=ypick_c)

        st4 = contextlib.ExitStack()
        t4 = alloc(nc, st4, sbufs=[
            ("p4_BIGA", (128, 16, 1024), F32),
            ("p4_gb", (128, 16), F32),
            ("p4_tmp", (128, 512), F32),
            ("p4_cwsb", (128, 128), F32),
            ("p4_glog", (128, 128), F32),
            ("p4_mcol", (128, 8), F32),
            ("p4_mcol16", (128, 16), F32),
            ("p4_m2col", (128, 16), F32),
            ("p4_rcol16", (128, 16), F32),
            ("p4_m8big", (128, 16, 8), F32),
        ])
        BIGA, gb4, tmp4 = t4["p4_BIGA"], t4["p4_gb"], t4["p4_tmp"]
        cwsb, glog, mcol = t4["p4_cwsb"], t4["p4_glog"], t4["p4_mcol"]
        mcol16, rcol16, m8big = t4["p4_mcol16"], t4["p4_rcol16"], t4["p4_m8big"]
        m2col = t4["p4_m2col"]

        st_xn1 = contextlib.ExitStack()
        xn1 = alloc(nc, st_xn1, sbufs=[("p4_xn1", (128, 8, TOK), BF16)])["p4_xn1"]
        stA = contextlib.ExitStack()
        sB = alloc(nc, stA, sbufs=[("p4_s", (128, 8, TOK), F32)])["p4_s"]

        stA1 = contextlib.ExitStack()
        tA1 = alloc(nc, stA1, sbufs=[
            ("p4_outwc", (128, 16, 256), BF16),
            ("p4_yb", (128, 16, 512), BF16),
            ("p4_xst2", (128, 2, 512), F32),
        ])
        outwc, yb4, xst2 = tA1["p4_outwc"], tA1["p4_yb"], tA1["p4_xst2"]

        prog.step(sync=lambda e: dma_s(e, [
            (gb4[:, 0:8], gamma), (gb4[:, 8:16], beta),
        ]))

        for n in range(4):
            n_sl = slice(n * 512, (n + 1) * 512)

            prog.step(sync=lambda e, n_sl=n_sl: dma_s(e, [
                (yb4[:, 0:8, :], y_cols[0:1024, n_sl]
                 .rearrange("(k p) t -> p k t", p=128)),
            ]), scalar=lambda s, n_sl=n_sl: dma_c(s, [
                (yb4[:, 8:16, :], y_cols[1024:2048, n_sl]
                 .rearrange("(k p) t -> p k t", p=128)),
            ]))
            for mw in range(4):
                def ldw(g, mw=mw, n_sl=n_sl):
                    return dma_g(g, [
                        (outwc[:], outwT[:, mw * 256:(mw + 1) * 256]
                         .rearrange("(k p) m -> p k m", p=128)),
                        (xst2[:], x_in[:, n_sl]
                         .rearrange("(j p) t -> p j t", p=128)[:, 2 * mw:2 * mw + 2, :]),
                    ])
                prog.step(gpsimd=ldw)

                def mms(t):
                    last = None
                    for mi in range(2):
                        for k in range(16):
                            last = t.matmul(ps[mi][:], outwc[:, k, mi * 128:(mi + 1) * 128],
                                            yb4[:, k, :], start=(k == 0), stop=(k == 15))
                    return last

                def evac(v, mw=mw, n_sl=n_sl):
                    last = None
                    for mi in range(2):
                        m = mw * 2 + mi
                        last = v.scalar_tensor_tensor(sB[:, m, n_sl], xst2[:, mi, :], 2.0,
                                                      ps[mi][:], ALU.mult, ALU.add)
                    return last
                prog.step(tensor=mms)
                prog.step(vector=evac)
        stA1.close()

        stS = contextlib.ExitStack()
        tS = alloc(nc, stS, sbufs=[
            ("p4_sq", (128, 2, 512), F32),
            ("p4_rows", (1, 4 * TOK), F32),
            ("p4_mv", (1, 2 * 512), F32),
        ])
        sq4, rows4, mv4 = tS["p4_sq"], tS["p4_rows"], tS["p4_mv"]
        a1row = rows4[:, 0 * TOK:1 * TOK]
        b1row = rows4[:, 1 * TOK:2 * TOK]
        a2row = rows4[:, 2 * TOK:3 * TOK]
        b2row_s = rows4[:, 3 * TOK:4 * TOK]

        emit_ln_stats(prog, lambda j, n_sl: sB[:, j, n_sl], sq4, ps[0], ps[1],
                      ones, mv4, a1row, b1row, consts)

        for n in range(4):
            n_sl = slice(n * 512, (n + 1) * 512)

            def bc1(t, n_sl=n_sl):
                t.matmul(ps[2][:], ones[0:1, 0:128], a1row[0:1, n_sl], start=True, stop=True)
                return t.matmul(ps[3][:], ones[0:1, 0:128], b1row[0:1, n_sl],
                                start=True, stop=True)
            prog.step(tensor=bc1)
            for g0 in range(0, 8, 2):
                def x1mk(v, g0=g0, n_sl=n_sl):
                    last = None
                    for i in range(2):
                        v.tensor_tensor(sq4[:, i, :], sB[:, g0 + i, n_sl], ps[2][:], ALU.mult)
                        last = v.tensor_tensor(sq4[:, i, :], sq4[:, i, :], ps[3][:], ALU.add)
                    return last
                prog.step(vector=x1mk)

                def mmx1(t, g0=g0):
                    last = None
                    for i in range(2):
                        j = g0 + i
                        last = t.matmul(ps[0][0:1, :], ones[:, 0:1], sq4[:, i, :],
                                        start=(j == 0), stop=(j == 7))
                    return last
                prog.step(tensor=mmx1)

                def sqx1(s):
                    last = None
                    for i in range(2):
                        last = s.activation(sq4[:, i, :], sq4[:, i, :], AF.Square)
                    return last
                prog.step(scalar=sqx1)

                def mmq1(t, g0=g0):
                    last = None
                    for i in range(2):
                        j = g0 + i
                        last = t.matmul(ps[1][0:1, :], ones[:, 0:1], sq4[:, i, :],
                                        start=(j == 0), stop=(j == 7))
                    return last
                prog.step(tensor=mmq1)

            def st2f(v):
                v.tensor_scalar(mv4[0:1, 0:512], ps[0][0:1, :], 1.0 / D, None, ALU.mult)
                v.tensor_scalar(mv4[0:1, 512:1024], ps[1][0:1, :], 1.0 / D, None, ALU.mult)
                v.tensor_tensor(tmp4[0:1, 0:512], mv4[0:1, 0:512], mv4[0:1, 0:512], ALU.mult)
                return v.tensor_tensor(mv4[0:1, 512:1024], mv4[0:1, 512:1024],
                                       tmp4[0:1, 0:512], ALU.subtract)
            prog.step(vector=st2f)

            def rstd2(s):
                s.activation(mv4[0:1, 512:1024], mv4[0:1, 512:1024], AF.Ln,
                             bias=consts[0:1, 0:1])
                return s.activation(mv4[0:1, 512:1024], mv4[0:1, 512:1024],
                                    AF.Exp, scale=-0.5)
            prog.step(scalar=rstd2)

            def rows2(v, n_sl=n_sl):
                v.tensor_copy(a2row[0:1, n_sl], mv4[0:1, 512:1024])
                return v.scalar_tensor_tensor(b2row_s[0:1, n_sl], mv4[0:1, 0:512], -1.0,
                                              mv4[0:1, 512:1024], ALU.mult, ALU.mult)
            prog.step(vector=rows2)

        for n in range(4):
            n_sl = slice(n * 512, (n + 1) * 512)

            def bc2(t, n_sl=n_sl):
                t.matmul(ps[2][:], ones[0:1, 0:128], a1row[0:1, n_sl], start=True, stop=True)
                t.matmul(ps[3][:], ones[0:1, 0:128], b1row[0:1, n_sl], start=True, stop=True)
                t.matmul(ps[4][:], ones[0:1, 0:128], a2row[0:1, n_sl], start=True, stop=True)
                return t.matmul(ps[5][:], ones[0:1, 0:128], b2row_s[0:1, n_sl],
                                start=True, stop=True)
            prog.step(tensor=bc2)

            for j in range(8):
                def vx(v, j=j, n_sl=n_sl):
                    v.tensor_tensor(tmp4[:], sB[:, j, n_sl], ps[2][:], ALU.mult)
                    v.tensor_tensor(tmp4[:], tmp4[:], ps[3][:], ALU.add)
                    v.tensor_tensor(sq4[:, 0, :], tmp4[:], ps[4][:], ALU.mult)
                    v.tensor_tensor(sq4[:, 0, :], sq4[:, 0, :], ps[5][:], ALU.add)
                    return v.tensor_scalar(xn1[:, j, n_sl], sq4[:, 0, :],
                                           gb4[:, j:j + 1], gb4[:, 8 + j:9 + j],
                                           ALU.mult, ALU.add)
                prog.step(vector=vx)

                for h in range(2):
                    def tr(t, h=h):
                        t.transpose(ps[6][:, 0:128], tmp4[:, 256 * h:256 * h + 128], idn[:])
                        return t.transpose(ps[7][:, 0:128],
                                           tmp4[:, 256 * h + 128:256 * h + 256], idn[:])
                    prog.step(tensor=tr)

                    def ev(v, j=j, n=n, h=h):
                        tau0 = n * 4 + 2 * h
                        v.tensor_scalar(BIGA[:, tau0, j * 128:(j + 1) * 128],
                                        ps[6][:, 0:128], 2.0, None, ALU.mult)
                        return v.tensor_scalar(BIGA[:, tau0 + 1, j * 128:(j + 1) * 128],
                                               ps[7][:, 0:128], 2.0, None, ALU.mult)
                    prog.step(vector=ev)
        stS.close()
        stA.close()

        stG = contextlib.ExitStack()
        gwb = alloc(nc, stG, sbufs=[("p4_gwb", (128, 8, 8), BF16)])["p4_gwb"]
        prog.step(gpsimd=lambda g: dma_g(g, [
            (gwb[:], gate_w.rearrange("(k p) e -> p k e", p=128)),
        ]))

        def gmm(t):
            last = None
            for tau in range(16):
                for k in range(8):
                    last = t.matmul(ps[0][:, tau * 8:(tau + 1) * 8],
                                    xn1[:, k, tau * 128:(tau + 1) * 128],
                                    gwb[:, k, :], start=(k == 0), stop=(k == 7))
            return last
        prog.step(tensor=gmm)

        prog.step(vector=lambda v: [v.tensor_reduce(mcol16[:, tau:tau + 1],
                                                    ps[0][:, tau * 8:(tau + 1) * 8],
                                                    mybir.AxisListType.X, ALU.max)
                                    for tau in range(16)][-1])

        prog.step(vector=lambda v: [v.tensor_scalar(glog[:, tau * 8:(tau + 1) * 8],
                                                    ps[0][:, tau * 8:(tau + 1) * 8],
                                                    mcol16[:, tau:tau + 1], None,
                                                    ALU.subtract)
                                    for tau in range(16)][-1])
        prog.step(scalar=lambda s: s.activation(glog[:], glog[:], AF.Exp))
        prog.step(vector=lambda v: [v.max(out=m8big[:, tau, :],
                                           in_=glog[:, tau * 8:(tau + 1) * 8])
                                    for tau in range(16)][-1])
        prog.step(vector=lambda v: [v.tensor_copy(m2col[:, tau:tau + 1],
                                                  m8big[:, tau, 1:2])
                                    for tau in range(16)][-1])
        prog.step(vector=lambda v: [v.tensor_tensor(mcol16[:, tau:tau + 1],
                                                    m8big[:, tau, 0:1],
                                                    m8big[:, tau, 1:2], ALU.add)
                                    for tau in range(16)][-1])
        prog.step(vector=lambda v: v.reciprocal(rcol16[:], mcol16[:]))
        prog.step(vector=lambda v: [v.tensor_scalar(cwsb[:, tau * 8:(tau + 1) * 8],
                                                    glog[:, tau * 8:(tau + 1) * 8],
                                                    m2col[:, tau:tau + 1], None, ALU.is_ge)
                                    for tau in range(16)][-1])
        prog.step(vector=lambda v: v.tensor_tensor(cwsb[:], cwsb[:], glog[:], ALU.mult))
        prog.step(vector=lambda v: [v.tensor_scalar(cwsb[:, tau * 8:(tau + 1) * 8],
                                                    cwsb[:, tau * 8:(tau + 1) * 8],
                                                    rcol16[:, tau:tau + 1], None, ALU.mult)
                                    for tau in range(16)][-1])
        stG.close()

        stM = contextlib.ExitStack()
        tM = alloc(nc, stM, sbufs=[
            ("p4_hbf", (128, 32, 1024), BF16),
            ("p4_w1c", (128, 2, 8, 128), BF16),
            ("p4_w2w", (128, 2, 8, 512), BF16),
            ("p4_b1s", (128, E * 32), F32),
            ("p4_b2t", (1, D), BF16),
            ("p4_ob", (1, 128), BF16),
        ])
        hbf, w1c, w2w, b1s, b2t = (tM["p4_hbf"], tM["p4_w1c"], tM["p4_w2w"],
                                   tM["p4_b1s"], tM["p4_b2t"])
        ones_bf = tM["p4_ob"]
        prog.step(sync=lambda e: dma_s(e, [(b1s[:], b1c)]),
                  vector=lambda v: v.memset(ones_bf[:], 1.0))

        for ex in range(E):
            prog.step(gpsimd=lambda g, e2=ex: dma_g(g, [
                (b2t[:], b2row[0:1, e2 * D:(e2 + 1) * D])]))
            for nn in range(2):
                tok0 = nn * 1024

                def ldw1(g, e2, hm, buf):
                    return dma_g(g, [(w1c[:, buf],
                                      w1[e2, :, hm * 128:(hm + 1) * 128]
                                      .rearrange("(k p) m -> p k m", p=128))])
                prog.step(gpsimd=lambda g, e2=ex: ldw1(g, e2, 0, 0))
                for hm in range(33):
                    def work(t, hm=hm, tok0=tok0):
                        if hm >= 32:
                            return None
                        last = None
                        for th in range(2):
                            for k in range(8):
                                last = t.matmul(ps[(hm % 2) * 2 + th][:],
                                                w1c[:, hm % 2, k, :],
                                                xn1[:, k, tok0 + th * 512:tok0 + (th + 1) * 512],
                                                start=(k == 0), stop=(k == 7))
                        return last

                    def pre_g(g, e2=ex, hm=hm):
                        if hm + 1 >= 32:
                            return None
                        return ldw1(g, e2, hm + 1, (hm + 1) % 2)

                    def gel(s, e2=ex, hm=hm):
                        if hm == 0:
                            return None
                        hp = hm - 1
                        s.activation(hbf[:, hp, 0:512], ps[(hp % 2) * 2][:], AF.Gelu,
                                     bias=b1s[:, e2 * 32 + hp:e2 * 32 + hp + 1])
                        return s.activation(hbf[:, hp, 512:1024], ps[(hp % 2) * 2 + 1][:],
                                            AF.Gelu, bias=b1s[:, e2 * 32 + hp:e2 * 32 + hp + 1])
                    prog.step(tensor=work, gpsimd=pre_g, scalar=gel)

                for nd in range(2):
                    def ldw2(g, e2, nd, kw, buf):
                        r0 = kw * 1024
                        return dma_g(g, [(w2w[:, buf],
                                          w2[e2, r0:r0 + 1024, nd * 512:(nd + 1) * 512]
                                          .rearrange("(k p) m -> p k m", p=128))])
                    prog.step(gpsimd=lambda g, e2=ex, nd=nd: ldw2(g, e2, nd, 0, 0))
                    for kw in range(4):
                        def mm2(t, kw=kw):
                            last = None
                            for ki in range(8):
                                k = kw * 8 + ki
                                for tm in range(8):
                                    last = t.matmul(ps[tm][:],
                                                    hbf[:, k, tm * 128:(tm + 1) * 128],
                                                    w2w[:, kw % 2, ki, :],
                                                    start=(k == 0), stop=False)
                            return last

                        def pw2_g(g, e2=ex, kw=kw, nd=nd):
                            if kw + 1 >= 4:
                                return None
                            return ldw2(g, e2, nd, kw + 1, (kw + 1) % 2)
                        prog.step(tensor=mm2, gpsimd=pw2_g)

                    def b2mm(t, nd=nd):
                        last = None
                        for tm in range(8):
                            last = t.matmul(ps[tm][:], ones_bf[0:1, :],
                                            b2t[0:1, nd * 512:(nd + 1) * 512],
                                            start=False, stop=True)
                        return last
                    prog.step(tensor=b2mm)

                    def moevac(v, e2=ex, nn=nn, nd=nd):
                        last = None
                        for tm in range(8):
                            tau = nn * 8 + tm
                            last = v.scalar_tensor_tensor(
                                BIGA[:, tau, nd * 512:(nd + 1) * 512], ps[tm][:],
                                cwsb[:, tau * 8 + e2:tau * 8 + e2 + 1],
                                BIGA[:, tau, nd * 512:(nd + 1) * 512], ALU.mult, ALU.add)
                        return last
                    prog.step(vector=moevac)
        stM.close()
        st_xn1.close()

        stQ = contextlib.ExitStack()
        tQ = alloc(nc, stQ, sbufs=[
            ("p4_x2bf", (128, 8, TOK), BF16),
            ("p4_wat", (128, 8, 3 * D), BF16),
            ("p4_tmq", (128, 1024), F32),
            ("p4_stg", (128, 8, 128), F32),
            ("p4_stgb", (128, 4, 512), BF16),
        ])
        x2bf, wat, tmq, stg, stgb = (tQ["p4_x2bf"], tQ["p4_wat"], tQ["p4_tmq"],
                                     tQ["p4_stg"], tQ["p4_stgb"])
        prog.step(gpsimd=lambda g: dma_g(g, [
            (wat[:], wattnT.rearrange("(k p) m -> p k m", p=128)),
        ]))

        for tau in range(16):
            def lnst(s, tau=tau):
                return s.activation(tmq[:], BIGA[:, tau, :], AF.Square,
                                    accum_out=mcol[:, 1:2])
            prog.step(scalar=lnst)

            def lnred(v, tau=tau):
                v.tensor_reduce(mcol[:, 0:1], BIGA[:, tau, :], mybir.AxisListType.X, ALU.add)
                v.tensor_scalar(mcol[:, 0:1], mcol[:, 0:1], 1.0 / D, None, ALU.mult)
                v.tensor_scalar(mcol[:, 1:2], mcol[:, 1:2], 1.0 / D, None, ALU.mult)
                v.tensor_tensor(mcol[:, 2:3], mcol[:, 0:1], mcol[:, 0:1], ALU.mult)
                v.tensor_tensor(mcol[:, 1:2], mcol[:, 1:2], mcol[:, 2:3], ALU.subtract)
                return v.tensor_scalar(mcol[:, 3:4], mcol[:, 0:1], -1.0, None, ALU.mult)
            prog.step(vector=lnred)

            def lnrs(s):
                s.activation(mcol[:, 1:2], mcol[:, 1:2], AF.Ln, bias=consts[:, 0:1])
                return s.activation(mcol[:, 1:2], mcol[:, 1:2], AF.Exp, scale=-0.5)
            prog.step(scalar=lnrs)

            def lnap(v, tau=tau):
                return v.tensor_scalar(tmq[:], BIGA[:, tau, :], mcol[:, 3:4],
                                       mcol[:, 1:2], ALU.add, ALU.mult)
            prog.step(vector=lnap)

            for h in range(4):
                def trx(t, h=h):
                    t.transpose(ps[6][:, 0:128], tmq[:, 256 * h:256 * h + 128], idn[:])
                    return t.transpose(ps[7][:, 0:128],
                                       tmq[:, 256 * h + 128:256 * h + 256], idn[:])
                prog.step(tensor=trx)

                def evx(v, h=h, tau=tau):
                    v.tensor_copy(stg[:, 2 * h, :], ps[6][:, 0:128])
                    v.tensor_copy(stg[:, 2 * h + 1, :], ps[7][:, 0:128])
                    v.tensor_copy(x2bf[:, 2 * h, tau * 128:(tau + 1) * 128], ps[6][:, 0:128])
                    return v.tensor_copy(x2bf[:, 2 * h + 1, tau * 128:(tau + 1) * 128],
                                         ps[7][:, 0:128])
                prog.step(vector=evx)

            prog.step(sync=lambda e, tau=tau: dma_s(e, [
                (x2_scr[:, tau * 128:(tau + 1) * 128]
                 .rearrange("(j p) t -> p j t", p=128), stg[:])]))

        for n in range(4):
            n_sl = slice(n * 512, (n + 1) * 512)
            for mw in range(4):
                def qmm(t, mw=mw, n_sl=n_sl):
                    last = None
                    for mi in range(4):
                        m = mw * 4 + mi
                        for k in range(8):
                            last = t.matmul(ps[mi][:], wat[:, k, m * 128:(m + 1) * 128],
                                            x2bf[:, k, n_sl], start=(k == 0), stop=(k == 7))
                    return last
                prog.step(tensor=qmm)

                def qev(v):
                    last = None
                    for mi in range(4):
                        last = v.tensor_copy(stgb[:, mi, :], ps[mi][:])
                    return last
                prog.step(vector=qev)

                def qst(e, mw=mw, n_sl=n_sl):
                    trs = []
                    for mi in range(4):
                        for half in range(2):
                            hraw = 8 * mw + 2 * mi + half
                            if hraw < 16:
                                slot, sub = hraw // 4, hraw % 4
                            else:
                                hk = hraw - 16
                                slot, sub = hk // 4, 4 + hk % 4
                            trs.append((qk_loc[slot, sub, :, n_sl],
                                        stgb[64 * half:64 * half + 64, mi, :]))
                    return dma_s(e, trs)
                prog.step(sync=qst)

        for tw in range(4):
            for nv in range(2):
                def vmm(t, tw=tw, nv=nv):
                    last = None
                    for ti in range(4):
                        tau = tw * 4 + ti
                        for k in range(8):
                            last = t.matmul(ps[ti][:], x2bf[:, k, tau * 128:(tau + 1) * 128],
                                            wat[:, k, 2048 + nv * 512:2048 + (nv + 1) * 512],
                                            start=(k == 0), stop=(k == 7))
                    return last
                prog.step(tensor=vmm)

                def vev(v):
                    last = None
                    for ti in range(4):
                        last = v.tensor_copy(stgb[:, ti, :], ps[ti][:])
                    return last
                prog.step(vector=vev)

                def vst(e, tw=tw, nv=nv):
                    trs = []
                    for hh in range(8):
                        h = 8 * nv + hh
                        trs.append((v_loc[h, :, 4 * tw:4 * tw + 4, :],
                                    stgb[:, :, 64 * hh:64 * hh + 64]))
                    return dma_s(e, trs)
                prog.step(sync=vst)
        stQ.close()
        st4.close()

        cc("AllGather", ALU.bypass, qk_loc_h, qk_all_h)
        cc("AllGather", ALU.bypass, v_loc_h, v_all_h)

        def qkpick(e):
            qk_flat = qk_all.rearrange("a b c d -> a (b c d)")
            qk_p_flat = qk_p.rearrange("a b c d -> a (b c d)")
            return dma_s(e, [(qk_p_flat[ji:ji + 1, :],
                              qk_flat[bass.ts(dyn["qv"][ji], 1), :])
                             for ji in range(4)])

        def vpick(s):
            v_flat = v_all.rearrange("(a b) p k c -> a (b p k c)", b=4)
            v_p_flat = v_p.rearrange("(a b) p k c -> a (b p k c)", b=4)
            return dma_c(s, [(v_p_flat[ji:ji + 1, :],
                              v_flat[bass.ts(dyn["qvc"][ji], 1), :])
                             for ji in range(4)])
        prog.step(sync=qkpick, scalar=vpick)

        st5 = contextlib.ExitStack()
        t5 = alloc(nc, st5, sbufs=[
            ("p5_q2", (128, S), BF16), ("p5_k2", (128, S), BF16),
            ("p5_vA", (128, 64, 65), BF16), ("p5_vB", (128, 64, 65), BF16),
            ("p5_msk", (128, 4, 512), F32),
            ("p5_onesb", (128, 2), BF16),
            ("p5_ebf", (128, 2, 3, 512), BF16),
            ("p5_etmp", (128, 2, 2, 512), F32),
            ("p5_onesT", (65, 64), F32),
            ("p5_sufA", (65, 16), F32), ("p5_sufB", (65, 16), F32),
            ("p5_pref", (65, 64), F32), ("p5_tcol", (65, 1), F32),
            ("p5_yv", (65, 2, 512), F32),
            ("p5_rrow", (1, 2, 512), F32),
            ("p5_ystg", (64, 2, S), BF16),
        ])
        q2, k2, vA, vB, msk = (t5["p5_q2"], t5["p5_k2"], t5["p5_vA"], t5["p5_vB"],
                               t5["p5_msk"])
        onesb, ebf, etmp, onesT = (t5["p5_onesb"], t5["p5_ebf"], t5["p5_etmp"],
                                   t5["p5_onesT"])
        sufA, sufB, pref, tcol = t5["p5_sufA"], t5["p5_sufB"], t5["p5_pref"], t5["p5_tcol"]
        yv, rrow, ystg = t5["p5_yv"], t5["p5_rrow"], t5["p5_ystg"]
        psS = [[ps[0], ps[1]], [ps[2], ps[3]]]
        pyP = [ps[4][0:65, :], ps[5][0:65, :]]
        pxt = ps[6]

        prog.step(sync=lambda e: dma_s(e, [
            (msk[:], dmask.rearrange("(r p) f -> p r f", p=128)),
        ]), vector=lambda v: (v.memset(onesT[:], 1.0), v.memset(vA[:], 1.0),
                              v.memset(vB[:], 1.0), v.memset(onesb[:], 1.0))[-1])

        for pg in range(2):
            pA, pB = 2 * pg, 2 * pg + 1

            def ldq(e, pA=pA, pB=pB):
                trs = []
                for ji in range(4):
                    c_sl = slice(2048 * ji, 2048 * (ji + 1))
                    trs.append((q2[0:64, c_sl], qk_p[ji, pA, :, :]))
                    trs.append((q2[64:128, c_sl], qk_p[ji, pB, :, :]))
                return dma_s(e, trs)

            def ldk(g, pA=pA, pB=pB):
                trs = []
                for ji in range(4):
                    c_sl = slice(2048 * ji, 2048 * (ji + 1))
                    trs.append((k2[0:64, c_sl], qk_p[ji, 4 + pA, :, :]))
                    trs.append((k2[64:128, c_sl], qk_p[ji, 4 + pB, :, :]))
                return dma_g(g, trs)

            def ldv(s, pA=pA, pB=pB):
                trs = []
                for ji in range(4):
                    trs.append((vA[:, 16 * ji:16 * ji + 16, 0:64],
                                v_p[4 * ji + pA, :, :, :]))
                    trs.append((vB[:, 16 * ji:16 * ji + 16, 0:64],
                                v_p[4 * ji + pB, :, :, :]))
                return dma_c(s, trs)
            prog.step(sync=ldq, gpsimd=ldk, scalar=ldv)

            for pi, vt, suf in ((0, vA, sufA), (1, vB, sufB)):
                def tsum(t, vt=vt):
                    last = None
                    for kt in range(64):
                        last = t.matmul(pxt[0:65, kt:kt + 1], vt[:, kt, :],
                                        onesb[:, 0:1], start=True, stop=True)
                    return last
                prog.step(tensor=tsum)
                prog.step(vector=lambda v: v.tensor_tensor_scan(
                    pref[:], onesT[:], pxt[0:65, 0:64], 0.0, ALU.mult, ALU.add))
                prog.step(vector=lambda v: v.tensor_scalar(
                    tcol[:], pref[:, 63:64], EM10, None, ALU.mult))
                prog.step(vector=lambda v, suf=suf: v.tensor_scalar(
                    suf[:], pref[:, 3:64:4], -EM10, tcol[:, 0:1], ALU.mult, ALU.add))

            for qt in range(16):
                q_sl = slice(qt * 512, (qt + 1) * 512)
                nkt = 4 * qt + 4
                for s in range(nkt + 3):

                    def work_t(t, s=s, qt=qt, nkt=nkt, q_sl=q_sl):
                        last = None
                        if s < nkt:
                            sl = s % 2
                            k_sl = slice(s * 128, (s + 1) * 128)
                            t.matmul(psS[0][sl][:], k2[0:64, k_sl], q2[0:64, q_sl],
                                     start=True, stop=True, tile_position=(0, 0))
                            last = t.matmul(psS[1][sl][:], k2[64:128, k_sl],
                                            q2[64:128, q_sl], start=True, stop=True,
                                            tile_position=(64, 0))
                        if s >= 3:
                            kt = s - 3
                            sl3 = kt % 3
                            for pi, vt in ((0, vA), (1, vB)):
                                last = t.matmul(pyP[pi], vt[:, kt, :],
                                                ebf[:, pi, sl3, :],
                                                start=(kt == 0), stop=(kt == nkt - 1))
                        return last

                    def work_v(v, s=s, qt=qt, nkt=nkt):
                        kt = s - 1
                        if kt < 0 or kt >= nkt or kt < 4 * qt:
                            return None
                        sl = kt % 2
                        rel = kt - 4 * qt
                        last = None
                        for pi in range(2):
                            v.tensor_scalar(etmp[:, pi, sl, :], psS[pi][sl][:],
                                            0.125, 10.0, ALU.mult, ALU.add)
                            last = v.tensor_tensor(etmp[:, pi, sl, :],
                                                   etmp[:, pi, sl, :],
                                                   msk[:, rel, :], ALU.mult)
                        return last

                    def work_s(sc, s=s, qt=qt, nkt=nkt):
                        last = None
                        ktn = s - 1
                        if 0 <= ktn < nkt and ktn < 4 * qt:
                            for pi in range(2):
                                last = sc.activation(ebf[:, pi, ktn % 3, :],
                                                     psS[pi][ktn % 2][:], AF.Exp,
                                                     scale=0.125)
                        ktd = s - 2
                        if 0 <= ktd < nkt and ktd >= 4 * qt:
                            for pi in range(2):
                                last = sc.activation(ebf[:, pi, ktd % 3, :],
                                                     etmp[:, pi, ktd % 2, :], AF.Exp,
                                                     bias=consts[:, 1:2])
                        return last
                    prog.step(tensor=work_t, vector=work_v, scalar=work_s)

                def ev1(v, qt=qt):
                    v.tensor_scalar(yv[:, 0, :], pyP[0], sufA[:, qt:qt + 1],
                                    None, ALU.add)
                    v.tensor_scalar(yv[:, 1, :], pyP[1], sufB[:, qt:qt + 1],
                                    None, ALU.add)
                    v.reciprocal(rrow[0:1, 0, :], yv[64:65, 0, :])
                    return v.reciprocal(rrow[0:1, 1, :], yv[64:65, 1, :])
                prog.step(vector=ev1)

                def bc5(t):
                    t.matmul(pxt[0:64, :], onesT[0:1, 0:64], rrow[0:1, 0, :],
                             start=True, stop=True)
                    return t.matmul(ps[7][0:64, :], onesT[0:1, 0:64], rrow[0:1, 1, :],
                                    start=True, stop=True)
                prog.step(tensor=bc5)

                def mu5(v, q_sl=q_sl):
                    v.tensor_tensor(ystg[:, 0, q_sl], yv[0:64, 0, :], pxt[0:64, :],
                                    ALU.mult)
                    return v.tensor_tensor(ystg[:, 1, q_sl], yv[0:64, 1, :],
                                           ps[7][0:64, :], ALU.mult)
                prog.step(vector=mu5)

            prog.step(sync=lambda e, pA=pA, pB=pB: dma_s(e, [
                (yat_loc[:, 64 * pA:64 * pA + 64, :].rearrange("u p t -> p u t"),
                 ystg[:, 0, :].rearrange("p (u t) -> p u t", u=4)),
                (yat_loc[:, 64 * pB:64 * pB + 64, :].rearrange("u p t -> p u t"),
                 ystg[:, 1, :].rearrange("p (u t) -> p u t", u=4)),
            ]))
        st5.close()

        cc("AllGather", ALU.bypass, yat_loc_h, yat_all_h)

        def yatpick_s(e):
            return dma_s(e, [(yat_cols[256 * ji:256 * (ji + 1), :],
                              yat_all[bass.ts(dyn["qv"][ji], 1), :, :])
                             for ji in range(2)])

        def yatpick_c(s):
            return dma_c(s, [(yat_cols[256 * ji:256 * (ji + 1), :],
                              yat_all[bass.ts(dyn["qvc"][ji], 1), :, :])
                             for ji in range(2, 4)])
        prog.step(sync=yatpick_s, scalar=yatpick_c)

        st6 = contextlib.ExitStack()
        t6 = alloc(nc, st6, sbufs=[
            ("p6_sx", (128, 8, TOK), F32),
            ("p6_x3b", (128, 8, TOK), BF16),
            ("p6_bx3", (128, 8, TOK), F32),
            ("p6_tmp", (128, 512), F32),
            ("p6_wcast", (128, 8, 512), BF16),
            ("p6_xs2", (128, 4, 512), F32),
            ("p6_sq", (128, 1, 512), F32),
            ("p6_mv", (1, 4 * 512), F32),
            ("p6_bcols", (128, 9), F32),
            ("p6_lcols", (128, 33), F32),
            ("p6_lrow", (1, 8), F32),
            ("p6_h1", (128, 2, 512), BF16),
            ("p6_wd1c", (128, 8, 256), BF16),
            ("p6_wd2c", (128, 2, 1024), BF16),
            ("p6_bd1s", (128, 16), F32),
            ("p6_bd2s", (128, 64), F32),
            ("p6_yb", (128, 8, 512), BF16),
        ])
        sx, x3b, bx3 = t6["p6_sx"], t6["p6_x3b"], t6["p6_bx3"]
        tmp6 = t6["p6_tmp"]
        wcast, xs2, sq6, mv6 = t6["p6_wcast"], t6["p6_xs2"], t6["p6_sq"], t6["p6_mv"]
        bcols, lcols, lrow = t6["p6_bcols"], t6["p6_lcols"], t6["p6_lrow"]
        h1, wd1c, wd2c, bd1s, bd2s, yb6 = (t6["p6_h1"], t6["p6_wd1c"], t6["p6_wd2c"],
                                           t6["p6_bd1s"], t6["p6_bd2s"], t6["p6_yb"])

        prog.step(sync=lambda e: dma_s(e, [
            (bcols[:, 0:8], bbase), (bd1s[:], bd1c), (bd2s[:], bd2c),
        ]))

        for n in range(4):
            n_sl = slice(n * 512, (n + 1) * 512)

            prog.step(sync=lambda e, n_sl=n_sl: dma_s(e, [
                (yb6[:], yat_cols[:, n_sl].rearrange("(k p) t -> p k t", p=128)),
            ]))
            for mw in range(2):
                prog.step(sync=lambda e, mw=mw, n_sl=n_sl: dma_s(e, [
                    (xs2[:], x2_scr[:, n_sl].rearrange("(j p) t -> p j t", p=128)
                     [:, 4 * mw:4 * mw + 4, :]),
                ]), gpsimd=lambda g, mw=mw: dma_g(g, [
                    (wcast[:], wprojT[:, mw * 512:(mw + 1) * 512]
                     .rearrange("(k p) m -> p k m", p=128)),
                ]))

                def mms6(t):
                    last = None
                    for mi in range(4):
                        for k in range(8):
                            last = t.matmul(ps[mi][:], wcast[:, k, mi * 128:(mi + 1) * 128],
                                            yb6[:, k, :], start=(k == 0), stop=(k == 7))
                    return last

                def evac6(v, mw=mw, n_sl=n_sl):
                    last = None
                    for mi in range(4):
                        m = mw * 4 + mi
                        last = v.scalar_tensor_tensor(sx[:, m, n_sl], xs2[:, mi, :], 1.0,
                                                      ps[mi][:], ALU.mult, ALU.add)
                    return last
                prog.step(tensor=mms6)
                prog.step(vector=evac6)

        def ln3_out(j, n, n_sl, v, x_ap, ps_a, ps_b):
            v.tensor_tensor(tmp6[:], x_ap, ps_a[:], ALU.mult)
            v.tensor_tensor(tmp6[:], tmp6[:], ps_b[:], ALU.add)
            v.tensor_copy(x3b[:, j, n_sl], tmp6[:])
            return v.tensor_copy(sx[:, j, n_sl], tmp6[:])
        emit_ln_fused(prog, lambda j, n_sl: sx[:, j, n_sl], ln3_out, sq6,
                      ps[0], ps[1], ps[2], ps[3], ones, mv6, consts)
        prog.step(sync=lambda e: dma_s(e, [
            (x3b_scr.rearrange("(j p) t -> p j t", p=128), x3b[:]),
        ]))

        for n in range(4):
            n_sl = slice(n * 512, (n + 1) * 512)
            for mw in range(2):
                prog.step(gpsimd=lambda g, mw=mw: dma_g(g, [
                    (wcast[:], Wb[:, mw * 512:(mw + 1) * 512]
                     .rearrange("(k p) m -> p k m", p=128)),
                ]))

                def mmb(t, mw=mw, n_sl=n_sl):
                    last = None
                    for mi in range(4):
                        for k in range(8):
                            last = t.matmul(ps[mi][:], wcast[:, k, mi * 128:(mi + 1) * 128],
                                            x3b[:, k, n_sl], start=(k == 0), stop=(k == 7))
                    return last

                def evb(v, mw=mw, n_sl=n_sl):
                    last = None
                    for mi in range(4):
                        m = mw * 4 + mi
                        v.tensor_scalar(tmp6[:], ps[mi][:], bcols[:, m:m + 1], None, ALU.add)
                        last = v.tensor_tensor(bx3[:, m, n_sl], tmp6[:], sx[:, m, n_sl],
                                               ALU.add)
                    return last
                prog.step(tensor=mmb)
                prog.step(vector=evb)
        prog.step(sync=lambda e: dma_s(e, [
            (bx3_scr.rearrange("(j p) t -> p j t", p=128), bx3[:]),
        ]))

        for e_ in range(E):
            prog.step(gpsimd=lambda g, e_=e_: dma_g(g, [
                (wd1c[:], Wd1[e_].rearrange("(k p) m -> p k m", p=128)),
                (wd2c[:], Wd2[e_].rearrange("(k p) m -> p k m", p=128)),
            ]))
            for n in range(4):
                n_sl = slice(n * 512, (n + 1) * 512)

                def mh(t, n_sl=n_sl):
                    last = None
                    for mi in range(2):
                        for k in range(8):
                            last = t.matmul(ps[6 - mi][:], wd1c[:, k, mi * 128:(mi + 1) * 128],
                                            x3b[:, k, n_sl], start=(k == 0), stop=(k == 7))
                    return last
                prog.step(tensor=mh)

                def eh(s_, e_=e_):
                    s_.activation(h1[:, 0, :], ps[6][:], AF.Relu,
                                  bias=bd1s[:, 2 * e_:2 * e_ + 1])
                    return s_.activation(h1[:, 1, :], ps[5][:], AF.Relu,
                                         bias=bd1s[:, 2 * e_ + 1:2 * e_ + 2])
                prog.step(scalar=eh)
                for mw in range(2):
                    def mr(t, mw=mw):
                        last = None
                        for mi in range(4):
                            for k in range(2):
                                last = t.matmul(
                                    ps[mi][:],
                                    wd2c[:, k, mw * 512 + mi * 128:mw * 512 + (mi + 1) * 128],
                                    h1[:, k, :], start=(k == 0), stop=(k == 1))
                        return last
                    prog.step(tensor=mr)

                    def dv2(v, mw=mw, n_sl=n_sl, e_=e_):
                        last = None
                        for mi in range(4):
                            m = mw * 4 + mi
                            last = v.scalar_tensor_tensor(
                                xs2[:, mi, :], ps[mi][:], bd2s[:, e_ * 8 + m:e_ * 8 + m + 1],
                                sx[:, m, n_sl], ALU.add, ALU.subtract)
                        return last
                    prog.step(vector=dv2)

                    def sqa(s_, n=n, mw=mw):
                        last = None
                        for mi in range(4):
                            ci = (n * 2 + mw) * 4 + mi
                            last = s_.activation(tmp6[:], xs2[:, mi, :], AF.Square,
                                                 accum_out=lcols[:, ci:ci + 1])
                        return last
                    prog.step(scalar=sqa)
            prog.step(vector=lambda v: v.tensor_reduce(
                lcols[:, 32:33], lcols[:, 0:32], mybir.AxisListType.X, ALU.add))
            prog.step(tensor=lambda t: t.matmul(ps[4][0:1, 0:1], ones[:, 0:1],
                                                lcols[:, 32:33], start=True, stop=True))
            prog.step(vector=lambda v, e_=e_: v.tensor_copy(lrow[0:1, e_:e_ + 1],
                                                            ps[4][0:1, 0:1]))
        prog.step(sync=lambda e: dma_s(e, [(lrow_loc, lrow[:])]))
        st6.close()

        cc("AllGather", ALU.bypass, lrow_loc_h, lparts_all_h)

        st7 = contextlib.ExitStack()
        t7 = alloc(nc, st7, sbufs=[
            ("p7_bx3s", (128, 8, TOK), F32),
            ("p7_x3bs", (128, 8, TOK), BF16),
            ("p7_a_all", (64, 8, TOK), BF16),
            ("p7_wauc", (64, 8, D), BF16),
            ("p7_wadc", (128, 8, 64), BF16),
            ("p7_lp", (8, 8), F32),
            ("p7_selb", (8, 2), F32),
            ("p7_baups", (8, D), F32),
            ("p7_badcs", (64, 8), F32),
            ("p7_lrow", (1, 8), F32),
            ("p7_mcol", (1, 2), F32),
            ("p7_ind", (1, 8), F32),
            ("p7_indc8", (8, 1), F32),
            ("p7_ind64", (64, 8), F32),
            ("p7_bcol", (128, 8), F32),
            ("p7_tmp", (128, 512), F32),
            ("p7_sq", (128, 2, 512), F32),
            ("p7_mv", (1, 4 * 512), F32),
            ("p7_ostg", (128, 8, 512), F32),
        ])
        bx3s, x3bs, a_all, wauc, wadc = (t7["p7_bx3s"], t7["p7_x3bs"], t7["p7_a_all"],
                                         t7["p7_wauc"], t7["p7_wadc"])
        lp, sel7, baups, badcs = t7["p7_lp"], t7["p7_selb"], t7["p7_baups"], t7["p7_badcs"]
        lrow7, mcol7, ind, indc8, ind64, bcol = (t7["p7_lrow"], t7["p7_mcol"], t7["p7_ind"],
                                                 t7["p7_indc8"], t7["p7_ind64"],
                                                 t7["p7_bcol"])
        tmp7, sq7, mv7, ostg = t7["p7_tmp"], t7["p7_sq"], t7["p7_mv"], t7["p7_ostg"]

        prog.step(sync=lambda e: dma_s(e, [
            (lp[:], lparts_all), (sel7[:], selb), (baups[:], baup8), (badcs[:], badc),
            (bx3s[:, 0:4, :], bx3_scr[0:512, :].rearrange("(j p) t -> p j t", p=128)),
        ]), scalar=lambda s: dma_c(s, [
            (bx3s[:, 4:8, :], bx3_scr[512:1024, :].rearrange("(j p) t -> p j t", p=128)),
            (x3bs[:], x3b_scr.rearrange("(j p) t -> p j t", p=128)),
        ]), gpsimd=lambda g: dma_g(g, [
            (wauc[:], Wau.rearrange("e k m -> k e m")),
        ]))

        prog.step(tensor=lambda t: t.matmul(ps[0][0:2, 0:8], sel7[:], lp[:],
                                            start=True, stop=True))
        prog.step(vector=lambda v: v.tensor_copy(lp[0:2, :], ps[0][0:2, 0:8]))
        prog.step(sync=lambda e: dma_s(e, [(l2scr[:], lp[0:2, :])]))

        def ownrow(e):
            return dma_s(e, [(lrow7[:], l2scr[bass.ts(dyn["r4"], 1), :])])
        prog.step(sync=ownrow)

        prog.step(vector=lambda v: v.tensor_reduce(mcol7[0:1, 0:1], lrow7[:],
                                                   mybir.AxisListType.X, ALU.min))
        prog.step(vector=lambda v: v.tensor_scalar(ind[:], lrow7[:], mcol7[0:1, 0:1],
                                                   None, ALU.is_le))
        prog.step(tensor=lambda t: (
            t.matmul(ps[0][0:8, 0:1], ind[:], ones[0:1, 0:1], start=True, stop=True),
            t.matmul(ps[1][0:64, 0:8], ones[0:1, 0:64], ind[:], start=True, stop=True))[-1])
        prog.step(vector=lambda v: (v.tensor_copy(indc8[:], ps[0][0:8, 0:1]),
                                    v.tensor_copy(ind64[:], ps[1][0:64, 0:8]))[-1])
        prog.step(tensor=lambda t: [t.matmul(ps[2][0:128, m:m + 1],
                                             baups[:, m * 128:(m + 1) * 128],
                                             indc8[:], start=True, stop=True)
                                    for m in range(8)][-1])
        prog.step(vector=lambda v: v.tensor_copy(bcol[:], ps[2][:, 0:8]))

        for e_ in range(E):
            prog.step(gpsimd=lambda g, e_=e_: dma_g(g, [
                (wadc[:], Wad[e_].rearrange("(k p) m -> p k m", p=128)),
            ]))
            for n in range(4):
                def ma(t, n=n):
                    last = None
                    for k in range(8):
                        last = t.matmul(ps[0][0:64, :], wadc[:, k, :],
                                        x3bs[:, k, n * 512:(n + 1) * 512],
                                        start=(k == 0), stop=(k == 7))
                    return last
                prog.step(tensor=ma)
                prog.step(scalar=lambda s, e_=e_, n=n: s.activation(
                    a_all[:, e_, n * 512:(n + 1) * 512], ps[0][0:64, :], AF.Relu,
                    bias=badcs[:, e_:e_ + 1]))
            prog.step(vector=lambda v, e_=e_: v.tensor_scalar(
                a_all[:, e_, :], a_all[:, e_, :], ind64[:, e_:e_ + 1], None, ALU.mult))

        for n in range(4):
            n_sl = slice(n * 512, (n + 1) * 512)
            for mw in range(2):
                def mu_(t, mw=mw, n_sl=n_sl):
                    last = None
                    for mi in range(4):
                        m = mw * 4 + mi
                        for e_ in range(E):
                            last = t.matmul(ps[mi][:], wauc[:, e_, m * 128:(m + 1) * 128],
                                            a_all[:, e_, n_sl],
                                            start=(e_ == 0), stop=(e_ == 7))
                    return last
                prog.step(tensor=mu_)

                def evu(v, mw=mw, n_sl=n_sl):
                    last = None
                    for mi in range(4):
                        m = mw * 4 + mi
                        v.tensor_scalar(tmp7[:], ps[mi][:], bcol[:, m:m + 1], None, ALU.add)
                        last = v.tensor_tensor(bx3s[:, m, n_sl], tmp7[:], bx3s[:, m, n_sl],
                                               ALU.add)
                    return last
                prog.step(vector=evu)

        def ln4_out(j, n, n_sl, v, x_ap, ps_a, ps_b):
            v.tensor_tensor(ostg[:, j, :], x_ap, ps_a[:], ALU.mult)
            return v.tensor_tensor(ostg[:, j, :], ostg[:, j, :], ps_b[:], ALU.add)

        def ln4_store(prog_, n, n_sl):
            prog_.step(sync=lambda e, n_sl=n_sl: dma_s(e, [
                (out_d[:, n_sl].rearrange("(j p) t -> p j t", p=128), ostg[:])]))
        emit_ln_fused(prog, lambda j, n_sl: bx3s[:, j, n_sl], ln4_out, sq7,
                      ps[0], ps[1], ps[2], ps[3], ones, mv7, consts,
                      post_chunk=ln4_store)
        st7.close()
    st0.close()
    return nc



def kernel(**I):
    P = host_prep(I)
    ident = np.eye(128, dtype=np.float32)
    cores = list(range(NC))

    nc = bass.Bass("TRN2", target_bir_lowering=False, debug=False)
    build_full(nc)
    in_maps = []
    for c in cores:
        in_maps.append({
            "x_fm": P["x_fm"][c], "lng": P["lng"], "lnb": P["lnb"],
            "inwT": P["inwT"][c], "convw": P["convw"][c], "convb": P["convb"][c],
            "xprojT": P["xprojT"][c], "A_log": P["A_log"], "dtw": P["dtw"][c],
            "dtb": P["dtb"][c], "Dc": P["Dc"][c], "sel": P["sel"],
            "outwT": P["outwT"], "gamma": P["gamma"], "beta": P["beta"],
            "gate_w": P["gate_w"], "w1": P["w1"], "b1c": P["b1c"],
            "w2": P["w2"], "b2row": P["b2row"], "wattnT": P["wattnT"],
            "ident": ident, "dmask": P["dmask"], "wprojT": P["wprojT"],
            "W_base": P["W_base"], "bbase": P["bbase"], "Wd1": P["Wd1"],
            "bd1c": P["bd1c"], "Wd2": P["Wd2"], "bd2c": P["bd2c"],
            "selb": P["selb"], "badc": P["badc"], "Wad": P["Wad"],
            "Wau": P["Wau"], "baup8": P["baup8"]})
    r = run_bass_kernel_spmd(nc, in_maps, core_ids=cores).results

    out = np.empty((B * S, D), np.float32)
    for c in cores:
        out[c * TOK:(c + 1) * TOK] = np.asarray(r[c]["out_slab"], np.float32).T
    return out.reshape(B, S, D)


# version 06 (final)
# speedup vs baseline: 1.2468x; 1.0293x over previous
import numpy as np
import ml_dtypes
import contextlib
import concourse.bass as bass
import concourse.mybir as mybir
from concourse.bass_utils import run_bass_kernel_spmd

F32 = mybir.dt.float32
BF16 = mybir.dt.bfloat16
AF = mybir.ActivationFunctionType
ALU = mybir.AluOpType
BF = ml_dtypes.bfloat16

NC = 8
B = 2
S = 8192
D = 1024
DI = 2048
DIC = DI // NC
NST = 16
HMOE = 4096
E = 8
TOK = (B * S) // NC
NH, HS = 16, 64
EPS = 1e-5
EM10 = float(np.exp(np.float64(-10.0)))

ENGS = ("sync", "tensor", "vector", "scalar", "gpsimd")


class Prog:

    def __init__(self, nc, block, tick, dsems):
        self.nc = nc
        self.block = block
        self.tick = tick
        self.k = 0
        self.dsems = dsems
        self.dctr = {e: 0 for e in dsems}
        self.ccn = 0

    def step(self, **work):
        self.k += 1
        prev = 5 * (self.k - 1)
        for name in ENGS:
            fn = work.get(name)

            def emit(eng, fn=fn):
                eng.wait_ge(self.tick, prev)
                last = fn(eng) if fn is not None else None
                if last is None:
                    last = eng.nop()
                last.then_inc(self.tick, 1)
            getattr(self.block, name)(emit)

    def dmas(self, eng_name):
        sem = self.dsems[eng_name]

        def go(eng, transfers, **kw):
            for out, in_ in transfers:
                eng.dma_start(out=out, in_=in_, **kw).then_inc(sem, 16)
                self.dctr[eng_name] += 16
            eng.wait_ge(sem, self.dctr[eng_name])
            return eng.nop()
        return go


def alloc(nc, st, sbufs=(), psums=(), sems=()):
    d = {}
    for name, shape, dt in sbufs:
        d[name] = st.enter_context(nc.sbuf_tensor(name, list(shape), dt))
    for name, shape, dt in psums:
        d[name] = st.enter_context(nc.psum_tensor(name, list(shape), dt))
    for name in sems:
        d[name] = st.enter_context(nc.semaphore(name))
    return d



def _col8(v):
    return np.ascontiguousarray(v.reshape(8, 128).T)


def host_prep(I):
    P = {}
    x = I["L0_attn_x"]
    xf = np.asarray(x).reshape(B * S, D)
    P["x_fm"] = [np.ascontiguousarray(xf[c * TOK:(c + 1) * TOK].T) for c in range(NC)]
    P["lng"] = _col8(I["L0_attn_ln_g"])
    P["lnb"] = _col8(I["L0_attn_ln_b"])
    in_w = I["L0_attn_in_w"]
    P["inwT"] = []
    for c in range(NC):
        rows = np.concatenate([in_w[DIC * c:DIC * (c + 1)],
                               in_w[DI + DIC * c:DI + DIC * (c + 1)]], axis=0)
        P["inwT"].append(np.ascontiguousarray(rows.T))
    cw_ = I["L0_attn_conv_w"][:, 0, :]
    P["convw"] = [np.ascontiguousarray(
        cw_[DIC * c:DIC * (c + 1)].reshape(2, 128, 4).transpose(1, 0, 2).reshape(128, 8))
        for c in range(NC)]
    P["convb"] = [np.ascontiguousarray(
        I["L0_attn_conv_b"][DIC * c:DIC * (c + 1)].reshape(2, 128).T) for c in range(NC)]
    xp_w = I["L0_attn_xproj_w"]
    P["xprojT"] = [np.ascontiguousarray(
        xp_w[:, DIC * c:DIC * (c + 1)].T.reshape(2, 128, 33).transpose(1, 0, 2).reshape(128, 66))
        for c in range(NC)]
    P["dtw"] = [np.ascontiguousarray(I["L0_attn_dt_w"][DIC * c:DIC * (c + 1), 0]
                                     .reshape(2, 128).T) for c in range(NC)]
    P["dtb"] = [np.ascontiguousarray(I["L0_attn_dt_b"][DIC * c:DIC * (c + 1)]
                                     .reshape(2, 128).T) for c in range(NC)]
    P["Dc"] = [np.ascontiguousarray(I["L0_attn_D"][DIC * c:DIC * (c + 1)]
                                    .reshape(2, 128).T) for c in range(NC)]
    P["A_log"] = np.ascontiguousarray(I["L0_attn_A_log"].reshape(1, 16))
    P["outwT"] = np.ascontiguousarray(I["L0_attn_out_w"].T)
    P["gate_w"] = np.ascontiguousarray(I["L0_ffn_gate_w"])
    P["w1"] = I["L0_ffn_w1"]
    P["b1c"] = np.ascontiguousarray(
        I["L0_ffn_b1"].reshape(8, 32, 128).transpose(2, 0, 1).reshape(128, 8 * 32))
    P["w2"] = I["L0_ffn_w2"]
    P["b2row"] = np.ascontiguousarray(I["L0_ffn_b2"].reshape(1, 8 * 1024))
    P["gamma"] = _col8(I["L0_ffn_gamma"])
    P["beta"] = _col8(I["L0_ffn_beta"])
    P["wattnT"] = np.ascontiguousarray(I["L1_attn_w_attn"].T)
    P["wprojT"] = np.ascontiguousarray(I["L1_attn_w_proj"].T)
    P["W_base"] = np.ascontiguousarray(I["L1_ffn_W_base"])
    P["bbase"] = _col8(I["L1_ffn_b_base"])
    P["Wd1"] = I["L1_ffn_Wd1"]
    P["bd1c"] = np.ascontiguousarray(
        I["L1_ffn_bd1"].reshape(8, 2, 128).transpose(2, 0, 1).reshape(128, 16))
    P["Wd2"] = I["L1_ffn_Wd2"]
    P["bd2c"] = np.ascontiguousarray(
        I["L1_ffn_bd2"].reshape(8, 8, 128).transpose(2, 0, 1).reshape(128, 64))
    P["Wad"] = I["L1_ffn_Wa_down"]
    P["badc"] = np.ascontiguousarray(I["L1_ffn_ba_down"].T)
    P["Wau"] = I["L1_ffn_Wa_up"]
    P["baup8"] = np.ascontiguousarray(I["L1_ffn_ba_up"])
    selm2 = np.zeros((8, 2), np.float32)
    selm2[0:4, 0] = 1.0
    selm2[4:8, 1] = 1.0
    P["selb"] = selm2
    selm = np.zeros((16, 16 * 128), np.float32)
    for n in range(16):
        selm[n, n * 128:(n + 1) * 128] = 1.0
    P["sel"] = selm
    m = np.zeros((4, 128, 512), np.float32)
    for r in range(4):
        for p in range(128):
            m[r, p, :] = (np.arange(512) >= p + 128 * r)
    P["dmask"] = np.ascontiguousarray(m.reshape(4 * 128, 512))
    return P



def emit_ln_stats(prog, x_tile, sq_sb, ps_mu, ps_sq, ones_col,
                  mvrow, arow, brow, eps_col, ncols=512, n_chunks=4, nd=8,
                  nfeat=1024):
    inv = 1.0 / nfeat
    G = sq_sb.shape[1]
    for n in range(n_chunks):
        n_sl = slice(n * ncols, (n + 1) * ncols)
        for g0 in range(0, nd, G):
            def sqs(s, g0=g0, n_sl=n_sl):
                last = None
                for i in range(G):
                    last = s.activation(sq_sb[:, i, :], x_tile(g0 + i, n_sl), AF.Square)
                return last
            prog.step(scalar=sqs)

            def mms(t, g0=g0, n_sl=n_sl):
                last = None
                for i in range(G):
                    j = g0 + i
                    t.matmul(ps_mu[0:1, :], ones_col[:, 0:1], x_tile(j, n_sl),
                             start=(j == 0), stop=(j == nd - 1))
                    last = t.matmul(ps_sq[0:1, :], ones_col[:, 0:1], sq_sb[:, i, :],
                                    start=(j == 0), stop=(j == nd - 1))
                return last
            prog.step(tensor=mms)

        mu = mvrow[0:1, 0:ncols]
        va = mvrow[0:1, ncols:2 * ncols]

        def stats(v, n_sl=n_sl, mu=mu, va=va):
            v.tensor_scalar(mu, ps_mu[0:1, :], inv, None, ALU.mult)
            v.tensor_scalar(va, ps_sq[0:1, :], inv, None, ALU.mult)
            v.tensor_tensor(arow[0:1, n_sl], mu, mu, ALU.mult)
            return v.tensor_tensor(va, va, arow[0:1, n_sl], ALU.subtract)
        prog.step(vector=stats)

        def rstd(s, n_sl=n_sl, va=va):
            s.activation(arow[0:1, n_sl], va, AF.Ln, bias=eps_col[0:1, 0:1])
            return s.activation(arow[0:1, n_sl], arow[0:1, n_sl], AF.Exp, scale=-0.5)
        prog.step(scalar=rstd)

        def bro(v, n_sl=n_sl, mu=mu):
            return v.scalar_tensor_tensor(brow[0:1, n_sl], mu, -1.0,
                                          arow[0:1, n_sl], ALU.mult, ALU.mult)
        prog.step(vector=bro)


def emit_ln_apply(prog, x_tile, out_fns, ps_a, ps_b, ones_row, arow, brow, tmp_sb,
                  ncols=512, n_chunks=4, nd=8):
    for n in range(n_chunks):
        n_sl = slice(n * ncols, (n + 1) * ncols)

        def bcast(t, n_sl=n_sl):
            t.matmul(ps_a[:, :], ones_row[0:1, :], arow[0:1, n_sl], start=True, stop=True)
            return t.matmul(ps_b[:, :], ones_row[0:1, :], brow[0:1, n_sl],
                            start=True, stop=True)
        prog.step(tensor=bcast)

        def apply(v, n=n, n_sl=n_sl):
            last = None
            for j in range(nd):
                v.tensor_tensor(tmp_sb[:, 0:ncols], x_tile(j, n_sl), ps_a[:, :], ALU.mult)
                v.tensor_tensor(tmp_sb[:, 0:ncols], tmp_sb[:, 0:ncols], ps_b[:, :], ALU.add)
                last = out_fns(j, n, n_sl, v, tmp_sb[:, 0:ncols])
            return last
        prog.step(vector=apply)


def emit_ln_fused(prog, x_tile, out_fn, sq_sb, ps_mu, ps_sq, ps_a, ps_b,
                  ones_col, mv4, eps_col, ncols=512, n_chunks=4, nd=8, nfeat=1024,
                  post_chunk=None):
    inv = 1.0 / nfeat
    G = sq_sb.shape[1]
    for n in range(n_chunks):
        n_sl = slice(n * ncols, (n + 1) * ncols)
        for g0 in range(0, nd, G):
            def sqs(s, g0=g0, n_sl=n_sl):
                last = None
                for i in range(G):
                    last = s.activation(sq_sb[:, i, :], x_tile(g0 + i, n_sl), AF.Square)
                return last
            prog.step(scalar=sqs)

            def mms(t, g0=g0, n_sl=n_sl):
                last = None
                for i in range(G):
                    j = g0 + i
                    t.matmul(ps_mu[0:1, :], ones_col[:, 0:1], x_tile(j, n_sl),
                             start=(j == 0), stop=(j == nd - 1))
                    last = t.matmul(ps_sq[0:1, :], ones_col[:, 0:1], sq_sb[:, i, :],
                                    start=(j == 0), stop=(j == nd - 1))
                return last
            prog.step(tensor=mms)

        mu = mv4[0:1, 0:ncols]
        va = mv4[0:1, ncols:2 * ncols]
        ar = mv4[0:1, 2 * ncols:3 * ncols]
        br = mv4[0:1, 3 * ncols:4 * ncols]

        def stats(v, mu=mu, va=va, ar=ar):
            v.tensor_scalar(mu, ps_mu[0:1, :], inv, None, ALU.mult)
            v.tensor_scalar(va, ps_sq[0:1, :], inv, None, ALU.mult)
            v.tensor_tensor(ar, mu, mu, ALU.mult)
            return v.tensor_tensor(va, va, ar, ALU.subtract)
        prog.step(vector=stats)

        def rstd(s, va=va, ar=ar):
            s.activation(ar, va, AF.Ln, bias=eps_col[0:1, 0:1])
            return s.activation(ar, ar, AF.Exp, scale=-0.5)
        prog.step(scalar=rstd)

        def bro(v, mu=mu, ar=ar, br=br):
            return v.scalar_tensor_tensor(br, mu, -1.0, ar, ALU.mult, ALU.mult)
        prog.step(vector=bro)

        def bcast(t, ar=ar, br=br):
            t.matmul(ps_a[:], ones_col[0:1, 0:128], ar, start=True, stop=True)
            return t.matmul(ps_b[:], ones_col[0:1, 0:128], br, start=True, stop=True)
        prog.step(tensor=bcast)

        def apply(v, n=n, n_sl=n_sl):
            last = None
            for j in range(nd):
                last = out_fn(j, n, n_sl, v, x_tile(j, n_sl), ps_a, ps_b)
            return last
        prog.step(vector=apply)
        if post_chunk is not None:
            post_chunk(prog, n, n_sl)



def build_full(nc):
    x_in = nc.dram_tensor("x_fm", [D, TOK], F32, kind="ExternalInput").ap()
    lng = nc.dram_tensor("lng", [128, 8], F32, kind="ExternalInput").ap()
    lnb = nc.dram_tensor("lnb", [128, 8], F32, kind="ExternalInput").ap()
    inwT = nc.dram_tensor("inwT", [D, 4 * 128], F32, kind="ExternalInput").ap()
    convw = nc.dram_tensor("convw", [128, 8], F32, kind="ExternalInput").ap()
    convb = nc.dram_tensor("convb", [128, 2], F32, kind="ExternalInput").ap()
    xprojT = nc.dram_tensor("xprojT", [128, 66], F32, kind="ExternalInput").ap()
    alog = nc.dram_tensor("A_log", [1, 16], F32, kind="ExternalInput").ap()
    dtw = nc.dram_tensor("dtw", [128, 2], F32, kind="ExternalInput").ap()
    dtb = nc.dram_tensor("dtb", [128, 2], F32, kind="ExternalInput").ap()
    Dc = nc.dram_tensor("Dc", [128, 2], F32, kind="ExternalInput").ap()
    sel = nc.dram_tensor("sel", [16, 16 * 128], F32, kind="ExternalInput").ap()
    outwT = nc.dram_tensor("outwT", [DI, D], F32, kind="ExternalInput").ap()
    gamma = nc.dram_tensor("gamma", [128, 8], F32, kind="ExternalInput").ap()
    beta = nc.dram_tensor("beta", [128, 8], F32, kind="ExternalInput").ap()
    gate_w = nc.dram_tensor("gate_w", [D, 8], F32, kind="ExternalInput").ap()
    w1 = nc.dram_tensor("w1", [E, D, HMOE], F32, kind="ExternalInput").ap()
    b1c = nc.dram_tensor("b1c", [128, E * 32], F32, kind="ExternalInput").ap()
    w2 = nc.dram_tensor("w2", [E, HMOE, D], F32, kind="ExternalInput").ap()
    b2row = nc.dram_tensor("b2row", [1, E * D], F32, kind="ExternalInput").ap()
    wattnT = nc.dram_tensor("wattnT", [D, 3 * D], F32, kind="ExternalInput").ap()
    ident = nc.dram_tensor("ident", [128, 128], F32, kind="ExternalInput").ap()
    dmask = nc.dram_tensor("dmask", [4 * 128, 512], F32, kind="ExternalInput").ap()
    wprojT = nc.dram_tensor("wprojT", [D, D], F32, kind="ExternalInput").ap()
    Wb = nc.dram_tensor("W_base", [D, D], F32, kind="ExternalInput").ap()
    bbase = nc.dram_tensor("bbase", [128, 8], F32, kind="ExternalInput").ap()
    Wd1 = nc.dram_tensor("Wd1", [E, D, 256], F32, kind="ExternalInput").ap()
    bd1c = nc.dram_tensor("bd1c", [128, 16], F32, kind="ExternalInput").ap()
    Wd2 = nc.dram_tensor("Wd2", [E, 256, D], F32, kind="ExternalInput").ap()
    bd2c = nc.dram_tensor("bd2c", [128, 64], F32, kind="ExternalInput").ap()
    selb = nc.dram_tensor("selb", [E, 2], F32, kind="ExternalInput").ap()
    badc = nc.dram_tensor("badc", [64, E], F32, kind="ExternalInput").ap()
    Wad = nc.dram_tensor("Wad", [E, D, 64], F32, kind="ExternalInput").ap()
    Wau = nc.dram_tensor("Wau", [E, 64, D], F32, kind="ExternalInput").ap()
    baup8 = nc.dram_tensor("baup8", [E, D], F32, kind="ExternalInput").ap()
    out_d = nc.dram_tensor("out_slab", [D, TOK], F32, kind="ExternalOutput").ap()

    xn0_loc_h = nc.dram_tensor("xn0_loc", [D, TOK], BF16, kind="Internal")
    xn0_all_h = nc.dram_tensor("xn0_all", [NC, D, TOK], BF16, kind="Internal")
    xs_scr = nc.dram_tensor("xs_scr", [2 * DIC, S], F32, kind="Internal").ap()
    z_scr = nc.dram_tensor("z_scr", [2 * DIC, S], BF16, kind="Internal").ap()
    xp_loc_h = nc.dram_tensor("xp_loc", [66, S], F32, kind="Internal")
    xp_sum_h = nc.dram_tensor("xp_sum", [66, S], F32, kind="Internal")
    y_loc_h = nc.dram_tensor("y_loc", [NC, 2 * 128, TOK], BF16, kind="Internal")
    y_all_h = nc.dram_tensor("y_all", [NC * NC, 2 * 128, TOK], BF16, kind="Internal")
    x2_scr = nc.dram_tensor("x2_scr", [D, TOK], F32, kind="Internal").ap()
    y_cols = nc.dram_tensor("y_cols", [DI, TOK], BF16, kind="Internal").ap()
    qk_loc_h = nc.dram_tensor("qk_loc", [4, 8, 64, TOK], BF16, kind="Internal")
    qk_all_h = nc.dram_tensor("qk_all", [32, 8, 64, TOK], BF16, kind="Internal")
    qk_p = nc.dram_tensor("qk_p", [4, 8, 64, TOK], BF16, kind="Internal").ap()
    v_loc_h = nc.dram_tensor("v_loc", [16, 128, 16, 64], BF16, kind="Internal")
    v_all_h = nc.dram_tensor("v_all", [NC * 16, 128, 16, 64], BF16, kind="Internal")
    v_p = nc.dram_tensor("v_p", [16, 128, 16, 64], BF16, kind="Internal").ap()
    yat_loc_h = nc.dram_tensor("yat_loc", [4, 256, TOK], BF16, kind="Internal")
    yat_all_h = nc.dram_tensor("yat_all", [NC * 4, 256, TOK], BF16, kind="Internal")
    yat_cols = nc.dram_tensor("yat_cols", [D, TOK], BF16, kind="Internal").ap()
    x3b_scr = nc.dram_tensor("x3b_scr", [D, TOK], BF16, kind="Internal").ap()
    bx3_scr = nc.dram_tensor("bx3_scr", [D, TOK], F32, kind="Internal").ap()
    lrow_loc_h = nc.dram_tensor("lrow_loc", [1, E], F32, kind="Internal")
    lparts_all_h = nc.dram_tensor("lparts_all", [NC, E], F32, kind="Internal")
    l2scr = nc.dram_tensor("l2scr", [2, E], F32, kind="Internal").ap()

    xn0_loc = xn0_loc_h.ap()
    xn0_all = xn0_all_h.ap()
    xp_loc = xp_loc_h.ap()
    xp_sum = xp_sum_h.ap()
    y_loc = y_loc_h.ap()
    y_all = y_all_h.ap()
    qk_loc = qk_loc_h.ap()
    qk_all = qk_all_h.ap()
    v_loc = v_loc_h.ap()
    v_all = v_all_h.ap()
    yat_loc = yat_loc_h.ap()
    yat_all = yat_all_h.ap()
    lrow_loc = lrow_loc_h.ap()
    lparts_all = lparts_all_h.ap()

    st0 = contextlib.ExitStack()
    sh = alloc(nc, st0, sbufs=[
        ("ones", (128, 128), F32),
        ("idn", (128, 128), F32),
        ("consts", (128, 4), F32),
    ], psums=[(f"ps{i}", (128, 512), F32) for i in range(8)],
       sems=["tick", "dm_sync", "dm_gp", "dm_sc", "dm_ve", "cc_sem"])
    ones, idn, consts = sh["ones"], sh["idn"], sh["consts"]
    ps = [sh[f"ps{i}"] for i in range(8)]
    tick, dm_sync, dm_gp, cc_sem = sh["tick"], sh["dm_sync"], sh["dm_gp"], sh["cc_sem"]

    with nc.Block() as block:
        prog = Prog(nc, block, tick, {"sync": dm_sync, "gpsimd": dm_gp,
                                      "scalar": sh["dm_sc"], "vector": sh["dm_ve"]})
        dma_s = prog.dmas("sync")
        dma_g = prog.dmas("gpsimd")
        dma_c = prog.dmas("scalar")
        dma_v = prog.dmas("vector")

        def cc(kind, op, in_h, out_h):
            def go(g):
                prog.ccn += 1
                g.collective_compute(
                    kind, op, replica_groups=[list(range(NC))],
                    ins=[in_h.ap().opt()], outs=[out_h.ap().opt()],
                ).then_inc(cc_sem)
                g.wait_ge(cc_sem, prog.ccn)
                return g.nop()
            prog.step(gpsimd=go)

        dyn = {}

        def mkidx_s(e):
            rank = e.partition_id()
            r4 = rank // 4
            dyn["r4"] = r4
            base = rank + 12 * r4
            dyn["qv"] = [base + 4 * ji for ji in range(4)]
            return None

        def mkidx_g(g):
            rank_g = g.partition_id()
            dyn["ym"] = [rank_g + 8 * j for j in range(4)]
            return None

        def mkidx_c(s):
            rank_c = s.partition_id()
            base_c = rank_c + 12 * (rank_c // 4)
            dyn["qvc"] = [base_c + 4 * ji for ji in range(4)]
            dyn["ymc"] = [rank_c + 8 * j for j in range(4, 8)]
            return None

        prog.step(sync=lambda e: (mkidx_s(e), dma_s(e, [(idn[:], ident)]))[-1],
                  gpsimd=mkidx_g,
                  scalar=mkidx_c,
                  vector=lambda v: (v.memset(ones[:], 1.0),
                                    v.memset(consts[:, 0:1], EPS),
                                    v.memset(consts[:, 1:2], -10.0))[-1])

        st1 = contextlib.ExitStack()
        t1 = alloc(nc, st1, sbufs=[
            ("p1_xT", (128, 8, TOK), F32), ("p1_xn", (128, 8, TOK), BF16),
            ("p1_sq", (128, 8, 512), F32), ("p1_rows", (1, 4 * TOK), F32),
            ("p1_gb", (128, 16), F32), ("p1_tmp", (128, 512), F32),
        ])
        xT, xn, sq1, rows1, gb1, tmp1 = (t1["p1_xT"], t1["p1_xn"], t1["p1_sq"],
                                         t1["p1_rows"], t1["p1_gb"], t1["p1_tmp"])
        arow = rows1[:, 2 * TOK:3 * TOK]
        brow = rows1[:, 3 * TOK:4 * TOK]

        prog.step(sync=lambda e: dma_s(e, [
            (xT[:, 0:4, :], x_in[0:512, :].rearrange("(j p) t -> p j t", p=128)),
            (gb1[:, 0:8], lng), (gb1[:, 8:16], lnb),
        ]), gpsimd=lambda g: dma_g(g, [
            (xT[:, 4:8, :], x_in[512:1024, :].rearrange("(j p) t -> p j t", p=128)),
        ]))

        emit_ln_stats(prog, lambda j, n_sl: xT[:, j, n_sl], sq1, ps[0], ps[1],
                      ones, rows1[:, 0:1024], arow, brow, consts)

        def out_fn1(j, n, n_sl, v, normed):
            return v.tensor_scalar(xn[:, j, n_sl], normed, gb1[:, j:j + 1],
                                   gb1[:, 8 + j:9 + j], ALU.mult, ALU.add)
        emit_ln_apply(prog, lambda j, n_sl: xT[:, j, n_sl], out_fn1,
                      ps[2], ps[3], ones, arow, brow, tmp1)

        prog.step(sync=lambda e: dma_s(e, [
            (xn0_loc.rearrange("(j p) t -> p j t", p=128), xn[:]),
        ]))
        st1.close()

        cc("AllGather", ALU.bypass, xn0_loc_h, xn0_all_h)

        TCH = 2048
        st2 = contextlib.ExitStack()
        t2 = alloc(nc, st2, sbufs=[
            ("p2_inw", (128, 8, 4 * 128), BF16), ("p2_xnc", (128, 8, TCH), BF16),
            ("p2_xi", (128, 2, TCH + 3), F32), ("p2_xc", (128, 2, TCH), F32),
            ("p2_xsc", (128, 2, TCH), F32), ("p2_zst", (128, 2, TCH), BF16),
            ("p2_cw", (128, 8), F32), ("p2_cb", (128, 2), F32),
            ("p2_xpj", (128, 2, 33), F32), ("p2_xpst", (33, 2, TCH), F32),
            ("p2_carry", (128, 2, 3), F32),
        ])
        inw, xnc, xi, xc, xsc, zst = (t2["p2_inw"], t2["p2_xnc"], t2["p2_xi"],
                                      t2["p2_xc"], t2["p2_xsc"], t2["p2_zst"])
        cw_sb, cb_sb, xpj, xpst, carry = (t2["p2_cw"], t2["p2_cb"], t2["p2_xpj"],
                                          t2["p2_xpst"], t2["p2_carry"])

        prog.step(gpsimd=lambda g: dma_g(g, [
            (inw[:], inwT.rearrange("(k p) m -> p k m", p=128)),
        ]), sync=lambda e: dma_s(e, [
            (cw_sb[:], convw), (cb_sb[:], convb),
            (xpj[:], xprojT.rearrange("p (d m) -> p d m", d=2)),
        ]))

        for b in range(B):
            for ch in range(S // TCH):
                rk = 4 * b + ch

                def ldxn(e, rk=rk):
                    return dma_s(e, [(xnc[:, 0:4, :], xn0_all[rk, 0:512, :]
                                      .rearrange("(j p) t -> p j t", p=128))])

                def ldxn_g(g, rk=rk):
                    return dma_g(g, [(xnc[:, 4:8, :], xn0_all[rk, 512:1024, :]
                                      .rearrange("(j p) t -> p j t", p=128))])
                prog.step(sync=ldxn, gpsimd=ldxn_g)

                for n in range(TCH // 512):
                    n_sl = slice(n * 512, (n + 1) * 512)

                    def mms(t, n_sl=n_sl):
                        last = None
                        for m in range(4):
                            for k in range(8):
                                last = t.matmul(ps[m][:], inw[:, k, m * 128:(m + 1) * 128],
                                                xnc[:, k, n_sl],
                                                start=(k == 0), stop=(k == 7))
                        return last

                    def evac(v, n=n, n_sl=n_sl, ch=ch):
                        x_sl = slice(3 + n * 512, 3 + (n + 1) * 512)
                        if n == 0:
                            if ch == 0:
                                v.memset(carry[:], 0.0)
                            v.tensor_copy(xi[:, :, 0:3], carry[:])
                        v.tensor_copy(xi[:, 0, x_sl], ps[0][:])
                        v.tensor_copy(xi[:, 1, x_sl], ps[1][:])
                        v.tensor_copy(zst[:, 0, n_sl], ps[2][:])
                        return v.tensor_copy(zst[:, 1, n_sl], ps[3][:])
                    prog.step(tensor=mms)
                    prog.step(vector=evac)

                def conv(v):
                    last = None
                    for dt in range(2):
                        v.tensor_scalar(xc[:, dt, :], xi[:, dt, 0:TCH],
                                        cw_sb[:, dt * 4:dt * 4 + 1], None, ALU.mult)
                        for k in range(1, 4):
                            last = v.scalar_tensor_tensor(
                                xc[:, dt, :], xi[:, dt, k:k + TCH],
                                cw_sb[:, dt * 4 + k:dt * 4 + k + 1],
                                xc[:, dt, :], ALU.mult, ALU.add)
                    return last
                prog.step(vector=conv)

                def silu_save(s):
                    s.activation(xsc[:, 0, :], xc[:, 0, :], AF.Silu, bias=cb_sb[:, 0:1])
                    return s.activation(xsc[:, 1, :], xc[:, 1, :], AF.Silu, bias=cb_sb[:, 1:2])

                def carry_save(v):
                    return v.tensor_copy(carry[:], xi[:, :, TCH:TCH + 3])
                prog.step(scalar=silu_save, vector=carry_save)

                def xpmm(t):
                    last = None
                    for n in range(4):
                        for dt in range(2):
                            last = t.matmul(ps[n][0:33, :], xpj[:, dt, :],
                                            xsc[:, dt, n * 512:(n + 1) * 512],
                                            start=(dt == 0), stop=(dt == 1))
                    return last
                prog.step(tensor=xpmm)

                def xpev(v):
                    last = None
                    for n in range(4):
                        last = v.tensor_copy(xpst[:, 0, n * 512:(n + 1) * 512], ps[n][0:33, :])
                    return last
                prog.step(vector=xpev)

                def store(e, b=b, ch=ch):
                    c_sl = slice(ch * TCH, (ch + 1) * TCH)
                    return dma_s(e, [
                        (xs_scr[b * 256:b * 256 + 128, c_sl], xsc[:, 0, :]),
                        (xs_scr[b * 256 + 128:b * 256 + 256, c_sl], xsc[:, 1, :]),
                        (z_scr[b * 256:b * 256 + 128, c_sl], zst[:, 0, :]),
                        (z_scr[b * 256 + 128:b * 256 + 256, c_sl], zst[:, 1, :]),
                        (xp_loc[b * 33:b * 33 + 33, c_sl], xpst[:, 0, :]),
                    ])
                prog.step(sync=store)
        st2.close()

        cc("AllReduce", ALU.add, xp_loc_h, xp_sum_h)

        st3 = contextlib.ExitStack()
        t3 = alloc(nc, st3, sbufs=[
            ("p3_xp0", (1, TCH), F32),
            ("p3_B_t", (16, TCH), BF16), ("p3_C_t", (16, TCH), BF16),
            ("p3_sel", (16, 16 * 128), BF16),
            ("p3_Arep", (128, 16), F32), ("p3_Arow", (1, 16), F32),
            ("p3_cols", (128, 8), F32),
            ("p3_hstate", (128, 64), F32),
            ("p3_xs", (128, TCH), F32), ("p3_z", (128, TCH), BF16),
            ("p3_delta", (128, TCH), F32), ("p3_u", (128, TCH), F32),
            ("p3_dA", (128, TCH), F32), ("p3_dBx", (128, TCH), F32),
            ("p3_h", (128, TCH), F32), ("p3_yacc", (128, TCH), F32),
            ("p3_sz", (128, TCH), F32), ("p3_ybf", (128, TCH), BF16),
        ])
        (xp0, B_t, C_t, sel_sb, Arep, Arow, cols, hstate, xs_t, z_t,
         delta, u_t, dA, dBx, h_t, yacc, sz, ybf) = (
            t3["p3_xp0"], t3["p3_B_t"], t3["p3_C_t"], t3["p3_sel"], t3["p3_Arep"],
            t3["p3_Arow"], t3["p3_cols"], t3["p3_hstate"], t3["p3_xs"], t3["p3_z"],
            t3["p3_delta"], t3["p3_u"], t3["p3_dA"], t3["p3_dBx"], t3["p3_h"],
            t3["p3_yacc"], t3["p3_sz"], t3["p3_ybf"])
        psB = [ps[0], ps[1], ps[2], ps[3]]
        psC = [ps[4], ps[5], ps[6], ps[7]]

        prog.step(sync=lambda e: dma_s(e, [
            (Arow[:], alog), (cols[:, 0:2], dtw), (cols[:, 2:4], dtb),
            (cols[:, 4:6], Dc),
        ]), gpsimd=lambda g: dma_g(g, [(sel_sb[:], sel)]),
            vector=lambda v: v.memset(hstate[:], 0.0))

        prog.step(scalar=lambda s: s.activation(Arow[:], Arow[:], AF.Exp))
        prog.step(vector=lambda v: v.tensor_scalar(Arow[:], Arow[:], -1.0, None, ALU.mult))
        prog.step(tensor=lambda t: t.matmul(ps[0][:, 0:16], ones[0:1, 0:128], Arow[:],
                                            start=True, stop=True))
        prog.step(vector=lambda v: v.tensor_copy(Arep[:], ps[0][:, 0:16]))

        for b in range(B):
            for g in range(2):
                for tsc in range(S // TCH):
                    t_sl = slice(tsc * TCH, (tsc + 1) * TCH)
                    row0 = b * 256 + g * 128

                    def lds(e, b=b, row0=row0, t_sl=t_sl):
                        return dma_s(e, [
                            (xs_t[:], xs_scr[row0:row0 + 128, t_sl]),
                            (z_t[:], z_scr[row0:row0 + 128, t_sl]),
                            (xp0[:], xp_sum[33 * b:33 * b + 1, t_sl]),
                        ])

                    def ldbc(g, b=b, t_sl=t_sl):
                        return dma_g(g, [
                            (B_t[:], xp_sum[1 + 33 * b:17 + 33 * b, t_sl]),
                            (C_t[:], xp_sum[17 + 33 * b:33 + 33 * b, t_sl]),
                        ])
                    prog.step(sync=lds, gpsimd=ldbc)

                    def x0b(t, t_sl=t_sl):
                        last = None
                        for n in range(4):
                            last = t.matmul(psB[n][:], ones[0:1, 0:128],
                                            xp0[0:1, n * 512:(n + 1) * 512],
                                            start=True, stop=True)
                        return last
                    prog.step(tensor=x0b)

                    def sdelta(s, g=g):
                        last = None
                        for n in range(4):
                            last = s.activation(delta[:, n * 512:(n + 1) * 512], psB[n][:],
                                                AF.Exp, scale=cols[:, g:g + 1],
                                                bias=cols[:, 2 + g:3 + g])
                        return last
                    prog.step(scalar=sdelta)
                    prog.step(scalar=lambda s: s.activation(delta[:], delta[:], AF.Ln, bias=1.0))
                    prog.step(vector=lambda v: v.tensor_tensor(u_t[:], delta[:], xs_t[:], ALU.mult))

                    for n in range(NST):
                        hcol = (b * 2 + g) * 16 + n

                        def brep(t, n=n):
                            last = None
                            for q in range(4):
                                q_sl = slice(q * 512, (q + 1) * 512)
                                t.matmul(psB[q][:], sel_sb[:, n * 128:(n + 1) * 128],
                                         B_t[:, q_sl], start=True, stop=True)
                                last = t.matmul(psC[q][:], sel_sb[:, n * 128:(n + 1) * 128],
                                                C_t[:, q_sl], start=True, stop=True)
                            return last

                        def dAexp(s, n=n):
                            return s.activation(dA[:], delta[:], AF.Exp,
                                                scale=Arep[:, n:n + 1])
                        prog.step(tensor=brep, scalar=dAexp)

                        def scan_grp(v, n=n, hcol=hcol):
                            for q in range(4):
                                v.tensor_tensor(dBx[:, q * 512:(q + 1) * 512],
                                                u_t[:, q * 512:(q + 1) * 512],
                                                psB[q][:], ALU.mult)
                            v.tensor_tensor_scan(h_t[:], dA[:], dBx[:],
                                                 hstate[:, hcol:hcol + 1], ALU.mult, ALU.add)
                            v.tensor_copy(hstate[:, hcol:hcol + 1], h_t[:, TCH - 1:TCH])
                            last = None
                            for q in range(4):
                                q_sl = slice(q * 512, (q + 1) * 512)
                                if n == 0:
                                    last = v.tensor_tensor(yacc[:, q_sl], h_t[:, q_sl],
                                                           psC[q][:], ALU.mult)
                                else:
                                    v.tensor_tensor(dBx[:, q_sl], h_t[:, q_sl],
                                                    psC[q][:], ALU.mult)
                                    last = v.tensor_tensor(yacc[:, q_sl], yacc[:, q_sl],
                                                           dBx[:, q_sl], ALU.add)
                            return last
                        prog.step(vector=scan_grp)

                    def ypre(v, g=g):
                        return v.scalar_tensor_tensor(yacc[:], xs_t[:], cols[:, 4 + g:5 + g],
                                                      yacc[:], ALU.mult, ALU.add)
                    prog.step(vector=ypre)
                    prog.step(scalar=lambda s: s.activation(sz[:], z_t[:], AF.Silu))
                    prog.step(vector=lambda v: v.tensor_tensor(ybf[:], yacc[:], sz[:], ALU.mult))
                    prog.step(sync=lambda e, b=b, tsc=tsc, g=g: dma_s(e, [
                        (y_loc[b * 4 + tsc, g * 128:(g + 1) * 128, :], ybf[:]),
                    ]))
        st3.close()

        cc("AllGather", ALU.bypass, y_loc_h, y_all_h)

        def ypick_g(g):
            return dma_g(g, [(y_cols[256 * j:256 * (j + 1), :],
                              y_all[bass.ts(dyn["ym"][j], 1), :, :])
                             for j in range(4)])

        def ypick_c(s):
            return dma_c(s, [(y_cols[256 * j:256 * (j + 1), :],
                              y_all[bass.ts(dyn["ymc"][j - 4], 1), :, :])
                             for j in range(4, 8)])
        prog.step(gpsimd=ypick_g, scalar=ypick_c)

        st4 = contextlib.ExitStack()
        t4 = alloc(nc, st4, sbufs=[
            ("p4_BIGA", (128, 16, 1024), F32),
            ("p4_gb", (128, 16), F32),
            ("p4_tmp", (128, 512), F32),
            ("p4_cwsb", (128, 128), F32),
            ("p4_glog", (128, 128), F32),
            ("p4_mcol", (128, 8), F32),
            ("p4_mcol16", (128, 16), F32),
            ("p4_m2col", (128, 16), F32),
            ("p4_rcol16", (128, 16), F32),
            ("p4_m8big", (128, 16, 8), F32),
        ])
        BIGA, gb4, tmp4 = t4["p4_BIGA"], t4["p4_gb"], t4["p4_tmp"]
        cwsb, glog, mcol = t4["p4_cwsb"], t4["p4_glog"], t4["p4_mcol"]
        mcol16, rcol16, m8big = t4["p4_mcol16"], t4["p4_rcol16"], t4["p4_m8big"]
        m2col = t4["p4_m2col"]

        st_xn1 = contextlib.ExitStack()
        xn1 = alloc(nc, st_xn1, sbufs=[("p4_xn1", (128, 8, TOK), BF16)])["p4_xn1"]
        stA = contextlib.ExitStack()
        sB = alloc(nc, stA, sbufs=[("p4_s", (128, 8, TOK), F32)])["p4_s"]

        stA1 = contextlib.ExitStack()
        tA1 = alloc(nc, stA1, sbufs=[
            ("p4_outwc", (128, 16, 256), BF16),
            ("p4_yb", (128, 16, 512), BF16),
            ("p4_xst2", (128, 2, 512), F32),
        ])
        outwc, yb4, xst2 = tA1["p4_outwc"], tA1["p4_yb"], tA1["p4_xst2"]

        prog.step(sync=lambda e: dma_s(e, [
            (gb4[:, 0:8], gamma), (gb4[:, 8:16], beta),
        ]))

        for n in range(4):
            n_sl = slice(n * 512, (n + 1) * 512)

            prog.step(sync=lambda e, n_sl=n_sl: dma_s(e, [
                (yb4[:, 0:8, :], y_cols[0:1024, n_sl]
                 .rearrange("(k p) t -> p k t", p=128)),
            ]), scalar=lambda s, n_sl=n_sl: dma_c(s, [
                (yb4[:, 8:16, :], y_cols[1024:2048, n_sl]
                 .rearrange("(k p) t -> p k t", p=128)),
            ]))
            for mw in range(4):
                def ldw(g, mw=mw, n_sl=n_sl):
                    return dma_g(g, [
                        (outwc[:], outwT[:, mw * 256:(mw + 1) * 256]
                         .rearrange("(k p) m -> p k m", p=128)),
                        (xst2[:], x_in[:, n_sl]
                         .rearrange("(j p) t -> p j t", p=128)[:, 2 * mw:2 * mw + 2, :]),
                    ])
                prog.step(gpsimd=ldw)

                def mms(t):
                    last = None
                    for mi in range(2):
                        for k in range(16):
                            last = t.matmul(ps[mi][:], outwc[:, k, mi * 128:(mi + 1) * 128],
                                            yb4[:, k, :], start=(k == 0), stop=(k == 15))
                    return last

                def evac(v, mw=mw, n_sl=n_sl):
                    last = None
                    for mi in range(2):
                        m = mw * 2 + mi
                        last = v.scalar_tensor_tensor(sB[:, m, n_sl], xst2[:, mi, :], 2.0,
                                                      ps[mi][:], ALU.mult, ALU.add)
                    return last
                prog.step(tensor=mms)
                prog.step(vector=evac)
        stA1.close()

        stS = contextlib.ExitStack()
        tS = alloc(nc, stS, sbufs=[
            ("p4_sq", (128, 2, 512), F32),
            ("p4_rows", (1, 4 * TOK), F32),
            ("p4_mv", (1, 2 * 512), F32),
        ])
        sq4, rows4, mv4 = tS["p4_sq"], tS["p4_rows"], tS["p4_mv"]
        a1row = rows4[:, 0 * TOK:1 * TOK]
        b1row = rows4[:, 1 * TOK:2 * TOK]
        a2row = rows4[:, 2 * TOK:3 * TOK]
        b2row_s = rows4[:, 3 * TOK:4 * TOK]

        emit_ln_stats(prog, lambda j, n_sl: sB[:, j, n_sl], sq4, ps[0], ps[1],
                      ones, mv4, a1row, b1row, consts)

        for n in range(4):
            n_sl = slice(n * 512, (n + 1) * 512)

            def bc1(t, n_sl=n_sl):
                t.matmul(ps[2][:], ones[0:1, 0:128], a1row[0:1, n_sl], start=True, stop=True)
                return t.matmul(ps[3][:], ones[0:1, 0:128], b1row[0:1, n_sl],
                                start=True, stop=True)
            prog.step(tensor=bc1)
            for g0 in range(0, 8, 2):
                def x1mk(v, g0=g0, n_sl=n_sl):
                    last = None
                    for i in range(2):
                        v.tensor_tensor(sq4[:, i, :], sB[:, g0 + i, n_sl], ps[2][:], ALU.mult)
                        last = v.tensor_tensor(sq4[:, i, :], sq4[:, i, :], ps[3][:], ALU.add)
                    return last
                prog.step(vector=x1mk)

                def mmx1(t, g0=g0):
                    last = None
                    for i in range(2):
                        j = g0 + i
                        last = t.matmul(ps[0][0:1, :], ones[:, 0:1], sq4[:, i, :],
                                        start=(j == 0), stop=(j == 7))
                    return last
                prog.step(tensor=mmx1)

                def sqx1(s):
                    last = None
                    for i in range(2):
                        last = s.activation(sq4[:, i, :], sq4[:, i, :], AF.Square)
                    return last
                prog.step(scalar=sqx1)

                def mmq1(t, g0=g0):
                    last = None
                    for i in range(2):
                        j = g0 + i
                        last = t.matmul(ps[1][0:1, :], ones[:, 0:1], sq4[:, i, :],
                                        start=(j == 0), stop=(j == 7))
                    return last
                prog.step(tensor=mmq1)

            def st2f(v):
                v.tensor_scalar(mv4[0:1, 0:512], ps[0][0:1, :], 1.0 / D, None, ALU.mult)
                v.tensor_scalar(mv4[0:1, 512:1024], ps[1][0:1, :], 1.0 / D, None, ALU.mult)
                v.tensor_tensor(tmp4[0:1, 0:512], mv4[0:1, 0:512], mv4[0:1, 0:512], ALU.mult)
                return v.tensor_tensor(mv4[0:1, 512:1024], mv4[0:1, 512:1024],
                                       tmp4[0:1, 0:512], ALU.subtract)
            prog.step(vector=st2f)

            def rstd2(s):
                s.activation(mv4[0:1, 512:1024], mv4[0:1, 512:1024], AF.Ln,
                             bias=consts[0:1, 0:1])
                return s.activation(mv4[0:1, 512:1024], mv4[0:1, 512:1024],
                                    AF.Exp, scale=-0.5)
            prog.step(scalar=rstd2)

            def rows2(v, n_sl=n_sl):
                v.tensor_copy(a2row[0:1, n_sl], mv4[0:1, 512:1024])
                return v.scalar_tensor_tensor(b2row_s[0:1, n_sl], mv4[0:1, 0:512], -1.0,
                                              mv4[0:1, 512:1024], ALU.mult, ALU.mult)
            prog.step(vector=rows2)

        for n in range(4):
            n_sl = slice(n * 512, (n + 1) * 512)

            def bc2(t, n_sl=n_sl):
                t.matmul(ps[2][:], ones[0:1, 0:128], a1row[0:1, n_sl], start=True, stop=True)
                t.matmul(ps[3][:], ones[0:1, 0:128], b1row[0:1, n_sl], start=True, stop=True)
                t.matmul(ps[4][:], ones[0:1, 0:128], a2row[0:1, n_sl], start=True, stop=True)
                return t.matmul(ps[5][:], ones[0:1, 0:128], b2row_s[0:1, n_sl],
                                start=True, stop=True)
            prog.step(tensor=bc2)

            for j in range(8):
                def vx(v, j=j, n_sl=n_sl):
                    v.tensor_tensor(tmp4[:], sB[:, j, n_sl], ps[2][:], ALU.mult)
                    v.tensor_tensor(tmp4[:], tmp4[:], ps[3][:], ALU.add)
                    v.tensor_tensor(sq4[:, 0, :], tmp4[:], ps[4][:], ALU.mult)
                    v.tensor_tensor(sq4[:, 0, :], sq4[:, 0, :], ps[5][:], ALU.add)
                    return v.tensor_scalar(xn1[:, j, n_sl], sq4[:, 0, :],
                                           gb4[:, j:j + 1], gb4[:, 8 + j:9 + j],
                                           ALU.mult, ALU.add)
                prog.step(vector=vx)

                for h in range(2):
                    def tr(t, h=h):
                        t.transpose(ps[6][:, 0:128], tmp4[:, 256 * h:256 * h + 128], idn[:])
                        return t.transpose(ps[7][:, 0:128],
                                           tmp4[:, 256 * h + 128:256 * h + 256], idn[:])
                    prog.step(tensor=tr)

                    def ev(v, j=j, n=n, h=h):
                        tau0 = n * 4 + 2 * h
                        v.tensor_scalar(BIGA[:, tau0, j * 128:(j + 1) * 128],
                                        ps[6][:, 0:128], 2.0, None, ALU.mult)
                        return v.tensor_scalar(BIGA[:, tau0 + 1, j * 128:(j + 1) * 128],
                                               ps[7][:, 0:128], 2.0, None, ALU.mult)
                    prog.step(vector=ev)
        stS.close()
        stA.close()

        stG = contextlib.ExitStack()
        gwb = alloc(nc, stG, sbufs=[("p4_gwb", (128, 8, 8), BF16)])["p4_gwb"]
        prog.step(gpsimd=lambda g: dma_g(g, [
            (gwb[:], gate_w.rearrange("(k p) e -> p k e", p=128)),
        ]))

        def gmm(t):
            last = None
            for tau in range(16):
                for k in range(8):
                    last = t.matmul(ps[0][:, tau * 8:(tau + 1) * 8],
                                    xn1[:, k, tau * 128:(tau + 1) * 128],
                                    gwb[:, k, :], start=(k == 0), stop=(k == 7))
            return last
        prog.step(tensor=gmm)

        prog.step(vector=lambda v: [v.tensor_reduce(mcol16[:, tau:tau + 1],
                                                    ps[0][:, tau * 8:(tau + 1) * 8],
                                                    mybir.AxisListType.X, ALU.max)
                                    for tau in range(16)][-1])

        prog.step(vector=lambda v: [v.tensor_scalar(glog[:, tau * 8:(tau + 1) * 8],
                                                    ps[0][:, tau * 8:(tau + 1) * 8],
                                                    mcol16[:, tau:tau + 1], None,
                                                    ALU.subtract)
                                    for tau in range(16)][-1])
        prog.step(scalar=lambda s: s.activation(glog[:], glog[:], AF.Exp))
        prog.step(vector=lambda v: [v.max(out=m8big[:, tau, :],
                                           in_=glog[:, tau * 8:(tau + 1) * 8])
                                    for tau in range(16)][-1])
        prog.step(vector=lambda v: [v.tensor_copy(m2col[:, tau:tau + 1],
                                                  m8big[:, tau, 1:2])
                                    for tau in range(16)][-1])
        prog.step(vector=lambda v: [v.tensor_tensor(mcol16[:, tau:tau + 1],
                                                    m8big[:, tau, 0:1],
                                                    m8big[:, tau, 1:2], ALU.add)
                                    for tau in range(16)][-1])
        prog.step(vector=lambda v: v.reciprocal(rcol16[:], mcol16[:]))
        prog.step(vector=lambda v: [v.tensor_scalar(cwsb[:, tau * 8:(tau + 1) * 8],
                                                    glog[:, tau * 8:(tau + 1) * 8],
                                                    m2col[:, tau:tau + 1], None, ALU.is_ge)
                                    for tau in range(16)][-1])
        prog.step(vector=lambda v: v.tensor_tensor(cwsb[:], cwsb[:], glog[:], ALU.mult))
        prog.step(vector=lambda v: [v.tensor_scalar(cwsb[:, tau * 8:(tau + 1) * 8],
                                                    cwsb[:, tau * 8:(tau + 1) * 8],
                                                    rcol16[:, tau:tau + 1], None, ALU.mult)
                                    for tau in range(16)][-1])
        stG.close()

        stM = contextlib.ExitStack()
        tM = alloc(nc, stM, sbufs=[
            ("p4_hbf", (128, 32, 1024), BF16),
            ("p4_w1c", (128, 4, 8, 128), BF16),
            ("p4_w2w", (128, 2, 8, 512), BF16),
            ("p4_b1s", (128, E * 32), F32),
            ("p4_b2t", (1, D), BF16),
            ("p4_ob", (1, 128), BF16),
        ])
        hbf, w1c, w2w, b1s, b2t = (tM["p4_hbf"], tM["p4_w1c"], tM["p4_w2w"],
                                   tM["p4_b1s"], tM["p4_b2t"])
        ones_bf = tM["p4_ob"]
        prog.step(sync=lambda e: dma_s(e, [(b1s[:], b1c)]),
                  vector=lambda v: v.memset(ones_bf[:], 1.0))

        for ex in range(E):
            prog.step(gpsimd=lambda g, e2=ex: dma_g(g, [
                (b2t[:], b2row[0:1, e2 * D:(e2 + 1) * D])]))
            for nn in range(2):
                tok0 = nn * 1024

                def ldw1(g, e2, hp, buf2):
                    return dma_g(g, [
                        (w1c[:, buf2 + c],
                         w1[e2, :, (2 * hp + c) * 128:(2 * hp + c + 1) * 128]
                         .rearrange("(k p) m -> p k m", p=128))
                        for c in range(2)])
                prog.step(gpsimd=lambda g, e2=ex: ldw1(g, e2, 0, 0))
                for hp in range(17):
                    def work(t, hp=hp, tok0=tok0):
                        if hp >= 16:
                            return None
                        last = None
                        for i in range(2):
                            for th in range(2):
                                for k in range(8):
                                    last = t.matmul(
                                        ps[(hp % 2) * 4 + i * 2 + th][:],
                                        w1c[:, (hp % 2) * 2 + i, k, :],
                                        xn1[:, k, tok0 + th * 512:tok0 + (th + 1) * 512],
                                        start=(k == 0), stop=(k == 7))
                        return last

                    def pre_g(g, e2=ex, hp=hp):
                        if hp + 1 >= 16:
                            return None
                        return ldw1(g, e2, hp + 1, ((hp + 1) % 2) * 2)

                    def gel(s, e2=ex, hp=hp):
                        if hp == 0:
                            return None
                        b0 = ((hp - 1) % 2) * 4
                        last = None
                        for i in range(2):
                            hpp = 2 * (hp - 1) + i
                            s.activation(hbf[:, hpp, 0:512], ps[b0 + i * 2][:], AF.Gelu,
                                         bias=b1s[:, e2 * 32 + hpp:e2 * 32 + hpp + 1])
                            last = s.activation(hbf[:, hpp, 512:1024],
                                                ps[b0 + i * 2 + 1][:], AF.Gelu,
                                                bias=b1s[:, e2 * 32 + hpp:e2 * 32 + hpp + 1])
                        return last
                    prog.step(tensor=work, gpsimd=pre_g, scalar=gel)

                for nd in range(2):
                    def ldw2(g, e2, nd, kw, buf):
                        r0 = kw * 1024
                        return dma_g(g, [(w2w[:, buf],
                                          w2[e2, r0:r0 + 1024, nd * 512:(nd + 1) * 512]
                                          .rearrange("(k p) m -> p k m", p=128))])
                    prog.step(gpsimd=lambda g, e2=ex, nd=nd: ldw2(g, e2, nd, 0, 0))
                    for kw in range(4):
                        def mm2(t, kw=kw):
                            last = None
                            for ki in range(8):
                                k = kw * 8 + ki
                                for tm in range(8):
                                    last = t.matmul(ps[tm][:],
                                                    hbf[:, k, tm * 128:(tm + 1) * 128],
                                                    w2w[:, kw % 2, ki, :],
                                                    start=(k == 0), stop=False)
                            return last

                        def pw2_g(g, e2=ex, kw=kw, nd=nd):
                            if kw + 1 >= 4:
                                return None
                            return ldw2(g, e2, nd, kw + 1, (kw + 1) % 2)
                        prog.step(tensor=mm2, gpsimd=pw2_g)

                    def b2mm(t, nd=nd):
                        last = None
                        for tm in range(8):
                            last = t.matmul(ps[tm][:], ones_bf[0:1, :],
                                            b2t[0:1, nd * 512:(nd + 1) * 512],
                                            start=False, stop=True)
                        return last
                    prog.step(tensor=b2mm)

                    def moevac(v, e2=ex, nn=nn, nd=nd):
                        last = None
                        for tm in range(8):
                            tau = nn * 8 + tm
                            last = v.scalar_tensor_tensor(
                                BIGA[:, tau, nd * 512:(nd + 1) * 512], ps[tm][:],
                                cwsb[:, tau * 8 + e2:tau * 8 + e2 + 1],
                                BIGA[:, tau, nd * 512:(nd + 1) * 512], ALU.mult, ALU.add)
                        return last
                    prog.step(vector=moevac)
        stM.close()
        st_xn1.close()

        stQ = contextlib.ExitStack()
        tQ = alloc(nc, stQ, sbufs=[
            ("p4_x2bf", (128, 8, TOK), BF16),
            ("p4_wat", (128, 8, 3 * D), BF16),
            ("p4_tmq", (128, 1024), F32),
            ("p4_stg", (128, 8, 128), F32),
            ("p4_stgb", (128, 4, 512), BF16),
        ])
        x2bf, wat, tmq, stg, stgb = (tQ["p4_x2bf"], tQ["p4_wat"], tQ["p4_tmq"],
                                     tQ["p4_stg"], tQ["p4_stgb"])
        prog.step(gpsimd=lambda g: dma_g(g, [
            (wat[:], wattnT.rearrange("(k p) m -> p k m", p=128)),
        ]))

        for tau in range(16):
            def lnst(s, tau=tau):
                return s.activation(tmq[:], BIGA[:, tau, :], AF.Square,
                                    accum_out=mcol[:, 1:2])
            prog.step(scalar=lnst)

            def lnred(v, tau=tau):
                v.tensor_reduce(mcol[:, 0:1], BIGA[:, tau, :], mybir.AxisListType.X, ALU.add)
                v.tensor_scalar(mcol[:, 0:1], mcol[:, 0:1], 1.0 / D, None, ALU.mult)
                v.tensor_scalar(mcol[:, 1:2], mcol[:, 1:2], 1.0 / D, None, ALU.mult)
                v.tensor_tensor(mcol[:, 2:3], mcol[:, 0:1], mcol[:, 0:1], ALU.mult)
                v.tensor_tensor(mcol[:, 1:2], mcol[:, 1:2], mcol[:, 2:3], ALU.subtract)
                return v.tensor_scalar(mcol[:, 3:4], mcol[:, 0:1], -1.0, None, ALU.mult)
            prog.step(vector=lnred)

            def lnrs(s):
                s.activation(mcol[:, 1:2], mcol[:, 1:2], AF.Ln, bias=consts[:, 0:1])
                return s.activation(mcol[:, 1:2], mcol[:, 1:2], AF.Exp, scale=-0.5)
            prog.step(scalar=lnrs)

            def lnap(v, tau=tau):
                return v.tensor_scalar(tmq[:], BIGA[:, tau, :], mcol[:, 3:4],
                                       mcol[:, 1:2], ALU.add, ALU.mult)
            prog.step(vector=lnap)

            for h in range(4):
                def trx(t, h=h):
                    t.transpose(ps[6][:, 0:128], tmq[:, 256 * h:256 * h + 128], idn[:])
                    return t.transpose(ps[7][:, 0:128],
                                       tmq[:, 256 * h + 128:256 * h + 256], idn[:])
                prog.step(tensor=trx)

                def evx(v, h=h, tau=tau):
                    v.tensor_copy(stg[:, 2 * h, :], ps[6][:, 0:128])
                    v.tensor_copy(stg[:, 2 * h + 1, :], ps[7][:, 0:128])
                    v.tensor_copy(x2bf[:, 2 * h, tau * 128:(tau + 1) * 128], ps[6][:, 0:128])
                    return v.tensor_copy(x2bf[:, 2 * h + 1, tau * 128:(tau + 1) * 128],
                                         ps[7][:, 0:128])
                prog.step(vector=evx)

            prog.step(sync=lambda e, tau=tau: dma_s(e, [
                (x2_scr[:, tau * 128:(tau + 1) * 128]
                 .rearrange("(j p) t -> p j t", p=128), stg[:])]))

        for n in range(4):
            n_sl = slice(n * 512, (n + 1) * 512)
            for mw in range(4):
                def qmm(t, mw=mw, n_sl=n_sl):
                    last = None
                    for mi in range(4):
                        m = mw * 4 + mi
                        for k in range(8):
                            last = t.matmul(ps[mi][:], wat[:, k, m * 128:(m + 1) * 128],
                                            x2bf[:, k, n_sl], start=(k == 0), stop=(k == 7))
                    return last
                prog.step(tensor=qmm)

                def qev(v):
                    last = None
                    for mi in range(4):
                        last = v.tensor_copy(stgb[:, mi, :], ps[mi][:])
                    return last
                prog.step(vector=qev)

                def qst(e, mw=mw, n_sl=n_sl):
                    trs = []
                    for mi in range(4):
                        for half in range(2):
                            hraw = 8 * mw + 2 * mi + half
                            if hraw < 16:
                                slot, sub = hraw // 4, hraw % 4
                            else:
                                hk = hraw - 16
                                slot, sub = hk // 4, 4 + hk % 4
                            trs.append((qk_loc[slot, sub, :, n_sl],
                                        stgb[64 * half:64 * half + 64, mi, :]))
                    return dma_s(e, trs)
                prog.step(sync=qst)

        for tw in range(4):
            for nv in range(2):
                def vmm(t, tw=tw, nv=nv):
                    last = None
                    for ti in range(4):
                        tau = tw * 4 + ti
                        for k in range(8):
                            last = t.matmul(ps[ti][:], x2bf[:, k, tau * 128:(tau + 1) * 128],
                                            wat[:, k, 2048 + nv * 512:2048 + (nv + 1) * 512],
                                            start=(k == 0), stop=(k == 7))
                    return last
                prog.step(tensor=vmm)

                def vev(v):
                    last = None
                    for ti in range(4):
                        last = v.tensor_copy(stgb[:, ti, :], ps[ti][:])
                    return last
                prog.step(vector=vev)

                def vst(e, tw=tw, nv=nv):
                    trs = []
                    for hh in range(8):
                        h = 8 * nv + hh
                        trs.append((v_loc[h, :, 4 * tw:4 * tw + 4, :],
                                    stgb[:, :, 64 * hh:64 * hh + 64]))
                    return dma_s(e, trs)
                prog.step(sync=vst)
        stQ.close()
        st4.close()

        cc("AllGather", ALU.bypass, qk_loc_h, qk_all_h)
        cc("AllGather", ALU.bypass, v_loc_h, v_all_h)

        def qkpick(e):
            qk_flat = qk_all.rearrange("a b c d -> a (b c d)")
            qk_p_flat = qk_p.rearrange("a b c d -> a (b c d)")
            return dma_s(e, [(qk_p_flat[ji:ji + 1, :],
                              qk_flat[bass.ts(dyn["qv"][ji], 1), :])
                             for ji in range(4)])

        def vpick(s):
            v_flat = v_all.rearrange("(a b) p k c -> a (b p k c)", b=4)
            v_p_flat = v_p.rearrange("(a b) p k c -> a (b p k c)", b=4)
            return dma_c(s, [(v_p_flat[ji:ji + 1, :],
                              v_flat[bass.ts(dyn["qvc"][ji], 1), :])
                             for ji in range(4)])
        prog.step(sync=qkpick, scalar=vpick)

        st5 = contextlib.ExitStack()
        t5 = alloc(nc, st5, sbufs=[
            ("p5_q2", (128, S), BF16), ("p5_k2", (128, S), BF16),
            ("p5_vA", (128, 64, 65), BF16), ("p5_vB", (128, 64, 65), BF16),
            ("p5_msk", (128, 4, 512), F32),
            ("p5_onesb", (128, 2), BF16),
            ("p5_ebf", (128, 2, 3, 512), BF16),
            ("p5_etmp", (128, 2, 2, 512), F32),
            ("p5_onesT", (65, 64), F32),
            ("p5_sufA", (65, 16), F32), ("p5_sufB", (65, 16), F32),
            ("p5_pref", (65, 64), F32), ("p5_tcol", (65, 1), F32),
            ("p5_yv", (65, 2, 512), F32),
            ("p5_rrow", (1, 2, 512), F32),
            ("p5_ystg", (64, 2, S), BF16),
        ])
        q2, k2, vA, vB, msk = (t5["p5_q2"], t5["p5_k2"], t5["p5_vA"], t5["p5_vB"],
                               t5["p5_msk"])
        onesb, ebf, etmp, onesT = (t5["p5_onesb"], t5["p5_ebf"], t5["p5_etmp"],
                                   t5["p5_onesT"])
        sufA, sufB, pref, tcol = t5["p5_sufA"], t5["p5_sufB"], t5["p5_pref"], t5["p5_tcol"]
        yv, rrow, ystg = t5["p5_yv"], t5["p5_rrow"], t5["p5_ystg"]
        psS = [[ps[0], ps[1]], [ps[2], ps[3]]]
        pyP = [ps[4][0:65, :], ps[5][0:65, :]]
        pxt = ps[6]

        prog.step(sync=lambda e: dma_s(e, [
            (msk[:], dmask.rearrange("(r p) f -> p r f", p=128)),
        ]), vector=lambda v: (v.memset(onesT[:], 1.0), v.memset(vA[:], 1.0),
                              v.memset(vB[:], 1.0), v.memset(onesb[:], 1.0))[-1])

        for pg in range(2):
            pA, pB = 2 * pg, 2 * pg + 1

            def ldq(e, pA=pA, pB=pB):
                trs = []
                for ji in range(4):
                    c_sl = slice(2048 * ji, 2048 * (ji + 1))
                    trs.append((q2[0:64, c_sl], qk_p[ji, pA, :, :]))
                    trs.append((q2[64:128, c_sl], qk_p[ji, pB, :, :]))
                return dma_s(e, trs)

            def ldk(g, pA=pA, pB=pB):
                trs = []
                for ji in range(4):
                    c_sl = slice(2048 * ji, 2048 * (ji + 1))
                    trs.append((k2[0:64, c_sl], qk_p[ji, 4 + pA, :, :]))
                    trs.append((k2[64:128, c_sl], qk_p[ji, 4 + pB, :, :]))
                return dma_g(g, trs)

            def ldv(s, pA=pA, pB=pB):
                trs = []
                for ji in range(4):
                    trs.append((vA[:, 16 * ji:16 * ji + 16, 0:64],
                                v_p[4 * ji + pA, :, :, :]))
                    trs.append((vB[:, 16 * ji:16 * ji + 16, 0:64],
                                v_p[4 * ji + pB, :, :, :]))
                return dma_c(s, trs)
            prog.step(sync=ldq, gpsimd=ldk, scalar=ldv)

            for pi, vt, suf in ((0, vA, sufA), (1, vB, sufB)):
                def tsum(t, vt=vt):
                    last = None
                    for kt in range(64):
                        last = t.matmul(pxt[0:65, kt:kt + 1], vt[:, kt, :],
                                        onesb[:, 0:1], start=True, stop=True)
                    return last
                prog.step(tensor=tsum)
                prog.step(vector=lambda v: v.tensor_tensor_scan(
                    pref[:], onesT[:], pxt[0:65, 0:64], 0.0, ALU.mult, ALU.add))
                prog.step(vector=lambda v: v.tensor_scalar(
                    tcol[:], pref[:, 63:64], EM10, None, ALU.mult))
                prog.step(vector=lambda v, suf=suf: v.tensor_scalar(
                    suf[:], pref[:, 3:64:4], -EM10, tcol[:, 0:1], ALU.mult, ALU.add))

            for qt in range(16):
                q_sl = slice(qt * 512, (qt + 1) * 512)
                nkt = 4 * qt + 4
                for s in range(nkt + 3):

                    def work_t(t, s=s, qt=qt, nkt=nkt, q_sl=q_sl):
                        last = None
                        if s < nkt:
                            sl = s % 2
                            k_sl = slice(s * 128, (s + 1) * 128)
                            t.matmul(psS[0][sl][:], k2[0:64, k_sl], q2[0:64, q_sl],
                                     start=True, stop=True, tile_position=(0, 0))
                            last = t.matmul(psS[1][sl][:], k2[64:128, k_sl],
                                            q2[64:128, q_sl], start=True, stop=True,
                                            tile_position=(64, 0))
                        if s >= 3:
                            kt = s - 3
                            sl3 = kt % 3
                            for pi, vt in ((0, vA), (1, vB)):
                                last = t.matmul(pyP[pi], vt[:, kt, :],
                                                ebf[:, pi, sl3, :],
                                                start=(kt == 0), stop=(kt == nkt - 1))
                        return last

                    def work_v(v, s=s, qt=qt, nkt=nkt):
                        kt = s - 1
                        if kt < 0 or kt >= nkt or kt < 4 * qt:
                            return None
                        sl = kt % 2
                        rel = kt - 4 * qt
                        last = None
                        for pi in range(2):
                            v.tensor_scalar(etmp[:, pi, sl, :], psS[pi][sl][:],
                                            0.125, 10.0, ALU.mult, ALU.add)
                            last = v.tensor_tensor(etmp[:, pi, sl, :],
                                                   etmp[:, pi, sl, :],
                                                   msk[:, rel, :], ALU.mult)
                        return last

                    def work_s(sc, s=s, qt=qt, nkt=nkt):
                        last = None
                        ktn = s - 1
                        if 0 <= ktn < nkt and ktn < 4 * qt:
                            for pi in range(2):
                                last = sc.activation(ebf[:, pi, ktn % 3, :],
                                                     psS[pi][ktn % 2][:], AF.Exp,
                                                     scale=0.125)
                        ktd = s - 2
                        if 0 <= ktd < nkt and ktd >= 4 * qt:
                            for pi in range(2):
                                last = sc.activation(ebf[:, pi, ktd % 3, :],
                                                     etmp[:, pi, ktd % 2, :], AF.Exp,
                                                     bias=consts[:, 1:2])
                        return last
                    prog.step(tensor=work_t, vector=work_v, scalar=work_s)

                def ev1(v, qt=qt):
                    v.tensor_scalar(yv[:, 0, :], pyP[0], sufA[:, qt:qt + 1],
                                    None, ALU.add)
                    v.tensor_scalar(yv[:, 1, :], pyP[1], sufB[:, qt:qt + 1],
                                    None, ALU.add)
                    v.reciprocal(rrow[0:1, 0, :], yv[64:65, 0, :])
                    return v.reciprocal(rrow[0:1, 1, :], yv[64:65, 1, :])
                prog.step(vector=ev1)

                def bc5(t):
                    t.matmul(pxt[0:64, :], onesT[0:1, 0:64], rrow[0:1, 0, :],
                             start=True, stop=True)
                    return t.matmul(ps[7][0:64, :], onesT[0:1, 0:64], rrow[0:1, 1, :],
                                    start=True, stop=True)
                prog.step(tensor=bc5)

                def mu5(v, q_sl=q_sl):
                    v.tensor_tensor(ystg[:, 0, q_sl], yv[0:64, 0, :], pxt[0:64, :],
                                    ALU.mult)
                    return v.tensor_tensor(ystg[:, 1, q_sl], yv[0:64, 1, :],
                                           ps[7][0:64, :], ALU.mult)
                prog.step(vector=mu5)

            prog.step(sync=lambda e, pA=pA, pB=pB: dma_s(e, [
                (yat_loc[:, 64 * pA:64 * pA + 64, :].rearrange("u p t -> p u t"),
                 ystg[:, 0, :].rearrange("p (u t) -> p u t", u=4)),
                (yat_loc[:, 64 * pB:64 * pB + 64, :].rearrange("u p t -> p u t"),
                 ystg[:, 1, :].rearrange("p (u t) -> p u t", u=4)),
            ]))
        st5.close()

        cc("AllGather", ALU.bypass, yat_loc_h, yat_all_h)

        def yatpick_s(e):
            return dma_s(e, [(yat_cols[256 * ji:256 * (ji + 1), :],
                              yat_all[bass.ts(dyn["qv"][ji], 1), :, :])
                             for ji in range(2)])

        def yatpick_c(s):
            return dma_c(s, [(yat_cols[256 * ji:256 * (ji + 1), :],
                              yat_all[bass.ts(dyn["qvc"][ji], 1), :, :])
                             for ji in range(2, 4)])
        prog.step(sync=yatpick_s, scalar=yatpick_c)

        st6 = contextlib.ExitStack()
        t6 = alloc(nc, st6, sbufs=[
            ("p6_sx", (128, 8, TOK), F32),
            ("p6_x3b", (128, 8, TOK), BF16),
            ("p6_bx3", (128, 8, TOK), F32),
            ("p6_tmp", (128, 512), F32),
            ("p6_wcast", (128, 8, 512), BF16),
            ("p6_xs2", (128, 4, 512), F32),
            ("p6_sq", (128, 1, 512), F32),
            ("p6_mv", (1, 4 * 512), F32),
            ("p6_bcols", (128, 9), F32),
            ("p6_lcols", (128, 33), F32),
            ("p6_lrow", (1, 8), F32),
            ("p6_h1", (128, 2, 512), BF16),
            ("p6_wd1c", (128, 8, 256), BF16),
            ("p6_wd2c", (128, 2, 1024), BF16),
            ("p6_bd1s", (128, 16), F32),
            ("p6_bd2s", (128, 64), F32),
            ("p6_yb", (128, 8, 512), BF16),
        ])
        sx, x3b, bx3 = t6["p6_sx"], t6["p6_x3b"], t6["p6_bx3"]
        tmp6 = t6["p6_tmp"]
        wcast, xs2, sq6, mv6 = t6["p6_wcast"], t6["p6_xs2"], t6["p6_sq"], t6["p6_mv"]
        bcols, lcols, lrow = t6["p6_bcols"], t6["p6_lcols"], t6["p6_lrow"]
        h1, wd1c, wd2c, bd1s, bd2s, yb6 = (t6["p6_h1"], t6["p6_wd1c"], t6["p6_wd2c"],
                                           t6["p6_bd1s"], t6["p6_bd2s"], t6["p6_yb"])

        prog.step(sync=lambda e: dma_s(e, [
            (bcols[:, 0:8], bbase), (bd1s[:], bd1c), (bd2s[:], bd2c),
        ]))

        for n in range(4):
            n_sl = slice(n * 512, (n + 1) * 512)

            prog.step(sync=lambda e, n_sl=n_sl: dma_s(e, [
                (yb6[:], yat_cols[:, n_sl].rearrange("(k p) t -> p k t", p=128)),
            ]))
            for mw in range(2):
                prog.step(sync=lambda e, mw=mw, n_sl=n_sl: dma_s(e, [
                    (xs2[:], x2_scr[:, n_sl].rearrange("(j p) t -> p j t", p=128)
                     [:, 4 * mw:4 * mw + 4, :]),
                ]), gpsimd=lambda g, mw=mw: dma_g(g, [
                    (wcast[:], wprojT[:, mw * 512:(mw + 1) * 512]
                     .rearrange("(k p) m -> p k m", p=128)),
                ]))

                def mms6(t):
                    last = None
                    for mi in range(4):
                        for k in range(8):
                            last = t.matmul(ps[mi][:], wcast[:, k, mi * 128:(mi + 1) * 128],
                                            yb6[:, k, :], start=(k == 0), stop=(k == 7))
                    return last

                def evac6(v, mw=mw, n_sl=n_sl):
                    last = None
                    for mi in range(4):
                        m = mw * 4 + mi
                        last = v.scalar_tensor_tensor(sx[:, m, n_sl], xs2[:, mi, :], 1.0,
                                                      ps[mi][:], ALU.mult, ALU.add)
                    return last
                prog.step(tensor=mms6)
                prog.step(vector=evac6)

        def ln3_out(j, n, n_sl, v, x_ap, ps_a, ps_b):
            v.tensor_tensor(tmp6[:], x_ap, ps_a[:], ALU.mult)
            v.tensor_tensor(tmp6[:], tmp6[:], ps_b[:], ALU.add)
            v.tensor_copy(x3b[:, j, n_sl], tmp6[:])
            return v.tensor_copy(sx[:, j, n_sl], tmp6[:])
        emit_ln_fused(prog, lambda j, n_sl: sx[:, j, n_sl], ln3_out, sq6,
                      ps[0], ps[1], ps[2], ps[3], ones, mv6, consts)
        prog.step(sync=lambda e: dma_s(e, [
            (x3b_scr.rearrange("(j p) t -> p j t", p=128), x3b[:]),
        ]))

        for n in range(4):
            n_sl = slice(n * 512, (n + 1) * 512)
            for mw in range(2):
                prog.step(gpsimd=lambda g, mw=mw: dma_g(g, [
                    (wcast[:], Wb[:, mw * 512:(mw + 1) * 512]
                     .rearrange("(k p) m -> p k m", p=128)),
                ]))

                def mmb(t, mw=mw, n_sl=n_sl):
                    last = None
                    for mi in range(4):
                        for k in range(8):
                            last = t.matmul(ps[mi][:], wcast[:, k, mi * 128:(mi + 1) * 128],
                                            x3b[:, k, n_sl], start=(k == 0), stop=(k == 7))
                    return last

                def evb(v, mw=mw, n_sl=n_sl):
                    last = None
                    for mi in range(4):
                        m = mw * 4 + mi
                        v.tensor_scalar(tmp6[:], ps[mi][:], bcols[:, m:m + 1], None, ALU.add)
                        last = v.tensor_tensor(bx3[:, m, n_sl], tmp6[:], sx[:, m, n_sl],
                                               ALU.add)
                    return last
                prog.step(tensor=mmb)
                prog.step(vector=evb)
        prog.step(sync=lambda e: dma_s(e, [
            (bx3_scr.rearrange("(j p) t -> p j t", p=128), bx3[:]),
        ]))

        for e_ in range(E):
            prog.step(gpsimd=lambda g, e_=e_: dma_g(g, [
                (wd1c[:], Wd1[e_].rearrange("(k p) m -> p k m", p=128)),
                (wd2c[:], Wd2[e_].rearrange("(k p) m -> p k m", p=128)),
            ]))
            for n in range(4):
                n_sl = slice(n * 512, (n + 1) * 512)

                def mh(t, n_sl=n_sl):
                    last = None
                    for mi in range(2):
                        for k in range(8):
                            last = t.matmul(ps[6 - mi][:], wd1c[:, k, mi * 128:(mi + 1) * 128],
                                            x3b[:, k, n_sl], start=(k == 0), stop=(k == 7))
                    return last
                prog.step(tensor=mh)

                def eh(s_, e_=e_):
                    s_.activation(h1[:, 0, :], ps[6][:], AF.Relu,
                                  bias=bd1s[:, 2 * e_:2 * e_ + 1])
                    return s_.activation(h1[:, 1, :], ps[5][:], AF.Relu,
                                         bias=bd1s[:, 2 * e_ + 1:2 * e_ + 2])
                prog.step(scalar=eh)
                for mw in range(2):
                    def mr(t, mw=mw):
                        last = None
                        for mi in range(4):
                            for k in range(2):
                                last = t.matmul(
                                    ps[mi][:],
                                    wd2c[:, k, mw * 512 + mi * 128:mw * 512 + (mi + 1) * 128],
                                    h1[:, k, :], start=(k == 0), stop=(k == 1))
                        return last
                    prog.step(tensor=mr)

                    def dv2(v, mw=mw, n_sl=n_sl, e_=e_):
                        last = None
                        for mi in range(4):
                            m = mw * 4 + mi
                            last = v.scalar_tensor_tensor(
                                xs2[:, mi, :], ps[mi][:], bd2s[:, e_ * 8 + m:e_ * 8 + m + 1],
                                sx[:, m, n_sl], ALU.add, ALU.subtract)
                        return last
                    prog.step(vector=dv2)

                    def sqa(s_, n=n, mw=mw):
                        last = None
                        for mi in range(4):
                            ci = (n * 2 + mw) * 4 + mi
                            last = s_.activation(tmp6[:], xs2[:, mi, :], AF.Square,
                                                 accum_out=lcols[:, ci:ci + 1])
                        return last
                    prog.step(scalar=sqa)
            prog.step(vector=lambda v: v.tensor_reduce(
                lcols[:, 32:33], lcols[:, 0:32], mybir.AxisListType.X, ALU.add))
            prog.step(tensor=lambda t: t.matmul(ps[4][0:1, 0:1], ones[:, 0:1],
                                                lcols[:, 32:33], start=True, stop=True))
            prog.step(vector=lambda v, e_=e_: v.tensor_copy(lrow[0:1, e_:e_ + 1],
                                                            ps[4][0:1, 0:1]))
        prog.step(sync=lambda e: dma_s(e, [(lrow_loc, lrow[:])]))
        st6.close()

        cc("AllGather", ALU.bypass, lrow_loc_h, lparts_all_h)

        st7 = contextlib.ExitStack()
        t7 = alloc(nc, st7, sbufs=[
            ("p7_bx3s", (128, 8, TOK), F32),
            ("p7_x3bs", (128, 8, TOK), BF16),
            ("p7_a_all", (64, 8, TOK), BF16),
            ("p7_wauc", (64, 8, D), BF16),
            ("p7_wadc", (128, 8, 64), BF16),
            ("p7_lp", (8, 8), F32),
            ("p7_selb", (8, 2), F32),
            ("p7_baups", (8, D), F32),
            ("p7_badcs", (64, 8), F32),
            ("p7_lrow", (1, 8), F32),
            ("p7_mcol", (1, 2), F32),
            ("p7_ind", (1, 8), F32),
            ("p7_indc8", (8, 1), F32),
            ("p7_ind64", (64, 8), F32),
            ("p7_bcol", (128, 8), F32),
            ("p7_tmp", (128, 512), F32),
            ("p7_sq", (128, 2, 512), F32),
            ("p7_mv", (1, 4 * 512), F32),
            ("p7_ostg", (128, 8, 512), F32),
        ])
        bx3s, x3bs, a_all, wauc, wadc = (t7["p7_bx3s"], t7["p7_x3bs"], t7["p7_a_all"],
                                         t7["p7_wauc"], t7["p7_wadc"])
        lp, sel7, baups, badcs = t7["p7_lp"], t7["p7_selb"], t7["p7_baups"], t7["p7_badcs"]
        lrow7, mcol7, ind, indc8, ind64, bcol = (t7["p7_lrow"], t7["p7_mcol"], t7["p7_ind"],
                                                 t7["p7_indc8"], t7["p7_ind64"],
                                                 t7["p7_bcol"])
        tmp7, sq7, mv7, ostg = t7["p7_tmp"], t7["p7_sq"], t7["p7_mv"], t7["p7_ostg"]

        prog.step(sync=lambda e: dma_s(e, [
            (lp[:], lparts_all), (sel7[:], selb), (baups[:], baup8), (badcs[:], badc),
            (bx3s[:, 0:4, :], bx3_scr[0:512, :].rearrange("(j p) t -> p j t", p=128)),
        ]), scalar=lambda s: dma_c(s, [
            (bx3s[:, 4:8, :], bx3_scr[512:1024, :].rearrange("(j p) t -> p j t", p=128)),
            (x3bs[:], x3b_scr.rearrange("(j p) t -> p j t", p=128)),
        ]), gpsimd=lambda g: dma_g(g, [
            (wauc[:], Wau.rearrange("e k m -> k e m")),
        ]))

        prog.step(tensor=lambda t: t.matmul(ps[0][0:2, 0:8], sel7[:], lp[:],
                                            start=True, stop=True))
        prog.step(vector=lambda v: v.tensor_copy(lp[0:2, :], ps[0][0:2, 0:8]))
        prog.step(sync=lambda e: dma_s(e, [(l2scr[:], lp[0:2, :])]))

        def ownrow(e):
            return dma_s(e, [(lrow7[:], l2scr[bass.ts(dyn["r4"], 1), :])])
        prog.step(sync=ownrow)

        prog.step(vector=lambda v: v.tensor_reduce(mcol7[0:1, 0:1], lrow7[:],
                                                   mybir.AxisListType.X, ALU.min))
        prog.step(vector=lambda v: v.tensor_scalar(ind[:], lrow7[:], mcol7[0:1, 0:1],
                                                   None, ALU.is_le))
        prog.step(tensor=lambda t: (
            t.matmul(ps[0][0:8, 0:1], ind[:], ones[0:1, 0:1], start=True, stop=True),
            t.matmul(ps[1][0:64, 0:8], ones[0:1, 0:64], ind[:], start=True, stop=True))[-1])
        prog.step(vector=lambda v: (v.tensor_copy(indc8[:], ps[0][0:8, 0:1]),
                                    v.tensor_copy(ind64[:], ps[1][0:64, 0:8]))[-1])
        prog.step(tensor=lambda t: [t.matmul(ps[2][0:128, m:m + 1],
                                             baups[:, m * 128:(m + 1) * 128],
                                             indc8[:], start=True, stop=True)
                                    for m in range(8)][-1])
        prog.step(vector=lambda v: v.tensor_copy(bcol[:], ps[2][:, 0:8]))

        for e_ in range(E):
            prog.step(gpsimd=lambda g, e_=e_: dma_g(g, [
                (wadc[:], Wad[e_].rearrange("(k p) m -> p k m", p=128)),
            ]))
            for n in range(4):
                def ma(t, n=n):
                    last = None
                    for k in range(8):
                        last = t.matmul(ps[0][0:64, :], wadc[:, k, :],
                                        x3bs[:, k, n * 512:(n + 1) * 512],
                                        start=(k == 0), stop=(k == 7))
                    return last
                prog.step(tensor=ma)
                prog.step(scalar=lambda s, e_=e_, n=n: s.activation(
                    a_all[:, e_, n * 512:(n + 1) * 512], ps[0][0:64, :], AF.Relu,
                    bias=badcs[:, e_:e_ + 1]))
            prog.step(vector=lambda v, e_=e_: v.tensor_scalar(
                a_all[:, e_, :], a_all[:, e_, :], ind64[:, e_:e_ + 1], None, ALU.mult))

        for n in range(4):
            n_sl = slice(n * 512, (n + 1) * 512)
            for mw in range(2):
                def mu_(t, mw=mw, n_sl=n_sl):
                    last = None
                    for mi in range(4):
                        m = mw * 4 + mi
                        for e_ in range(E):
                            last = t.matmul(ps[mi][:], wauc[:, e_, m * 128:(m + 1) * 128],
                                            a_all[:, e_, n_sl],
                                            start=(e_ == 0), stop=(e_ == 7))
                    return last
                prog.step(tensor=mu_)

                def evu(v, mw=mw, n_sl=n_sl):
                    last = None
                    for mi in range(4):
                        m = mw * 4 + mi
                        v.tensor_scalar(tmp7[:], ps[mi][:], bcol[:, m:m + 1], None, ALU.add)
                        last = v.tensor_tensor(bx3s[:, m, n_sl], tmp7[:], bx3s[:, m, n_sl],
                                               ALU.add)
                    return last
                prog.step(vector=evu)

        def ln4_out(j, n, n_sl, v, x_ap, ps_a, ps_b):
            v.tensor_tensor(ostg[:, j, :], x_ap, ps_a[:], ALU.mult)
            return v.tensor_tensor(ostg[:, j, :], ostg[:, j, :], ps_b[:], ALU.add)

        def ln4_store(prog_, n, n_sl):
            prog_.step(sync=lambda e, n_sl=n_sl: dma_s(e, [
                (out_d[:, n_sl].rearrange("(j p) t -> p j t", p=128), ostg[:])]))
        emit_ln_fused(prog, lambda j, n_sl: bx3s[:, j, n_sl], ln4_out, sq7,
                      ps[0], ps[1], ps[2], ps[3], ones, mv7, consts,
                      post_chunk=ln4_store)
        st7.close()
    st0.close()
    return nc



def kernel(**I):
    P = host_prep(I)
    ident = np.eye(128, dtype=np.float32)
    cores = list(range(NC))

    nc = bass.Bass("TRN2", target_bir_lowering=False, debug=False)
    build_full(nc)
    in_maps = []
    for c in cores:
        in_maps.append({
            "x_fm": P["x_fm"][c], "lng": P["lng"], "lnb": P["lnb"],
            "inwT": P["inwT"][c], "convw": P["convw"][c], "convb": P["convb"][c],
            "xprojT": P["xprojT"][c], "A_log": P["A_log"], "dtw": P["dtw"][c],
            "dtb": P["dtb"][c], "Dc": P["Dc"][c], "sel": P["sel"],
            "outwT": P["outwT"], "gamma": P["gamma"], "beta": P["beta"],
            "gate_w": P["gate_w"], "w1": P["w1"], "b1c": P["b1c"],
            "w2": P["w2"], "b2row": P["b2row"], "wattnT": P["wattnT"],
            "ident": ident, "dmask": P["dmask"], "wprojT": P["wprojT"],
            "W_base": P["W_base"], "bbase": P["bbase"], "Wd1": P["Wd1"],
            "bd1c": P["bd1c"], "Wd2": P["Wd2"], "bd2c": P["bd2c"],
            "selb": P["selb"], "badc": P["badc"], "Wad": P["Wad"],
            "Wau": P["Wau"], "baup8": P["baup8"]})
    r = run_bass_kernel_spmd(nc, in_maps, core_ids=cores).results

    out = np.empty((B * S, D), np.float32)
    for c in cores:
        out[c * TOK:(c + 1) * TOK] = np.asarray(r[c]["out_slab"], np.float32).T
    return out.reshape(B, S, D)


# version 07
# speedup vs baseline: 1.2684x; 1.0173x over previous
import numpy as np
import ml_dtypes
import contextlib
import concourse.bass as bass
import concourse.mybir as mybir
from concourse.bass_utils import run_bass_kernel_spmd

F32 = mybir.dt.float32
BF16 = mybir.dt.bfloat16
AF = mybir.ActivationFunctionType
ALU = mybir.AluOpType
BF = ml_dtypes.bfloat16

NC = 8
B = 2
S = 8192
D = 1024
DI = 2048
DIC = DI // NC
NST = 16
HMOE = 4096
E = 8
TOK = (B * S) // NC
NH, HS = 16, 64
EPS = 1e-5
EM10 = float(np.exp(np.float64(-10.0)))

ENGS = ("sync", "tensor", "vector", "scalar", "gpsimd")


class Prog:

    def __init__(self, nc, block, tick, dsems):
        self.nc = nc
        self.block = block
        self.tick = tick
        self.k = 0
        self.dsems = dsems
        self.dctr = {e: 0 for e in dsems}
        self.ccn = 0

    def step(self, **work):
        self.k += 1
        prev = 5 * (self.k - 1)
        for name in ENGS:
            fn = work.get(name)

            def emit(eng, fn=fn):
                eng.wait_ge(self.tick, prev)
                last = fn(eng) if fn is not None else None
                if last is None:
                    last = eng.nop()
                last.then_inc(self.tick, 1)
            getattr(self.block, name)(emit)

    def dmas(self, eng_name):
        sem = self.dsems[eng_name]

        def go(eng, transfers, **kw):
            for out, in_ in transfers:
                eng.dma_start(out=out, in_=in_, **kw).then_inc(sem, 16)
                self.dctr[eng_name] += 16
            eng.wait_ge(sem, self.dctr[eng_name])
            return eng.nop()
        return go


def alloc(nc, st, sbufs=(), psums=(), sems=()):
    d = {}
    for name, shape, dt in sbufs:
        d[name] = st.enter_context(nc.sbuf_tensor(name, list(shape), dt))
    for name, shape, dt in psums:
        d[name] = st.enter_context(nc.psum_tensor(name, list(shape), dt))
    for name in sems:
        d[name] = st.enter_context(nc.semaphore(name))
    return d



def _col8(v):
    return np.ascontiguousarray(v.reshape(8, 128).T)


def host_prep(I):
    P = {}
    x = I["L0_attn_x"]
    xf = np.asarray(x).reshape(B * S, D)
    P["x_fm"] = [np.ascontiguousarray(xf[c * TOK:(c + 1) * TOK].T) for c in range(NC)]
    P["lng"] = _col8(I["L0_attn_ln_g"])
    P["lnb"] = _col8(I["L0_attn_ln_b"])
    in_w = I["L0_attn_in_w"]
    P["inwT"] = []
    for c in range(NC):
        rows = np.concatenate([in_w[DIC * c:DIC * (c + 1)],
                               in_w[DI + DIC * c:DI + DIC * (c + 1)]], axis=0)
        P["inwT"].append(np.ascontiguousarray(rows.T))
    cw_ = I["L0_attn_conv_w"][:, 0, :]
    P["convw"] = [np.ascontiguousarray(
        cw_[DIC * c:DIC * (c + 1)].reshape(2, 128, 4).transpose(1, 0, 2).reshape(128, 8))
        for c in range(NC)]
    P["convb"] = [np.ascontiguousarray(
        I["L0_attn_conv_b"][DIC * c:DIC * (c + 1)].reshape(2, 128).T) for c in range(NC)]
    xp_w = I["L0_attn_xproj_w"]
    P["xprojT"] = [np.ascontiguousarray(
        xp_w[:, DIC * c:DIC * (c + 1)].T.reshape(2, 128, 33).transpose(1, 0, 2).reshape(128, 66))
        for c in range(NC)]
    P["dtw"] = [np.ascontiguousarray(I["L0_attn_dt_w"][DIC * c:DIC * (c + 1), 0]
                                     .reshape(2, 128).T) for c in range(NC)]
    P["dtb"] = [np.ascontiguousarray(I["L0_attn_dt_b"][DIC * c:DIC * (c + 1)]
                                     .reshape(2, 128).T) for c in range(NC)]
    P["Dc"] = [np.ascontiguousarray(I["L0_attn_D"][DIC * c:DIC * (c + 1)]
                                    .reshape(2, 128).T) for c in range(NC)]
    P["A_log"] = np.ascontiguousarray(I["L0_attn_A_log"].reshape(1, 16))
    P["outwT"] = np.ascontiguousarray(I["L0_attn_out_w"].T)
    P["gate_w"] = np.ascontiguousarray(I["L0_ffn_gate_w"])
    P["w1"] = I["L0_ffn_w1"]
    P["b1c"] = np.ascontiguousarray(
        I["L0_ffn_b1"].reshape(8, 32, 128).transpose(2, 0, 1).reshape(128, 8 * 32))
    P["w2"] = I["L0_ffn_w2"]
    P["b2row"] = np.ascontiguousarray(I["L0_ffn_b2"].reshape(1, 8 * 1024))
    P["gamma"] = _col8(I["L0_ffn_gamma"])
    P["beta"] = _col8(I["L0_ffn_beta"])
    P["wattnT"] = np.ascontiguousarray(I["L1_attn_w_attn"].T)
    P["wprojT"] = np.ascontiguousarray(I["L1_attn_w_proj"].T)
    P["W_base"] = np.ascontiguousarray(I["L1_ffn_W_base"])
    P["bbase"] = _col8(I["L1_ffn_b_base"])
    P["Wd1"] = I["L1_ffn_Wd1"]
    P["bd1c"] = np.ascontiguousarray(
        I["L1_ffn_bd1"].reshape(8, 2, 128).transpose(2, 0, 1).reshape(128, 16))
    P["Wd2"] = I["L1_ffn_Wd2"]
    P["bd2c"] = np.ascontiguousarray(
        I["L1_ffn_bd2"].reshape(8, 8, 128).transpose(2, 0, 1).reshape(128, 64))
    P["Wad"] = I["L1_ffn_Wa_down"]
    P["badc"] = np.ascontiguousarray(I["L1_ffn_ba_down"].T)
    P["Wau"] = I["L1_ffn_Wa_up"]
    P["baup8"] = np.ascontiguousarray(I["L1_ffn_ba_up"])
    selm2 = np.zeros((8, 2), np.float32)
    selm2[0:4, 0] = 1.0
    selm2[4:8, 1] = 1.0
    P["selb"] = selm2
    selm = np.zeros((16, 16 * 128), np.float32)
    for n in range(16):
        selm[n, n * 128:(n + 1) * 128] = 1.0
    P["sel"] = selm
    m = np.zeros((4, 128, 512), np.float32)
    for r in range(4):
        for p in range(128):
            m[r, p, :] = (np.arange(512) >= p + 128 * r)
    P["dmask"] = np.ascontiguousarray(m.reshape(4 * 128, 512))
    return P



def emit_ln_stats(prog, x_tile, sq_sb, ps_mu, ps_sq, ones_col,
                  mvrow, arow, brow, eps_col, ncols=512, n_chunks=4, nd=8,
                  nfeat=1024):
    inv = 1.0 / nfeat
    G = sq_sb.shape[1]
    for n in range(n_chunks):
        n_sl = slice(n * ncols, (n + 1) * ncols)
        for g0 in range(0, nd, G):
            def sqs(s, g0=g0, n_sl=n_sl):
                last = None
                for i in range(G):
                    last = s.activation(sq_sb[:, i, :], x_tile(g0 + i, n_sl), AF.Square)
                return last
            prog.step(scalar=sqs)

            def mms(t, g0=g0, n_sl=n_sl):
                last = None
                for i in range(G):
                    j = g0 + i
                    t.matmul(ps_mu[0:1, :], ones_col[:, 0:1], x_tile(j, n_sl),
                             start=(j == 0), stop=(j == nd - 1))
                    last = t.matmul(ps_sq[0:1, :], ones_col[:, 0:1], sq_sb[:, i, :],
                                    start=(j == 0), stop=(j == nd - 1))
                return last
            prog.step(tensor=mms)

        mu = mvrow[0:1, 0:ncols]
        va = mvrow[0:1, ncols:2 * ncols]

        def stats(v, n_sl=n_sl, mu=mu, va=va):
            v.tensor_scalar(mu, ps_mu[0:1, :], inv, None, ALU.mult)
            v.tensor_scalar(va, ps_sq[0:1, :], inv, None, ALU.mult)
            v.tensor_tensor(arow[0:1, n_sl], mu, mu, ALU.mult)
            return v.tensor_tensor(va, va, arow[0:1, n_sl], ALU.subtract)
        prog.step(vector=stats)

        def rstd(s, n_sl=n_sl, va=va):
            s.activation(arow[0:1, n_sl], va, AF.Ln, bias=eps_col[0:1, 0:1])
            return s.activation(arow[0:1, n_sl], arow[0:1, n_sl], AF.Exp, scale=-0.5)
        prog.step(scalar=rstd)

        def bro(v, n_sl=n_sl, mu=mu):
            return v.scalar_tensor_tensor(brow[0:1, n_sl], mu, -1.0,
                                          arow[0:1, n_sl], ALU.mult, ALU.mult)
        prog.step(vector=bro)


def emit_ln_apply(prog, x_tile, out_fns, ps_a, ps_b, ones_row, arow, brow, tmp_sb,
                  ncols=512, n_chunks=4, nd=8):
    for n in range(n_chunks):
        n_sl = slice(n * ncols, (n + 1) * ncols)

        def bcast(t, n_sl=n_sl):
            t.matmul(ps_a[:, :], ones_row[0:1, :], arow[0:1, n_sl], start=True, stop=True)
            return t.matmul(ps_b[:, :], ones_row[0:1, :], brow[0:1, n_sl],
                            start=True, stop=True)
        prog.step(tensor=bcast)

        def apply(v, n=n, n_sl=n_sl):
            last = None
            for j in range(nd):
                v.tensor_tensor(tmp_sb[:, 0:ncols], x_tile(j, n_sl), ps_a[:, :], ALU.mult)
                v.tensor_tensor(tmp_sb[:, 0:ncols], tmp_sb[:, 0:ncols], ps_b[:, :], ALU.add)
                last = out_fns(j, n, n_sl, v, tmp_sb[:, 0:ncols])
            return last
        prog.step(vector=apply)


def emit_ln_fused(prog, x_tile, out_fn, sq_sb, ps_mu, ps_sq, ps_a, ps_b,
                  ones_col, mv4, eps_col, ncols=512, n_chunks=4, nd=8, nfeat=1024,
                  post_chunk=None):
    inv = 1.0 / nfeat
    G = sq_sb.shape[1]
    for n in range(n_chunks):
        n_sl = slice(n * ncols, (n + 1) * ncols)
        for g0 in range(0, nd, G):
            def sqs(s, g0=g0, n_sl=n_sl):
                last = None
                for i in range(G):
                    last = s.activation(sq_sb[:, i, :], x_tile(g0 + i, n_sl), AF.Square)
                return last
            prog.step(scalar=sqs)

            def mms(t, g0=g0, n_sl=n_sl):
                last = None
                for i in range(G):
                    j = g0 + i
                    t.matmul(ps_mu[0:1, :], ones_col[:, 0:1], x_tile(j, n_sl),
                             start=(j == 0), stop=(j == nd - 1))
                    last = t.matmul(ps_sq[0:1, :], ones_col[:, 0:1], sq_sb[:, i, :],
                                    start=(j == 0), stop=(j == nd - 1))
                return last
            prog.step(tensor=mms)

        mu = mv4[0:1, 0:ncols]
        va = mv4[0:1, ncols:2 * ncols]
        ar = mv4[0:1, 2 * ncols:3 * ncols]
        br = mv4[0:1, 3 * ncols:4 * ncols]

        def stats(v, mu=mu, va=va, ar=ar):
            v.tensor_scalar(mu, ps_mu[0:1, :], inv, None, ALU.mult)
            v.tensor_scalar(va, ps_sq[0:1, :], inv, None, ALU.mult)
            v.tensor_tensor(ar, mu, mu, ALU.mult)
            return v.tensor_tensor(va, va, ar, ALU.subtract)
        prog.step(vector=stats)

        def rstd(s, va=va, ar=ar):
            s.activation(ar, va, AF.Ln, bias=eps_col[0:1, 0:1])
            return s.activation(ar, ar, AF.Exp, scale=-0.5)
        prog.step(scalar=rstd)

        def bro(v, mu=mu, ar=ar, br=br):
            return v.scalar_tensor_tensor(br, mu, -1.0, ar, ALU.mult, ALU.mult)
        prog.step(vector=bro)

        def bcast(t, ar=ar, br=br):
            t.matmul(ps_a[:], ones_col[0:1, 0:128], ar, start=True, stop=True)
            return t.matmul(ps_b[:], ones_col[0:1, 0:128], br, start=True, stop=True)
        prog.step(tensor=bcast)

        def apply(v, n=n, n_sl=n_sl):
            last = None
            for j in range(nd):
                last = out_fn(j, n, n_sl, v, x_tile(j, n_sl), ps_a, ps_b)
            return last
        prog.step(vector=apply)
        if post_chunk is not None:
            post_chunk(prog, n, n_sl)



def build_full(nc):
    x_in = nc.dram_tensor("x_fm", [D, TOK], F32, kind="ExternalInput").ap()
    lng = nc.dram_tensor("lng", [128, 8], F32, kind="ExternalInput").ap()
    lnb = nc.dram_tensor("lnb", [128, 8], F32, kind="ExternalInput").ap()
    inwT = nc.dram_tensor("inwT", [D, 4 * 128], F32, kind="ExternalInput").ap()
    convw = nc.dram_tensor("convw", [128, 8], F32, kind="ExternalInput").ap()
    convb = nc.dram_tensor("convb", [128, 2], F32, kind="ExternalInput").ap()
    xprojT = nc.dram_tensor("xprojT", [128, 66], F32, kind="ExternalInput").ap()
    alog = nc.dram_tensor("A_log", [1, 16], F32, kind="ExternalInput").ap()
    dtw = nc.dram_tensor("dtw", [128, 2], F32, kind="ExternalInput").ap()
    dtb = nc.dram_tensor("dtb", [128, 2], F32, kind="ExternalInput").ap()
    Dc = nc.dram_tensor("Dc", [128, 2], F32, kind="ExternalInput").ap()
    sel = nc.dram_tensor("sel", [16, 16 * 128], F32, kind="ExternalInput").ap()
    outwT = nc.dram_tensor("outwT", [DI, D], F32, kind="ExternalInput").ap()
    gamma = nc.dram_tensor("gamma", [128, 8], F32, kind="ExternalInput").ap()
    beta = nc.dram_tensor("beta", [128, 8], F32, kind="ExternalInput").ap()
    gate_w = nc.dram_tensor("gate_w", [D, 8], F32, kind="ExternalInput").ap()
    w1 = nc.dram_tensor("w1", [E, D, HMOE], F32, kind="ExternalInput").ap()
    b1c = nc.dram_tensor("b1c", [128, E * 32], F32, kind="ExternalInput").ap()
    w2 = nc.dram_tensor("w2", [E, HMOE, D], F32, kind="ExternalInput").ap()
    b2row = nc.dram_tensor("b2row", [1, E * D], F32, kind="ExternalInput").ap()
    wattnT = nc.dram_tensor("wattnT", [D, 3 * D], F32, kind="ExternalInput").ap()
    ident = nc.dram_tensor("ident", [128, 128], F32, kind="ExternalInput").ap()
    dmask = nc.dram_tensor("dmask", [4 * 128, 512], F32, kind="ExternalInput").ap()
    wprojT = nc.dram_tensor("wprojT", [D, D], F32, kind="ExternalInput").ap()
    Wb = nc.dram_tensor("W_base", [D, D], F32, kind="ExternalInput").ap()
    bbase = nc.dram_tensor("bbase", [128, 8], F32, kind="ExternalInput").ap()
    Wd1 = nc.dram_tensor("Wd1", [E, D, 256], F32, kind="ExternalInput").ap()
    bd1c = nc.dram_tensor("bd1c", [128, 16], F32, kind="ExternalInput").ap()
    Wd2 = nc.dram_tensor("Wd2", [E, 256, D], F32, kind="ExternalInput").ap()
    bd2c = nc.dram_tensor("bd2c", [128, 64], F32, kind="ExternalInput").ap()
    selb = nc.dram_tensor("selb", [E, 2], F32, kind="ExternalInput").ap()
    badc = nc.dram_tensor("badc", [64, E], F32, kind="ExternalInput").ap()
    Wad = nc.dram_tensor("Wad", [E, D, 64], F32, kind="ExternalInput").ap()
    Wau = nc.dram_tensor("Wau", [E, 64, D], F32, kind="ExternalInput").ap()
    baup8 = nc.dram_tensor("baup8", [E, D], F32, kind="ExternalInput").ap()
    out_d = nc.dram_tensor("out_slab", [D, TOK], F32, kind="ExternalOutput").ap()

    xn0_loc_h = nc.dram_tensor("xn0_loc", [D, TOK], BF16, kind="Internal")
    xn0_all_h = nc.dram_tensor("xn0_all", [NC, D, TOK], BF16, kind="Internal")
    xs_scr = nc.dram_tensor("xs_scr", [2 * DIC, S], F32, kind="Internal").ap()
    z_scr = nc.dram_tensor("z_scr", [2 * DIC, S], BF16, kind="Internal").ap()
    xp_loc_h = nc.dram_tensor("xp_loc", [66, S], F32, kind="Internal")
    xp_sum_h = nc.dram_tensor("xp_sum", [66, S], F32, kind="Internal")
    y_loc_h = nc.dram_tensor("y_loc", [NC, 2 * 128, TOK], BF16, kind="Internal")
    y_all_h = nc.dram_tensor("y_all", [NC * NC, 2 * 128, TOK], BF16, kind="Internal")
    x2_scr = nc.dram_tensor("x2_scr", [D, TOK], F32, kind="Internal").ap()
    y_cols = nc.dram_tensor("y_cols", [DI, TOK], BF16, kind="Internal").ap()
    qk_loc_h = nc.dram_tensor("qk_loc", [4, 8, 64, TOK], BF16, kind="Internal")
    qk_all_h = nc.dram_tensor("qk_all", [32, 8, 64, TOK], BF16, kind="Internal")
    qk_p = nc.dram_tensor("qk_p", [4, 8, 64, TOK], BF16, kind="Internal").ap()
    v_loc_h = nc.dram_tensor("v_loc", [16, 128, 16, 64], BF16, kind="Internal")
    v_all_h = nc.dram_tensor("v_all", [NC * 16, 128, 16, 64], BF16, kind="Internal")
    v_p = nc.dram_tensor("v_p", [16, 128, 16, 64], BF16, kind="Internal").ap()
    yat_loc_h = nc.dram_tensor("yat_loc", [4, 256, TOK], BF16, kind="Internal")
    yat_all_h = nc.dram_tensor("yat_all", [NC * 4, 256, TOK], BF16, kind="Internal")
    yat_cols = nc.dram_tensor("yat_cols", [D, TOK], BF16, kind="Internal").ap()
    x3b_scr = nc.dram_tensor("x3b_scr", [D, TOK], BF16, kind="Internal").ap()
    bx3_scr = nc.dram_tensor("bx3_scr", [D, TOK], F32, kind="Internal").ap()
    lrow_loc_h = nc.dram_tensor("lrow_loc", [1, E], F32, kind="Internal")
    lparts_all_h = nc.dram_tensor("lparts_all", [NC, E], F32, kind="Internal")
    l2scr = nc.dram_tensor("l2scr", [2, E], F32, kind="Internal").ap()

    xn0_loc = xn0_loc_h.ap()
    xn0_all = xn0_all_h.ap()
    xp_loc = xp_loc_h.ap()
    xp_sum = xp_sum_h.ap()
    y_loc = y_loc_h.ap()
    y_all = y_all_h.ap()
    qk_loc = qk_loc_h.ap()
    qk_all = qk_all_h.ap()
    v_loc = v_loc_h.ap()
    v_all = v_all_h.ap()
    yat_loc = yat_loc_h.ap()
    yat_all = yat_all_h.ap()
    lrow_loc = lrow_loc_h.ap()
    lparts_all = lparts_all_h.ap()

    st0 = contextlib.ExitStack()
    sh = alloc(nc, st0, sbufs=[
        ("ones", (128, 128), F32),
        ("idn", (128, 128), F32),
        ("consts", (128, 4), F32),
    ], psums=[(f"ps{i}", (128, 512), F32) for i in range(8)],
       sems=["tick", "dm_sync", "dm_gp", "dm_sc", "dm_ve", "cc_sem"])
    ones, idn, consts = sh["ones"], sh["idn"], sh["consts"]
    ps = [sh[f"ps{i}"] for i in range(8)]
    tick, dm_sync, dm_gp, cc_sem = sh["tick"], sh["dm_sync"], sh["dm_gp"], sh["cc_sem"]

    with nc.Block() as block:
        prog = Prog(nc, block, tick, {"sync": dm_sync, "gpsimd": dm_gp,
                                      "scalar": sh["dm_sc"], "vector": sh["dm_ve"]})
        dma_s = prog.dmas("sync")
        dma_g = prog.dmas("gpsimd")
        dma_c = prog.dmas("scalar")
        dma_v = prog.dmas("vector")

        def cc(kind, op, in_h, out_h):
            def go(g):
                prog.ccn += 1
                g.collective_compute(
                    kind, op, replica_groups=[list(range(NC))],
                    ins=[in_h.ap().opt()], outs=[out_h.ap().opt()],
                ).then_inc(cc_sem)
                g.wait_ge(cc_sem, prog.ccn)
                return g.nop()
            prog.step(gpsimd=go)

        dyn = {}

        def mkidx_s(e):
            rank = e.partition_id()
            r4 = rank // 4
            dyn["r4"] = r4
            base = rank + 12 * r4
            dyn["qv"] = [base + 4 * ji for ji in range(4)]
            return None

        def mkidx_g(g):
            rank_g = g.partition_id()
            dyn["ym"] = [rank_g + 8 * j for j in range(4)]
            return None

        def mkidx_c(s):
            rank_c = s.partition_id()
            base_c = rank_c + 12 * (rank_c // 4)
            dyn["qvc"] = [base_c + 4 * ji for ji in range(4)]
            dyn["ymc"] = [rank_c + 8 * j for j in range(4, 8)]
            return None

        prog.step(sync=lambda e: (mkidx_s(e), dma_s(e, [(idn[:], ident)]))[-1],
                  gpsimd=mkidx_g,
                  scalar=mkidx_c,
                  vector=lambda v: (v.memset(ones[:], 1.0),
                                    v.memset(consts[:, 0:1], EPS),
                                    v.memset(consts[:, 1:2], -10.0))[-1])

        st1 = contextlib.ExitStack()
        t1 = alloc(nc, st1, sbufs=[
            ("p1_xT", (128, 8, TOK), F32), ("p1_xn", (128, 8, TOK), BF16),
            ("p1_sq", (128, 8, 512), F32), ("p1_rows", (1, 4 * TOK), F32),
            ("p1_gb", (128, 16), F32), ("p1_tmp", (128, 512), F32),
        ])
        xT, xn, sq1, rows1, gb1, tmp1 = (t1["p1_xT"], t1["p1_xn"], t1["p1_sq"],
                                         t1["p1_rows"], t1["p1_gb"], t1["p1_tmp"])
        arow = rows1[:, 2 * TOK:3 * TOK]
        brow = rows1[:, 3 * TOK:4 * TOK]

        prog.step(sync=lambda e: dma_s(e, [
            (xT[:, 0:4, :], x_in[0:512, :].rearrange("(j p) t -> p j t", p=128)),
            (gb1[:, 0:8], lng), (gb1[:, 8:16], lnb),
        ]), gpsimd=lambda g: dma_g(g, [
            (xT[:, 4:8, :], x_in[512:1024, :].rearrange("(j p) t -> p j t", p=128)),
        ]))

        emit_ln_stats(prog, lambda j, n_sl: xT[:, j, n_sl], sq1, ps[0], ps[1],
                      ones, rows1[:, 0:1024], arow, brow, consts)

        def out_fn1(j, n, n_sl, v, normed):
            return v.tensor_scalar(xn[:, j, n_sl], normed, gb1[:, j:j + 1],
                                   gb1[:, 8 + j:9 + j], ALU.mult, ALU.add)
        emit_ln_apply(prog, lambda j, n_sl: xT[:, j, n_sl], out_fn1,
                      ps[2], ps[3], ones, arow, brow, tmp1)

        prog.step(sync=lambda e: dma_s(e, [
            (xn0_loc.rearrange("(j p) t -> p j t", p=128), xn[:]),
        ]))
        st1.close()

        cc("AllGather", ALU.bypass, xn0_loc_h, xn0_all_h)

        TCH = 2048
        st2 = contextlib.ExitStack()
        t2 = alloc(nc, st2, sbufs=[
            ("p2_inw", (128, 8, 4 * 128), BF16), ("p2_xnc", (128, 8, TCH), BF16),
            ("p2_xi", (128, 2, TCH + 3), F32), ("p2_xc", (128, 2, TCH), F32),
            ("p2_xsc", (128, 2, TCH), F32), ("p2_zst", (128, 2, TCH), BF16),
            ("p2_cw", (128, 8), F32), ("p2_cb", (128, 2), F32),
            ("p2_xpj", (128, 2, 33), F32), ("p2_xpst", (33, 2, TCH), F32),
            ("p2_carry", (128, 2, 3), F32),
        ])
        inw, xnc, xi, xc, xsc, zst = (t2["p2_inw"], t2["p2_xnc"], t2["p2_xi"],
                                      t2["p2_xc"], t2["p2_xsc"], t2["p2_zst"])
        cw_sb, cb_sb, xpj, xpst, carry = (t2["p2_cw"], t2["p2_cb"], t2["p2_xpj"],
                                          t2["p2_xpst"], t2["p2_carry"])

        prog.step(gpsimd=lambda g: dma_g(g, [
            (inw[:], inwT.rearrange("(k p) m -> p k m", p=128)),
        ]), sync=lambda e: dma_s(e, [
            (cw_sb[:], convw), (cb_sb[:], convb),
            (xpj[:], xprojT.rearrange("p (d m) -> p d m", d=2)),
        ]))

        for b in range(B):
            for ch in range(S // TCH):
                rk = 4 * b + ch

                def ldxn(e, rk=rk):
                    return dma_s(e, [(xnc[:, 0:4, :], xn0_all[rk, 0:512, :]
                                      .rearrange("(j p) t -> p j t", p=128))])

                def ldxn_g(g, rk=rk):
                    return dma_g(g, [(xnc[:, 4:8, :], xn0_all[rk, 512:1024, :]
                                      .rearrange("(j p) t -> p j t", p=128))])
                prog.step(sync=ldxn, gpsimd=ldxn_g)

                for n in range(TCH // 512):
                    n_sl = slice(n * 512, (n + 1) * 512)

                    def mms(t, n_sl=n_sl):
                        last = None
                        for m in range(4):
                            for k in range(8):
                                last = t.matmul(ps[m][:], inw[:, k, m * 128:(m + 1) * 128],
                                                xnc[:, k, n_sl],
                                                start=(k == 0), stop=(k == 7))
                        return last

                    def evac(v, n=n, n_sl=n_sl, ch=ch):
                        x_sl = slice(3 + n * 512, 3 + (n + 1) * 512)
                        if n == 0:
                            if ch == 0:
                                v.memset(carry[:], 0.0)
                            v.tensor_copy(xi[:, :, 0:3], carry[:])
                        v.tensor_copy(xi[:, 0, x_sl], ps[0][:])
                        v.tensor_copy(xi[:, 1, x_sl], ps[1][:])
                        v.tensor_copy(zst[:, 0, n_sl], ps[2][:])
                        return v.tensor_copy(zst[:, 1, n_sl], ps[3][:])
                    prog.step(tensor=mms)
                    prog.step(vector=evac)

                def conv(v):
                    last = None
                    for dt in range(2):
                        v.tensor_scalar(xc[:, dt, :], xi[:, dt, 0:TCH],
                                        cw_sb[:, dt * 4:dt * 4 + 1], None, ALU.mult)
                        for k in range(1, 4):
                            last = v.scalar_tensor_tensor(
                                xc[:, dt, :], xi[:, dt, k:k + TCH],
                                cw_sb[:, dt * 4 + k:dt * 4 + k + 1],
                                xc[:, dt, :], ALU.mult, ALU.add)
                    return last
                prog.step(vector=conv)

                def silu_save(s):
                    s.activation(xsc[:, 0, :], xc[:, 0, :], AF.Silu, bias=cb_sb[:, 0:1])
                    return s.activation(xsc[:, 1, :], xc[:, 1, :], AF.Silu, bias=cb_sb[:, 1:2])

                def carry_save(v):
                    return v.tensor_copy(carry[:], xi[:, :, TCH:TCH + 3])
                prog.step(scalar=silu_save, vector=carry_save)

                def xpmm(t):
                    last = None
                    for n in range(4):
                        for dt in range(2):
                            last = t.matmul(ps[n][0:33, :], xpj[:, dt, :],
                                            xsc[:, dt, n * 512:(n + 1) * 512],
                                            start=(dt == 0), stop=(dt == 1))
                    return last
                prog.step(tensor=xpmm)

                def xpev(v):
                    last = None
                    for n in range(4):
                        last = v.tensor_copy(xpst[:, 0, n * 512:(n + 1) * 512], ps[n][0:33, :])
                    return last
                prog.step(vector=xpev)

                def store(e, b=b, ch=ch):
                    c_sl = slice(ch * TCH, (ch + 1) * TCH)
                    return dma_s(e, [
                        (xs_scr[b * 256:b * 256 + 128, c_sl], xsc[:, 0, :]),
                        (xs_scr[b * 256 + 128:b * 256 + 256, c_sl], xsc[:, 1, :]),
                        (z_scr[b * 256:b * 256 + 128, c_sl], zst[:, 0, :]),
                        (z_scr[b * 256 + 128:b * 256 + 256, c_sl], zst[:, 1, :]),
                        (xp_loc[b * 33:b * 33 + 33, c_sl], xpst[:, 0, :]),
                    ])
                prog.step(sync=store)
        st2.close()

        cc("AllReduce", ALU.add, xp_loc_h, xp_sum_h)

        st3 = contextlib.ExitStack()
        t3 = alloc(nc, st3, sbufs=[
            ("p3_xp0", (1, TCH), F32),
            ("p3_B_t", (16, TCH), BF16), ("p3_C_t", (16, TCH), BF16),
            ("p3_sel", (16, 16 * 128), BF16),
            ("p3_Arep", (128, 16), F32), ("p3_Arow", (1, 16), F32),
            ("p3_cols", (128, 8), F32),
            ("p3_hstate", (128, 64), F32),
            ("p3_xs", (128, TCH), F32), ("p3_z", (128, TCH), BF16),
            ("p3_delta", (128, TCH), F32), ("p3_u", (128, TCH), F32),
            ("p3_dA", (128, TCH), F32), ("p3_dBx", (128, TCH), F32),
            ("p3_h", (128, TCH), F32), ("p3_yacc", (128, TCH), F32),
            ("p3_sz", (128, TCH), F32), ("p3_ybf", (128, TCH), BF16),
        ])
        (xp0, B_t, C_t, sel_sb, Arep, Arow, cols, hstate, xs_t, z_t,
         delta, u_t, dA, dBx, h_t, yacc, sz, ybf) = (
            t3["p3_xp0"], t3["p3_B_t"], t3["p3_C_t"], t3["p3_sel"], t3["p3_Arep"],
            t3["p3_Arow"], t3["p3_cols"], t3["p3_hstate"], t3["p3_xs"], t3["p3_z"],
            t3["p3_delta"], t3["p3_u"], t3["p3_dA"], t3["p3_dBx"], t3["p3_h"],
            t3["p3_yacc"], t3["p3_sz"], t3["p3_ybf"])
        psB = [ps[0], ps[1], ps[2], ps[3]]
        psC = [ps[4], ps[5], ps[6], ps[7]]

        prog.step(sync=lambda e: dma_s(e, [
            (Arow[:], alog), (cols[:, 0:2], dtw), (cols[:, 2:4], dtb),
            (cols[:, 4:6], Dc),
        ]), gpsimd=lambda g: dma_g(g, [(sel_sb[:], sel)]),
            vector=lambda v: v.memset(hstate[:], 0.0))

        prog.step(scalar=lambda s: s.activation(Arow[:], Arow[:], AF.Exp))
        prog.step(vector=lambda v: v.tensor_scalar(Arow[:], Arow[:], -1.0, None, ALU.mult))
        prog.step(tensor=lambda t: t.matmul(ps[0][:, 0:16], ones[0:1, 0:128], Arow[:],
                                            start=True, stop=True))
        prog.step(vector=lambda v: v.tensor_copy(Arep[:], ps[0][:, 0:16]))

        for b in range(B):
            for g in range(2):
                for tsc in range(S // TCH):
                    t_sl = slice(tsc * TCH, (tsc + 1) * TCH)
                    row0 = b * 256 + g * 128

                    def lds(e, b=b, row0=row0, t_sl=t_sl):
                        return dma_s(e, [
                            (xs_t[:], xs_scr[row0:row0 + 128, t_sl]),
                            (z_t[:], z_scr[row0:row0 + 128, t_sl]),
                            (xp0[:], xp_sum[33 * b:33 * b + 1, t_sl]),
                        ])

                    def ldbc(g, b=b, t_sl=t_sl):
                        return dma_g(g, [
                            (B_t[:], xp_sum[1 + 33 * b:17 + 33 * b, t_sl]),
                            (C_t[:], xp_sum[17 + 33 * b:33 + 33 * b, t_sl]),
                        ])
                    prog.step(sync=lds, gpsimd=ldbc)

                    def x0b(t, t_sl=t_sl):
                        last = None
                        for n in range(4):
                            last = t.matmul(psB[n][:], ones[0:1, 0:128],
                                            xp0[0:1, n * 512:(n + 1) * 512],
                                            start=True, stop=True)
                        return last
                    prog.step(tensor=x0b)

                    def sdelta(s, g=g):
                        last = None
                        for n in range(4):
                            last = s.activation(delta[:, n * 512:(n + 1) * 512], psB[n][:],
                                                AF.Exp, scale=cols[:, g:g + 1],
                                                bias=cols[:, 2 + g:3 + g])
                        return last
                    prog.step(scalar=sdelta)
                    prog.step(scalar=lambda s: s.activation(delta[:], delta[:], AF.Ln, bias=1.0))
                    prog.step(vector=lambda v: v.tensor_tensor(u_t[:], delta[:], xs_t[:], ALU.mult))

                    for n in range(NST):
                        hcol = (b * 2 + g) * 16 + n

                        def brep(t, n=n):
                            last = None
                            for q in range(4):
                                q_sl = slice(q * 512, (q + 1) * 512)
                                t.matmul(psB[q][:], sel_sb[:, n * 128:(n + 1) * 128],
                                         B_t[:, q_sl], start=True, stop=True)
                                last = t.matmul(psC[q][:], sel_sb[:, n * 128:(n + 1) * 128],
                                                C_t[:, q_sl], start=True, stop=True)
                            return last

                        def dAexp(s, n=n):
                            return s.activation(dA[:], delta[:], AF.Exp,
                                                scale=Arep[:, n:n + 1])
                        prog.step(tensor=brep, scalar=dAexp)

                        def scan_grp(v, n=n, hcol=hcol):
                            for q in range(4):
                                v.tensor_tensor(dBx[:, q * 512:(q + 1) * 512],
                                                u_t[:, q * 512:(q + 1) * 512],
                                                psB[q][:], ALU.mult)
                            v.tensor_tensor_scan(h_t[:], dA[:], dBx[:],
                                                 hstate[:, hcol:hcol + 1], ALU.mult, ALU.add)
                            v.tensor_copy(hstate[:, hcol:hcol + 1], h_t[:, TCH - 1:TCH])
                            last = None
                            for q in range(4):
                                q_sl = slice(q * 512, (q + 1) * 512)
                                if n == 0:
                                    last = v.tensor_tensor(yacc[:, q_sl], h_t[:, q_sl],
                                                           psC[q][:], ALU.mult)
                                else:
                                    v.tensor_tensor(dBx[:, q_sl], h_t[:, q_sl],
                                                    psC[q][:], ALU.mult)
                                    last = v.tensor_tensor(yacc[:, q_sl], yacc[:, q_sl],
                                                           dBx[:, q_sl], ALU.add)
                            return last
                        prog.step(vector=scan_grp)

                    def ypre(v, g=g):
                        return v.scalar_tensor_tensor(yacc[:], xs_t[:], cols[:, 4 + g:5 + g],
                                                      yacc[:], ALU.mult, ALU.add)
                    prog.step(vector=ypre)
                    prog.step(scalar=lambda s: s.activation(sz[:], z_t[:], AF.Silu))
                    prog.step(vector=lambda v: v.tensor_tensor(ybf[:], yacc[:], sz[:], ALU.mult))
                    prog.step(sync=lambda e, b=b, tsc=tsc, g=g: dma_s(e, [
                        (y_loc[b * 4 + tsc, g * 128:(g + 1) * 128, :], ybf[:]),
                    ]))
        st3.close()

        cc("AllGather", ALU.bypass, y_loc_h, y_all_h)

        def ypick_g(g):
            return dma_g(g, [(y_cols[256 * j:256 * (j + 1), :],
                              y_all[bass.ts(dyn["ym"][j], 1), :, :])
                             for j in range(4)])

        def ypick_c(s):
            return dma_c(s, [(y_cols[256 * j:256 * (j + 1), :],
                              y_all[bass.ts(dyn["ymc"][j - 4], 1), :, :])
                             for j in range(4, 8)])
        prog.step(gpsimd=ypick_g, scalar=ypick_c)

        st4 = contextlib.ExitStack()
        t4 = alloc(nc, st4, sbufs=[
            ("p4_BIGA", (128, 16, 1024), F32),
            ("p4_gb", (128, 16), F32),
            ("p4_tmp", (128, 512), F32),
            ("p4_cwsb", (128, 128), F32),
            ("p4_glog", (128, 128), F32),
            ("p4_mcol", (128, 8), F32),
            ("p4_mcol16", (128, 16), F32),
            ("p4_m2col", (128, 16), F32),
            ("p4_rcol16", (128, 16), F32),
            ("p4_m8big", (128, 16, 8), F32),
        ])
        BIGA, gb4, tmp4 = t4["p4_BIGA"], t4["p4_gb"], t4["p4_tmp"]
        cwsb, glog, mcol = t4["p4_cwsb"], t4["p4_glog"], t4["p4_mcol"]
        mcol16, rcol16, m8big = t4["p4_mcol16"], t4["p4_rcol16"], t4["p4_m8big"]
        m2col = t4["p4_m2col"]

        st_xn1 = contextlib.ExitStack()
        xn1 = alloc(nc, st_xn1, sbufs=[("p4_xn1", (128, 8, TOK), BF16)])["p4_xn1"]
        stA = contextlib.ExitStack()
        sB = alloc(nc, stA, sbufs=[("p4_s", (128, 8, TOK), F32)])["p4_s"]

        stA1 = contextlib.ExitStack()
        tA1 = alloc(nc, stA1, sbufs=[
            ("p4_outwc", (128, 16, 256), BF16),
            ("p4_yb", (128, 16, 512), BF16),
            ("p4_xst2", (128, 2, 512), F32),
        ])
        outwc, yb4, xst2 = tA1["p4_outwc"], tA1["p4_yb"], tA1["p4_xst2"]

        prog.step(sync=lambda e: dma_s(e, [
            (gb4[:, 0:8], gamma), (gb4[:, 8:16], beta),
        ]))

        for n in range(4):
            n_sl = slice(n * 512, (n + 1) * 512)

            prog.step(sync=lambda e, n_sl=n_sl: dma_s(e, [
                (yb4[:, 0:8, :], y_cols[0:1024, n_sl]
                 .rearrange("(k p) t -> p k t", p=128)),
            ]), scalar=lambda s, n_sl=n_sl: dma_c(s, [
                (yb4[:, 8:16, :], y_cols[1024:2048, n_sl]
                 .rearrange("(k p) t -> p k t", p=128)),
            ]))
            for mw in range(4):
                def ldw(g, mw=mw, n_sl=n_sl):
                    return dma_g(g, [
                        (outwc[:], outwT[:, mw * 256:(mw + 1) * 256]
                         .rearrange("(k p) m -> p k m", p=128)),
                        (xst2[:], x_in[:, n_sl]
                         .rearrange("(j p) t -> p j t", p=128)[:, 2 * mw:2 * mw + 2, :]),
                    ])
                prog.step(gpsimd=ldw)

                def mms(t):
                    last = None
                    for mi in range(2):
                        for k in range(16):
                            last = t.matmul(ps[mi][:], outwc[:, k, mi * 128:(mi + 1) * 128],
                                            yb4[:, k, :], start=(k == 0), stop=(k == 15))
                    return last

                def evac(v, mw=mw, n_sl=n_sl):
                    last = None
                    for mi in range(2):
                        m = mw * 2 + mi
                        last = v.scalar_tensor_tensor(sB[:, m, n_sl], xst2[:, mi, :], 2.0,
                                                      ps[mi][:], ALU.mult, ALU.add)
                    return last
                prog.step(tensor=mms)
                prog.step(vector=evac)
        stA1.close()

        stS = contextlib.ExitStack()
        tS = alloc(nc, stS, sbufs=[
            ("p4_sq", (128, 2, 512), F32),
            ("p4_rows", (1, 4 * TOK), F32),
            ("p4_mv", (1, 2 * 512), F32),
        ])
        sq4, rows4, mv4 = tS["p4_sq"], tS["p4_rows"], tS["p4_mv"]
        a1row = rows4[:, 0 * TOK:1 * TOK]
        b1row = rows4[:, 1 * TOK:2 * TOK]
        a2row = rows4[:, 2 * TOK:3 * TOK]
        b2row_s = rows4[:, 3 * TOK:4 * TOK]

        emit_ln_stats(prog, lambda j, n_sl: sB[:, j, n_sl], sq4, ps[0], ps[1],
                      ones, mv4, a1row, b1row, consts)

        for n in range(4):
            n_sl = slice(n * 512, (n + 1) * 512)

            def bc1(t, n_sl=n_sl):
                t.matmul(ps[2][:], ones[0:1, 0:128], a1row[0:1, n_sl], start=True, stop=True)
                return t.matmul(ps[3][:], ones[0:1, 0:128], b1row[0:1, n_sl],
                                start=True, stop=True)
            prog.step(tensor=bc1)
            for g0 in range(0, 8, 2):
                def x1mk(v, g0=g0, n_sl=n_sl):
                    last = None
                    for i in range(2):
                        v.tensor_tensor(sq4[:, i, :], sB[:, g0 + i, n_sl], ps[2][:], ALU.mult)
                        last = v.tensor_tensor(sq4[:, i, :], sq4[:, i, :], ps[3][:], ALU.add)
                    return last
                prog.step(vector=x1mk)

                def mmx1(t, g0=g0):
                    last = None
                    for i in range(2):
                        j = g0 + i
                        last = t.matmul(ps[0][0:1, :], ones[:, 0:1], sq4[:, i, :],
                                        start=(j == 0), stop=(j == 7))
                    return last
                prog.step(tensor=mmx1)

                def sqx1(s):
                    last = None
                    for i in range(2):
                        last = s.activation(sq4[:, i, :], sq4[:, i, :], AF.Square)
                    return last
                prog.step(scalar=sqx1)

                def mmq1(t, g0=g0):
                    last = None
                    for i in range(2):
                        j = g0 + i
                        last = t.matmul(ps[1][0:1, :], ones[:, 0:1], sq4[:, i, :],
                                        start=(j == 0), stop=(j == 7))
                    return last
                prog.step(tensor=mmq1)

            def st2f(v):
                v.tensor_scalar(mv4[0:1, 0:512], ps[0][0:1, :], 1.0 / D, None, ALU.mult)
                v.tensor_scalar(mv4[0:1, 512:1024], ps[1][0:1, :], 1.0 / D, None, ALU.mult)
                v.tensor_tensor(tmp4[0:1, 0:512], mv4[0:1, 0:512], mv4[0:1, 0:512], ALU.mult)
                return v.tensor_tensor(mv4[0:1, 512:1024], mv4[0:1, 512:1024],
                                       tmp4[0:1, 0:512], ALU.subtract)
            prog.step(vector=st2f)

            def rstd2(s):
                s.activation(mv4[0:1, 512:1024], mv4[0:1, 512:1024], AF.Ln,
                             bias=consts[0:1, 0:1])
                return s.activation(mv4[0:1, 512:1024], mv4[0:1, 512:1024],
                                    AF.Exp, scale=-0.5)
            prog.step(scalar=rstd2)

            def rows2(v, n_sl=n_sl):
                v.tensor_copy(a2row[0:1, n_sl], mv4[0:1, 512:1024])
                return v.scalar_tensor_tensor(b2row_s[0:1, n_sl], mv4[0:1, 0:512], -1.0,
                                              mv4[0:1, 512:1024], ALU.mult, ALU.mult)
            prog.step(vector=rows2)

        for n in range(4):
            n_sl = slice(n * 512, (n + 1) * 512)

            def bc2(t, n_sl=n_sl):
                t.matmul(ps[2][:], ones[0:1, 0:128], a1row[0:1, n_sl], start=True, stop=True)
                t.matmul(ps[3][:], ones[0:1, 0:128], b1row[0:1, n_sl], start=True, stop=True)
                t.matmul(ps[4][:], ones[0:1, 0:128], a2row[0:1, n_sl], start=True, stop=True)
                return t.matmul(ps[5][:], ones[0:1, 0:128], b2row_s[0:1, n_sl],
                                start=True, stop=True)
            prog.step(tensor=bc2)

            for j in range(8):
                def vx(v, j=j, n_sl=n_sl):
                    v.tensor_tensor(tmp4[:], sB[:, j, n_sl], ps[2][:], ALU.mult)
                    v.tensor_tensor(tmp4[:], tmp4[:], ps[3][:], ALU.add)
                    v.tensor_tensor(sq4[:, 0, :], tmp4[:], ps[4][:], ALU.mult)
                    v.tensor_tensor(sq4[:, 0, :], sq4[:, 0, :], ps[5][:], ALU.add)
                    return v.tensor_scalar(xn1[:, j, n_sl], sq4[:, 0, :],
                                           gb4[:, j:j + 1], gb4[:, 8 + j:9 + j],
                                           ALU.mult, ALU.add)
                prog.step(vector=vx)

                for h in range(2):
                    def tr(t, h=h):
                        t.transpose(ps[6][:, 0:128], tmp4[:, 256 * h:256 * h + 128], idn[:])
                        return t.transpose(ps[7][:, 0:128],
                                           tmp4[:, 256 * h + 128:256 * h + 256], idn[:])
                    prog.step(tensor=tr)

                    def ev(v, j=j, n=n, h=h):
                        tau0 = n * 4 + 2 * h
                        v.tensor_scalar(BIGA[:, tau0, j * 128:(j + 1) * 128],
                                        ps[6][:, 0:128], 2.0, None, ALU.mult)
                        return v.tensor_scalar(BIGA[:, tau0 + 1, j * 128:(j + 1) * 128],
                                               ps[7][:, 0:128], 2.0, None, ALU.mult)
                    prog.step(vector=ev)
        stS.close()
        stA.close()

        stG = contextlib.ExitStack()
        gwb = alloc(nc, stG, sbufs=[("p4_gwb", (128, 8, 8), BF16)])["p4_gwb"]
        prog.step(gpsimd=lambda g: dma_g(g, [
            (gwb[:], gate_w.rearrange("(k p) e -> p k e", p=128)),
        ]))

        def gmm(t):
            last = None
            for tau in range(16):
                for k in range(8):
                    last = t.matmul(ps[0][:, tau * 8:(tau + 1) * 8],
                                    xn1[:, k, tau * 128:(tau + 1) * 128],
                                    gwb[:, k, :], start=(k == 0), stop=(k == 7))
            return last
        prog.step(tensor=gmm)

        prog.step(vector=lambda v: [v.tensor_reduce(mcol16[:, tau:tau + 1],
                                                    ps[0][:, tau * 8:(tau + 1) * 8],
                                                    mybir.AxisListType.X, ALU.max)
                                    for tau in range(16)][-1])

        prog.step(vector=lambda v: [v.tensor_scalar(glog[:, tau * 8:(tau + 1) * 8],
                                                    ps[0][:, tau * 8:(tau + 1) * 8],
                                                    mcol16[:, tau:tau + 1], None,
                                                    ALU.subtract)
                                    for tau in range(16)][-1])
        prog.step(scalar=lambda s: s.activation(glog[:], glog[:], AF.Exp))
        prog.step(vector=lambda v: [v.max(out=m8big[:, tau, :],
                                           in_=glog[:, tau * 8:(tau + 1) * 8])
                                    for tau in range(16)][-1])
        prog.step(vector=lambda v: [v.tensor_copy(m2col[:, tau:tau + 1],
                                                  m8big[:, tau, 1:2])
                                    for tau in range(16)][-1])
        prog.step(vector=lambda v: [v.tensor_tensor(mcol16[:, tau:tau + 1],
                                                    m8big[:, tau, 0:1],
                                                    m8big[:, tau, 1:2], ALU.add)
                                    for tau in range(16)][-1])
        prog.step(vector=lambda v: v.reciprocal(rcol16[:], mcol16[:]))
        prog.step(vector=lambda v: [v.tensor_scalar(cwsb[:, tau * 8:(tau + 1) * 8],
                                                    glog[:, tau * 8:(tau + 1) * 8],
                                                    m2col[:, tau:tau + 1], None, ALU.is_ge)
                                    for tau in range(16)][-1])
        prog.step(vector=lambda v: v.tensor_tensor(cwsb[:], cwsb[:], glog[:], ALU.mult))
        prog.step(vector=lambda v: [v.tensor_scalar(cwsb[:, tau * 8:(tau + 1) * 8],
                                                    cwsb[:, tau * 8:(tau + 1) * 8],
                                                    rcol16[:, tau:tau + 1], None, ALU.mult)
                                    for tau in range(16)][-1])
        stG.close()

        stM = contextlib.ExitStack()
        tM = alloc(nc, stM, sbufs=[
            ("p4_hbf", (128, 32, 1024), BF16),
            ("p4_w1c", (128, 4, 8, 128), BF16),
            ("p4_w2w", (128, 2, 8, 512), BF16),
            ("p4_b1s", (128, E * 32), F32),
            ("p4_b2t", (1, D), BF16),
            ("p4_ob", (1, 128), BF16),
        ])
        hbf, w1c, w2w, b1s, b2t = (tM["p4_hbf"], tM["p4_w1c"], tM["p4_w2w"],
                                   tM["p4_b1s"], tM["p4_b2t"])
        ones_bf = tM["p4_ob"]
        prog.step(sync=lambda e: dma_s(e, [(b1s[:], b1c)]),
                  vector=lambda v: v.memset(ones_bf[:], 1.0))

        for ex in range(E):
            prog.step(gpsimd=lambda g, e2=ex: dma_g(g, [
                (b2t[:], b2row[0:1, e2 * D:(e2 + 1) * D])]))
            for nn in range(2):
                tok0 = nn * 1024

                def ldw1(g, e2, hp, buf2):
                    return dma_g(g, [
                        (w1c[:, buf2 + c],
                         w1[e2, :, (2 * hp + c) * 128:(2 * hp + c + 1) * 128]
                         .rearrange("(k p) m -> p k m", p=128))
                        for c in range(2)])

                def ldw2(g, e2, nd, kw, buf):
                    r0 = kw * 1024
                    return dma_g(g, [(w2w[:, buf],
                                      w2[e2, r0:r0 + 1024, nd * 512:(nd + 1) * 512]
                                      .rearrange("(k p) m -> p k m", p=128))])
                if ex == 0 and nn == 0:
                    prog.step(gpsimd=lambda g, e2=ex: ldw1(g, e2, 0, 0))
                for hp in range(17):
                    def work(t, hp=hp, tok0=tok0):
                        if hp >= 16:
                            return None
                        last = None
                        for i in range(2):
                            for th in range(2):
                                for k in range(8):
                                    last = t.matmul(
                                        ps[(hp % 2) * 4 + i * 2 + th][:],
                                        w1c[:, (hp % 2) * 2 + i, k, :],
                                        xn1[:, k, tok0 + th * 512:tok0 + (th + 1) * 512],
                                        start=(k == 0), stop=(k == 7))
                        return last

                    def pre_g(g, e2=ex, hp=hp):
                        if hp == 15:
                            return ldw2(g, e2, 0, 0, 0)
                        if hp + 1 >= 16:
                            return None
                        return ldw1(g, e2, hp + 1, ((hp + 1) % 2) * 2)

                    def gel(s, e2=ex, hp=hp):
                        if hp == 0:
                            return None
                        b0 = ((hp - 1) % 2) * 4
                        last = None
                        for i in range(2):
                            hpp = 2 * (hp - 1) + i
                            s.activation(hbf[:, hpp, 0:512], ps[b0 + i * 2][:], AF.Gelu,
                                         bias=b1s[:, e2 * 32 + hpp:e2 * 32 + hpp + 1])
                            last = s.activation(hbf[:, hpp, 512:1024],
                                                ps[b0 + i * 2 + 1][:], AF.Gelu,
                                                bias=b1s[:, e2 * 32 + hpp:e2 * 32 + hpp + 1])
                        return last
                    prog.step(tensor=work, gpsimd=pre_g, scalar=gel)

                for nd in range(2):
                    for kw in range(4):
                        def mm2(t, kw=kw):
                            last = None
                            for ki in range(8):
                                k = kw * 8 + ki
                                for tm in range(8):
                                    last = t.matmul(ps[tm][:],
                                                    hbf[:, k, tm * 128:(tm + 1) * 128],
                                                    w2w[:, kw % 2, ki, :],
                                                    start=(k == 0), stop=False)
                            return last

                        def pw2_g(g, e2=ex, kw=kw, nd=nd, nn=nn):
                            if kw + 1 < 4:
                                return ldw2(g, e2, nd, kw + 1, (kw + 1) % 2)
                            if nd == 0:
                                return ldw2(g, e2, 1, 0, 0)
                            if nn == 0:
                                return ldw1(g, e2, 0, 0)
                            if e2 + 1 < E:
                                return ldw1(g, e2 + 1, 0, 0)
                            return None
                        prog.step(tensor=mm2, gpsimd=pw2_g)

                    def b2mm(t, nd=nd):
                        last = None
                        for tm in range(8):
                            last = t.matmul(ps[tm][:], ones_bf[0:1, :],
                                            b2t[0:1, nd * 512:(nd + 1) * 512],
                                            start=False, stop=True)
                        return last
                    prog.step(tensor=b2mm)

                    def moevac(v, e2=ex, nn=nn, nd=nd):
                        last = None
                        for tm in range(8):
                            tau = nn * 8 + tm
                            last = v.scalar_tensor_tensor(
                                BIGA[:, tau, nd * 512:(nd + 1) * 512], ps[tm][:],
                                cwsb[:, tau * 8 + e2:tau * 8 + e2 + 1],
                                BIGA[:, tau, nd * 512:(nd + 1) * 512], ALU.mult, ALU.add)
                        return last
                    prog.step(vector=moevac)
        stM.close()
        st_xn1.close()

        stQ = contextlib.ExitStack()
        tQ = alloc(nc, stQ, sbufs=[
            ("p4_x2bf", (128, 8, TOK), BF16),
            ("p4_wat", (128, 8, 3 * D), BF16),
            ("p4_tmq", (128, 1024), F32),
            ("p4_stg", (128, 8, 128), F32),
            ("p4_stgb", (128, 4, 512), BF16),
        ])
        x2bf, wat, tmq, stg, stgb = (tQ["p4_x2bf"], tQ["p4_wat"], tQ["p4_tmq"],
                                     tQ["p4_stg"], tQ["p4_stgb"])
        prog.step(gpsimd=lambda g: dma_g(g, [
            (wat[:], wattnT.rearrange("(k p) m -> p k m", p=128)),
        ]))

        for tau in range(16):
            def lnst(s, tau=tau):
                return s.activation(tmq[:], BIGA[:, tau, :], AF.Square,
                                    accum_out=mcol[:, 1:2])
            prog.step(scalar=lnst)

            def lnred(v, tau=tau):
                v.tensor_reduce(mcol[:, 0:1], BIGA[:, tau, :], mybir.AxisListType.X, ALU.add)
                v.tensor_scalar(mcol[:, 0:1], mcol[:, 0:1], 1.0 / D, None, ALU.mult)
                v.tensor_scalar(mcol[:, 1:2], mcol[:, 1:2], 1.0 / D, None, ALU.mult)
                v.tensor_tensor(mcol[:, 2:3], mcol[:, 0:1], mcol[:, 0:1], ALU.mult)
                v.tensor_tensor(mcol[:, 1:2], mcol[:, 1:2], mcol[:, 2:3], ALU.subtract)
                return v.tensor_scalar(mcol[:, 3:4], mcol[:, 0:1], -1.0, None, ALU.mult)
            prog.step(vector=lnred)

            def lnrs(s):
                s.activation(mcol[:, 1:2], mcol[:, 1:2], AF.Ln, bias=consts[:, 0:1])
                return s.activation(mcol[:, 1:2], mcol[:, 1:2], AF.Exp, scale=-0.5)
            prog.step(scalar=lnrs)

            def lnap(v, tau=tau):
                return v.tensor_scalar(tmq[:], BIGA[:, tau, :], mcol[:, 3:4],
                                       mcol[:, 1:2], ALU.add, ALU.mult)
            prog.step(vector=lnap)

            for h in range(4):
                def trx(t, h=h):
                    t.transpose(ps[6][:, 0:128], tmq[:, 256 * h:256 * h + 128], idn[:])
                    return t.transpose(ps[7][:, 0:128],
                                       tmq[:, 256 * h + 128:256 * h + 256], idn[:])
                prog.step(tensor=trx)

                def evx(v, h=h, tau=tau):
                    v.tensor_copy(stg[:, 2 * h, :], ps[6][:, 0:128])
                    v.tensor_copy(stg[:, 2 * h + 1, :], ps[7][:, 0:128])
                    v.tensor_copy(x2bf[:, 2 * h, tau * 128:(tau + 1) * 128], ps[6][:, 0:128])
                    return v.tensor_copy(x2bf[:, 2 * h + 1, tau * 128:(tau + 1) * 128],
                                         ps[7][:, 0:128])
                prog.step(vector=evx)

            prog.step(sync=lambda e, tau=tau: dma_s(e, [
                (x2_scr[:, tau * 128:(tau + 1) * 128]
                 .rearrange("(j p) t -> p j t", p=128), stg[:])]))

        for n in range(4):
            n_sl = slice(n * 512, (n + 1) * 512)
            for mw in range(4):
                def qmm(t, mw=mw, n_sl=n_sl):
                    last = None
                    for mi in range(4):
                        m = mw * 4 + mi
                        for k in range(8):
                            last = t.matmul(ps[mi][:], wat[:, k, m * 128:(m + 1) * 128],
                                            x2bf[:, k, n_sl], start=(k == 0), stop=(k == 7))
                    return last
                prog.step(tensor=qmm)

                def qev(v):
                    last = None
                    for mi in range(4):
                        last = v.tensor_copy(stgb[:, mi, :], ps[mi][:])
                    return last
                prog.step(vector=qev)

                def qst(e, mw=mw, n_sl=n_sl):
                    trs = []
                    for mi in range(4):
                        for half in range(2):
                            hraw = 8 * mw + 2 * mi + half
                            if hraw < 16:
                                slot, sub = hraw // 4, hraw % 4
                            else:
                                hk = hraw - 16
                                slot, sub = hk // 4, 4 + hk % 4
                            trs.append((qk_loc[slot, sub, :, n_sl],
                                        stgb[64 * half:64 * half + 64, mi, :]))
                    return dma_s(e, trs)
                prog.step(sync=qst)

        for tw in range(4):
            for nv in range(2):
                def vmm(t, tw=tw, nv=nv):
                    last = None
                    for ti in range(4):
                        tau = tw * 4 + ti
                        for k in range(8):
                            last = t.matmul(ps[ti][:], x2bf[:, k, tau * 128:(tau + 1) * 128],
                                            wat[:, k, 2048 + nv * 512:2048 + (nv + 1) * 512],
                                            start=(k == 0), stop=(k == 7))
                    return last
                prog.step(tensor=vmm)

                def vev(v):
                    last = None
                    for ti in range(4):
                        last = v.tensor_copy(stgb[:, ti, :], ps[ti][:])
                    return last
                prog.step(vector=vev)

                def vst(e, tw=tw, nv=nv):
                    trs = []
                    for hh in range(8):
                        h = 8 * nv + hh
                        trs.append((v_loc[h, :, 4 * tw:4 * tw + 4, :],
                                    stgb[:, :, 64 * hh:64 * hh + 64]))
                    return dma_s(e, trs)
                prog.step(sync=vst)
        stQ.close()
        st4.close()

        cc("AllGather", ALU.bypass, qk_loc_h, qk_all_h)
        cc("AllGather", ALU.bypass, v_loc_h, v_all_h)

        def qkpick(e):
            qk_flat = qk_all.rearrange("a b c d -> a (b c d)")
            qk_p_flat = qk_p.rearrange("a b c d -> a (b c d)")
            return dma_s(e, [(qk_p_flat[ji:ji + 1, :],
                              qk_flat[bass.ts(dyn["qv"][ji], 1), :])
                             for ji in range(4)])

        def vpick(s):
            v_flat = v_all.rearrange("(a b) p k c -> a (b p k c)", b=4)
            v_p_flat = v_p.rearrange("(a b) p k c -> a (b p k c)", b=4)
            return dma_c(s, [(v_p_flat[ji:ji + 1, :],
                              v_flat[bass.ts(dyn["qvc"][ji], 1), :])
                             for ji in range(4)])
        prog.step(sync=qkpick, scalar=vpick)

        st5 = contextlib.ExitStack()
        t5 = alloc(nc, st5, sbufs=[
            ("p5_q2", (128, S), BF16), ("p5_k2", (128, S), BF16),
            ("p5_vA", (128, 64, 65), BF16), ("p5_vB", (128, 64, 65), BF16),
            ("p5_msk", (128, 4, 512), F32),
            ("p5_onesb", (128, 2), BF16),
            ("p5_ebf", (128, 2, 3, 512), BF16),
            ("p5_etmp", (128, 2, 2, 512), F32),
            ("p5_onesT", (65, 64), F32),
            ("p5_sufA", (65, 16), F32), ("p5_sufB", (65, 16), F32),
            ("p5_pref", (65, 64), F32), ("p5_tcol", (65, 1), F32),
            ("p5_yv", (65, 2, 512), F32),
            ("p5_rrow", (1, 2, 512), F32),
            ("p5_ystg", (64, 2, S), BF16),
        ])
        q2, k2, vA, vB, msk = (t5["p5_q2"], t5["p5_k2"], t5["p5_vA"], t5["p5_vB"],
                               t5["p5_msk"])
        onesb, ebf, etmp, onesT = (t5["p5_onesb"], t5["p5_ebf"], t5["p5_etmp"],
                                   t5["p5_onesT"])
        sufA, sufB, pref, tcol = t5["p5_sufA"], t5["p5_sufB"], t5["p5_pref"], t5["p5_tcol"]
        yv, rrow, ystg = t5["p5_yv"], t5["p5_rrow"], t5["p5_ystg"]
        psS = [[ps[0], ps[1]], [ps[2], ps[3]]]
        pyP = [ps[4][0:65, :], ps[5][0:65, :]]
        pxt = ps[6]

        prog.step(sync=lambda e: dma_s(e, [
            (msk[:], dmask.rearrange("(r p) f -> p r f", p=128)),
        ]), vector=lambda v: (v.memset(onesT[:], 1.0), v.memset(vA[:], 1.0),
                              v.memset(vB[:], 1.0), v.memset(onesb[:], 1.0))[-1])

        for pg in range(2):
            pA, pB = 2 * pg, 2 * pg + 1

            def ldq(e, pA=pA, pB=pB):
                trs = []
                for ji in range(4):
                    c_sl = slice(2048 * ji, 2048 * (ji + 1))
                    trs.append((q2[0:64, c_sl], qk_p[ji, pA, :, :]))
                    trs.append((q2[64:128, c_sl], qk_p[ji, pB, :, :]))
                return dma_s(e, trs)

            def ldk(g, pA=pA, pB=pB):
                trs = []
                for ji in range(4):
                    c_sl = slice(2048 * ji, 2048 * (ji + 1))
                    trs.append((k2[0:64, c_sl], qk_p[ji, 4 + pA, :, :]))
                    trs.append((k2[64:128, c_sl], qk_p[ji, 4 + pB, :, :]))
                return dma_g(g, trs)

            def ldv(s, pA=pA, pB=pB):
                trs = []
                for ji in range(4):
                    trs.append((vA[:, 16 * ji:16 * ji + 16, 0:64],
                                v_p[4 * ji + pA, :, :, :]))
                    trs.append((vB[:, 16 * ji:16 * ji + 16, 0:64],
                                v_p[4 * ji + pB, :, :, :]))
                return dma_c(s, trs)
            prog.step(sync=ldq, gpsimd=ldk, scalar=ldv)

            for pi, vt, suf in ((0, vA, sufA), (1, vB, sufB)):
                def tsum(t, vt=vt):
                    last = None
                    for kt in range(64):
                        last = t.matmul(pxt[0:65, kt:kt + 1], vt[:, kt, :],
                                        onesb[:, 0:1], start=True, stop=True)
                    return last
                prog.step(tensor=tsum)
                prog.step(vector=lambda v: v.tensor_tensor_scan(
                    pref[:], onesT[:], pxt[0:65, 0:64], 0.0, ALU.mult, ALU.add))
                prog.step(vector=lambda v: v.tensor_scalar(
                    tcol[:], pref[:, 63:64], EM10, None, ALU.mult))
                prog.step(vector=lambda v, suf=suf: v.tensor_scalar(
                    suf[:], pref[:, 3:64:4], -EM10, tcol[:, 0:1], ALU.mult, ALU.add))

            for qt in range(16):
                q_sl = slice(qt * 512, (qt + 1) * 512)
                nkt = 4 * qt + 4
                for s in range(nkt + 3):

                    def work_t(t, s=s, qt=qt, nkt=nkt, q_sl=q_sl):
                        last = None
                        if s < nkt:
                            sl = s % 2
                            k_sl = slice(s * 128, (s + 1) * 128)
                            t.matmul(psS[0][sl][:], k2[0:64, k_sl], q2[0:64, q_sl],
                                     start=True, stop=True, tile_position=(0, 0))
                            last = t.matmul(psS[1][sl][:], k2[64:128, k_sl],
                                            q2[64:128, q_sl], start=True, stop=True,
                                            tile_position=(64, 0))
                        if s >= 3:
                            kt = s - 3
                            sl3 = kt % 3
                            for pi, vt in ((0, vA), (1, vB)):
                                last = t.matmul(pyP[pi], vt[:, kt, :],
                                                ebf[:, pi, sl3, :],
                                                start=(kt == 0), stop=(kt == nkt - 1))
                        return last

                    def work_v(v, s=s, qt=qt, nkt=nkt):
                        kt = s - 1
                        if kt < 0 or kt >= nkt or kt < 4 * qt:
                            return None
                        sl = kt % 2
                        rel = kt - 4 * qt
                        last = None
                        for pi in range(2):
                            v.tensor_scalar(etmp[:, pi, sl, :], psS[pi][sl][:],
                                            0.125, 10.0, ALU.mult, ALU.add)
                            last = v.tensor_tensor(etmp[:, pi, sl, :],
                                                   etmp[:, pi, sl, :],
                                                   msk[:, rel, :], ALU.mult)
                        return last

                    def work_s(sc, s=s, qt=qt, nkt=nkt):
                        last = None
                        ktn = s - 1
                        if 0 <= ktn < nkt and ktn < 4 * qt:
                            for pi in range(2):
                                last = sc.activation(ebf[:, pi, ktn % 3, :],
                                                     psS[pi][ktn % 2][:], AF.Exp,
                                                     scale=0.125)
                        ktd = s - 2
                        if 0 <= ktd < nkt and ktd >= 4 * qt:
                            for pi in range(2):
                                last = sc.activation(ebf[:, pi, ktd % 3, :],
                                                     etmp[:, pi, ktd % 2, :], AF.Exp,
                                                     bias=consts[:, 1:2])
                        return last
                    prog.step(tensor=work_t, vector=work_v, scalar=work_s)

                def ev1(v, qt=qt):
                    v.tensor_scalar(yv[:, 0, :], pyP[0], sufA[:, qt:qt + 1],
                                    None, ALU.add)
                    v.tensor_scalar(yv[:, 1, :], pyP[1], sufB[:, qt:qt + 1],
                                    None, ALU.add)
                    v.reciprocal(rrow[0:1, 0, :], yv[64:65, 0, :])
                    return v.reciprocal(rrow[0:1, 1, :], yv[64:65, 1, :])
                prog.step(vector=ev1)

                def bc5(t):
                    t.matmul(pxt[0:64, :], onesT[0:1, 0:64], rrow[0:1, 0, :],
                             start=True, stop=True)
                    return t.matmul(ps[7][0:64, :], onesT[0:1, 0:64], rrow[0:1, 1, :],
                                    start=True, stop=True)
                prog.step(tensor=bc5)

                def mu5(v, q_sl=q_sl):
                    v.tensor_tensor(ystg[:, 0, q_sl], yv[0:64, 0, :], pxt[0:64, :],
                                    ALU.mult)
                    return v.tensor_tensor(ystg[:, 1, q_sl], yv[0:64, 1, :],
                                           ps[7][0:64, :], ALU.mult)
                prog.step(vector=mu5)

            prog.step(sync=lambda e, pA=pA, pB=pB: dma_s(e, [
                (yat_loc[:, 64 * pA:64 * pA + 64, :].rearrange("u p t -> p u t"),
                 ystg[:, 0, :].rearrange("p (u t) -> p u t", u=4)),
                (yat_loc[:, 64 * pB:64 * pB + 64, :].rearrange("u p t -> p u t"),
                 ystg[:, 1, :].rearrange("p (u t) -> p u t", u=4)),
            ]))
        st5.close()

        cc("AllGather", ALU.bypass, yat_loc_h, yat_all_h)

        def yatpick_s(e):
            return dma_s(e, [(yat_cols[256 * ji:256 * (ji + 1), :],
                              yat_all[bass.ts(dyn["qv"][ji], 1), :, :])
                             for ji in range(2)])

        def yatpick_c(s):
            return dma_c(s, [(yat_cols[256 * ji:256 * (ji + 1), :],
                              yat_all[bass.ts(dyn["qvc"][ji], 1), :, :])
                             for ji in range(2, 4)])
        prog.step(sync=yatpick_s, scalar=yatpick_c)

        st6 = contextlib.ExitStack()
        t6 = alloc(nc, st6, sbufs=[
            ("p6_sx", (128, 8, TOK), F32),
            ("p6_x3b", (128, 8, TOK), BF16),
            ("p6_bx3", (128, 8, TOK), F32),
            ("p6_tmp", (128, 512), F32),
            ("p6_wcast", (128, 8, 512), BF16),
            ("p6_xs2", (128, 4, 512), F32),
            ("p6_sq", (128, 1, 512), F32),
            ("p6_mv", (1, 4 * 512), F32),
            ("p6_bcols", (128, 9), F32),
            ("p6_lcols", (128, 33), F32),
            ("p6_lrow", (1, 8), F32),
            ("p6_h1", (128, 2, 512), BF16),
            ("p6_wd1c", (128, 8, 256), BF16),
            ("p6_wd2c", (128, 2, 1024), BF16),
            ("p6_bd1s", (128, 16), F32),
            ("p6_bd2s", (128, 64), F32),
            ("p6_yb", (128, 8, 512), BF16),
        ])
        sx, x3b, bx3 = t6["p6_sx"], t6["p6_x3b"], t6["p6_bx3"]
        tmp6 = t6["p6_tmp"]
        wcast, xs2, sq6, mv6 = t6["p6_wcast"], t6["p6_xs2"], t6["p6_sq"], t6["p6_mv"]
        bcols, lcols, lrow = t6["p6_bcols"], t6["p6_lcols"], t6["p6_lrow"]
        h1, wd1c, wd2c, bd1s, bd2s, yb6 = (t6["p6_h1"], t6["p6_wd1c"], t6["p6_wd2c"],
                                           t6["p6_bd1s"], t6["p6_bd2s"], t6["p6_yb"])

        prog.step(sync=lambda e: dma_s(e, [
            (bcols[:, 0:8], bbase), (bd1s[:], bd1c), (bd2s[:], bd2c),
        ]))

        for n in range(4):
            n_sl = slice(n * 512, (n + 1) * 512)

            prog.step(sync=lambda e, n_sl=n_sl: dma_s(e, [
                (yb6[:], yat_cols[:, n_sl].rearrange("(k p) t -> p k t", p=128)),
            ]))
            for mw in range(2):
                prog.step(sync=lambda e, mw=mw, n_sl=n_sl: dma_s(e, [
                    (xs2[:], x2_scr[:, n_sl].rearrange("(j p) t -> p j t", p=128)
                     [:, 4 * mw:4 * mw + 4, :]),
                ]), gpsimd=lambda g, mw=mw: dma_g(g, [
                    (wcast[:], wprojT[:, mw * 512:(mw + 1) * 512]
                     .rearrange("(k p) m -> p k m", p=128)),
                ]))

                def mms6(t):
                    last = None
                    for mi in range(4):
                        for k in range(8):
                            last = t.matmul(ps[mi][:], wcast[:, k, mi * 128:(mi + 1) * 128],
                                            yb6[:, k, :], start=(k == 0), stop=(k == 7))
                    return last

                def evac6(v, mw=mw, n_sl=n_sl):
                    last = None
                    for mi in range(4):
                        m = mw * 4 + mi
                        last = v.scalar_tensor_tensor(sx[:, m, n_sl], xs2[:, mi, :], 1.0,
                                                      ps[mi][:], ALU.mult, ALU.add)
                    return last
                prog.step(tensor=mms6)
                prog.step(vector=evac6)

        def ln3_out(j, n, n_sl, v, x_ap, ps_a, ps_b):
            v.tensor_tensor(tmp6[:], x_ap, ps_a[:], ALU.mult)
            v.tensor_tensor(tmp6[:], tmp6[:], ps_b[:], ALU.add)
            v.tensor_copy(x3b[:, j, n_sl], tmp6[:])
            return v.tensor_copy(sx[:, j, n_sl], tmp6[:])
        emit_ln_fused(prog, lambda j, n_sl: sx[:, j, n_sl], ln3_out, sq6,
                      ps[0], ps[1], ps[2], ps[3], ones, mv6, consts)
        prog.step(sync=lambda e: dma_s(e, [
            (x3b_scr.rearrange("(j p) t -> p j t", p=128), x3b[:]),
        ]))

        for n in range(4):
            n_sl = slice(n * 512, (n + 1) * 512)
            for mw in range(2):
                prog.step(gpsimd=lambda g, mw=mw: dma_g(g, [
                    (wcast[:], Wb[:, mw * 512:(mw + 1) * 512]
                     .rearrange("(k p) m -> p k m", p=128)),
                ]))

                def mmb(t, mw=mw, n_sl=n_sl):
                    last = None
                    for mi in range(4):
                        for k in range(8):
                            last = t.matmul(ps[mi][:], wcast[:, k, mi * 128:(mi + 1) * 128],
                                            x3b[:, k, n_sl], start=(k == 0), stop=(k == 7))
                    return last

                def evb(v, mw=mw, n_sl=n_sl):
                    last = None
                    for mi in range(4):
                        m = mw * 4 + mi
                        v.tensor_scalar(tmp6[:], ps[mi][:], bcols[:, m:m + 1], None, ALU.add)
                        last = v.tensor_tensor(bx3[:, m, n_sl], tmp6[:], sx[:, m, n_sl],
                                               ALU.add)
                    return last
                prog.step(tensor=mmb)
                prog.step(vector=evb)
        prog.step(sync=lambda e: dma_s(e, [
            (bx3_scr.rearrange("(j p) t -> p j t", p=128), bx3[:]),
        ]))

        for e_ in range(E):
            prog.step(gpsimd=lambda g, e_=e_: dma_g(g, [
                (wd1c[:], Wd1[e_].rearrange("(k p) m -> p k m", p=128)),
                (wd2c[:], Wd2[e_].rearrange("(k p) m -> p k m", p=128)),
            ]))
            for n in range(4):
                n_sl = slice(n * 512, (n + 1) * 512)

                def mh(t, n_sl=n_sl):
                    last = None
                    for mi in range(2):
                        for k in range(8):
                            last = t.matmul(ps[6 - mi][:], wd1c[:, k, mi * 128:(mi + 1) * 128],
                                            x3b[:, k, n_sl], start=(k == 0), stop=(k == 7))
                    return last
                prog.step(tensor=mh)

                def eh(s_, e_=e_):
                    s_.activation(h1[:, 0, :], ps[6][:], AF.Relu,
                                  bias=bd1s[:, 2 * e_:2 * e_ + 1])
                    return s_.activation(h1[:, 1, :], ps[5][:], AF.Relu,
                                         bias=bd1s[:, 2 * e_ + 1:2 * e_ + 2])
                prog.step(scalar=eh)
                for mw in range(2):
                    def mr(t, mw=mw):
                        last = None
                        for mi in range(4):
                            for k in range(2):
                                last = t.matmul(
                                    ps[mi][:],
                                    wd2c[:, k, mw * 512 + mi * 128:mw * 512 + (mi + 1) * 128],
                                    h1[:, k, :], start=(k == 0), stop=(k == 1))
                        return last
                    prog.step(tensor=mr)

                    def dv2(v, mw=mw, n_sl=n_sl, e_=e_):
                        last = None
                        for mi in range(4):
                            m = mw * 4 + mi
                            last = v.scalar_tensor_tensor(
                                xs2[:, mi, :], ps[mi][:], bd2s[:, e_ * 8 + m:e_ * 8 + m + 1],
                                sx[:, m, n_sl], ALU.add, ALU.subtract)
                        return last
                    prog.step(vector=dv2)

                    def sqa(s_, n=n, mw=mw):
                        last = None
                        for mi in range(4):
                            ci = (n * 2 + mw) * 4 + mi
                            last = s_.activation(tmp6[:], xs2[:, mi, :], AF.Square,
                                                 accum_out=lcols[:, ci:ci + 1])
                        return last
                    prog.step(scalar=sqa)
            prog.step(vector=lambda v: v.tensor_reduce(
                lcols[:, 32:33], lcols[:, 0:32], mybir.AxisListType.X, ALU.add))
            prog.step(tensor=lambda t: t.matmul(ps[4][0:1, 0:1], ones[:, 0:1],
                                                lcols[:, 32:33], start=True, stop=True))
            prog.step(vector=lambda v, e_=e_: v.tensor_copy(lrow[0:1, e_:e_ + 1],
                                                            ps[4][0:1, 0:1]))
        prog.step(sync=lambda e: dma_s(e, [(lrow_loc, lrow[:])]))
        st6.close()

        cc("AllGather", ALU.bypass, lrow_loc_h, lparts_all_h)

        st7 = contextlib.ExitStack()
        t7 = alloc(nc, st7, sbufs=[
            ("p7_bx3s", (128, 8, TOK), F32),
            ("p7_x3bs", (128, 8, TOK), BF16),
            ("p7_a_all", (64, 8, TOK), BF16),
            ("p7_wauc", (64, 8, D), BF16),
            ("p7_wadc", (128, 8, 64), BF16),
            ("p7_lp", (8, 8), F32),
            ("p7_selb", (8, 2), F32),
            ("p7_baups", (8, D), F32),
            ("p7_badcs", (64, 8), F32),
            ("p7_lrow", (1, 8), F32),
            ("p7_mcol", (1, 2), F32),
            ("p7_ind", (1, 8), F32),
            ("p7_indc8", (8, 1), F32),
            ("p7_ind64", (64, 8), F32),
            ("p7_bcol", (128, 8), F32),
            ("p7_tmp", (128, 512), F32),
            ("p7_sq", (128, 2, 512), F32),
            ("p7_mv", (1, 4 * 512), F32),
            ("p7_ostg", (128, 8, 512), F32),
        ])
        bx3s, x3bs, a_all, wauc, wadc = (t7["p7_bx3s"], t7["p7_x3bs"], t7["p7_a_all"],
                                         t7["p7_wauc"], t7["p7_wadc"])
        lp, sel7, baups, badcs = t7["p7_lp"], t7["p7_selb"], t7["p7_baups"], t7["p7_badcs"]
        lrow7, mcol7, ind, indc8, ind64, bcol = (t7["p7_lrow"], t7["p7_mcol"], t7["p7_ind"],
                                                 t7["p7_indc8"], t7["p7_ind64"],
                                                 t7["p7_bcol"])
        tmp7, sq7, mv7, ostg = t7["p7_tmp"], t7["p7_sq"], t7["p7_mv"], t7["p7_ostg"]

        prog.step(sync=lambda e: dma_s(e, [
            (lp[:], lparts_all), (sel7[:], selb), (baups[:], baup8), (badcs[:], badc),
            (bx3s[:, 0:4, :], bx3_scr[0:512, :].rearrange("(j p) t -> p j t", p=128)),
        ]), scalar=lambda s: dma_c(s, [
            (bx3s[:, 4:8, :], bx3_scr[512:1024, :].rearrange("(j p) t -> p j t", p=128)),
            (x3bs[:], x3b_scr.rearrange("(j p) t -> p j t", p=128)),
        ]), gpsimd=lambda g: dma_g(g, [
            (wauc[:], Wau.rearrange("e k m -> k e m")),
        ]))

        prog.step(tensor=lambda t: t.matmul(ps[0][0:2, 0:8], sel7[:], lp[:],
                                            start=True, stop=True))
        prog.step(vector=lambda v: v.tensor_copy(lp[0:2, :], ps[0][0:2, 0:8]))
        prog.step(sync=lambda e: dma_s(e, [(l2scr[:], lp[0:2, :])]))

        def ownrow(e):
            return dma_s(e, [(lrow7[:], l2scr[bass.ts(dyn["r4"], 1), :])])
        prog.step(sync=ownrow)

        prog.step(vector=lambda v: v.tensor_reduce(mcol7[0:1, 0:1], lrow7[:],
                                                   mybir.AxisListType.X, ALU.min))
        prog.step(vector=lambda v: v.tensor_scalar(ind[:], lrow7[:], mcol7[0:1, 0:1],
                                                   None, ALU.is_le))
        prog.step(tensor=lambda t: (
            t.matmul(ps[0][0:8, 0:1], ind[:], ones[0:1, 0:1], start=True, stop=True),
            t.matmul(ps[1][0:64, 0:8], ones[0:1, 0:64], ind[:], start=True, stop=True))[-1])
        prog.step(vector=lambda v: (v.tensor_copy(indc8[:], ps[0][0:8, 0:1]),
                                    v.tensor_copy(ind64[:], ps[1][0:64, 0:8]))[-1])
        prog.step(tensor=lambda t: [t.matmul(ps[2][0:128, m:m + 1],
                                             baups[:, m * 128:(m + 1) * 128],
                                             indc8[:], start=True, stop=True)
                                    for m in range(8)][-1])
        prog.step(vector=lambda v: v.tensor_copy(bcol[:], ps[2][:, 0:8]))

        for e_ in range(E):
            prog.step(gpsimd=lambda g, e_=e_: dma_g(g, [
                (wadc[:], Wad[e_].rearrange("(k p) m -> p k m", p=128)),
            ]))
            for n in range(4):
                def ma(t, n=n):
                    last = None
                    for k in range(8):
                        last = t.matmul(ps[0][0:64, :], wadc[:, k, :],
                                        x3bs[:, k, n * 512:(n + 1) * 512],
                                        start=(k == 0), stop=(k == 7))
                    return last
                prog.step(tensor=ma)
                prog.step(scalar=lambda s, e_=e_, n=n: s.activation(
                    a_all[:, e_, n * 512:(n + 1) * 512], ps[0][0:64, :], AF.Relu,
                    bias=badcs[:, e_:e_ + 1]))
            prog.step(vector=lambda v, e_=e_: v.tensor_scalar(
                a_all[:, e_, :], a_all[:, e_, :], ind64[:, e_:e_ + 1], None, ALU.mult))

        for n in range(4):
            n_sl = slice(n * 512, (n + 1) * 512)
            for mw in range(2):
                def mu_(t, mw=mw, n_sl=n_sl):
                    last = None
                    for mi in range(4):
                        m = mw * 4 + mi
                        for e_ in range(E):
                            last = t.matmul(ps[mi][:], wauc[:, e_, m * 128:(m + 1) * 128],
                                            a_all[:, e_, n_sl],
                                            start=(e_ == 0), stop=(e_ == 7))
                    return last
                prog.step(tensor=mu_)

                def evu(v, mw=mw, n_sl=n_sl):
                    last = None
                    for mi in range(4):
                        m = mw * 4 + mi
                        v.tensor_scalar(tmp7[:], ps[mi][:], bcol[:, m:m + 1], None, ALU.add)
                        last = v.tensor_tensor(bx3s[:, m, n_sl], tmp7[:], bx3s[:, m, n_sl],
                                               ALU.add)
                    return last
                prog.step(vector=evu)

        def ln4_out(j, n, n_sl, v, x_ap, ps_a, ps_b):
            v.tensor_tensor(ostg[:, j, :], x_ap, ps_a[:], ALU.mult)
            return v.tensor_tensor(ostg[:, j, :], ostg[:, j, :], ps_b[:], ALU.add)

        def ln4_store(prog_, n, n_sl):
            prog_.step(sync=lambda e, n_sl=n_sl: dma_s(e, [
                (out_d[:, n_sl].rearrange("(j p) t -> p j t", p=128), ostg[:])]))
        emit_ln_fused(prog, lambda j, n_sl: bx3s[:, j, n_sl], ln4_out, sq7,
                      ps[0], ps[1], ps[2], ps[3], ones, mv7, consts,
                      post_chunk=ln4_store)
        st7.close()
    st0.close()
    return nc



def kernel(**I):
    P = host_prep(I)
    ident = np.eye(128, dtype=np.float32)
    cores = list(range(NC))

    nc = bass.Bass("TRN2", target_bir_lowering=False, debug=False)
    build_full(nc)
    in_maps = []
    for c in cores:
        in_maps.append({
            "x_fm": P["x_fm"][c], "lng": P["lng"], "lnb": P["lnb"],
            "inwT": P["inwT"][c], "convw": P["convw"][c], "convb": P["convb"][c],
            "xprojT": P["xprojT"][c], "A_log": P["A_log"], "dtw": P["dtw"][c],
            "dtb": P["dtb"][c], "Dc": P["Dc"][c], "sel": P["sel"],
            "outwT": P["outwT"], "gamma": P["gamma"], "beta": P["beta"],
            "gate_w": P["gate_w"], "w1": P["w1"], "b1c": P["b1c"],
            "w2": P["w2"], "b2row": P["b2row"], "wattnT": P["wattnT"],
            "ident": ident, "dmask": P["dmask"], "wprojT": P["wprojT"],
            "W_base": P["W_base"], "bbase": P["bbase"], "Wd1": P["Wd1"],
            "bd1c": P["bd1c"], "Wd2": P["Wd2"], "bd2c": P["bd2c"],
            "selb": P["selb"], "badc": P["badc"], "Wad": P["Wad"],
            "Wau": P["Wau"], "baup8": P["baup8"]})
    r = run_bass_kernel_spmd(nc, in_maps, core_ids=cores).results

    out = np.empty((B * S, D), np.float32)
    for c in cores:
        out[c * TOK:(c + 1) * TOK] = np.asarray(r[c]["out_slab"], np.float32).T
    return out.reshape(B, S, D)
